# version 02
# baseline (speedup 1.0000x reference)
import numpy as np
import ml_dtypes
from contextlib import ExitStack

import concourse.bass as bass
import concourse.mybir as mybir
from concourse.bass_utils import run_bass_kernel_spmd

F32 = mybir.dt.float32
BF16 = mybir.dt.bfloat16
AF = mybir.ActivationFunctionType
ALU = mybir.AluOpType
AX = mybir.AxisListType

D = 1024
F_CUT = 99
DEV_DUMP = False
EPS0 = 1e-6
LNEPS = 1e-5

ENGS = ("sync", "gpsimd", "vector", "scalar", "tensor")


class PB:

    ENG_ORD = {"vector": "zzv", "scalar": "zzs", "tensor": "zzt"}
    MILESTONE_ENG = {"dv": "vector", "ac": "scalar", "pe": "tensor"}

    def __init__(self, nc, paranoid=False):
        self.nc = nc
        self.paranoid = paranoid
        self.streams = {e: [] for e in ENGS}
        self.cnt = {}
        self._sems = {}
        self.dma_sem_eng = {}

    def sem(self, name):
        if name not in self._sems:
            self._sems[name] = self.nc.alloc_semaphore(name)
            self.cnt[name] = 0
        return self._sems[name]

    def do(self, eng, fn, inc=None, n=1):
        if self.paranoid and eng in self.ENG_ORD:
            z = self.ENG_ORD[eng]
            sem = self.sem(z)
            prev = self.cnt[z]
            self.cnt[z] = prev + 1
            tgt = prev + 1
            if prev > 0:
                self.streams[eng].append(
                    lambda e, sem=sem, prev=prev: e.wait_ge(sem, prev)
                )

            def wrapped(e, fn=fn, sem=sem):
                fn(e).then_inc(sem, 1)

            self.streams[eng].append(wrapped)
            return (z, tgt)
        if inc is None:
            self.streams[eng].append(fn)
            return None
        sem = self.sem(inc)
        self.cnt[inc] += n
        tgt = self.cnt[inc]

        def wrapped(e, fn=fn, sem=sem, n=n):
            fn(e).then_inc(sem, n)

        self.streams[eng].append(wrapped)
        return (inc, tgt)

    def wait(self, eng, ms):
        if ms is None:
            return
        name, val = ms
        if val <= 0:
            return
        sem = self.sem(name)
        self.streams[eng].append(lambda e, sem=sem, val=val: e.wait_ge(sem, val))

    def dma(self, eng, out, in_, sem):
        prev = self.dma_sem_eng.get(sem)
        assert prev is None or prev == eng, f"dma sem {sem} used from {prev} and {eng}"
        self.dma_sem_eng[sem] = eng
        return self.do(
            eng, lambda e, out=out, in_=in_: e.dma_start(out=out, in_=in_), inc=sem, n=16
        )

    def now(self, name):
        if self.paranoid and name in self.MILESTONE_ENG:
            z = self.ENG_ORD[self.MILESTONE_ENG[name]]
            return (z, self.cnt.get(z, 0))
        return (name, self.cnt.get(name, 0))

    def barrier(self):
        bar = self.sem("bar")
        for s, eng in self.dma_sem_eng.items():
            self.wait(eng, (s, self.cnt[s]))
        self.cnt["bar"] += len(ENGS)
        tgt = self.cnt["bar"]
        for eng in ENGS:
            self.streams[eng].append(lambda e, bar=bar: e.sem_inc(bar, 1))
            self.streams[eng].append(lambda e, bar=bar, tgt=tgt: e.wait_ge(bar, tgt))

    def flush(self, block):
        for eng in ENGS:
            lst = self.streams[eng]
            if not lst:
                continue
            getattr(block, eng)(lambda e, lst=lst: [f(e) for f in lst])
            self.streams[eng] = []



def emit_ln(pb, c, xin_ap, out_ap, junk, wst, psum_in=False):
    pb.wait("scalar", pb.now("dv"))
    acs = pb.do(
        "scalar",
        lambda e: e.activation(
            out=junk, in_=xin_ap, func=AF.Square, accum_out=wst[:, 3:4]
        ),
        inc="ac",
    )
    if psum_in:
        pb.wait("vector", acs)
    r1 = pb.do("vector", lambda e: e.reduce_sum(wst[:, 2:3], xin_ap, axis=AX.X), inc="dv")
    pb.wait("vector", r1)
    pb.do("vector", lambda e: e.tensor_scalar_mul(wst[:, 1:2], wst[:, 2:3], 1.0 / D))
    pb.wait("vector", acs)
    r3 = pb.do("vector", lambda e: e.tensor_scalar_mul(wst[:, 4:5], wst[:, 3:4], 1.0 / D), inc="dv")
    pb.wait("vector", r3)
    r4 = pb.do("vector", lambda e: e.tensor_mul(wst[:, 5:6], wst[:, 1:2], wst[:, 1:2]), inc="dv")
    pb.wait("vector", r4)
    dvv = pb.do("vector", lambda e: e.tensor_sub(wst[:, 6:7], wst[:, 4:5], wst[:, 5:6]), inc="dv")
    pb.wait("scalar", dvv)
    acv = pb.do(
        "scalar",
        lambda e: e.activation(out=wst[:, 7:8], in_=wst[:, 6:7], func=AF.Sqrt, bias=c["lneps"][:, 0:1]),
        inc="ac",
    )
    pb.wait("vector", acv)
    dvr2 = pb.do("vector", lambda e: e.reciprocal(wst[:, 0:1], wst[:, 7:8]), inc="dv")
    pb.wait("vector", dvr2)
    vap = pb.do(
        "vector",
        lambda e: e.scalar_tensor_tensor(
            out=out_ap, in0=xin_ap, scalar=wst[:, 1:2],
            in1=wst[:, 0:1].to_broadcast([128, D]),
            op0=ALU.subtract, op1=ALU.mult,
        ),
        inc="dv",
    )
    return vap


def emit_proj(pb, c, lnT, w_sbuf, bias_row, psum_ap, n_chunks=8, nout=D):
    nh = (nout + 511) // 512

    def fn(e, lnT=lnT, w=w_sbuf, b=bias_row, p=psum_ap, n=n_chunks, nout=nout, nh=nh):
        ins = None
        for cc in range(n):
            lhs = lnT[:, 128 * cc : 128 * cc + 128]
            for half in range(nh):
                w0 = nout * cc + 512 * half
                nw = min(512, nout - 512 * half)
                ins = e.matmul(
                    p[:, 512 * half : 512 * half + nw],
                    lhs,
                    w[:, w0 : w0 + nw],
                    start=(cc == 0), stop=(cc == n - 1 and b is None),
                )
        if b is not None:
            for half in range(nh):
                nw = min(512, nout - 512 * half)
                ins = e.matmul(
                    p[:, 512 * half : 512 * half + nw],
                    c["ones_row"][0:1, :],
                    b[0:1, 512 * half : 512 * half + nw],
                    start=False, stop=True,
                )
        return ins

    return pb.do("tensor", fn, inc="pe")


def emit_transpose8(pb, c, src_bf, pt, n=8):
    def fn(e, s=src_bf, pt=pt, n=n):
        ins = None
        for cc in range(n):
            ins = e.transpose(
                pt[:, 128 * cc : 128 * cc + 128],
                s[:, 128 * cc : 128 * cc + 128],
                c["ident_bf"][:, :],
            )
        return ins

    return pb.do("tensor", fn, inc="pe")



def emit_l0_attn(pb, c, ctx, T, x_in, scr_q, scr_k, scr_v, scr_out, n_sweeps=6):
    nc = pb.nc
    NT = T // 128

    wq = ctx.enter_context(nc.sbuf_tensor("l0a_wq", [128, 8 * 1024], BF16))
    wk = ctx.enter_context(nc.sbuf_tensor("l0a_wk", [128, 8 * 1024], BF16))
    wv = ctx.enter_context(nc.sbuf_tensor("l0a_wv", [128, 8 * 1024], BF16))
    wo = ctx.enter_context(nc.sbuf_tensor("l0a_wo", [128, 8 * 1024], BF16))
    bq = ctx.enter_context(nc.sbuf_tensor("l0a_bq", [1, D], BF16))
    bk = ctx.enter_context(nc.sbuf_tensor("l0a_bk", [1, D], BF16))
    bv = ctx.enter_context(nc.sbuf_tensor("l0a_bv", [1, D], BF16))
    bo = ctx.enter_context(nc.sbuf_tensor("l0a_bo", [1, D], BF16))
    xin = [ctx.enter_context(nc.sbuf_tensor(f"l0a_xin{i}", [128, D], F32)) for i in range(2)]
    ln_bf = ctx.enter_context(nc.sbuf_tensor("l0a_ln", [128, D], BF16))
    lnT = ctx.enter_context(nc.sbuf_tensor("l0a_lnT", [128, D], BF16))
    q_bf = [ctx.enter_context(nc.sbuf_tensor(f"l0a_q{i}", [128, D], BF16)) for i in range(2)]
    k_bf = [ctx.enter_context(nc.sbuf_tensor(f"l0a_k{i}", [128, D], BF16)) for i in range(2)]
    v_bf = [ctx.enter_context(nc.sbuf_tensor(f"l0a_v{i}", [128, D], BF16)) for i in range(2)]
    wst = ctx.enter_context(nc.sbuf_tensor("l0a_wst", [128, 8], F32))
    sqsum = ctx.enter_context(nc.sbuf_tensor("l0a_qsum", [1, D], F32))
    sksum = ctx.enter_context(nc.sbuf_tensor("l0a_ksum", [1, D], F32))
    rowbf = ctx.enter_context(nc.sbuf_tensor("l0a_rowbf", [1, D], BF16))
    bc1 = ctx.enter_context(nc.sbuf_tensor("l0a_bc1", [128, D], F32))
    bc2 = ctx.enter_context(nc.sbuf_tensor("l0a_bc2", [128, D], F32))
    tmpf = ctx.enter_context(nc.sbuf_tensor("l0a_tmpf", [128, D], F32))
    s16a = ctx.enter_context(nc.sbuf_tensor("l0a_s16a", [128, 16], F32))
    s16b = ctx.enter_context(nc.sbuf_tensor("l0a_s16b", [128, 16], F32))
    si_all = ctx.enter_context(nc.sbuf_tensor("l0a_si", [128, 16 * NT], F32))
    sa_all = ctx.enter_context(nc.sbuf_tensor("l0a_sa", [128, 16 * NT], F32))
    csT = ctx.enter_context(nc.sbuf_tensor("l0a_csT", [16, T], F32))
    scT = ctx.enter_context(nc.sbuf_tensor("l0a_scT", [16, T], F32))
    s16 = ctx.enter_context(nc.sbuf_tensor("l0a_s16", [16, 4], F32))
    kv_s = ctx.enter_context(nc.sbuf_tensor("l0a_kv", [64, 1024], BF16))
    qsiT = ctx.enter_context(nc.sbuf_tensor("l0a_qsiT", [64, 16 * 128], BF16))
    aT = ctx.enter_context(nc.sbuf_tensor("l0a_aT", [128, D], BF16))
    y1 = ctx.enter_context(nc.sbuf_tensor("l0a_y1", [128, D], F32))
    x1o = [ctx.enter_context(nc.sbuf_tensor(f"l0a_x1o{i}", [128, D], F32)) for i in range(2)]

    pA = ctx.enter_context(nc.psum_tensor("l0a_pA", [128, D], F32))
    pB = ctx.enter_context(nc.psum_tensor("l0a_pB", [128, D], F32))
    pt = ctx.enter_context(nc.psum_tensor("l0a_pt", [128, D], BF16))
    ps1 = ctx.enter_context(nc.psum_tensor("l0a_ps1", [1, D], F32))

    for w_s, name in ((wq, "l0_wqt"), (wk, "l0_wkt"), (wv, "l0_wvt"), (wo, "l0_wot")):
        src = c[name].rearrange("(c p) n -> p c n", p=128)
        dst = w_s[:, :].rearrange("p (c n) -> p c n", c=8)
        pb.dma("gpsimd", dst, src, "wl")
    for b_s, name in ((bq, "l0_bq"), (bk, "l0_bk"), (bv, "l0_bv"), (bo, "l0_bo")):
        pb.dma("gpsimd", b_s[:, :], c[name], "wl")
    wl_all = pb.now("wl")

    ld_x = {}
    for t in range(min(2, NT)):
        ld_x[t] = pb.dma("sync", xin[t % 2][:, :], x_in[128 * t : 128 * t + 128, :], f"lx{t % 2}")

    ac_sq, ac_sk, dv_sv, st_q, st_k, st_v = {}, {}, {}, {}, {}, {}
    dv_qsum = dv_ksum = None
    for t in range(NT):
        bt = t % 2
        pb.wait("vector", ld_x[t])
        pb.wait("scalar", ld_x[t])
        apply_v = emit_ln(pb, c, xin[bt][:, :], ln_bf[:, :], tmpf[:, :], wst)
        if t + 2 < NT:
            pb.wait("sync", apply_v)
            ld_x[t + 2] = pb.dma("sync", xin[bt][:, :], x_in[128 * (t + 2) : 128 * (t + 2) + 128, :], f"lx{bt}")
        pb.wait("tensor", apply_v)
        pe_tr = emit_transpose8(pb, c, ln_bf, pt)
        pb.wait("vector", pe_tr)
        dv_lnT = pb.do("vector", lambda e: e.tensor_copy(lnT[:, :], pt[:, :]), inc="dv")
        pb.wait("tensor", dv_lnT)
        if t == 0:
            pb.wait("tensor", wl_all)
        if t >= 1:
            pb.wait("tensor", dv_sv[t - 1])
        pe_q = emit_proj(pb, c, lnT, wq, bq, pA)
        if t >= 1:
            pb.wait("tensor", ac_sk[t - 1])
        pe_k = emit_proj(pb, c, lnT, wk, bk, pB)
        pb.wait("scalar", pe_q)
        if t >= 2:
            pb.wait("scalar", st_q[t - 2])
        ac_sq[t] = pb.do(
            "scalar",
            lambda e, bt=bt: e.activation(out=q_bf[bt][:, :], in_=pA[:, :], func=AF.Sigmoid),
            inc="ac",
        )
        pb.wait("tensor", ac_sq[t])
        pe_v = emit_proj(pb, c, lnT, wv, bv, pA)
        pb.wait("scalar", pe_k)
        if t >= 2:
            pb.wait("scalar", st_k[t - 2])
        ac_sk[t] = pb.do(
            "scalar",
            lambda e, bt=bt: e.activation(out=k_bf[bt][:, :], in_=pB[:, :], func=AF.Sigmoid),
            inc="ac",
        )
        pb.wait("vector", pe_v)
        if t >= 2:
            pb.wait("vector", st_v[t - 2])
        dv_sv[t] = pb.do("vector", lambda e, bt=bt: e.tensor_copy(v_bf[bt][:, :], pA[:, :]), inc="dv")
        pb.wait("tensor", ac_sq[t])
        if dv_ksum is not None:
            pb.wait("tensor", dv_ksum)

        def qsum_fn(e, q=q_bf[bt]):
            e.matmul(ps1[0:1, 0:512], c["ones_col"][:, 0:1], q[:, 0:512], start=True, stop=True)
            return e.matmul(ps1[0:1, 512:1024], c["ones_col"][:, 0:1], q[:, 512:1024], start=True, stop=True)

        pe_qs = pb.do("tensor", qsum_fn, inc="pe")
        pb.wait("vector", pe_qs)
        if t == 0:
            dv_qsum = pb.do("vector", lambda e: e.tensor_copy(sqsum[0:1, :], ps1[0:1, :]), inc="dv")
        else:
            dv_qsum = pb.do("vector", lambda e: e.tensor_add(sqsum[0:1, :], sqsum[0:1, :], ps1[0:1, :]), inc="dv")
        pb.wait("tensor", ac_sk[t])
        pb.wait("tensor", dv_qsum)

        def ksum_fn(e, k=k_bf[bt]):
            e.matmul(ps1[0:1, 0:512], c["ones_col"][:, 0:1], k[:, 0:512], start=True, stop=True)
            return e.matmul(ps1[0:1, 512:1024], c["ones_col"][:, 0:1], k[:, 512:1024], start=True, stop=True)

        pe_ks = pb.do("tensor", ksum_fn, inc="pe")
        pb.wait("vector", pe_ks)
        if t == 0:
            dv_ksum = pb.do("vector", lambda e: e.tensor_copy(sksum[0:1, :], ps1[0:1, :]), inc="dv")
        else:
            dv_ksum = pb.do("vector", lambda e: e.tensor_add(sksum[0:1, :], sksum[0:1, :], ps1[0:1, :]), inc="dv")
        pb.wait("sync", ac_sq[t])
        st_q[t] = pb.dma("sync", scr_q[128 * t : 128 * t + 128, :], q_bf[bt][:, :], f"sq{bt}")
        pb.wait("sync", ac_sk[t])
        st_k[t] = pb.dma("sync", scr_k[128 * t : 128 * t + 128, :], k_bf[bt][:, :], f"sk{bt}")
        pb.wait("sync", dv_sv[t])
        st_v[t] = pb.dma("sync", scr_v[128 * t : 128 * t + 128, :], v_bf[bt][:, :], f"sv{bt}")

    pb.barrier()
    if n_sweeps <= 1:
        return

    def bcast_row(src_row_ap, dst_bc):
        dvr = pb.do(
            "vector",
            lambda e: e.tensor_scalar_add(rowbf[0:1, :], src_row_ap, EPS0),
            inc="dv",
        )
        pb.wait("tensor", dvr)

        def fn(e):
            e.matmul(pA[:, 0:512], c["ones_row"][0:1, :], rowbf[0:1, 0:512], start=True, stop=True)
            return e.matmul(pA[:, 512:1024], c["ones_row"][0:1, :], rowbf[0:1, 512:1024], start=True, stop=True)

        pev = pb.do("tensor", fn, inc="pe")
        pb.wait("vector", pev)
        return pb.do("vector", lambda e: e.tensor_copy(dst_bc[:, :], pA[:, :]), inc="dv")

    bcast_row(sksum[0:1, :], bc1)
    bcast_row(sqsum[0:1, :], bc2)

    ld_q, ld_k, ld_v = {}, {}, {}
    for t in range(min(2, NT)):
        ld_q[t] = pb.dma("sync", q_bf[t % 2][:, :], scr_q[128 * t : 128 * t + 128, :], f"lq{t % 2}")
        ld_k[t] = pb.dma("sync", k_bf[t % 2][:, :], scr_k[128 * t : 128 * t + 128, :], f"lk{t % 2}")

    pe_s2 = pe_s1 = None
    for t in range(NT):
        bt = t % 2
        pb.wait("vector", ld_q[t])
        pb.wait("vector", ld_k[t])
        if pe_s1 is not None:
            pb.wait("vector", pe_s1)
        pb.do(
            "vector",
            lambda e, q=q_bf[bt]: e.scalar_tensor_tensor(
                out=tmpf[:, :], in0=q[:, :], scalar=EPS0, in1=bc1[:, :],
                op0=ALU.add, op1=ALU.mult,
            ),
        )
        rqa = pb.do(
            "vector",
            lambda e: e.reduce_sum(
                s16a[:, :], tmpf[:, :].rearrange("p (h d) -> p h d", h=16), axis=AX.X
            ),
            inc="dv",
        )
        pb.wait("vector", rqa)
        dvrq = pb.do("vector", lambda e, t=t: e.reciprocal(si_all[:, 16 * t : 16 * t + 16], s16a[:, :]), inc="dv")
        pb.wait("vector", dvrq)
        dv_qsi = pb.do(
            "vector",
            lambda e, q=q_bf[bt], t=t: e.tensor_tensor(
                out=ln_bf[:, :].rearrange("p (h d) -> p h d", h=16),
                in0=q[:, :].rearrange("p (h d) -> p h d", h=16),
                in1=si_all[:, 16 * t : 16 * t + 16].unsqueeze(2).to_broadcast([128, 16, 64]),
                op=ALU.mult,
            ),
            inc="dv",
        )
        pb.wait("tensor", dv_qsi)

        def s2fn(e, t=t):
            e.matmul(pB[0:1, 0:512], c["ones_col"][:, 0:1], ln_bf[:, 0:512], start=(t == 0), stop=(t == NT - 1))
            return e.matmul(pB[0:1, 512:1024], c["ones_col"][:, 0:1], ln_bf[:, 512:1024], start=(t == 0), stop=(t == NT - 1))

        pe_s2 = pb.do("tensor", s2fn, inc="pe")
        pb.wait("vector", pe_s2)
        pb.do(
            "vector",
            lambda e, k=k_bf[bt]: e.scalar_tensor_tensor(
                out=tmpf[:, :], in0=k[:, :], scalar=EPS0, in1=bc2[:, :],
                op0=ALU.add, op1=ALU.mult,
            ),
        )
        rkb = pb.do(
            "vector",
            lambda e: e.reduce_sum(
                s16b[:, :], tmpf[:, :].rearrange("p (h d) -> p h d", h=16), axis=AX.X
            ),
            inc="dv",
        )
        pb.wait("vector", rkb)
        dvrk = pb.do("vector", lambda e: e.reciprocal(s16b[:, :], s16b[:, :]), inc="dv")
        pb.wait("vector", dvrk)
        dv_kso = pb.do(
            "vector",
            lambda e, k=k_bf[bt]: e.tensor_tensor(
                out=ln_bf[:, :].rearrange("p (h d) -> p h d", h=16),
                in0=k[:, :].rearrange("p (h d) -> p h d", h=16),
                in1=s16b[:, :].unsqueeze(2).to_broadcast([128, 16, 64]),
                op=ALU.mult,
            ),
            inc="dv",
        )
        pb.wait("tensor", dv_kso)

        def s1fn(e, t=t):
            e.matmul(ps1[0:1, 0:512], c["ones_col"][:, 0:1], ln_bf[:, 0:512], start=(t == 0), stop=(t == NT - 1))
            return e.matmul(ps1[0:1, 512:1024], c["ones_col"][:, 0:1], ln_bf[:, 512:1024], start=(t == 0), stop=(t == NT - 1))

        pe_s1 = pb.do("tensor", s1fn, inc="pe")
        if t + 2 < NT:
            pb.wait("sync", pe_s1)
            ld_q[t + 2] = pb.dma("sync", q_bf[bt][:, :], scr_q[128 * (t + 2) : 128 * (t + 2) + 128, :], f"lq{bt}")
            ld_k[t + 2] = pb.dma("sync", k_bf[bt][:, :], scr_k[128 * (t + 2) : 128 * (t + 2) + 128, :], f"lk{bt}")

    pb.wait("vector", pe_s1)
    pb.wait("vector", pe_s2)
    bcast_row(ps1[0:1, :], bc1)
    bcast_row(pB[0:1, :], bc2)

    pb.barrier()
    if n_sweeps <= 2:
        return

    for t in range(min(2, NT)):
        ld_q[t] = pb.dma("sync", q_bf[t % 2][:, :], scr_q[128 * t : 128 * t + 128, :], f"lq{t % 2}")
        ld_k[t] = pb.dma("sync", k_bf[t % 2][:, :], scr_k[128 * t : 128 * t + 128, :], f"lk{t % 2}")
    ac_sa = pe_trc = None
    dv_cs = {}
    for t in range(NT):
        bt = t % 2
        pb.wait("vector", ld_q[t])
        pb.wait("vector", ld_k[t])
        if ac_sa is not None:
            pb.wait("vector", ac_sa)
        if pe_trc is not None:
            pb.wait("vector", pe_trc)
        pb.do(
            "vector",
            lambda e, q=q_bf[bt]: e.scalar_tensor_tensor(
                out=tmpf[:, :], in0=q[:, :], scalar=EPS0, in1=bc1[:, :],
                op0=ALU.add, op1=ALU.mult,
            ),
        )
        dvr = pb.do(
            "vector",
            lambda e: e.reduce_sum(
                s16a[:, :], tmpf[:, :].rearrange("p (h d) -> p h d", h=16), axis=AX.X
            ),
            inc="dv",
        )
        pb.wait("scalar", dvr)
        ac_sa = pb.do(
            "scalar",
            lambda e, t=t: e.activation(
                out=sa_all[:, 16 * t : 16 * t + 16], in_=s16a[:, :], func=AF.Sigmoid
            ),
            inc="ac",
        )
        pb.do(
            "vector",
            lambda e, k=k_bf[bt]: e.scalar_tensor_tensor(
                out=tmpf[:, :], in0=k[:, :], scalar=EPS0, in1=bc2[:, :],
                op0=ALU.add, op1=ALU.mult,
            ),
        )
        rcb = pb.do(
            "vector",
            lambda e: e.reduce_sum(
                s16b[:, :], tmpf[:, :].rearrange("p (h d) -> p h d", h=16), axis=AX.X
            ),
            inc="dv",
        )
        pb.wait("vector", rcb)
        rcm = pb.do("vector", lambda e: e.tensor_scalar_min(s16b[:, :], s16b[:, :], 1.0), inc="dv")
        pb.wait("vector", rcm)
        dv_cl = pb.do("vector", lambda e: e.tensor_scalar_max(s16b[:, :], s16b[:, :], -1.0), inc="dv")
        pb.wait("tensor", dv_cl)
        if t >= 1:
            pb.wait("tensor", dv_cs[t - 1])
        pe_trc = pb.do(
            "tensor",
            lambda e: e.transpose(pA[0:16, 0:128], s16b[:, :], c["ident_f32"][:, :]),
            inc="pe",
        )
        pb.wait("vector", pe_trc)
        dv_cs[t] = pb.do(
            "vector",
            lambda e, t=t: e.tensor_copy(csT[0:16, 128 * t : 128 * t + 128], pA[0:16, 0:128]),
            inc="dv",
        )
        if t + 2 < NT:
            pb.wait("sync", dv_cs[t])
            ld_q[t + 2] = pb.dma("sync", q_bf[bt][:, :], scr_q[128 * (t + 2) : 128 * (t + 2) + 128, :], f"lq{bt}")
            ld_k[t + 2] = pb.dma("sync", k_bf[bt][:, :], scr_k[128 * (t + 2) : 128 * (t + 2) + 128, :], f"lk{bt}")

    pb.barrier()
    if n_sweeps <= 3:
        return

    rdm = pb.do("vector", lambda e: e.reduce_max(s16[0:16, 0:1], csT[0:16, :], axis=AX.X), inc="dv")
    pb.wait("vector", rdm)
    dvm = pb.do("vector", lambda e: e.tensor_scalar_mul(s16[0:16, 1:2], s16[0:16, 0:1], -1.0), inc="dv")
    pb.wait("scalar", dvm)
    acce = pb.do(
        "scalar",
        lambda e: e.activation(
            out=scT[0:16, :], in_=csT[0:16, :], func=AF.Exp,
            bias=s16[0:16, 1:2], accum_out=s16[0:16, 2:3],
        ),
        inc="ac",
    )
    pb.wait("vector", acce)
    rrc = pb.do("vector", lambda e: e.reciprocal(s16[0:16, 3:4], s16[0:16, 2:3]), inc="dv")
    pb.wait("vector", rrc)
    dvsc = pb.do("vector", lambda e: e.tensor_scalar_mul(s16[0:16, 3:4], s16[0:16, 3:4], float(T)), inc="dv")
    pb.wait("vector", dvsc)
    dvscT = pb.do(
        "vector",
        lambda e: e.tensor_scalar(
            scT[0:16, :], scT[0:16, :], s16[0:16, 3:4], None, op0=ALU.mult
        ),
        inc="dv",
    )
    if DEV_DUMP:
        pb.wait("sync", dvscT)
        pb.dma("sync", c["scr_dbg"][0:16, 0:T], csT[0:16, :], "dbg1")
        pb.dma("sync", c["scr_dbg"][16:32, 0:T], scT[0:16, :], "sdbg2")
        pb.dma("sync", c["scr_dbg"][32:33, 0:D], bc1[0:1, :], "dbg3")
        pb.dma("sync", c["scr_dbg"][33:34, 0:D], bc2[0:1, :], "dbg4")
        pb.dma("sync", c["scr_dbg"][34:35, 0:D], sksum[0:1, :], "dbg5")
        pb.dma("sync", c["scr_dbg"][35:36, 0:D], sqsum[0:1, :], "dbg6")
        pb.dma("sync", c["scr_dbg"][36:37, 0:16].rearrange("a (b c) -> a b c", b=4), s16[0:16, 0:4].rearrange("(a b) c -> a b c", a=1), "dbg7") if False else None
        pb.dma("sync", c["scr_dbg"][40:56, 0:4], s16[0:16, 0:4], "dbg8")

    pb.barrier()
    if n_sweeps <= 4:
        return

    for t in range(min(2, NT)):
        ld_k[t] = pb.dma("sync", k_bf[t % 2][:, :], scr_k[128 * t : 128 * t + 128, :], f"lk{t % 2}")
        ld_v[t] = pb.dma("sync", v_bf[t % 2][:, :], scr_v[128 * t : 128 * t + 128, :], f"lv{t % 2}")
    dv_vsc = {}
    pe_kv = None
    for t in range(NT):
        bt = t % 2
        if t >= 1:
            pb.wait("tensor", dv_vsc[t - 1])
        pe_tr = pb.do(
            "tensor",
            lambda e, t=t: e.transpose(
                pA[:, 0:16], scT[0:16, 128 * t : 128 * t + 128], c["ident_f32"][0:16, 0:16]
            ),
            inc="pe",
        )
        pb.wait("vector", pe_tr)
        pb.wait("vector", ld_v[t])
        if pe_kv is not None:
            pb.wait("vector", pe_kv)
        dvcp = pb.do("vector", lambda e: e.tensor_copy(s16a[:, :], pA[:, 0:16]), inc="dv")
        pb.wait("vector", dvcp)
        dv_vsc[t] = pb.do(
            "vector",
            lambda e, v=v_bf[bt]: e.tensor_tensor(
                out=ln_bf[:, :].rearrange("p (h d) -> p h d", h=16),
                in0=v[:, :].rearrange("p (h d) -> p h d", h=16),
                in1=s16a[:, :].unsqueeze(2).to_broadcast([128, 16, 64]),
                op=ALU.mult,
            ),
            inc="dv",
        )
        pb.wait("tensor", dv_vsc[t])
        pb.wait("tensor", ld_k[t])

        def kvfn(e, k=k_bf[bt], t=t):
            ins = None
            for h in range(16):
                ins = e.matmul(
                    pB[0:64, 64 * h : 64 * h + 64],
                    k[:, 64 * h : 64 * h + 64],
                    ln_bf[:, 64 * h : 64 * h + 64],
                    start=(t == 0 and h % 8 == 0),
                    stop=(t == NT - 1 and h % 8 == 7),
                )
            return ins

        pe_kv = pb.do("tensor", kvfn, inc="pe")
        if t + 2 < NT:
            pb.wait("sync", pe_kv)
            pb.wait("sync", dv_vsc[t])
            ld_k[t + 2] = pb.dma("sync", k_bf[bt][:, :], scr_k[128 * (t + 2) : 128 * (t + 2) + 128, :], f"lk{bt}")
            ld_v[t + 2] = pb.dma("sync", v_bf[bt][:, :], scr_v[128 * (t + 2) : 128 * (t + 2) + 128, :], f"lv{bt}")
    pb.wait("vector", pe_kv)
    dv_kvs = pb.do("vector", lambda e: e.tensor_copy(kv_s[0:64, :], pB[0:64, :]), inc="dv")

    pb.barrier()
    if n_sweeps <= 5:
        return

    for t in range(min(2, NT)):
        ld_q[t] = pb.dma("sync", q_bf[t % 2][:, :], scr_q[128 * t : 128 * t + 128, :], f"lq{t % 2}")
        ld_x[t] = pb.dma("sync", xin[t % 2][:, :], x_in[128 * t : 128 * t + 128, :], f"lx{t % 2}")
    st_o = {}
    dv_fin = {}
    pe_y = None
    for t in range(NT):
        bt = t % 2
        pb.wait("vector", ld_q[t])
        if pe_y is not None:
            pb.wait("vector", pe_y)
        if F_CUT < 9 and t >= 0:
            emit_f_cut(pb, c, t, bt, NT, locals())
            continue
        dv_q = pb.do(
            "vector",
            lambda e, q=q_bf[bt], t=t: e.tensor_tensor(
                out=ln_bf[:, :].rearrange("p (h d) -> p h d", h=16),
                in0=q[:, :].rearrange("p (h d) -> p h d", h=16),
                in1=si_all[:, 16 * t : 16 * t + 16].unsqueeze(2).to_broadcast([128, 16, 64]),
                op=ALU.mult,
            ),
            inc="dv",
        )
        pb.wait("tensor", dv_q)
        if t >= 1:
            pb.wait("tensor", dv_fin[t - 1])
        dv_qT = None
        for grp in range(2):
            def trf(e, grp=grp):
                ins = None
                for hh in range(8):
                    h = 8 * grp + hh
                    ins = e.transpose(
                        pt[0:64, 128 * hh : 128 * hh + 128],
                        ln_bf[:, 64 * h : 64 * h + 64],
                        c["ident_bf"][:, :],
                    )
                return ins

            if grp == 1:
                pb.wait("tensor", dv_qT)
            pe_tr = pb.do("tensor", trf, inc="pe")
            pb.wait("vector", pe_tr)
            dv_qT = pb.do(
                "vector",
                lambda e, grp=grp: e.tensor_copy(
                    qsiT[0:64, 1024 * grp : 1024 * grp + 1024], pt[0:64, 0:1024]
                ),
                inc="dv",
            )
        pb.wait("tensor", dv_qT)

        def outfn(e):
            ins = None
            for h in range(16):
                ins = e.matmul(
                    pA[:, 64 * h : 64 * h + 64],
                    qsiT[0:64, 128 * h : 128 * h + 128],
                    kv_s[0:64, 64 * h : 64 * h + 64],
                    start=True, stop=True,
                )
            return ins

        pe_out = pb.do("tensor", outfn, inc="pe")
        pb.wait("vector", pe_out)
        dv_at = pb.do(
            "vector",
            lambda e, t=t: e.tensor_tensor(
                out=ln_bf[:, :].rearrange("p (h d) -> p h d", h=16),
                in0=pA[:, :].rearrange("p (h d) -> p h d", h=16),
                in1=sa_all[:, 16 * t : 16 * t + 16].unsqueeze(2).to_broadcast([128, 16, 64]),
                op=ALU.mult,
            ),
            inc="dv",
        )
        pb.wait("tensor", dv_at)
        pe_tr2 = emit_transpose8(pb, c, ln_bf, pt)
        pb.wait("vector", pe_tr2)
        dv_aT = pb.do("vector", lambda e: e.tensor_copy(aT[:, :], pt[:, :]), inc="dv")
        pb.wait("tensor", dv_aT)
        pe_y = emit_proj(pb, c, aT, wo, bo, pB)
        pb.wait("vector", pe_y)
        pb.wait("scalar", pe_y)
        emit_ln(pb, c, pB[:, :], y1[:, :], tmpf[:, :], wst, psum_in=True)
        pb.wait("vector", ld_x[t])
        if t >= 2:
            pb.wait("vector", st_o[t - 2])
        dv_fin[t] = pb.do(
            "vector", lambda e, bt=bt: e.tensor_add(x1o[bt][:, :], xin[bt][:, :], y1[:, :]), inc="dv"
        )
        pb.wait("sync", dv_fin[t])
        st_o[t] = pb.dma("sync", scr_out[128 * t : 128 * t + 128, :], x1o[bt][:, :], f"so{bt}")
        if t + 2 < NT:
            ld_q[t + 2] = pb.dma("sync", q_bf[bt][:, :], scr_q[128 * (t + 2) : 128 * (t + 2) + 128, :], f"lq{bt}")
            ld_x[t + 2] = pb.dma("sync", xin[bt][:, :], x_in[128 * (t + 2) : 128 * (t + 2) + 128, :], f"lx{bt}")

    pb.barrier()




def emit_f_cut(pb, c, t, bt, NT, L):
    xin, ln_bf, q_bf, qsiT, kv_s, sa_all, si_all, aT, y1, x1o, tmpf, wst = (
        L["xin"], L["ln_bf"], L["q_bf"], L["qsiT"], L["kv_s"], L["sa_all"],
        L["si_all"], L["aT"], L["y1"], L["x1o"], L["tmpf"], L["wst"])
    pA, pB, pt = L["pA"], L["pB"], L["pt"]
    wo, bo = L["wo"], L["bo"]
    scr_out, scr_q, x_in = L["scr_out"], L["scr_q"], L["x_in"]
    ld_q, ld_x, st_o, dv_fin = L["ld_q"], L["ld_x"], L["st_o"], L["dv_fin"]
    ALU_ = ALU
    last_dv = None
    dv_q = pb.do(
        "vector",
        lambda e, q=q_bf[bt], t=t: e.tensor_tensor(
            out=ln_bf[:, :].rearrange("p (h d) -> p h d", h=16),
            in0=q[:, :].rearrange("p (h d) -> p h d", h=16),
            in1=si_all[:, 16 * t : 16 * t + 16].unsqueeze(2).to_broadcast([128, 16, 64]),
            op=ALU_.mult,
        ),
        inc="dv",
    )
    last_pe = None
    if F_CUT >= 2:
        pb.wait("tensor", dv_q)
        if t >= 1:
            pb.wait("tensor", dv_fin[t - 1])
        dv_qT = None
        for grp in range(2):
            def trf(e, grp=grp):
                ins = None
                for hh in range(8):
                    h = 8 * grp + hh
                    ins = e.transpose(
                        pt[0:64, 128 * hh : 128 * hh + 128],
                        ln_bf[:, 64 * h : 64 * h + 64],
                        c["ident_bf"][:, :],
                    )
                return ins
            if grp == 1:
                pb.wait("tensor", dv_qT)
            pe_tr = pb.do("tensor", trf, inc="pe")
            pb.wait("vector", pe_tr)
            dv_qT = pb.do(
                "vector",
                lambda e, grp=grp: e.tensor_copy(
                    qsiT[0:64, 1024 * grp : 1024 * grp + 1024], pt[0:64, 0:1024]
                ),
                inc="dv",
            )
        last_pe = pe_tr
    if F_CUT >= 3:
        pb.wait("tensor", dv_qT)
        def outfn(e):
            ins = None
            for h in range(16):
                ins = e.matmul(
                    pA[:, 64 * h : 64 * h + 64],
                    qsiT[0:64, 128 * h : 128 * h + 128],
                    kv_s[0:64, 64 * h : 64 * h + 64],
                    start=True, stop=True,
                )
            return ins
        pe_out = pb.do("tensor", outfn, inc="pe")
        pb.wait("vector", pe_out)
        dv_at = pb.do(
            "vector",
            lambda e, t=t: e.tensor_tensor(
                out=ln_bf[:, :].rearrange("p (h d) -> p h d", h=16),
                in0=pA[:, :].rearrange("p (h d) -> p h d", h=16),
                in1=sa_all[:, 16 * t : 16 * t + 16].unsqueeze(2).to_broadcast([128, 16, 64]),
                op=ALU_.mult,
            ),
            inc="dv",
        )
        last_pe = pe_out
    if F_CUT >= 4:
        pb.wait("tensor", dv_at)
        pe_tr2 = emit_transpose8(pb, c, ln_bf, pt)
        pb.wait("vector", pe_tr2)
        dv_aT = pb.do("vector", lambda e: e.tensor_copy(aT[:, :], pt[:, :]), inc="dv")
        last_pe = pe_tr2
    if F_CUT >= 5:
        pb.wait("tensor", dv_aT)
        pe_y = emit_proj(pb, c, aT, wo, bo, pB)
        last_pe = pe_y
    if F_CUT == 6:
        pb.wait("vector", pe_y)
        pb.wait("scalar", pe_y)
        acs = pb.do(
            "scalar",
            lambda e: e.activation(out=tmpf[:, :], in_=pB[:, :], func=AF.Square, accum_out=wst[:, 3:4]),
            inc="ac",
        )
        pb.wait("vector", acs)
        pb.do("vector", lambda e: e.tensor_copy(y1[:, :], pB[:, :]))
    elif F_CUT == 7:
        pb.wait("vector", pe_y)
        pb.wait("scalar", pe_y)
        pb.wait("scalar", pb.now("dv"))
        acs = pb.do(
            "scalar",
            lambda e: e.activation(out=tmpf[:, :], in_=pB[:, :], func=AF.Square, accum_out=wst[:, 3:4]),
            inc="ac",
        )
        pb.do("vector", lambda e: e.reduce_sum(wst[:, 2:3], pB[:, :], axis=AX.X))
        pb.do("vector", lambda e: e.tensor_scalar_mul(wst[:, 1:2], wst[:, 2:3], 1.0 / D))
        pb.wait("vector", acs)
        pb.do("vector", lambda e: e.tensor_scalar_mul(wst[:, 4:5], wst[:, 3:4], 1.0 / D))
        pb.do("vector", lambda e: e.tensor_mul(wst[:, 5:6], wst[:, 1:2], wst[:, 1:2]))
        dvv = pb.do("vector", lambda e: e.tensor_sub(wst[:, 6:7], wst[:, 4:5], wst[:, 5:6]), inc="dv")
        pb.wait("scalar", dvv)
        acv = pb.do(
            "scalar",
            lambda e: e.activation(out=wst[:, 7:8], in_=wst[:, 6:7], func=AF.Sqrt, bias=c["lneps"][:, 0:1]),
            inc="ac",
        )
        pb.wait("vector", acv)
        dvr2 = pb.do("vector", lambda e: e.reciprocal(wst[:, 0:1], wst[:, 7:8]), inc="dv")
        pb.wait("vector", dvr2)
        pb.do("vector", lambda e: e.tensor_copy(y1[:, :], pB[:, :]))
    elif F_CUT >= 8:
        pb.wait("vector", pe_y)
        pb.wait("scalar", pe_y)
        emit_ln(pb, c, pB[:, :], y1[:, :], tmpf[:, :], wst, psum_in=True)
    pb.wait("vector", ld_x[t])
    if t >= 2:
        pb.wait("vector", st_o[t - 2])
    if F_CUT >= 6:
        pass
    if F_CUT >= 6:
        dv_fin[t] = pb.do(
            "vector", lambda e, bt=bt: e.tensor_add(x1o[bt][:, :], xin[bt][:, :], y1[:, :]), inc="dv"
        )
    else:
        if last_pe is not None:
            pb.wait("vector", last_pe)
        dv_fin[t] = pb.do(
            "vector", lambda e, bt=bt: e.tensor_copy(x1o[bt][:, :], xin[bt][:, :]), inc="dv"
        )
    pb.wait("sync", dv_fin[t])
    st_o[t] = pb.dma("sync", scr_out[128 * t : 128 * t + 128, :], x1o[bt][:, :], f"so{bt}")
    if t + 2 < NT:
        ld_q[t + 2] = pb.dma("sync", q_bf[bt][:, :], scr_q[128 * (t + 2) : 128 * (t + 2) + 128, :], f"lq{bt}")
        ld_x[t + 2] = pb.dma("sync", xin[bt][:, :], x_in[128 * (t + 2) : 128 * (t + 2) + 128, :], f"lx{bt}")




def emit_l0_ffn(pb, c, ctx, T, x_in, scr_out):
    nc = pb.nc
    NT = T // 128

    wr = ctx.enter_context(nc.sbuf_tensor("l0f_wr", [128, 8 * 1024], BF16))
    wk2 = ctx.enter_context(nc.sbuf_tensor("l0f_wk", [128, 8 * 1024], BF16))
    wv2 = ctx.enter_context(nc.sbuf_tensor("l0f_wv", [128, 8 * 1024], BF16))
    mur = ctx.enter_context(nc.sbuf_tensor("l0f_murs", [128, 8], F32))
    muk = ctx.enter_context(nc.sbuf_tensor("l0f_muks", [128, 8], F32))
    omr = ctx.enter_context(nc.sbuf_tensor("l0f_omr", [128, 8], F32))
    omk = ctx.enter_context(nc.sbuf_tensor("l0f_omk", [128, 8], F32))
    zr = ctx.enter_context(nc.sbuf_tensor("l0f_zr", [128, 8], BF16))
    xin = [ctx.enter_context(nc.sbuf_tensor(f"l0f_xin{i}", [128, D], F32)) for i in range(2)]
    ln_bf = ctx.enter_context(nc.sbuf_tensor("l0f_ln", [128, D], BF16))
    LT = ctx.enter_context(nc.sbuf_tensor("l0f_LT", [128, 8 * 129], BF16))
    xrT = ctx.enter_context(nc.sbuf_tensor("l0f_xrT", [128, D], BF16))
    xkT = ctx.enter_context(nc.sbuf_tensor("l0f_xkT", [128, D], BF16))
    r_f = ctx.enter_context(nc.sbuf_tensor("l0f_r", [128, D], F32))
    relu_f = ctx.enter_context(nc.sbuf_tensor("l0f_relu", [128, D], F32))
    ka_bf = ctx.enter_context(nc.sbuf_tensor("l0f_ka", [128, D], BF16))
    kaT = ctx.enter_context(nc.sbuf_tensor("l0f_kaT", [128, D], BF16))
    yb = ctx.enter_context(nc.sbuf_tensor("l0f_yb", [128, D], F32))
    y1 = ctx.enter_context(nc.sbuf_tensor("l0f_y1", [128, D], F32))
    wst = ctx.enter_context(nc.sbuf_tensor("l0f_wst", [128, 8], F32))
    tmpf = ctx.enter_context(nc.sbuf_tensor("l0f_tmpf", [128, D], F32))
    x2o = [ctx.enter_context(nc.sbuf_tensor(f"l0f_x2o{i}", [128, D], F32)) for i in range(2)]

    pA = ctx.enter_context(nc.psum_tensor("l0f_pA", [128, D], F32))
    pB = ctx.enter_context(nc.psum_tensor("l0f_pB", [128, D], F32))
    pt = ctx.enter_context(nc.psum_tensor("l0f_pt", [128, D], BF16))

    for w_s, name in ((wr, "l0f_wrt"), (wk2, "l0f_wkt"), (wv2, "l0f_wvt")):
        src = c[name].rearrange("(c p) n -> p c n", p=128)
        dst = w_s[:, :].rearrange("p (c n) -> p c n", c=8)
        pb.dma("gpsimd", dst, src, "wl")
    pb.dma("gpsimd", mur[:, :], c["l0f_mur"], "wl")
    pb.dma("gpsimd", muk[:, :], c["l0f_muk"], "wl")
    pb.dma("gpsimd", zr[:, :], c["zeros8"], "wl")
    wl_all = pb.now("wl")
    pb.wait("vector", wl_all)
    pb.do("vector", lambda e: e.tensor_scalar_sub(omr[:, :], mur[:, :], 1.0))
    pb.do("vector", lambda e: e.tensor_scalar_mul(omr[:, :], omr[:, :], -1.0))
    pb.do("vector", lambda e: e.tensor_scalar_sub(omk[:, :], muk[:, :], 1.0))
    dv_om = pb.do("vector", lambda e: e.tensor_scalar_mul(omk[:, :], omk[:, :], -1.0), inc="dv")

    LT3 = LT[:, :].rearrange("p (c n) -> p c n", c=8)
    ld_x = {}
    for t in range(min(2, NT)):
        ld_x[t] = pb.dma("sync", xin[t % 2][:, :], x_in[128 * t : 128 * t + 128, :], f"lx{t % 2}")
    st_o = {}
    pe_v = pe_r = pe_k = None
    ac_rl = None
    dv_fin = {}
    for t in range(NT):
        bt = t % 2
        pb.wait("vector", ld_x[t])
        pb.wait("scalar", ld_x[t])
        apply_v = emit_ln(pb, c, xin[bt][:, :], ln_bf[:, :], tmpf[:, :], wst)
        if t == 0:
            dv_pc = pb.do("vector", lambda e: e.tensor_copy(LT3[:, :, 0], zr[:, :]), inc="dv")
        else:
            dv_pc = pb.do("vector", lambda e: e.tensor_copy(LT3[:, :, 0], LT3[:, :, 128]), inc="dv")
        pb.wait("tensor", apply_v)
        pb.wait("tensor", dv_pc)
        if t >= 1:
            pb.wait("tensor", pe_k)
        pe_tr = emit_transpose8(pb, c, ln_bf, pt)
        pb.wait("vector", pe_tr)
        dv_LT = pb.do(
            "vector",
            lambda e: e.tensor_copy(
                LT3[:, :, 1:129], pt[:, :].rearrange("p (c n) -> p c n", c=8)
            ),
            inc="dv",
        )
        if t == 0:
            pb.wait("vector", dv_om)
        for cc in range(8):
            pb.do(
                "vector",
                lambda e, cc=cc: e.tensor_scalar(
                    xrT[:, 128 * cc : 128 * cc + 128], LT3[:, cc, 1:129],
                    mur[:, cc : cc + 1], None, op0=ALU.mult,
                ),
            )
            pb.do(
                "vector",
                lambda e, cc=cc: e.scalar_tensor_tensor(
                    out=xrT[:, 128 * cc : 128 * cc + 128], in0=LT3[:, cc, 0:128],
                    scalar=omr[:, cc : cc + 1], in1=xrT[:, 128 * cc : 128 * cc + 128],
                    op0=ALU.mult, op1=ALU.add,
                ),
            )
            pb.do(
                "vector",
                lambda e, cc=cc: e.tensor_scalar(
                    xkT[:, 128 * cc : 128 * cc + 128], LT3[:, cc, 1:129],
                    muk[:, cc : cc + 1], None, op0=ALU.mult,
                ),
            )
            pb.do(
                "vector",
                lambda e, cc=cc: e.scalar_tensor_tensor(
                    out=xkT[:, 128 * cc : 128 * cc + 128], in0=LT3[:, cc, 0:128],
                    scalar=omk[:, cc : cc + 1], in1=xkT[:, 128 * cc : 128 * cc + 128],
                    op0=ALU.mult, op1=ALU.add,
                ),
            )
        dv_mix = pb.do("vector", lambda e: e.tensor_copy(wst[:, 0:1], omr[:, 0:1]), inc="dv")
        if DEV_DUMP and t == 0:
            pb.wait("gpsimd", dv_mix)
            pb.dma("gpsimd", c["scr_dbg"][0:64, 0:1024], xrT[0:64, :], "gdbg1")
            pb.dma("gpsimd", c["scr_dbg"][0:64, 1024:2048], xkT[0:64, :], "gdbg2")
            pb.dma("gpsimd", c["scr_dbg"][0:64, 2048:3072], LT[0:64, 0:1024], "gdbg3")
        pb.wait("tensor", dv_mix)
        if t == 0:
            pb.wait("tensor", wl_all)
        if pe_v is not None:
            pb.wait("tensor", pb.now("dv"))
        pe_r = emit_proj(pb, c, xrT, wr, None, pA)
        if ac_rl is not None:
            pb.wait("tensor", ac_rl)
        pe_k = emit_proj(pb, c, xkT, wk2, None, pB)
        pb.wait("scalar", pe_r)
        ac_sg = pb.do(
            "scalar",
            lambda e: e.activation(out=r_f[:, :], in_=pA[:, :], func=AF.Sigmoid),
            inc="ac",
        )
        pb.wait("scalar", pe_k)
        ac_rl = pb.do(
            "scalar",
            lambda e: e.activation(out=relu_f[:, :], in_=pB[:, :], func=AF.Relu),
            inc="ac",
        )
        pb.wait("vector", ac_rl)
        dv_ka = pb.do("vector", lambda e: e.tensor_mul(ka_bf[:, :], relu_f[:, :], relu_f[:, :]), inc="dv")
        if DEV_DUMP and t == 0:
            pb.wait("sync", ac_sg)
            pb.dma("sync", c["scr_dbg"][0:64, 3072:4096], r_f[0:64, :], "dbg4")
            pb.wait("sync", dv_ka)
            pb.dma("sync", c["scr_dbg"][0:64, 4096:5120], relu_f[0:64, :], "dbg5")
        pb.wait("tensor", dv_ka)
        pe_tr2 = emit_transpose8(pb, c, ka_bf, pt)
        pb.wait("vector", pe_tr2)
        dv_kaT = pb.do("vector", lambda e: e.tensor_copy(kaT[:, :], pt[:, :]), inc="dv")
        pb.wait("tensor", dv_kaT)
        pb.wait("tensor", ac_sg)
        pe_v = emit_proj(pb, c, kaT, wv2, None, pA)
        pb.wait("vector", pe_v)
        dv_y = pb.do("vector", lambda e: e.tensor_mul(yb[:, :], r_f[:, :], pA[:, :]), inc="dv")
        pb.wait("scalar", dv_y)
        emit_ln(pb, c, yb[:, :], y1[:, :], tmpf[:, :], wst)
        if t >= 2:
            pb.wait("vector", st_o[t - 2])
        dv_fin[t] = pb.do(
            "vector", lambda e, bt=bt: e.tensor_add(x2o[bt][:, :], xin[bt][:, :], y1[:, :]), inc="dv"
        )
        pb.wait("sync", dv_fin[t])
        st_o[t] = pb.dma("sync", scr_out[128 * t : 128 * t + 128, :], x2o[bt][:, :], f"so{bt}")
        if t + 2 < NT:
            ld_x[t + 2] = pb.dma("sync", xin[bt][:, :], x_in[128 * (t + 2) : 128 * (t + 2) + 128, :], f"lx{bt}")

    pb.barrier()



def emit_l1_attn(pb, c, ctx, T, x_in, scr_q, scr_out):
    nc = pb.nc
    NT = T // 128
    NKB = T // 128

    xin = [ctx.enter_context(nc.sbuf_tensor(f"l1_xin{i}", [128, D], F32)) for i in range(2)]
    ln_bf = ctx.enter_context(nc.sbuf_tensor("l1_ln", [128, D], BF16))
    lnT = ctx.enter_context(nc.sbuf_tensor("l1_lnT", [128, D], BF16))
    wst = ctx.enter_context(nc.sbuf_tensor("l1_wst", [128, 8], F32))
    kT = ctx.enter_context(nc.sbuf_tensor("l1_kT", [128, 8 * T], BF16))
    V = ctx.enter_context(nc.sbuf_tensor("l1_V", [128, T * 8], BF16))
    q_bf = [ctx.enter_context(nc.sbuf_tensor(f"l1_q{i}", [128, D], BF16)) for i in range(2)]
    probs = ctx.enter_context(nc.sbuf_tensor("l1_probs", [128, T], BF16))
    pTs = ctx.enter_context(nc.sbuf_tensor("l1_pTs", [128, 1024], BF16))
    rs = ctx.enter_context(nc.sbuf_tensor("l1_rs", [128, 4], F32))
    ctxb = ctx.enter_context(nc.sbuf_tensor("l1_ctx", [128, D], F32))
    y1 = ctx.enter_context(nc.sbuf_tensor("l1_y1", [128, D], F32))
    x3o = [ctx.enter_context(nc.sbuf_tensor(f"l1_x3o{i}", [128, D], F32)) for i in range(2)]

    kT3 = kT[:, :].rearrange("p (g n) -> p g n", g=8)
    V3 = V[:, :].rearrange("p (t n) -> p t n", t=NT)

    pt = ctx.enter_context(nc.psum_tensor("l1_pt", [128, D], BF16))
    ph1 = ExitStack()
    pA = ph1.enter_context(nc.psum_tensor("l1_pA", [128, D], F32))
    pB = ph1.enter_context(nc.psum_tensor("l1_pB", [128, D], F32))
    wq = ph1.enter_context(nc.sbuf_tensor("l1_wqs", [128, 8 * 1024], BF16))
    wk = ph1.enter_context(nc.sbuf_tensor("l1_wks", [128, 8 * 1024], BF16))
    bq = ph1.enter_context(nc.sbuf_tensor("l1_bqs", [1, D], BF16))
    bk = ph1.enter_context(nc.sbuf_tensor("l1_bks", [1, D], BF16))

    for w_s, name in ((wq, "l1_wq"), (wk, "l1_wk")):
        src = c[name].rearrange("(c p) n -> p c n", p=128)
        dst = w_s[:, :].rearrange("p (c n) -> p c n", c=8)
        pb.dma("gpsimd", dst, src, "wl")
    pb.dma("gpsimd", bq[:, :], c["l1_bq"], "wl")
    pb.dma("gpsimd", bk[:, :], c["l1_bk"], "wl")
    wl_all = pb.now("wl")

    ld_x = {}
    for t in range(min(2, NT)):
        ld_x[t] = pb.dma("sync", xin[t % 2][:, :], x_in[128 * t : 128 * t + 128, :], f"lx{t % 2}")
    st_q = {}
    ac_q = None
    dv_kT = None
    for t in range(NT):
        bt = t % 2
        pb.wait("vector", ld_x[t])
        pb.wait("scalar", ld_x[t])
        apply_v = emit_ln(pb, c, xin[bt][:, :], ln_bf[:, :], probs[:, 0:1024], wst)
        if t + 2 < NT:
            pb.wait("sync", apply_v)
            ld_x[t + 2] = pb.dma("sync", xin[bt][:, :], x_in[128 * (t + 2) : 128 * (t + 2) + 128, :], f"lx{bt}")
        pb.wait("tensor", apply_v)
        if dv_kT is not None:
            pb.wait("tensor", dv_kT)
        pe_tr = emit_transpose8(pb, c, ln_bf, pt)
        pb.wait("vector", pe_tr)
        dv_lnT = pb.do("vector", lambda e: e.tensor_copy(lnT[:, :], pt[:, :]), inc="dv")
        pb.wait("tensor", dv_lnT)
        if t == 0:
            pb.wait("tensor", wl_all)
        if ac_q is not None:
            pb.wait("tensor", ac_q)
        pe_q = emit_proj(pb, c, lnT, wq, bq, pA)
        pe_k = emit_proj(pb, c, lnT, wk, bk, pB)
        pb.wait("scalar", pe_q)
        if t >= 2:
            pb.wait("scalar", st_q[t - 2])
        ac_q = pb.do(
            "scalar",
            lambda e, bt=bt: e.activation(out=q_bf[bt][:, :], in_=pA[:, :], func=AF.Identity),
            inc="ac",
        )
        pb.wait("sync", ac_q)
        st_q[t] = pb.dma("sync", scr_q[128 * t : 128 * t + 128, :], q_bf[bt][:, :], f"sq{bt}")
        pb.wait("vector", pe_k)
        dv_kb = pb.do("vector", lambda e: e.tensor_copy(ln_bf[:, :], pB[:, :]), inc="dv")
        pb.wait("tensor", dv_kb)
        pe_tr2 = emit_transpose8(pb, c, ln_bf, pt)
        pb.wait("vector", pe_tr2)
        dv_kT = pb.do(
            "vector",
            lambda e, t=t: e.tensor_copy(
                kT3[:, :, 128 * t : 128 * t + 128],
                pt[:, :].rearrange("p (c n) -> p c n", c=8),
            ),
            inc="dv",
        )
    pb.barrier()

    wv = wq
    bv = bq
    src = c["l1_wv"].rearrange("(c p) n -> p c n", p=128)
    pb.dma("gpsimd", wv[:, :].rearrange("p (c n) -> p c n", c=8), src, "wl")
    pb.dma("gpsimd", bv[:, :], c["l1_bv"], "wl")
    wl_all = pb.now("wl")
    for t in range(min(2, NT)):
        ld_x[t] = pb.dma("sync", xin[t % 2][:, :], x_in[128 * t : 128 * t + 128, :], f"lx{t % 2}")
    dv_V = None
    for t in range(NT):
        bt = t % 2
        pb.wait("vector", ld_x[t])
        pb.wait("scalar", ld_x[t])
        apply_v = emit_ln(pb, c, xin[bt][:, :], ln_bf[:, :], probs[:, 0:1024], wst)
        if t + 2 < NT:
            pb.wait("sync", apply_v)
            ld_x[t + 2] = pb.dma("sync", xin[bt][:, :], x_in[128 * (t + 2) : 128 * (t + 2) + 128, :], f"lx{bt}")
        pb.wait("tensor", apply_v)
        if dv_V is not None:
            pb.wait("tensor", dv_V)
        pe_tr = emit_transpose8(pb, c, ln_bf, pt)
        pb.wait("vector", pe_tr)
        dv_lnT = pb.do("vector", lambda e: e.tensor_copy(lnT[:, :], pt[:, :]), inc="dv")
        pb.wait("tensor", dv_lnT)
        if t == 0:
            pb.wait("tensor", wl_all)
        pe_v = emit_proj(pb, c, lnT, wv, bv, pA)
        pb.wait("vector", pe_v)
        dv_V = pb.do(
            "vector",
            lambda e, t=t: e.tensor_copy(V3[:, t, :], pA[:, :]),
            inc="dv",
        )
    pb.barrier()

    pb.flush(pb.block)
    ph1.close()
    Dacc = ctx.enter_context(nc.sbuf_tensor("l1_Dacc", [128, T], F32))
    Dinv = ctx.enter_context(nc.sbuf_tensor("l1_Dinv", [128, T], BF16))
    KHALF = T // 2
    NS = (KHALF + 511) // 512
    pS = ctx.enter_context(nc.psum_tensor("l1_pS", [128, min(2048, KHALF)], F32))
    pO = ctx.enter_context(nc.psum_tensor("l1_pO", [128, 512], F32))
    ld_q = {}
    for t in range(min(2, NT)):
        ld_q[t] = pb.dma("sync", q_bf[t % 2][:, :], scr_q[128 * t : 128 * t + 128, :], f"lq{t % 2}")
        ld_x[t] = pb.dma("sync", xin[t % 2][:, :], x_in[128 * t : 128 * t + 128, :], f"lx{t % 2}")
    st_o = {}
    dv_fin = {}
    pe_last_pv = None
    dv_ctx = None
    dv_pT = None
    for t in range(NT):
        bt = t % 2
        pb.wait("vector", ld_q[t])
        if t >= 1:
            pb.wait("vector", pb.now("pe"))
        pb.wait("tensor", ld_q[t])
        if dv_ctx is not None:
            pb.wait("tensor", dv_ctx)
        if dv_pT is not None:
            pb.wait("tensor", dv_pT)
        pe_trq = emit_transpose8(pb, c, q_bf[bt], pt)
        pb.wait("vector", pe_trq)
        dv_qT = pb.do("vector", lambda e: e.tensor_copy(lnT[:, :], pt[:, :]), inc="dv")
        for g in range(8):
            ac_exp = {}
            for hk in range(2):
                pb.wait("tensor", dv_qT)
                pb.wait("tensor", pb.now("ac"))

                def sfn(e, g=g, hk=hk):
                    ins = None
                    for j in range(NS):
                        w = min(512, KHALF - 512 * j)
                        ins = e.matmul(
                            pS[:, 512 * j : 512 * j + w],
                            lnT[:, 128 * g : 128 * g + 128],
                            kT3[:, g, KHALF * hk + 512 * j : KHALF * hk + 512 * j + w],
                            start=True, stop=True,
                        )
                    return ins

                pe_s = pb.do("tensor", sfn, inc="pe")
                pb.wait("scalar", pe_s)
                pb.wait("scalar", pb.now("dv"))
                ac_exp[hk] = pb.do(
                    "scalar",
                    lambda e, hk=hk: e.activation(
                        out=probs[:, KHALF * hk : KHALF * hk + KHALF], in_=pS[:, 0:KHALF],
                        func=AF.Exp, bias=c["c30"][:, 0:1], scale=1.0 / 32.0,
                    ),
                    inc="ac",
                )
            pb.wait("vector", ac_exp[1])
            if g == 0:
                dv_D = pb.do("vector", lambda e: e.tensor_copy(Dacc[:, :], probs[:, :]), inc="dv")
            else:
                dv_D = pb.do("vector", lambda e: e.tensor_add(Dacc[:, :], Dacc[:, :], probs[:, :]), inc="dv")
        pb.wait("vector", dv_D)
        def _recip_fn(e):
            with nc.allow_low_precision(reason="Dinv feeds a bf16 matmul operand"):
                return e.reciprocal(Dinv[:, :], Dacc[:, :])

        dv_Di = pb.do("vector", _recip_fn, inc="dv")
        for g in range(8):
            ac_exp = {}
            for hk in range(2):
                pb.wait("tensor", dv_qT)
                pb.wait("tensor", pb.now("ac"))

                def sfn(e, g=g, hk=hk):
                    ins = None
                    for j in range(NS):
                        w = min(512, KHALF - 512 * j)
                        ins = e.matmul(
                            pS[:, 512 * j : 512 * j + w],
                            lnT[:, 128 * g : 128 * g + 128],
                            kT3[:, g, KHALF * hk + 512 * j : KHALF * hk + 512 * j + w],
                            start=True, stop=True,
                        )
                    return ins

                pe_s = pb.do("tensor", sfn, inc="pe")
                pb.wait("scalar", pe_s)
                pb.wait("scalar", pb.now("dv"))
                pb.wait("scalar", pb.now("pe"))
                ac_exp[hk] = pb.do(
                    "scalar",
                    lambda e, hk=hk: e.activation(
                        out=probs[:, KHALF * hk : KHALF * hk + KHALF], in_=pS[:, 0:KHALF],
                        func=AF.Exp, bias=c["c30"][:, 0:1], scale=1.0 / 32.0,
                    ),
                    inc="ac",
                )
            pb.wait("vector", ac_exp[1])
            pb.wait("vector", dv_Di)
            dv_P = pb.do(
                "vector",
                lambda e: e.tensor_tensor(out=probs[:, :], in0=probs[:, :], in1=Dinv[:, :], op=ALU.mult),
                inc="dv",
            )
            pvk = None
            pb.wait("tensor", dv_P)
            if dv_ctx is not None:
                pb.wait("tensor", dv_ctx)
            for kb4 in range((NKB + 7) // 8):
                nj = min(8, NKB - 8 * kb4)
                def trp(e, kb4=kb4, nj=nj):
                    ins = None
                    for j in range(nj):
                        ins = e.transpose(
                            pt[:, 128 * j : 128 * j + 128],
                            probs[:, 1024 * kb4 + 128 * j : 1024 * kb4 + 128 * j + 128],
                            c["ident_bf"][:, :],
                        )
                    return ins
                if dv_pT is not None:
                    pb.wait("tensor", dv_pT)
                pe_tp = pb.do("tensor", trp, inc="pe")
                pb.wait("vector", pe_tp)
                dv_pT = pb.do(
                    "vector",
                    lambda e, kb4=kb4, nj=nj: e.tensor_copy(
                        pTs[:, 0 : 128 * nj], pt[:, 0 : 128 * nj]
                    ),
                    inc="dv",
                )
                pb.wait("tensor", dv_pT)
                def pvf(e, g=g, kb4=kb4, nj=nj, t=t):
                    ins = None
                    for j in range(nj):
                        kb = 8 * kb4 + j
                        ins = e.matmul(
                            pO[:, 128 * (g % 4) : 128 * (g % 4) + 128],
                            pTs[:, 128 * j : 128 * j + 128],
                            V3[:, kb, 128 * g : 128 * g + 128],
                            start=(kb == 0), stop=(kb == NKB - 1),
                        )
                    return ins
                pvk = pb.do("tensor", pvf, inc="pe")
            pb.wait("vector", pvk)
            dv_ctx = pb.do(
                "vector",
                lambda e, g=g: e.tensor_copy(
                    ctxb[:, 128 * g : 128 * g + 128], pO[:, 128 * (g % 4) : 128 * (g % 4) + 128]
                ),
                inc="dv",
            )
            pe_last_pv = pvk
        pb.wait("scalar", dv_ctx)
        emit_ln(pb, c, ctxb[:, :], y1[:, :], probs[:, 0:1024], wst)
        pb.wait("vector", ld_x[t])
        if t >= 2:
            pb.wait("vector", st_o[t - 2])
        dv_fin[t] = pb.do(
            "vector", lambda e, bt=bt: e.tensor_add(x3o[bt][:, :], xin[bt][:, :], y1[:, :]), inc="dv"
        )
        pb.wait("sync", dv_fin[t])
        st_o[t] = pb.dma("sync", scr_out[128 * t : 128 * t + 128, :], x3o[bt][:, :], f"so{bt}")
        if t + 2 < NT:
            ld_q[t + 2] = pb.dma("sync", q_bf[bt][:, :], scr_q[128 * (t + 2) : 128 * (t + 2) + 128, :], f"lq{bt}")
            ld_x[t + 2] = pb.dma("sync", xin[bt][:, :], x_in[128 * (t + 2) : 128 * (t + 2) + 128, :], f"lx{bt}")

    pb.barrier()



def emit_l1_moe(pb, c, ctx, T, x_in, out_t):
    nc = pb.nc
    NT = T // 128
    NH = 2
    TH = T // NH
    NTH = TH // 128
    NE_ALL = 33

    xin = [ctx.enter_context(nc.sbuf_tensor(f"moe_xin{i}", [128, D], F32)) for i in range(2)]
    ln_bf = ctx.enter_context(nc.sbuf_tensor("moe_ln", [128, D], BF16))
    wst = ctx.enter_context(nc.sbuf_tensor("moe_wst", [128, 8], F32))
    tmpf = ctx.enter_context(nc.sbuf_tensor("moe_tmpf", [128, D], F32))
    gwT = ctx.enter_context(nc.sbuf_tensor("moe_gwTs", [128, 8 * 32], BF16))
    flatT = ctx.enter_context(nc.sbuf_tensor("moe_flatT", [128, 8 * TH], BF16))
    comb = ctx.enter_context(nc.sbuf_tensor("moe_comb", [128, 32 * NTH], F32))
    scor = ctx.enter_context(nc.sbuf_tensor("moe_scor", [128, 32], F32))
    gs8 = ctx.enter_context(nc.sbuf_tensor("moe_gs8", [128, 8], F32))
    top8 = ctx.enter_context(nc.sbuf_tensor("moe_top8", [128, 8], F32))
    msk = ctx.enter_context(nc.sbuf_tensor("moe_msk", [128, 32], F32))
    rsum = ctx.enter_context(nc.sbuf_tensor("moe_rsum", [128, 2], F32))
    acc = ctx.enter_context(nc.sbuf_tensor("moe_acc", [128, 1024 * NTH], F32))
    wg = [ctx.enter_context(nc.sbuf_tensor(f"moe_wgs{i}", [128, 8 * 512], BF16)) for i in range(2)]
    wu = [ctx.enter_context(nc.sbuf_tensor(f"moe_wus{i}", [128, 8 * 512], BF16)) for i in range(2)]
    wd = [ctx.enter_context(nc.sbuf_tensor(f"moe_wds{i}", [128, 4 * 1024], BF16)) for i in range(2)]
    hs = ctx.enter_context(nc.sbuf_tensor("moe_hs", [128, 512], F32))
    h_bf = ctx.enter_context(nc.sbuf_tensor("moe_h", [128, 512], BF16))
    hT = ctx.enter_context(nc.sbuf_tensor("moe_hT", [128, 512], BF16))
    y1 = ctx.enter_context(nc.sbuf_tensor("moe_y1", [128, D], F32))
    oo = [ctx.enter_context(nc.sbuf_tensor(f"moe_oo{i}", [128, D], F32)) for i in range(2)]

    pA = ctx.enter_context(nc.psum_tensor("moe_pA", [128, 512], F32))
    pB = ctx.enter_context(nc.psum_tensor("moe_pB", [128, 512], F32))
    pt = ctx.enter_context(nc.psum_tensor("moe_pt", [128, 1024], BF16))
    pC = ctx.enter_context(nc.psum_tensor("moe_pC", [128, D], F32))
    pR = ctx.enter_context(nc.psum_tensor("moe_pR", [128, 32], F32))

    flatT3 = flatT[:, :].rearrange("p (c n) -> p c n", c=8)

    pb.dma("gpsimd", gwT[:, :].rearrange("p (c n) -> p c n", c=8),
           c["moe_gwt"].rearrange("(c p) n -> p c n", p=128), "wl")
    wl_gw = pb.now("wl")

    def load_expert(eidx, slot):
        if eidx < 32:
            wg_src = c["moe_wg"][eidx].rearrange("(c p) n -> p c n", p=128)
            wu_src = c["moe_wu"][eidx].rearrange("(c p) n -> p c n", p=128)
            wd_src = c["moe_wd"][eidx].rearrange("(c p) n -> p c n", p=128)
        else:
            wg_src = c["moe_sg"].rearrange("(c p) n -> p c n", p=128)
            wu_src = c["moe_su"].rearrange("(c p) n -> p c n", p=128)
            wd_src = c["moe_sd"].rearrange("(c p) n -> p c n", p=128)
        pb.dma("gpsimd", wg[slot][:, :].rearrange("p (c n) -> p c n", c=8), wg_src, f"we{slot}")
        pb.dma("gpsimd", wu[slot][:, :].rearrange("p (c n) -> p c n", c=8), wu_src, f"we{slot}")
        return pb.dma("gpsimd", wd[slot][:, :].rearrange("p (c n) -> p c n", c=4), wd_src, f"we{slot}")

    ld_x = {}
    st_o = {}
    for h in range(NH):
        t0 = h * NTH
        for t in range(min(2, NTH)):
            ld_x[t] = pb.dma("sync", xin[t % 2][:, :], x_in[128 * (t0 + t) : 128 * (t0 + t) + 128, :], f"lx{t % 2}")
        dv_fT = None
        for t in range(NTH):
            bt = t % 2
            pb.wait("vector", ld_x[t])
            pb.wait("scalar", ld_x[t])
            apply_v = emit_ln(pb, c, xin[bt][:, :], ln_bf[:, :], tmpf[:, :], wst)
            if t + 2 < NTH:
                pb.wait("sync", apply_v)
                ld_x[t + 2] = pb.dma("sync", xin[bt][:, :], x_in[128 * (t0 + t + 2) : 128 * (t0 + t + 2) + 128, :], f"lx{bt}")
            pb.wait("tensor", apply_v)
            if dv_fT is not None:
                pb.wait("tensor", dv_fT)
            pe_tr = emit_transpose8(pb, c, ln_bf, pt)
            pb.wait("vector", pe_tr)
            dv_fT = pb.do(
                "vector",
                lambda e, t=t: e.tensor_copy(
                    flatT3[:, :, 128 * t : 128 * t + 128],
                    pt[:, :].rearrange("p (c n) -> p c n", c=8),
                ),
                inc="dv",
            )
            pb.wait("tensor", dv_fT)
            pb.wait("tensor", pb.now("ac"))
            if t == 0 and h == 0:
                pb.wait("tensor", wl_gw)

            def lfn(e, t=t):
                ins = None
                for cc in range(8):
                    ins = e.matmul(
                        pR[:, 0:32],
                        flatT3[:, cc, 128 * t : 128 * t + 128],
                        gwT[:, 32 * cc : 32 * cc + 32],
                        start=(cc == 0), stop=(cc == 7),
                    )
                return ins

            pe_lg = pb.do("tensor", lfn, inc="pe")
            pb.wait("scalar", pe_lg)
            ac_sc = pb.do(
                "scalar",
                lambda e: e.activation(out=scor[:, :], in_=pR[:, 0:32], func=AF.Sigmoid),
                inc="ac",
            )
            pb.wait("vector", ac_sc)
            r1 = pb.do(
                "vector",
                lambda e: e.reduce_sum(
                    gs8[:, :], scor[:, :].rearrange("p (g i) -> p g i", g=8), axis=AX.X
                ),
                inc="dv",
            )
            pb.wait("vector", r1)
            r2 = pb.do("vector", lambda e: e.max(out=top8[:, :], in_=gs8[:, :]), inc="dv")
            pb.wait("vector", r2)
            r3 = pb.do(
                "vector",
                lambda e: e.tensor_scalar(msk[:, 0:8], gs8[:, :], top8[:, 3:4], None, op0=ALU.is_ge),
                inc="dv",
            )
            pb.wait("vector", r3)
            r4 = pb.do(
                "vector",
                lambda e: e.tensor_tensor(
                    out=msk[:, :].rearrange("p (g i) -> p g i", g=8),
                    in0=scor[:, :].rearrange("p (g i) -> p g i", g=8),
                    in1=msk[:, 0:8].unsqueeze(2).to_broadcast([128, 8, 4]),
                    op=ALU.mult,
                ),
                inc="dv",
            )
            pb.wait("vector", r4)
            r5 = pb.do("vector", lambda e: e.max(out=top8[:, :], in_=msk[:, :]), inc="dv")
            pb.wait("vector", r5)
            r6 = pb.do(
                "vector",
                lambda e, t=t: e.tensor_scalar(
                    comb[:, 32 * t : 32 * t + 32], msk[:, :], top8[:, 7:8], None, op0=ALU.is_ge
                ),
                inc="dv",
            )
            pb.wait("vector", r6)
            r7 = pb.do(
                "vector",
                lambda e, t=t: e.tensor_mul(
                    comb[:, 32 * t : 32 * t + 32], comb[:, 32 * t : 32 * t + 32], msk[:, :]
                ),
                inc="dv",
            )
            pb.wait("vector", r7)
            r8 = pb.do(
                "vector",
                lambda e, t=t: e.reduce_sum(rsum[:, 0:1], comb[:, 32 * t : 32 * t + 32], axis=AX.X),
                inc="dv",
            )
            pb.wait("vector", r8)
            r9 = pb.do("vector", lambda e: e.tensor_scalar_max(rsum[:, 0:1], rsum[:, 0:1], 1e-12), inc="dv")
            pb.wait("vector", r9)
            r10 = pb.do("vector", lambda e: e.reciprocal(rsum[:, 1:2], rsum[:, 0:1]), inc="dv")
            pb.wait("vector", r10)
            pb.do(
                "vector",
                lambda e, t=t: e.tensor_scalar(
                    comb[:, 32 * t : 32 * t + 32], comb[:, 32 * t : 32 * t + 32],
                    rsum[:, 1:2], None, op0=ALU.mult,
                ),
                inc="dv",
            )
        pb.barrier()

        ld_w = {}
        ld_w[0] = load_expert(32, 0)
        ld_w[1] = load_expert(0, 1)
        pe_dn = {}
        dv_acc = {}
        for ei in range(NE_ALL):
            slot = ei % 2
            eidx = 32 if ei == 0 else ei - 1
            pb.wait("tensor", ld_w[ei])
            for t in range(NTH):
                def gfn(e, t=t, slot=slot):
                    ins = None
                    for cc in range(8):
                        ins = e.matmul(
                            pA[:, 0:512],
                            flatT3[:, cc, 128 * t : 128 * t + 128],
                            wg[slot][:, 512 * cc : 512 * cc + 512],
                            start=(cc == 0), stop=(cc == 7),
                        )
                    return ins

                def ufn(e, t=t, slot=slot):
                    ins = None
                    for cc in range(8):
                        ins = e.matmul(
                            pB[:, 0:512],
                            flatT3[:, cc, 128 * t : 128 * t + 128],
                            wu[slot][:, 512 * cc : 512 * cc + 512],
                            start=(cc == 0), stop=(cc == 7),
                        )
                    return ins

                if (ei, t) != (0, 0):
                    pb.wait("tensor", pb.now("ac"))
                pe_g = pb.do("tensor", gfn, inc="pe")
                if (ei, t) != (0, 0):
                    pb.wait("tensor", pb.now("dv"))
                pe_u = pb.do("tensor", ufn, inc="pe")
                pb.wait("scalar", pe_g)
                ac_si = pb.do(
                    "scalar",
                    lambda e: e.activation(out=hs[:, :], in_=pA[:, :], func=AF.Silu),
                    inc="ac",
                )
                pb.wait("vector", ac_si)
                pb.wait("vector", pe_u)
                dv_h = pb.do("vector", lambda e: e.tensor_mul(h_bf[:, :], hs[:, :], pB[:, :]), inc="dv")
                pb.wait("tensor", dv_h)

                def tfn(e):
                    ins = None
                    for j in range(4):
                        ins = e.transpose(
                            pt[:, 128 * j : 128 * j + 128],
                            h_bf[:, 128 * j : 128 * j + 128],
                            c["ident_bf"][:, :],
                        )
                    return ins

                pe_th = pb.do("tensor", tfn, inc="pe")
                pb.wait("vector", pe_th)
                dv_hT = pb.do("vector", lambda e: e.tensor_copy(hT[:, :], pt[:, 0:512]), inc="dv")
                pb.wait("tensor", dv_hT)
                if dv_acc.get(t) is not None:
                    pb.wait("tensor", dv_acc[t])

                def dfn(e, slot=slot):
                    ins = None
                    for j in range(4):
                        for half in range(2):
                            ins = e.matmul(
                                pC[:, 512 * half : 512 * half + 512],
                                hT[:, 128 * j : 128 * j + 128],
                                wd[slot][:, 1024 * j + 512 * half : 1024 * j + 512 * half + 512],
                                start=(j == 0), stop=(j == 3),
                            )
                    return ins

                pe_dn[t] = pb.do("tensor", dfn, inc="pe")
                pb.wait("vector", pe_dn[t])
                if ei == 0:
                    dv_acc[t] = pb.do(
                        "vector",
                        lambda e, t=t: e.tensor_copy(acc[:, 1024 * t : 1024 * t + 1024], pC[:, :]),
                        inc="dv",
                    )
                else:
                    dv_acc[t] = pb.do(
                        "vector",
                        lambda e, t=t, eidx=eidx: e.scalar_tensor_tensor(
                            out=acc[:, 1024 * t : 1024 * t + 1024], in0=pC[:, :],
                            scalar=comb[:, 32 * t + eidx : 32 * t + eidx + 1],
                            in1=acc[:, 1024 * t : 1024 * t + 1024],
                            op0=ALU.mult, op1=ALU.add,
                        ),
                        inc="dv",
                    )
            if ei + 2 < NE_ALL:
                pb.wait("gpsimd", pe_dn[NTH - 1])
                ld_w[ei + 2] = load_expert(ei + 1, slot)
        pb.barrier()

        for t in range(min(2, NTH)):
            ld_x[t] = pb.dma("sync", xin[t % 2][:, :], x_in[128 * (t0 + t) : 128 * (t0 + t) + 128, :], f"lx{t % 2}")
        for t in range(NTH):
            bt = t % 2
            pb.wait("vector", ld_x[t])
            pb.wait("scalar", ld_x[t])
            emit_ln(pb, c, acc[:, 1024 * t : 1024 * t + 1024], y1[:, :], tmpf[:, :], wst)
            if t >= 2:
                pb.wait("vector", st_o[t - 2])
            dv_f = pb.do(
                "vector", lambda e, bt=bt: e.tensor_add(oo[bt][:, :], xin[bt][:, :], y1[:, :]), inc="dv"
            )
            pb.wait("sync", dv_f)
            st_o[t] = pb.dma("sync", out_t[128 * (t0 + t) : 128 * (t0 + t) + 128, :], oo[bt][:, :], f"so{bt}")
            if t + 2 < NTH:
                pb.wait("sync", dv_f)
                ld_x[t + 2] = pb.dma("sync", xin[bt][:, :], x_in[128 * (t0 + t + 2) : 128 * (t0 + t + 2) + 128, :], f"lx{bt}")
        pb.barrier()



def build_program(T=4096, dev=False, stages=("l0a", "l0f", "l1a", "moe"), n_sweeps=6, paranoid=True):
    nc = bass.Bass("TRN2", target_bir_lowering=False, detect_race_conditions=False)
    scratch_kind = "ExternalOutput" if dev else "Internal"

    c = {}
    c["x"] = nc.dram_tensor("x", [T, D], F32, kind="ExternalInput")[:, :]
    for nm, shp in [
        ("l0_wqt", [D, D]), ("l0_wkt", [D, D]), ("l0_wvt", [D, D]), ("l0_wot", [D, D]),
        ("l0_bq", [1, D]), ("l0_bk", [1, D]), ("l0_bv", [1, D]), ("l0_bo", [1, D]),
        ("l0f_wrt", [D, D]), ("l0f_wkt", [D, D]), ("l0f_wvt", [D, D]),
        ("l0f_mur", [128, 8]), ("l0f_muk", [128, 8]),
        ("l1_wq", [D, D]), ("l1_wk", [D, D]), ("l1_wv", [D, D]),
        ("l1_bq", [1, D]), ("l1_bk", [1, D]), ("l1_bv", [1, D]),
        ("moe_gwt", [D, 32]),
        ("moe_sg", [D, 512]), ("moe_su", [D, 512]), ("moe_sd", [512, D]),
    ]:
        c[nm] = nc.dram_tensor(nm, shp, F32, kind="ExternalInput")[:, :]
    c["moe_wg"] = nc.dram_tensor("moe_wg", [32, D, 512], F32, kind="ExternalInput")[:, :, :]
    c["moe_wu"] = nc.dram_tensor("moe_wu", [32, D, 512], F32, kind="ExternalInput")[:, :, :]
    c["moe_wd"] = nc.dram_tensor("moe_wd", [32, 512, D], F32, kind="ExternalInput")[:, :, :]
    identf = nc.dram_tensor("identf", [128, 128], F32, kind="ExternalInput")[:, :]
    identb = nc.dram_tensor("identb", [128, 128], BF16, kind="ExternalInput")[:, :]
    onesr = nc.dram_tensor("onesr", [1, 128], BF16, kind="ExternalInput")[:, :]
    onesc = nc.dram_tensor("onesc", [128, 1], BF16, kind="ExternalInput")[:, :]
    c["zeros8"] = nc.dram_tensor("zeros8", [128, 8], BF16, kind="ExternalInput")[:, :]

    scr_q = nc.dram_tensor("scr_q", [T, D], BF16, kind=scratch_kind)[:, :]
    scr_k = nc.dram_tensor("scr_k", [T, D], BF16, kind=scratch_kind)[:, :]
    scr_v = nc.dram_tensor("scr_v", [T, D], BF16, kind=scratch_kind)[:, :]
    scr_a = nc.dram_tensor("scr_a", [T, D], F32, kind="ExternalOutput" if dev else "Internal")[:, :]
    scr_b = nc.dram_tensor("scr_b", [T, D], F32, kind="ExternalOutput" if dev else "Internal")[:, :]
    out_t = nc.dram_tensor("out", [T, D], F32, kind="ExternalOutput")[:, :]
    scr_dbg = nc.dram_tensor("scr_dbg", [64, 8192], F32, kind="ExternalOutput" if dev else "Internal")[:, :]

    pb = PB(nc, paranoid=paranoid)
    with ExitStack() as ctx:
        block = ctx.enter_context(nc.Block())
        pb.block = block
        ident_f32 = nc.alloc_sbuf_tensor("g_identf", [128, 128], F32)
        ident_bf = nc.alloc_sbuf_tensor("g_identb", [128, 128], BF16)
        ones_row = nc.alloc_sbuf_tensor("g_onesr", [1, 128], BF16)
        ones_col = nc.alloc_sbuf_tensor("g_onesc", [128, 1], BF16)
        lneps = nc.alloc_sbuf_tensor("g_lneps", [128, 1], F32)
        c30 = nc.alloc_sbuf_tensor("g_c30", [128, 1], F32)
        pb.dma("sync", ident_f32[:, :], identf, "lc")
        pb.dma("sync", ident_bf[:, :], identb, "lc")
        pb.dma("sync", ones_row[:, :], onesr, "lc")
        ldc = pb.dma("sync", ones_col[:, :], onesc, "lc")
        pb.do("vector", lambda e: e.memset(lneps[:, 0:1], LNEPS))
        mz = pb.do("vector", lambda e: e.memset(c30[:, 0:1], -30.0), inc="dv")
        for eng in ("tensor", "vector", "scalar"):
            pb.wait(eng, ldc)
        pb.wait("scalar", mz)
        cc = dict(c)
        cc["lneps"] = lneps
        cc["c30"] = c30
        cc["ident_f32"] = ident_f32
        cc["ident_bf"] = ident_bf
        cc["ones_row"] = ones_row
        cc["ones_col"] = ones_col

        cc["scr_dbg"] = scr_dbg
        if "l0a" in stages:
            with ExitStack() as sctx:
                emit_l0_attn(pb, cc, sctx, T, c["x"], scr_q, scr_k, scr_v, scr_a, n_sweeps=n_sweeps)
                pb.flush(block)
        if "l0f" in stages:
            with ExitStack() as sctx:
                emit_l0_ffn(pb, cc, sctx, T, scr_a, scr_b)
                pb.flush(block)
        if "l1a" in stages:
            with ExitStack() as sctx:
                emit_l1_attn(pb, cc, sctx, T, scr_b, scr_q, scr_a)
                pb.flush(block)
        if "moe" in stages:
            with ExitStack() as sctx:
                emit_l1_moe(pb, cc, sctx, T, scr_a, out_t)
                pb.flush(block)

        pb.flush(block)
    return nc


def prep_inputs(**inp):
    tr = lambda a: np.ascontiguousarray(np.asarray(a, dtype=np.float32).T)
    f = lambda k: np.asarray(inp[k], dtype=np.float32)
    m = {
        "l0_wqt": tr(inp["L0_attn_Wq"]),
        "l0_wkt": tr(inp["L0_attn_Wk"]),
        "l0_wvt": tr(inp["L0_attn_Wv"]),
        "l0_wot": tr(inp["L0_attn_Wo"]),
        "l0_bq": f("L0_attn_bq").reshape(1, D),
        "l0_bk": f("L0_attn_bk").reshape(1, D),
        "l0_bv": f("L0_attn_bv").reshape(1, D),
        "l0_bo": f("L0_attn_bo").reshape(1, D),
        "l0f_wrt": tr(inp["L0_ffn_W_r"]),
        "l0f_wkt": tr(inp["L0_ffn_W_k"]),
        "l0f_wvt": tr(inp["L0_ffn_W_v"]),
        "l0f_mur": np.ascontiguousarray(f("L0_ffn_mu_r").reshape(8, 128).T),
        "l0f_muk": np.ascontiguousarray(f("L0_ffn_mu_k").reshape(8, 128).T),
        "l1_wq": np.ascontiguousarray(f("L1_attn_Wq")),
        "l1_wk": np.ascontiguousarray(f("L1_attn_Wk")),
        "l1_wv": np.ascontiguousarray(f("L1_attn_Wv")),
        "l1_bq": f("L1_attn_bq").reshape(1, D),
        "l1_bk": f("L1_attn_bk").reshape(1, D),
        "l1_bv": f("L1_attn_bv").reshape(1, D),
        "moe_gwt": tr(inp["L1_ffn_gate_w"]),
        "moe_wg": np.ascontiguousarray(np.transpose(f("L1_ffn_expert_gate"), (0, 2, 1))),
        "moe_wu": np.ascontiguousarray(np.transpose(f("L1_ffn_expert_up"), (0, 2, 1))),
        "moe_wd": np.ascontiguousarray(np.transpose(f("L1_ffn_expert_down"), (0, 2, 1))),
        "moe_sg": tr(inp["L1_ffn_shared_gate_w"]),
        "moe_su": tr(inp["L1_ffn_shared_up_w"]),
        "moe_sd": tr(inp["L1_ffn_shared_down_w"]),
        "identf": np.eye(128, dtype=np.float32),
        "identb": np.eye(128, dtype=ml_dtypes.bfloat16),
        "onesr": np.ones((1, 128), dtype=ml_dtypes.bfloat16),
        "onesc": np.ones((128, 1), dtype=ml_dtypes.bfloat16),
        "zeros8": np.zeros((128, 8), dtype=ml_dtypes.bfloat16),
    }
    return m


def kernel(**inp):
    x = np.asarray(inp["L0_attn_x"], dtype=np.float32)
    base = prep_inputs(**inp)
    nc = build_program(T=4096, dev=False)
    in_maps = []
    for b in range(8):
        mm = dict(base)
        mm["x"] = np.ascontiguousarray(x[b])
        in_maps.append(mm)
    res = run_bass_kernel_spmd(nc, in_maps, core_ids=list(range(8)))
    return np.stack([np.asarray(r["out"]) for r in res.results], axis=0)


# version 03
# speedup vs baseline: 1.1600x; 1.1600x over previous
import numpy as np
import ml_dtypes
from contextlib import ExitStack

import concourse.bass as bass
import concourse.mybir as mybir
from concourse.bass_utils import run_bass_kernel_spmd

F32 = mybir.dt.float32
BF16 = mybir.dt.bfloat16
AF = mybir.ActivationFunctionType
ALU = mybir.AluOpType
AX = mybir.AxisListType

D = 1024
F_CUT = 99
DEV_DUMP = False
EPS0 = 1e-6
LNEPS = 1e-5

ENGS = ("sync", "gpsimd", "vector", "scalar", "tensor")


class PB:

    ENG_ORD = {"vector": "zzv", "scalar": "zzs", "tensor": "zzt"}
    MILESTONE_ENG = {"dv": "vector", "ac": "scalar", "pe": "tensor"}

    def __init__(self, nc, paranoid=False):
        self.nc = nc
        self.paranoid = paranoid
        self.streams = {e: [] for e in ENGS}
        self.cnt = {}
        self._sems = {}
        self.dma_sem_eng = {}

    def sem(self, name):
        if name not in self._sems:
            self._sems[name] = self.nc.alloc_semaphore(name)
            self.cnt[name] = 0
        return self._sems[name]

    def do(self, eng, fn, inc=None, n=1):
        if self.paranoid and eng in self.ENG_ORD:
            z = self.ENG_ORD[eng]
            sem = self.sem(z)
            prev = self.cnt[z]
            self.cnt[z] = prev + 1
            tgt = prev + 1
            if prev > 0:
                self.streams[eng].append(
                    lambda e, sem=sem, prev=prev: e.wait_ge(sem, prev)
                )

            def wrapped(e, fn=fn, sem=sem):
                fn(e).then_inc(sem, 1)

            self.streams[eng].append(wrapped)
            return (z, tgt)
        if inc is None:
            self.streams[eng].append(fn)
            return None
        sem = self.sem(inc)
        self.cnt[inc] += n
        tgt = self.cnt[inc]

        def wrapped(e, fn=fn, sem=sem, n=n):
            fn(e).then_inc(sem, n)

        self.streams[eng].append(wrapped)
        return (inc, tgt)

    def wait(self, eng, ms):
        if ms is None:
            return
        name, val = ms
        if val <= 0:
            return
        sem = self.sem(name)
        self.streams[eng].append(lambda e, sem=sem, val=val: e.wait_ge(sem, val))

    def dma(self, eng, out, in_, sem):
        prev = self.dma_sem_eng.get(sem)
        assert prev is None or prev == eng, f"dma sem {sem} used from {prev} and {eng}"
        self.dma_sem_eng[sem] = eng
        return self.do(
            eng, lambda e, out=out, in_=in_: e.dma_start(out=out, in_=in_), inc=sem, n=16
        )

    def now(self, name):
        if self.paranoid and name in self.MILESTONE_ENG:
            z = self.ENG_ORD[self.MILESTONE_ENG[name]]
            return (z, self.cnt.get(z, 0))
        return (name, self.cnt.get(name, 0))

    def barrier(self):
        bar = self.sem("bar")
        for s, eng in self.dma_sem_eng.items():
            self.wait(eng, (s, self.cnt[s]))
        self.cnt["bar"] += len(ENGS)
        tgt = self.cnt["bar"]
        for eng in ENGS:
            self.streams[eng].append(lambda e, bar=bar: e.sem_inc(bar, 1))
            self.streams[eng].append(lambda e, bar=bar, tgt=tgt: e.wait_ge(bar, tgt))

    def flush(self, block):
        for eng in ENGS:
            lst = self.streams[eng]
            if not lst:
                continue
            getattr(block, eng)(lambda e, lst=lst: [f(e) for f in lst])
            self.streams[eng] = []



def emit_ln(pb, c, xin_ap, out_ap, junk, wst, psum_in=False):
    pb.wait("scalar", pb.now("dv"))
    acs = pb.do(
        "scalar",
        lambda e: e.activation(
            out=junk, in_=xin_ap, func=AF.Square, accum_out=wst[:, 3:4]
        ),
        inc="ac",
    )
    if psum_in:
        pb.wait("vector", acs)
    r1 = pb.do("vector", lambda e: e.reduce_sum(wst[:, 2:3], xin_ap, axis=AX.X), inc="dv")
    pb.wait("vector", r1)
    pb.do("vector", lambda e: e.tensor_scalar_mul(wst[:, 1:2], wst[:, 2:3], 1.0 / D))
    pb.wait("vector", acs)
    r3 = pb.do("vector", lambda e: e.tensor_scalar_mul(wst[:, 4:5], wst[:, 3:4], 1.0 / D), inc="dv")
    pb.wait("vector", r3)
    r4 = pb.do("vector", lambda e: e.tensor_mul(wst[:, 5:6], wst[:, 1:2], wst[:, 1:2]), inc="dv")
    pb.wait("vector", r4)
    dvv = pb.do("vector", lambda e: e.tensor_sub(wst[:, 6:7], wst[:, 4:5], wst[:, 5:6]), inc="dv")
    pb.wait("scalar", dvv)
    acv = pb.do(
        "scalar",
        lambda e: e.activation(out=wst[:, 7:8], in_=wst[:, 6:7], func=AF.Sqrt, bias=c["lneps"][:, 0:1]),
        inc="ac",
    )
    pb.wait("vector", acv)
    dvr2 = pb.do("vector", lambda e: e.reciprocal(wst[:, 0:1], wst[:, 7:8]), inc="dv")
    pb.wait("vector", dvr2)
    vap = pb.do(
        "vector",
        lambda e: e.scalar_tensor_tensor(
            out=out_ap, in0=xin_ap, scalar=wst[:, 1:2],
            in1=wst[:, 0:1].to_broadcast([128, D]),
            op0=ALU.subtract, op1=ALU.mult,
        ),
        inc="dv",
    )
    return vap


def emit_proj(pb, c, lnT, w_sbuf, bias_row, psum_ap, n_chunks=8, nout=D):
    nh = (nout + 511) // 512

    def fn(e, lnT=lnT, w=w_sbuf, b=bias_row, p=psum_ap, n=n_chunks, nout=nout, nh=nh):
        ins = None
        for cc in range(n):
            lhs = lnT[:, 128 * cc : 128 * cc + 128]
            for half in range(nh):
                w0 = nout * cc + 512 * half
                nw = min(512, nout - 512 * half)
                ins = e.matmul(
                    p[:, 512 * half : 512 * half + nw],
                    lhs,
                    w[:, w0 : w0 + nw],
                    start=(cc == 0), stop=(cc == n - 1 and b is None),
                )
        if b is not None:
            for half in range(nh):
                nw = min(512, nout - 512 * half)
                ins = e.matmul(
                    p[:, 512 * half : 512 * half + nw],
                    c["ones_row"][0:1, :],
                    b[0:1, 512 * half : 512 * half + nw],
                    start=False, stop=True,
                )
        return ins

    return pb.do("tensor", fn, inc="pe")


def emit_transpose8(pb, c, src_bf, pt, n=8):
    def fn(e, s=src_bf, pt=pt, n=n):
        ins = None
        for cc in range(n):
            ins = e.transpose(
                pt[:, 128 * cc : 128 * cc + 128],
                s[:, 128 * cc : 128 * cc + 128],
                c["ident_bf"][:, :],
            )
        return ins

    return pb.do("tensor", fn, inc="pe")



def emit_l0_attn(pb, c, ctx, T, x_in, scr_q, scr_k, scr_v, scr_out, n_sweeps=6):
    nc = pb.nc
    NT = T // 128

    wq = ctx.enter_context(nc.sbuf_tensor("l0a_wq", [128, 8 * 1024], BF16))
    wk = ctx.enter_context(nc.sbuf_tensor("l0a_wk", [128, 8 * 1024], BF16))
    wv = ctx.enter_context(nc.sbuf_tensor("l0a_wv", [128, 8 * 1024], BF16))
    wo = ctx.enter_context(nc.sbuf_tensor("l0a_wo", [128, 8 * 1024], BF16))
    bq = ctx.enter_context(nc.sbuf_tensor("l0a_bq", [1, D], BF16))
    bk = ctx.enter_context(nc.sbuf_tensor("l0a_bk", [1, D], BF16))
    bv = ctx.enter_context(nc.sbuf_tensor("l0a_bv", [1, D], BF16))
    bo = ctx.enter_context(nc.sbuf_tensor("l0a_bo", [1, D], BF16))
    xin = [ctx.enter_context(nc.sbuf_tensor(f"l0a_xin{i}", [128, D], F32)) for i in range(2)]
    ln_bf = ctx.enter_context(nc.sbuf_tensor("l0a_ln", [128, D], BF16))
    lnT = ctx.enter_context(nc.sbuf_tensor("l0a_lnT", [128, D], BF16))
    q_bf = [ctx.enter_context(nc.sbuf_tensor(f"l0a_q{i}", [128, D], BF16)) for i in range(2)]
    k_bf = [ctx.enter_context(nc.sbuf_tensor(f"l0a_k{i}", [128, D], BF16)) for i in range(2)]
    v_bf = [ctx.enter_context(nc.sbuf_tensor(f"l0a_v{i}", [128, D], BF16)) for i in range(2)]
    wst = ctx.enter_context(nc.sbuf_tensor("l0a_wst", [128, 8], F32))
    sqsum = ctx.enter_context(nc.sbuf_tensor("l0a_qsum", [1, D], F32))
    sksum = ctx.enter_context(nc.sbuf_tensor("l0a_ksum", [1, D], F32))
    rowbf = ctx.enter_context(nc.sbuf_tensor("l0a_rowbf", [1, D], BF16))
    bc1 = ctx.enter_context(nc.sbuf_tensor("l0a_bc1", [128, D], F32))
    bc2 = ctx.enter_context(nc.sbuf_tensor("l0a_bc2", [128, D], F32))
    tmpf = ctx.enter_context(nc.sbuf_tensor("l0a_tmpf", [128, D], F32))
    s16a = ctx.enter_context(nc.sbuf_tensor("l0a_s16a", [128, 16], F32))
    s16b = ctx.enter_context(nc.sbuf_tensor("l0a_s16b", [128, 16], F32))
    si_all = ctx.enter_context(nc.sbuf_tensor("l0a_si", [128, 16 * NT], F32))
    sa_all = ctx.enter_context(nc.sbuf_tensor("l0a_sa", [128, 16 * NT], F32))
    csT = ctx.enter_context(nc.sbuf_tensor("l0a_csT", [16, T], F32))
    scT = ctx.enter_context(nc.sbuf_tensor("l0a_scT", [16, T], F32))
    s16 = ctx.enter_context(nc.sbuf_tensor("l0a_s16", [16, 4], F32))
    kv_s = ctx.enter_context(nc.sbuf_tensor("l0a_kv", [64, 1024], BF16))
    qsiT = ctx.enter_context(nc.sbuf_tensor("l0a_qsiT", [64, 16 * 128], BF16))
    aT = ctx.enter_context(nc.sbuf_tensor("l0a_aT", [128, D], BF16))
    y1 = ctx.enter_context(nc.sbuf_tensor("l0a_y1", [128, D], F32))
    x1o = [ctx.enter_context(nc.sbuf_tensor(f"l0a_x1o{i}", [128, D], F32)) for i in range(2)]

    pA = ctx.enter_context(nc.psum_tensor("l0a_pA", [128, D], F32))
    pB = ctx.enter_context(nc.psum_tensor("l0a_pB", [128, D], F32))
    pt = ctx.enter_context(nc.psum_tensor("l0a_pt", [128, D], BF16))
    ps1 = ctx.enter_context(nc.psum_tensor("l0a_ps1", [1, D], F32))

    for w_s, name in ((wq, "l0_wqt"), (wk, "l0_wkt"), (wv, "l0_wvt"), (wo, "l0_wot")):
        src = c[name].rearrange("(c p) n -> p c n", p=128)
        dst = w_s[:, :].rearrange("p (c n) -> p c n", c=8)
        pb.dma("gpsimd", dst, src, "wl")
    for b_s, name in ((bq, "l0_bq"), (bk, "l0_bk"), (bv, "l0_bv"), (bo, "l0_bo")):
        pb.dma("gpsimd", b_s[:, :], c[name], "wl")
    wl_all = pb.now("wl")

    ld_x = {}
    for t in range(min(2, NT)):
        ld_x[t] = pb.dma("sync", xin[t % 2][:, :], x_in[128 * t : 128 * t + 128, :], f"lx{t % 2}")

    ac_sq, ac_sk, dv_sv, st_q, st_k, st_v = {}, {}, {}, {}, {}, {}
    dv_qsum = dv_ksum = None
    for t in range(NT):
        bt = t % 2
        pb.wait("vector", ld_x[t])
        pb.wait("scalar", ld_x[t])
        apply_v = emit_ln(pb, c, xin[bt][:, :], ln_bf[:, :], tmpf[:, :], wst)
        if t + 2 < NT:
            pb.wait("sync", apply_v)
            ld_x[t + 2] = pb.dma("sync", xin[bt][:, :], x_in[128 * (t + 2) : 128 * (t + 2) + 128, :], f"lx{bt}")
        pb.wait("tensor", apply_v)
        pe_tr = emit_transpose8(pb, c, ln_bf, pt)
        pb.wait("vector", pe_tr)
        dv_lnT = pb.do("vector", lambda e: e.tensor_copy(lnT[:, :], pt[:, :]), inc="dv")
        pb.wait("tensor", dv_lnT)
        if t == 0:
            pb.wait("tensor", wl_all)
        if t >= 1:
            pb.wait("tensor", dv_sv[t - 1])
        pe_q = emit_proj(pb, c, lnT, wq, bq, pA)
        if t >= 1:
            pb.wait("tensor", ac_sk[t - 1])
        pe_k = emit_proj(pb, c, lnT, wk, bk, pB)
        pb.wait("scalar", pe_q)
        if t >= 2:
            pb.wait("scalar", st_q[t - 2])
        ac_sq[t] = pb.do(
            "scalar",
            lambda e, bt=bt: e.activation(out=q_bf[bt][:, :], in_=pA[:, :], func=AF.Sigmoid),
            inc="ac",
        )
        pb.wait("tensor", ac_sq[t])
        pe_v = emit_proj(pb, c, lnT, wv, bv, pA)
        pb.wait("scalar", pe_k)
        if t >= 2:
            pb.wait("scalar", st_k[t - 2])
        ac_sk[t] = pb.do(
            "scalar",
            lambda e, bt=bt: e.activation(out=k_bf[bt][:, :], in_=pB[:, :], func=AF.Sigmoid),
            inc="ac",
        )
        pb.wait("vector", pe_v)
        if t >= 2:
            pb.wait("vector", st_v[t - 2])
        dv_sv[t] = pb.do("vector", lambda e, bt=bt: e.tensor_copy(v_bf[bt][:, :], pA[:, :]), inc="dv")
        pb.wait("tensor", ac_sq[t])
        if dv_ksum is not None:
            pb.wait("tensor", dv_ksum)

        def qsum_fn(e, q=q_bf[bt]):
            e.matmul(ps1[0:1, 0:512], c["ones_col"][:, 0:1], q[:, 0:512], start=True, stop=True)
            return e.matmul(ps1[0:1, 512:1024], c["ones_col"][:, 0:1], q[:, 512:1024], start=True, stop=True)

        pe_qs = pb.do("tensor", qsum_fn, inc="pe")
        pb.wait("vector", pe_qs)
        if t == 0:
            dv_qsum = pb.do("vector", lambda e: e.tensor_copy(sqsum[0:1, :], ps1[0:1, :]), inc="dv")
        else:
            dv_qsum = pb.do("vector", lambda e: e.tensor_add(sqsum[0:1, :], sqsum[0:1, :], ps1[0:1, :]), inc="dv")
        pb.wait("tensor", ac_sk[t])
        pb.wait("tensor", dv_qsum)

        def ksum_fn(e, k=k_bf[bt]):
            e.matmul(ps1[0:1, 0:512], c["ones_col"][:, 0:1], k[:, 0:512], start=True, stop=True)
            return e.matmul(ps1[0:1, 512:1024], c["ones_col"][:, 0:1], k[:, 512:1024], start=True, stop=True)

        pe_ks = pb.do("tensor", ksum_fn, inc="pe")
        pb.wait("vector", pe_ks)
        if t == 0:
            dv_ksum = pb.do("vector", lambda e: e.tensor_copy(sksum[0:1, :], ps1[0:1, :]), inc="dv")
        else:
            dv_ksum = pb.do("vector", lambda e: e.tensor_add(sksum[0:1, :], sksum[0:1, :], ps1[0:1, :]), inc="dv")
        pb.wait("sync", ac_sq[t])
        st_q[t] = pb.dma("sync", scr_q[128 * t : 128 * t + 128, :], q_bf[bt][:, :], f"sq{bt}")
        pb.wait("sync", ac_sk[t])
        st_k[t] = pb.dma("sync", scr_k[128 * t : 128 * t + 128, :], k_bf[bt][:, :], f"sk{bt}")
        pb.wait("sync", dv_sv[t])
        st_v[t] = pb.dma("sync", scr_v[128 * t : 128 * t + 128, :], v_bf[bt][:, :], f"sv{bt}")

    pb.barrier()
    if n_sweeps <= 1:
        return

    def bcast_row(src_row_ap, dst_bc):
        dvr = pb.do(
            "vector",
            lambda e: e.tensor_scalar_add(rowbf[0:1, :], src_row_ap, EPS0),
            inc="dv",
        )
        pb.wait("tensor", dvr)

        def fn(e):
            e.matmul(pA[:, 0:512], c["ones_row"][0:1, :], rowbf[0:1, 0:512], start=True, stop=True)
            return e.matmul(pA[:, 512:1024], c["ones_row"][0:1, :], rowbf[0:1, 512:1024], start=True, stop=True)

        pev = pb.do("tensor", fn, inc="pe")
        pb.wait("vector", pev)
        return pb.do("vector", lambda e: e.tensor_copy(dst_bc[:, :], pA[:, :]), inc="dv")

    bcast_row(sksum[0:1, :], bc1)
    bcast_row(sqsum[0:1, :], bc2)

    ld_q, ld_k, ld_v = {}, {}, {}
    for t in range(min(2, NT)):
        ld_q[t] = pb.dma("sync", q_bf[t % 2][:, :], scr_q[128 * t : 128 * t + 128, :], f"lq{t % 2}")
        ld_k[t] = pb.dma("sync", k_bf[t % 2][:, :], scr_k[128 * t : 128 * t + 128, :], f"lk{t % 2}")

    pe_s2 = pe_s1 = None
    for t in range(NT):
        bt = t % 2
        pb.wait("vector", ld_q[t])
        pb.wait("vector", ld_k[t])
        if pe_s1 is not None:
            pb.wait("vector", pe_s1)
        pb.do(
            "vector",
            lambda e, q=q_bf[bt]: e.scalar_tensor_tensor(
                out=tmpf[:, :], in0=q[:, :], scalar=EPS0, in1=bc1[:, :],
                op0=ALU.add, op1=ALU.mult,
            ),
        )
        rqa = pb.do(
            "vector",
            lambda e: e.reduce_sum(
                s16a[:, :], tmpf[:, :].rearrange("p (h d) -> p h d", h=16), axis=AX.X
            ),
            inc="dv",
        )
        pb.wait("vector", rqa)
        dvrq = pb.do("vector", lambda e, t=t: e.reciprocal(si_all[:, 16 * t : 16 * t + 16], s16a[:, :]), inc="dv")
        pb.wait("vector", dvrq)
        dv_qsi = pb.do(
            "vector",
            lambda e, q=q_bf[bt], t=t: e.tensor_tensor(
                out=ln_bf[:, :].rearrange("p (h d) -> p h d", h=16),
                in0=q[:, :].rearrange("p (h d) -> p h d", h=16),
                in1=si_all[:, 16 * t : 16 * t + 16].unsqueeze(2).to_broadcast([128, 16, 64]),
                op=ALU.mult,
            ),
            inc="dv",
        )
        pb.wait("tensor", dv_qsi)

        def s2fn(e, t=t):
            e.matmul(pB[0:1, 0:512], c["ones_col"][:, 0:1], ln_bf[:, 0:512], start=(t == 0), stop=(t == NT - 1))
            return e.matmul(pB[0:1, 512:1024], c["ones_col"][:, 0:1], ln_bf[:, 512:1024], start=(t == 0), stop=(t == NT - 1))

        pe_s2 = pb.do("tensor", s2fn, inc="pe")
        pb.wait("vector", pe_s2)
        pb.do(
            "vector",
            lambda e, k=k_bf[bt]: e.scalar_tensor_tensor(
                out=tmpf[:, :], in0=k[:, :], scalar=EPS0, in1=bc2[:, :],
                op0=ALU.add, op1=ALU.mult,
            ),
        )
        rkb = pb.do(
            "vector",
            lambda e: e.reduce_sum(
                s16b[:, :], tmpf[:, :].rearrange("p (h d) -> p h d", h=16), axis=AX.X
            ),
            inc="dv",
        )
        pb.wait("vector", rkb)
        dvrk = pb.do("vector", lambda e: e.reciprocal(s16b[:, :], s16b[:, :]), inc="dv")
        pb.wait("vector", dvrk)
        dv_kso = pb.do(
            "vector",
            lambda e, k=k_bf[bt]: e.tensor_tensor(
                out=ln_bf[:, :].rearrange("p (h d) -> p h d", h=16),
                in0=k[:, :].rearrange("p (h d) -> p h d", h=16),
                in1=s16b[:, :].unsqueeze(2).to_broadcast([128, 16, 64]),
                op=ALU.mult,
            ),
            inc="dv",
        )
        pb.wait("tensor", dv_kso)

        def s1fn(e, t=t):
            e.matmul(ps1[0:1, 0:512], c["ones_col"][:, 0:1], ln_bf[:, 0:512], start=(t == 0), stop=(t == NT - 1))
            return e.matmul(ps1[0:1, 512:1024], c["ones_col"][:, 0:1], ln_bf[:, 512:1024], start=(t == 0), stop=(t == NT - 1))

        pe_s1 = pb.do("tensor", s1fn, inc="pe")
        if t + 2 < NT:
            pb.wait("sync", pe_s1)
            ld_q[t + 2] = pb.dma("sync", q_bf[bt][:, :], scr_q[128 * (t + 2) : 128 * (t + 2) + 128, :], f"lq{bt}")
            ld_k[t + 2] = pb.dma("sync", k_bf[bt][:, :], scr_k[128 * (t + 2) : 128 * (t + 2) + 128, :], f"lk{bt}")

    pb.wait("vector", pe_s1)
    pb.wait("vector", pe_s2)
    bcast_row(ps1[0:1, :], bc1)
    bcast_row(pB[0:1, :], bc2)

    pb.barrier()
    if n_sweeps <= 2:
        return

    for t in range(min(2, NT)):
        ld_q[t] = pb.dma("sync", q_bf[t % 2][:, :], scr_q[128 * t : 128 * t + 128, :], f"lq{t % 2}")
        ld_k[t] = pb.dma("sync", k_bf[t % 2][:, :], scr_k[128 * t : 128 * t + 128, :], f"lk{t % 2}")
    ac_sa = pe_trc = None
    dv_cs = {}
    for t in range(NT):
        bt = t % 2
        pb.wait("vector", ld_q[t])
        pb.wait("vector", ld_k[t])
        if ac_sa is not None:
            pb.wait("vector", ac_sa)
        if pe_trc is not None:
            pb.wait("vector", pe_trc)
        pb.do(
            "vector",
            lambda e, q=q_bf[bt]: e.scalar_tensor_tensor(
                out=tmpf[:, :], in0=q[:, :], scalar=EPS0, in1=bc1[:, :],
                op0=ALU.add, op1=ALU.mult,
            ),
        )
        dvr = pb.do(
            "vector",
            lambda e: e.reduce_sum(
                s16a[:, :], tmpf[:, :].rearrange("p (h d) -> p h d", h=16), axis=AX.X
            ),
            inc="dv",
        )
        pb.wait("scalar", dvr)
        ac_sa = pb.do(
            "scalar",
            lambda e, t=t: e.activation(
                out=sa_all[:, 16 * t : 16 * t + 16], in_=s16a[:, :], func=AF.Sigmoid
            ),
            inc="ac",
        )
        pb.do(
            "vector",
            lambda e, k=k_bf[bt]: e.scalar_tensor_tensor(
                out=tmpf[:, :], in0=k[:, :], scalar=EPS0, in1=bc2[:, :],
                op0=ALU.add, op1=ALU.mult,
            ),
        )
        rcb = pb.do(
            "vector",
            lambda e: e.reduce_sum(
                s16b[:, :], tmpf[:, :].rearrange("p (h d) -> p h d", h=16), axis=AX.X
            ),
            inc="dv",
        )
        pb.wait("vector", rcb)
        rcm = pb.do("vector", lambda e: e.tensor_scalar_min(s16b[:, :], s16b[:, :], 1.0), inc="dv")
        pb.wait("vector", rcm)
        dv_cl = pb.do("vector", lambda e: e.tensor_scalar_max(s16b[:, :], s16b[:, :], -1.0), inc="dv")
        pb.wait("tensor", dv_cl)
        if t >= 1:
            pb.wait("tensor", dv_cs[t - 1])
        pe_trc = pb.do(
            "tensor",
            lambda e: e.transpose(pA[0:16, 0:128], s16b[:, :], c["ident_f32"][:, :]),
            inc="pe",
        )
        pb.wait("vector", pe_trc)
        dv_cs[t] = pb.do(
            "vector",
            lambda e, t=t: e.tensor_copy(csT[0:16, 128 * t : 128 * t + 128], pA[0:16, 0:128]),
            inc="dv",
        )
        if t + 2 < NT:
            pb.wait("sync", dv_cs[t])
            ld_q[t + 2] = pb.dma("sync", q_bf[bt][:, :], scr_q[128 * (t + 2) : 128 * (t + 2) + 128, :], f"lq{bt}")
            ld_k[t + 2] = pb.dma("sync", k_bf[bt][:, :], scr_k[128 * (t + 2) : 128 * (t + 2) + 128, :], f"lk{bt}")

    pb.barrier()
    if n_sweeps <= 3:
        return

    rdm = pb.do("vector", lambda e: e.reduce_max(s16[0:16, 0:1], csT[0:16, :], axis=AX.X), inc="dv")
    pb.wait("vector", rdm)
    dvm = pb.do("vector", lambda e: e.tensor_scalar_mul(s16[0:16, 1:2], s16[0:16, 0:1], -1.0), inc="dv")
    pb.wait("scalar", dvm)
    acce = pb.do(
        "scalar",
        lambda e: e.activation(
            out=scT[0:16, :], in_=csT[0:16, :], func=AF.Exp,
            bias=s16[0:16, 1:2], accum_out=s16[0:16, 2:3],
        ),
        inc="ac",
    )
    pb.wait("vector", acce)
    rrc = pb.do("vector", lambda e: e.reciprocal(s16[0:16, 3:4], s16[0:16, 2:3]), inc="dv")
    pb.wait("vector", rrc)
    dvsc = pb.do("vector", lambda e: e.tensor_scalar_mul(s16[0:16, 3:4], s16[0:16, 3:4], float(T)), inc="dv")
    pb.wait("vector", dvsc)
    dvscT = pb.do(
        "vector",
        lambda e: e.tensor_scalar(
            scT[0:16, :], scT[0:16, :], s16[0:16, 3:4], None, op0=ALU.mult
        ),
        inc="dv",
    )
    if DEV_DUMP:
        pb.wait("sync", dvscT)
        pb.dma("sync", c["scr_dbg"][0:16, 0:T], csT[0:16, :], "dbg1")
        pb.dma("sync", c["scr_dbg"][16:32, 0:T], scT[0:16, :], "sdbg2")
        pb.dma("sync", c["scr_dbg"][32:33, 0:D], bc1[0:1, :], "dbg3")
        pb.dma("sync", c["scr_dbg"][33:34, 0:D], bc2[0:1, :], "dbg4")
        pb.dma("sync", c["scr_dbg"][34:35, 0:D], sksum[0:1, :], "dbg5")
        pb.dma("sync", c["scr_dbg"][35:36, 0:D], sqsum[0:1, :], "dbg6")
        pb.dma("sync", c["scr_dbg"][36:37, 0:16].rearrange("a (b c) -> a b c", b=4), s16[0:16, 0:4].rearrange("(a b) c -> a b c", a=1), "dbg7") if False else None
        pb.dma("sync", c["scr_dbg"][40:56, 0:4], s16[0:16, 0:4], "dbg8")

    pb.barrier()
    if n_sweeps <= 4:
        return

    for t in range(min(2, NT)):
        ld_k[t] = pb.dma("sync", k_bf[t % 2][:, :], scr_k[128 * t : 128 * t + 128, :], f"lk{t % 2}")
        ld_v[t] = pb.dma("sync", v_bf[t % 2][:, :], scr_v[128 * t : 128 * t + 128, :], f"lv{t % 2}")
    dv_vsc = {}
    pe_kv = None
    for t in range(NT):
        bt = t % 2
        if t >= 1:
            pb.wait("tensor", dv_vsc[t - 1])
        pe_tr = pb.do(
            "tensor",
            lambda e, t=t: e.transpose(
                pA[:, 0:16], scT[0:16, 128 * t : 128 * t + 128], c["ident_f32"][0:16, 0:16]
            ),
            inc="pe",
        )
        pb.wait("vector", pe_tr)
        pb.wait("vector", ld_v[t])
        if pe_kv is not None:
            pb.wait("vector", pe_kv)
        dvcp = pb.do("vector", lambda e: e.tensor_copy(s16a[:, :], pA[:, 0:16]), inc="dv")
        pb.wait("vector", dvcp)
        dv_vsc[t] = pb.do(
            "vector",
            lambda e, v=v_bf[bt]: e.tensor_tensor(
                out=ln_bf[:, :].rearrange("p (h d) -> p h d", h=16),
                in0=v[:, :].rearrange("p (h d) -> p h d", h=16),
                in1=s16a[:, :].unsqueeze(2).to_broadcast([128, 16, 64]),
                op=ALU.mult,
            ),
            inc="dv",
        )
        pb.wait("tensor", dv_vsc[t])
        pb.wait("tensor", ld_k[t])

        def kvfn(e, k=k_bf[bt], t=t):
            ins = None
            for h in range(16):
                ins = e.matmul(
                    pB[0:64, 64 * h : 64 * h + 64],
                    k[:, 64 * h : 64 * h + 64],
                    ln_bf[:, 64 * h : 64 * h + 64],
                    start=(t == 0 and h % 8 == 0),
                    stop=(t == NT - 1 and h % 8 == 7),
                )
            return ins

        pe_kv = pb.do("tensor", kvfn, inc="pe")
        if t + 2 < NT:
            pb.wait("sync", pe_kv)
            pb.wait("sync", dv_vsc[t])
            ld_k[t + 2] = pb.dma("sync", k_bf[bt][:, :], scr_k[128 * (t + 2) : 128 * (t + 2) + 128, :], f"lk{bt}")
            ld_v[t + 2] = pb.dma("sync", v_bf[bt][:, :], scr_v[128 * (t + 2) : 128 * (t + 2) + 128, :], f"lv{bt}")
    pb.wait("vector", pe_kv)
    dv_kvs = pb.do("vector", lambda e: e.tensor_copy(kv_s[0:64, :], pB[0:64, :]), inc="dv")

    pb.barrier()
    if n_sweeps <= 5:
        return

    for t in range(min(2, NT)):
        ld_q[t] = pb.dma("sync", q_bf[t % 2][:, :], scr_q[128 * t : 128 * t + 128, :], f"lq{t % 2}")
        ld_x[t] = pb.dma("sync", xin[t % 2][:, :], x_in[128 * t : 128 * t + 128, :], f"lx{t % 2}")
    st_o = {}
    dv_fin = {}
    pe_y = None
    for t in range(NT):
        bt = t % 2
        pb.wait("vector", ld_q[t])
        if pe_y is not None:
            pb.wait("vector", pe_y)
        if F_CUT < 9 and t >= 0:
            emit_f_cut(pb, c, t, bt, NT, locals())
            continue
        dv_q = pb.do(
            "vector",
            lambda e, q=q_bf[bt], t=t: e.tensor_tensor(
                out=ln_bf[:, :].rearrange("p (h d) -> p h d", h=16),
                in0=q[:, :].rearrange("p (h d) -> p h d", h=16),
                in1=si_all[:, 16 * t : 16 * t + 16].unsqueeze(2).to_broadcast([128, 16, 64]),
                op=ALU.mult,
            ),
            inc="dv",
        )
        pb.wait("tensor", dv_q)
        if t >= 1:
            pb.wait("tensor", dv_fin[t - 1])
        dv_qT = None
        for grp in range(2):
            def trf(e, grp=grp):
                ins = None
                for hh in range(8):
                    h = 8 * grp + hh
                    ins = e.transpose(
                        pt[0:64, 128 * hh : 128 * hh + 128],
                        ln_bf[:, 64 * h : 64 * h + 64],
                        c["ident_bf"][:, :],
                    )
                return ins

            if grp == 1:
                pb.wait("tensor", dv_qT)
            pe_tr = pb.do("tensor", trf, inc="pe")
            pb.wait("vector", pe_tr)
            dv_qT = pb.do(
                "vector",
                lambda e, grp=grp: e.tensor_copy(
                    qsiT[0:64, 1024 * grp : 1024 * grp + 1024], pt[0:64, 0:1024]
                ),
                inc="dv",
            )
        pb.wait("tensor", dv_qT)

        def outfn(e):
            ins = None
            for h in range(16):
                ins = e.matmul(
                    pA[:, 64 * h : 64 * h + 64],
                    qsiT[0:64, 128 * h : 128 * h + 128],
                    kv_s[0:64, 64 * h : 64 * h + 64],
                    start=True, stop=True,
                )
            return ins

        pe_out = pb.do("tensor", outfn, inc="pe")
        pb.wait("vector", pe_out)
        dv_at = pb.do(
            "vector",
            lambda e, t=t: e.tensor_tensor(
                out=ln_bf[:, :].rearrange("p (h d) -> p h d", h=16),
                in0=pA[:, :].rearrange("p (h d) -> p h d", h=16),
                in1=sa_all[:, 16 * t : 16 * t + 16].unsqueeze(2).to_broadcast([128, 16, 64]),
                op=ALU.mult,
            ),
            inc="dv",
        )
        pb.wait("tensor", dv_at)
        pe_tr2 = emit_transpose8(pb, c, ln_bf, pt)
        pb.wait("vector", pe_tr2)
        dv_aT = pb.do("vector", lambda e: e.tensor_copy(aT[:, :], pt[:, :]), inc="dv")
        pb.wait("tensor", dv_aT)
        pe_y = emit_proj(pb, c, aT, wo, bo, pB)
        pb.wait("vector", pe_y)
        pb.wait("scalar", pe_y)
        emit_ln(pb, c, pB[:, :], y1[:, :], tmpf[:, :], wst, psum_in=True)
        pb.wait("vector", ld_x[t])
        if t >= 2:
            pb.wait("vector", st_o[t - 2])
        dv_fin[t] = pb.do(
            "vector", lambda e, bt=bt: e.tensor_add(x1o[bt][:, :], xin[bt][:, :], y1[:, :]), inc="dv"
        )
        pb.wait("sync", dv_fin[t])
        st_o[t] = pb.dma("sync", scr_out[128 * t : 128 * t + 128, :], x1o[bt][:, :], f"so{bt}")
        if t + 2 < NT:
            ld_q[t + 2] = pb.dma("sync", q_bf[bt][:, :], scr_q[128 * (t + 2) : 128 * (t + 2) + 128, :], f"lq{bt}")
            ld_x[t + 2] = pb.dma("sync", xin[bt][:, :], x_in[128 * (t + 2) : 128 * (t + 2) + 128, :], f"lx{bt}")

    pb.barrier()




def emit_f_cut(pb, c, t, bt, NT, L):
    xin, ln_bf, q_bf, qsiT, kv_s, sa_all, si_all, aT, y1, x1o, tmpf, wst = (
        L["xin"], L["ln_bf"], L["q_bf"], L["qsiT"], L["kv_s"], L["sa_all"],
        L["si_all"], L["aT"], L["y1"], L["x1o"], L["tmpf"], L["wst"])
    pA, pB, pt = L["pA"], L["pB"], L["pt"]
    wo, bo = L["wo"], L["bo"]
    scr_out, scr_q, x_in = L["scr_out"], L["scr_q"], L["x_in"]
    ld_q, ld_x, st_o, dv_fin = L["ld_q"], L["ld_x"], L["st_o"], L["dv_fin"]
    ALU_ = ALU
    last_dv = None
    dv_q = pb.do(
        "vector",
        lambda e, q=q_bf[bt], t=t: e.tensor_tensor(
            out=ln_bf[:, :].rearrange("p (h d) -> p h d", h=16),
            in0=q[:, :].rearrange("p (h d) -> p h d", h=16),
            in1=si_all[:, 16 * t : 16 * t + 16].unsqueeze(2).to_broadcast([128, 16, 64]),
            op=ALU_.mult,
        ),
        inc="dv",
    )
    last_pe = None
    if F_CUT >= 2:
        pb.wait("tensor", dv_q)
        if t >= 1:
            pb.wait("tensor", dv_fin[t - 1])
        dv_qT = None
        for grp in range(2):
            def trf(e, grp=grp):
                ins = None
                for hh in range(8):
                    h = 8 * grp + hh
                    ins = e.transpose(
                        pt[0:64, 128 * hh : 128 * hh + 128],
                        ln_bf[:, 64 * h : 64 * h + 64],
                        c["ident_bf"][:, :],
                    )
                return ins
            if grp == 1:
                pb.wait("tensor", dv_qT)
            pe_tr = pb.do("tensor", trf, inc="pe")
            pb.wait("vector", pe_tr)
            dv_qT = pb.do(
                "vector",
                lambda e, grp=grp: e.tensor_copy(
                    qsiT[0:64, 1024 * grp : 1024 * grp + 1024], pt[0:64, 0:1024]
                ),
                inc="dv",
            )
        last_pe = pe_tr
    if F_CUT >= 3:
        pb.wait("tensor", dv_qT)
        def outfn(e):
            ins = None
            for h in range(16):
                ins = e.matmul(
                    pA[:, 64 * h : 64 * h + 64],
                    qsiT[0:64, 128 * h : 128 * h + 128],
                    kv_s[0:64, 64 * h : 64 * h + 64],
                    start=True, stop=True,
                )
            return ins
        pe_out = pb.do("tensor", outfn, inc="pe")
        pb.wait("vector", pe_out)
        dv_at = pb.do(
            "vector",
            lambda e, t=t: e.tensor_tensor(
                out=ln_bf[:, :].rearrange("p (h d) -> p h d", h=16),
                in0=pA[:, :].rearrange("p (h d) -> p h d", h=16),
                in1=sa_all[:, 16 * t : 16 * t + 16].unsqueeze(2).to_broadcast([128, 16, 64]),
                op=ALU_.mult,
            ),
            inc="dv",
        )
        last_pe = pe_out
    if F_CUT >= 4:
        pb.wait("tensor", dv_at)
        pe_tr2 = emit_transpose8(pb, c, ln_bf, pt)
        pb.wait("vector", pe_tr2)
        dv_aT = pb.do("vector", lambda e: e.tensor_copy(aT[:, :], pt[:, :]), inc="dv")
        last_pe = pe_tr2
    if F_CUT >= 5:
        pb.wait("tensor", dv_aT)
        pe_y = emit_proj(pb, c, aT, wo, bo, pB)
        last_pe = pe_y
    if F_CUT == 6:
        pb.wait("vector", pe_y)
        pb.wait("scalar", pe_y)
        acs = pb.do(
            "scalar",
            lambda e: e.activation(out=tmpf[:, :], in_=pB[:, :], func=AF.Square, accum_out=wst[:, 3:4]),
            inc="ac",
        )
        pb.wait("vector", acs)
        pb.do("vector", lambda e: e.tensor_copy(y1[:, :], pB[:, :]))
    elif F_CUT == 7:
        pb.wait("vector", pe_y)
        pb.wait("scalar", pe_y)
        pb.wait("scalar", pb.now("dv"))
        acs = pb.do(
            "scalar",
            lambda e: e.activation(out=tmpf[:, :], in_=pB[:, :], func=AF.Square, accum_out=wst[:, 3:4]),
            inc="ac",
        )
        pb.do("vector", lambda e: e.reduce_sum(wst[:, 2:3], pB[:, :], axis=AX.X))
        pb.do("vector", lambda e: e.tensor_scalar_mul(wst[:, 1:2], wst[:, 2:3], 1.0 / D))
        pb.wait("vector", acs)
        pb.do("vector", lambda e: e.tensor_scalar_mul(wst[:, 4:5], wst[:, 3:4], 1.0 / D))
        pb.do("vector", lambda e: e.tensor_mul(wst[:, 5:6], wst[:, 1:2], wst[:, 1:2]))
        dvv = pb.do("vector", lambda e: e.tensor_sub(wst[:, 6:7], wst[:, 4:5], wst[:, 5:6]), inc="dv")
        pb.wait("scalar", dvv)
        acv = pb.do(
            "scalar",
            lambda e: e.activation(out=wst[:, 7:8], in_=wst[:, 6:7], func=AF.Sqrt, bias=c["lneps"][:, 0:1]),
            inc="ac",
        )
        pb.wait("vector", acv)
        dvr2 = pb.do("vector", lambda e: e.reciprocal(wst[:, 0:1], wst[:, 7:8]), inc="dv")
        pb.wait("vector", dvr2)
        pb.do("vector", lambda e: e.tensor_copy(y1[:, :], pB[:, :]))
    elif F_CUT >= 8:
        pb.wait("vector", pe_y)
        pb.wait("scalar", pe_y)
        emit_ln(pb, c, pB[:, :], y1[:, :], tmpf[:, :], wst, psum_in=True)
    pb.wait("vector", ld_x[t])
    if t >= 2:
        pb.wait("vector", st_o[t - 2])
    if F_CUT >= 6:
        pass
    if F_CUT >= 6:
        dv_fin[t] = pb.do(
            "vector", lambda e, bt=bt: e.tensor_add(x1o[bt][:, :], xin[bt][:, :], y1[:, :]), inc="dv"
        )
    else:
        if last_pe is not None:
            pb.wait("vector", last_pe)
        dv_fin[t] = pb.do(
            "vector", lambda e, bt=bt: e.tensor_copy(x1o[bt][:, :], xin[bt][:, :]), inc="dv"
        )
    pb.wait("sync", dv_fin[t])
    st_o[t] = pb.dma("sync", scr_out[128 * t : 128 * t + 128, :], x1o[bt][:, :], f"so{bt}")
    if t + 2 < NT:
        ld_q[t + 2] = pb.dma("sync", q_bf[bt][:, :], scr_q[128 * (t + 2) : 128 * (t + 2) + 128, :], f"lq{bt}")
        ld_x[t + 2] = pb.dma("sync", xin[bt][:, :], x_in[128 * (t + 2) : 128 * (t + 2) + 128, :], f"lx{bt}")




def emit_l0_ffn(pb, c, ctx, T, x_in, scr_out):
    nc = pb.nc
    NT = T // 128

    wr = ctx.enter_context(nc.sbuf_tensor("l0f_wr", [128, 8 * 1024], BF16))
    wk2 = ctx.enter_context(nc.sbuf_tensor("l0f_wk", [128, 8 * 1024], BF16))
    wv2 = ctx.enter_context(nc.sbuf_tensor("l0f_wv", [128, 8 * 1024], BF16))
    mur = ctx.enter_context(nc.sbuf_tensor("l0f_murs", [128, 8], F32))
    muk = ctx.enter_context(nc.sbuf_tensor("l0f_muks", [128, 8], F32))
    omr = ctx.enter_context(nc.sbuf_tensor("l0f_omr", [128, 8], F32))
    omk = ctx.enter_context(nc.sbuf_tensor("l0f_omk", [128, 8], F32))
    zr = ctx.enter_context(nc.sbuf_tensor("l0f_zr", [128, 8], BF16))
    xin = [ctx.enter_context(nc.sbuf_tensor(f"l0f_xin{i}", [128, D], F32)) for i in range(2)]
    ln_bf = ctx.enter_context(nc.sbuf_tensor("l0f_ln", [128, D], BF16))
    LT = ctx.enter_context(nc.sbuf_tensor("l0f_LT", [128, 8 * 129], BF16))
    xrT = ctx.enter_context(nc.sbuf_tensor("l0f_xrT", [128, D], BF16))
    xkT = ctx.enter_context(nc.sbuf_tensor("l0f_xkT", [128, D], BF16))
    r_f = ctx.enter_context(nc.sbuf_tensor("l0f_r", [128, D], F32))
    relu_f = ctx.enter_context(nc.sbuf_tensor("l0f_relu", [128, D], F32))
    ka_bf = ctx.enter_context(nc.sbuf_tensor("l0f_ka", [128, D], BF16))
    kaT = ctx.enter_context(nc.sbuf_tensor("l0f_kaT", [128, D], BF16))
    yb = ctx.enter_context(nc.sbuf_tensor("l0f_yb", [128, D], F32))
    y1 = ctx.enter_context(nc.sbuf_tensor("l0f_y1", [128, D], F32))
    wst = ctx.enter_context(nc.sbuf_tensor("l0f_wst", [128, 8], F32))
    tmpf = ctx.enter_context(nc.sbuf_tensor("l0f_tmpf", [128, D], F32))
    x2o = [ctx.enter_context(nc.sbuf_tensor(f"l0f_x2o{i}", [128, D], F32)) for i in range(2)]

    pA = ctx.enter_context(nc.psum_tensor("l0f_pA", [128, D], F32))
    pB = ctx.enter_context(nc.psum_tensor("l0f_pB", [128, D], F32))
    pt = ctx.enter_context(nc.psum_tensor("l0f_pt", [128, D], BF16))

    for w_s, name in ((wr, "l0f_wrt"), (wk2, "l0f_wkt"), (wv2, "l0f_wvt")):
        src = c[name].rearrange("(c p) n -> p c n", p=128)
        dst = w_s[:, :].rearrange("p (c n) -> p c n", c=8)
        pb.dma("gpsimd", dst, src, "wl")
    pb.dma("gpsimd", mur[:, :], c["l0f_mur"], "wl")
    pb.dma("gpsimd", muk[:, :], c["l0f_muk"], "wl")
    pb.dma("gpsimd", zr[:, :], c["zeros8"], "wl")
    wl_all = pb.now("wl")
    pb.wait("vector", wl_all)
    pb.do("vector", lambda e: e.tensor_scalar_sub(omr[:, :], mur[:, :], 1.0))
    pb.do("vector", lambda e: e.tensor_scalar_mul(omr[:, :], omr[:, :], -1.0))
    pb.do("vector", lambda e: e.tensor_scalar_sub(omk[:, :], muk[:, :], 1.0))
    dv_om = pb.do("vector", lambda e: e.tensor_scalar_mul(omk[:, :], omk[:, :], -1.0), inc="dv")

    LT3 = LT[:, :].rearrange("p (c n) -> p c n", c=8)
    ld_x = {}
    for t in range(min(2, NT)):
        ld_x[t] = pb.dma("sync", xin[t % 2][:, :], x_in[128 * t : 128 * t + 128, :], f"lx{t % 2}")
    st_o = {}
    pe_v = pe_r = pe_k = None
    ac_rl = None
    dv_fin = {}
    for t in range(NT):
        bt = t % 2
        pb.wait("vector", ld_x[t])
        pb.wait("scalar", ld_x[t])
        apply_v = emit_ln(pb, c, xin[bt][:, :], ln_bf[:, :], tmpf[:, :], wst)
        if t == 0:
            dv_pc = pb.do("vector", lambda e: e.tensor_copy(LT3[:, :, 0], zr[:, :]), inc="dv")
        else:
            dv_pc = pb.do("vector", lambda e: e.tensor_copy(LT3[:, :, 0], LT3[:, :, 128]), inc="dv")
        pb.wait("tensor", apply_v)
        pb.wait("tensor", dv_pc)
        if t >= 1:
            pb.wait("tensor", pe_k)
        pe_tr = emit_transpose8(pb, c, ln_bf, pt)
        pb.wait("vector", pe_tr)
        dv_LT = pb.do(
            "vector",
            lambda e: e.tensor_copy(
                LT3[:, :, 1:129], pt[:, :].rearrange("p (c n) -> p c n", c=8)
            ),
            inc="dv",
        )
        if t == 0:
            pb.wait("vector", dv_om)
        for cc in range(8):
            pb.do(
                "vector",
                lambda e, cc=cc: e.tensor_scalar(
                    xrT[:, 128 * cc : 128 * cc + 128], LT3[:, cc, 1:129],
                    mur[:, cc : cc + 1], None, op0=ALU.mult,
                ),
            )
            pb.do(
                "vector",
                lambda e, cc=cc: e.scalar_tensor_tensor(
                    out=xrT[:, 128 * cc : 128 * cc + 128], in0=LT3[:, cc, 0:128],
                    scalar=omr[:, cc : cc + 1], in1=xrT[:, 128 * cc : 128 * cc + 128],
                    op0=ALU.mult, op1=ALU.add,
                ),
            )
            pb.do(
                "vector",
                lambda e, cc=cc: e.tensor_scalar(
                    xkT[:, 128 * cc : 128 * cc + 128], LT3[:, cc, 1:129],
                    muk[:, cc : cc + 1], None, op0=ALU.mult,
                ),
            )
            pb.do(
                "vector",
                lambda e, cc=cc: e.scalar_tensor_tensor(
                    out=xkT[:, 128 * cc : 128 * cc + 128], in0=LT3[:, cc, 0:128],
                    scalar=omk[:, cc : cc + 1], in1=xkT[:, 128 * cc : 128 * cc + 128],
                    op0=ALU.mult, op1=ALU.add,
                ),
            )
        dv_mix = pb.do("vector", lambda e: e.tensor_copy(wst[:, 0:1], omr[:, 0:1]), inc="dv")
        if DEV_DUMP and t == 0:
            pb.wait("gpsimd", dv_mix)
            pb.dma("gpsimd", c["scr_dbg"][0:64, 0:1024], xrT[0:64, :], "gdbg1")
            pb.dma("gpsimd", c["scr_dbg"][0:64, 1024:2048], xkT[0:64, :], "gdbg2")
            pb.dma("gpsimd", c["scr_dbg"][0:64, 2048:3072], LT[0:64, 0:1024], "gdbg3")
        pb.wait("tensor", dv_mix)
        if t == 0:
            pb.wait("tensor", wl_all)
        if pe_v is not None:
            pb.wait("tensor", pb.now("dv"))
        pe_r = emit_proj(pb, c, xrT, wr, None, pA)
        if ac_rl is not None:
            pb.wait("tensor", ac_rl)
        pe_k = emit_proj(pb, c, xkT, wk2, None, pB)
        pb.wait("scalar", pe_r)
        ac_sg = pb.do(
            "scalar",
            lambda e: e.activation(out=r_f[:, :], in_=pA[:, :], func=AF.Sigmoid),
            inc="ac",
        )
        pb.wait("scalar", pe_k)
        ac_rl = pb.do(
            "scalar",
            lambda e: e.activation(out=relu_f[:, :], in_=pB[:, :], func=AF.Relu),
            inc="ac",
        )
        pb.wait("vector", ac_rl)
        dv_ka = pb.do("vector", lambda e: e.tensor_mul(ka_bf[:, :], relu_f[:, :], relu_f[:, :]), inc="dv")
        if DEV_DUMP and t == 0:
            pb.wait("sync", ac_sg)
            pb.dma("sync", c["scr_dbg"][0:64, 3072:4096], r_f[0:64, :], "dbg4")
            pb.wait("sync", dv_ka)
            pb.dma("sync", c["scr_dbg"][0:64, 4096:5120], relu_f[0:64, :], "dbg5")
        pb.wait("tensor", dv_ka)
        pe_tr2 = emit_transpose8(pb, c, ka_bf, pt)
        pb.wait("vector", pe_tr2)
        dv_kaT = pb.do("vector", lambda e: e.tensor_copy(kaT[:, :], pt[:, :]), inc="dv")
        pb.wait("tensor", dv_kaT)
        pb.wait("tensor", ac_sg)
        pe_v = emit_proj(pb, c, kaT, wv2, None, pA)
        pb.wait("vector", pe_v)
        dv_y = pb.do("vector", lambda e: e.tensor_mul(yb[:, :], r_f[:, :], pA[:, :]), inc="dv")
        pb.wait("scalar", dv_y)
        emit_ln(pb, c, yb[:, :], y1[:, :], tmpf[:, :], wst)
        if t >= 2:
            pb.wait("vector", st_o[t - 2])
        dv_fin[t] = pb.do(
            "vector", lambda e, bt=bt: e.tensor_add(x2o[bt][:, :], xin[bt][:, :], y1[:, :]), inc="dv"
        )
        pb.wait("sync", dv_fin[t])
        st_o[t] = pb.dma("sync", scr_out[128 * t : 128 * t + 128, :], x2o[bt][:, :], f"so{bt}")
        if t + 2 < NT:
            ld_x[t + 2] = pb.dma("sync", xin[bt][:, :], x_in[128 * (t + 2) : 128 * (t + 2) + 128, :], f"lx{bt}")

    pb.barrier()



def emit_l1_attn(pb, c, ctx, T, x_in, scr_q, scr_out):
    nc = pb.nc
    NT = T // 128
    NKB = T // 128

    xin = [ctx.enter_context(nc.sbuf_tensor(f"l1_xin{i}", [128, D], F32)) for i in range(2)]
    ln_bf = ctx.enter_context(nc.sbuf_tensor("l1_ln", [128, D], BF16))
    lnT = ctx.enter_context(nc.sbuf_tensor("l1_lnT", [128, D], BF16))
    wst = ctx.enter_context(nc.sbuf_tensor("l1_wst", [128, 8], F32))
    kT = ctx.enter_context(nc.sbuf_tensor("l1_kT", [128, 8 * T], BF16))
    V = ctx.enter_context(nc.sbuf_tensor("l1_V", [128, T * 8], BF16))
    q_bf = [ctx.enter_context(nc.sbuf_tensor(f"l1_q{i}", [128, D], BF16)) for i in range(2)]
    probs = ctx.enter_context(nc.sbuf_tensor("l1_probs", [128, T], BF16))
    pTs = ctx.enter_context(nc.sbuf_tensor("l1_pTs", [128, 1024], BF16))
    rs = ctx.enter_context(nc.sbuf_tensor("l1_rs", [128, 4], F32))
    ctxb = ctx.enter_context(nc.sbuf_tensor("l1_ctx", [128, D], BF16))
    y1 = ctx.enter_context(nc.sbuf_tensor("l1_y1", [128, D], F32))
    x3o = [ctx.enter_context(nc.sbuf_tensor(f"l1_x3o{i}", [128, D], F32)) for i in range(2)]

    kT3 = kT[:, :].rearrange("p (g n) -> p g n", g=8)
    V3 = V[:, :].rearrange("p (t n) -> p t n", t=NT)

    ph1 = ExitStack()
    pt = ph1.enter_context(nc.psum_tensor("l1_pt", [128, D], BF16))
    pA = ph1.enter_context(nc.psum_tensor("l1_pA", [128, D], F32))
    pB = ph1.enter_context(nc.psum_tensor("l1_pB", [128, D], F32))
    wq = ph1.enter_context(nc.sbuf_tensor("l1_wqs", [128, 8 * 1024], BF16))
    wk = ph1.enter_context(nc.sbuf_tensor("l1_wks", [128, 8 * 1024], BF16))
    bq = ph1.enter_context(nc.sbuf_tensor("l1_bqs", [1, D], BF16))
    bk = ph1.enter_context(nc.sbuf_tensor("l1_bks", [1, D], BF16))

    for w_s, name in ((wq, "l1_wq"), (wk, "l1_wk")):
        src = c[name].rearrange("(c p) n -> p c n", p=128)
        dst = w_s[:, :].rearrange("p (c n) -> p c n", c=8)
        pb.dma("gpsimd", dst, src, "wl")
    pb.dma("gpsimd", bq[:, :], c["l1_bq"], "wl")
    pb.dma("gpsimd", bk[:, :], c["l1_bk"], "wl")
    wl_all = pb.now("wl")

    ld_x = {}
    for t in range(min(2, NT)):
        ld_x[t] = pb.dma("sync", xin[t % 2][:, :], x_in[128 * t : 128 * t + 128, :], f"lx{t % 2}")
    st_q = {}
    ac_q = None
    dv_kT = None
    for t in range(NT):
        bt = t % 2
        pb.wait("vector", ld_x[t])
        pb.wait("scalar", ld_x[t])
        apply_v = emit_ln(pb, c, xin[bt][:, :], ln_bf[:, :], probs[:, 0:1024], wst)
        if t + 2 < NT:
            pb.wait("sync", apply_v)
            ld_x[t + 2] = pb.dma("sync", xin[bt][:, :], x_in[128 * (t + 2) : 128 * (t + 2) + 128, :], f"lx{bt}")
        pb.wait("tensor", apply_v)
        if dv_kT is not None:
            pb.wait("tensor", dv_kT)
        pe_tr = emit_transpose8(pb, c, ln_bf, pt)
        pb.wait("vector", pe_tr)
        dv_lnT = pb.do("vector", lambda e: e.tensor_copy(lnT[:, :], pt[:, :]), inc="dv")
        pb.wait("tensor", dv_lnT)
        if t == 0:
            pb.wait("tensor", wl_all)
        if ac_q is not None:
            pb.wait("tensor", ac_q)
        pe_q = emit_proj(pb, c, lnT, wq, bq, pA)
        pe_k = emit_proj(pb, c, lnT, wk, bk, pB)
        pb.wait("scalar", pe_q)
        if t >= 2:
            pb.wait("scalar", st_q[t - 2])
        ac_q = pb.do(
            "scalar",
            lambda e, bt=bt: e.activation(out=q_bf[bt][:, :], in_=pA[:, :], func=AF.Identity),
            inc="ac",
        )
        pb.wait("sync", ac_q)
        st_q[t] = pb.dma("sync", scr_q[128 * t : 128 * t + 128, :], q_bf[bt][:, :], f"sq{bt}")
        pb.wait("vector", pe_k)
        dv_kb = pb.do("vector", lambda e: e.tensor_copy(ln_bf[:, :], pB[:, :]), inc="dv")
        pb.wait("tensor", dv_kb)
        pe_tr2 = emit_transpose8(pb, c, ln_bf, pt)
        pb.wait("vector", pe_tr2)
        dv_kT = pb.do(
            "vector",
            lambda e, t=t: e.tensor_copy(
                kT3[:, :, 128 * t : 128 * t + 128],
                pt[:, :].rearrange("p (c n) -> p c n", c=8),
            ),
            inc="dv",
        )
    pb.barrier()

    wv = wq
    bv = bq
    src = c["l1_wv"].rearrange("(c p) n -> p c n", p=128)
    pb.dma("gpsimd", wv[:, :].rearrange("p (c n) -> p c n", c=8), src, "wl")
    pb.dma("gpsimd", bv[:, :], c["l1_bv"], "wl")
    wl_all = pb.now("wl")
    for t in range(min(2, NT)):
        ld_x[t] = pb.dma("sync", xin[t % 2][:, :], x_in[128 * t : 128 * t + 128, :], f"lx{t % 2}")
    dv_V = None
    for t in range(NT):
        bt = t % 2
        pb.wait("vector", ld_x[t])
        pb.wait("scalar", ld_x[t])
        apply_v = emit_ln(pb, c, xin[bt][:, :], ln_bf[:, :], probs[:, 0:1024], wst)
        if t + 2 < NT:
            pb.wait("sync", apply_v)
            ld_x[t + 2] = pb.dma("sync", xin[bt][:, :], x_in[128 * (t + 2) : 128 * (t + 2) + 128, :], f"lx{bt}")
        pb.wait("tensor", apply_v)
        if dv_V is not None:
            pb.wait("tensor", dv_V)
        pe_tr = emit_transpose8(pb, c, ln_bf, pt)
        pb.wait("vector", pe_tr)
        dv_lnT = pb.do("vector", lambda e: e.tensor_copy(lnT[:, :], pt[:, :]), inc="dv")
        pb.wait("tensor", dv_lnT)
        if t == 0:
            pb.wait("tensor", wl_all)
        pe_v = emit_proj(pb, c, lnT, wv, bv, pA)
        pb.wait("vector", pe_v)
        dv_V = pb.do(
            "vector",
            lambda e, t=t: e.tensor_copy(V3[:, t, :], pA[:, :]),
            inc="dv",
        )
    pb.barrier()

    pb.flush(pb.block)
    ph1.close()
    QS = 256
    NQS = T // QS
    KHALF = T // 2
    pST = ctx.enter_context(nc.psum_tensor("l1_pST", [128, 2048], F32))
    pCtx = ctx.enter_context(nc.psum_tensor("l1_pCtx", [128, 2048], F32))
    qTs = ctx.enter_context(nc.sbuf_tensor("l1_qTs", [128, 8 * QS], BF16))
    Ebuf = ctx.enter_context(nc.sbuf_tensor("l1_Ebuf", [128, 8 * QS], BF16))
    Pbuf = ctx.enter_context(nc.sbuf_tensor("l1_Pbuf", [128, 8 * QS], BF16))
    cTs = ctx.enter_context(nc.sbuf_tensor("l1_cTs", [128, 8 * QS], BF16))
    Dred = ctx.enter_context(nc.sbuf_tensor("l1_D", [128, QS], F32))
    Dinv = ctx.enter_context(nc.sbuf_tensor("l1_Dinv", [128, QS], BF16))

    ld_q = {}
    ld_x = {}
    for t in range(min(2, 2 * NQS)):
        ld_q[t] = pb.dma("sync", q_bf[t % 2][:, :], scr_q[128 * t : 128 * t + 128, :], f"lq{t % 2}")
        ld_x[t] = pb.dma("sync", xin[t % 2][:, :], x_in[128 * t : 128 * t + 128, :], f"lx{t % 2}")
    st_o = {}
    dv_fin = {}
    ac_exp = None
    dv_ct2 = None
    pvk_prev = None
    dv_qT = None
    for qs in range(NQS):
        ms_tq = []
        for sub in range(2):
            t = 2 * qs + sub
            bt = t % 2
            pb.wait("sync", ld_q[t])
            if pvk_prev is not None:
                pb.wait("sync", pvk_prev)
            pb.dma_sem_eng.setdefault(f"tq{sub}", "sync")
            for g in range(8):
                dv_qT = pb.do(
                    "sync",
                    lambda e, g=g, sub=sub, bt=bt: e.dma_start_transpose(
                        qTs[:, 256 * g + 128 * sub : 256 * g + 128 * sub + 128],
                        q_bf[bt][:, 128 * g : 128 * g + 128],
                    ),
                    inc=f"tq{sub}", n=16,
                )
            ms_tq.append(dv_qT)
        dv_P = None
        ac_exp = None
        for kb in range(NKB):
            if kb == 0:
                for ms in ms_tq:
                    pb.wait("tensor", ms)
            if ac_exp is not None:
                pb.wait("tensor", ac_exp)

            def stfn(e, kb=kb):
                ins = None
                for g in range(8):
                    ins = e.matmul(
                        pST[:, 256 * g : 256 * g + 256],
                        kT3[:, g, 128 * kb : 128 * kb + 128],
                        qTs[:, 256 * g : 256 * g + 256],
                        start=True, stop=True,
                    )
                return ins

            pe_st = pb.do("tensor", stfn, inc="pe")
            pb.wait("scalar", pe_st)
            if dv_P is not None:
                pb.wait("scalar", dv_P)
            ac_exp = pb.do(
                "scalar",
                lambda e: e.activation(
                    out=Ebuf[:, :], in_=pST[:, :], func=AF.Exp,
                    bias=c["c30"][:, 0:1], scale=1.0 / 32.0,
                ),
                inc="ac",
            )
            pb.wait("vector", ac_exp)
            pb.do(
                "vector",
                lambda e: e.reduce_sum(
                    Dred[:, :],
                    Ebuf[:, :].rearrange("p (g q) -> p q g", g=8),
                    axis=AX.X,
                ),
            )

            def _recf(e):
                with nc.allow_low_precision(reason="Dinv feeds a bf16 matmul operand"):
                    return e.reciprocal(Dinv[:, :], Dred[:, :])

            dvr = pb.do("vector", _recf, inc="dv")
            pb.wait("vector", dvr)
            if pvk_prev is not None:
                pb.wait("vector", pvk_prev)
            dv_P = pb.do(
                "vector",
                lambda e: e.tensor_tensor(
                    out=Pbuf[:, :].rearrange("p (g q) -> p g q", g=8),
                    in0=Ebuf[:, :].rearrange("p (g q) -> p g q", g=8),
                    in1=Dinv[:, :].unsqueeze(1).to_broadcast([128, 8, QS]),
                    op=ALU.mult,
                ),
                inc="dv",
            )
            pb.wait("tensor", dv_P)
            if kb == 0 and dv_ct2 is not None:
                pb.wait("tensor", dv_ct2)

            def pvfn(e, kb=kb):
                ins = None
                for g in range(8):
                    ins = e.matmul(
                        pCtx[:, 256 * g : 256 * g + 256],
                        V3[:, kb, 128 * g : 128 * g + 128],
                        Pbuf[:, 256 * g : 256 * g + 256],
                        start=(kb == 0 and g % 2 == 0),
                        stop=(kb == NKB - 1 and g % 2 == 1),
                    )
                return ins

            pvk_prev = pb.do("tensor", pvfn, inc="pe")
        pb.wait("vector", pvk_prev)
        dv_ce = pb.do("vector", lambda e: e.tensor_copy(cTs[:, :], pCtx[:, :]), inc="dv")
        dv_ct2 = None
        for sub in range(2):
            t = 2 * qs + sub
            bt = t % 2
            pb.wait("sync", dv_ce)
            if t >= 1 and dv_fin.get(t - 1) is not None:
                pb.wait("sync", dv_fin[t - 1])
            pb.dma_sem_eng.setdefault(f"tc{sub}", "sync")
            for g in range(8):
                bt_ms = pb.do(
                    "sync",
                    lambda e, g=g, sub=sub: e.dma_start_transpose(
                        ctxb[:, 128 * g : 128 * g + 128],
                        cTs[:, 256 * g + 128 * sub : 256 * g + 128 * sub + 128],
                    ),
                    inc=f"tc{sub}", n=16,
                )
            dv_ct2 = bt_ms
            pb.wait("scalar", bt_ms)
            pb.wait("vector", bt_ms)
            emit_ln(pb, c, ctxb[:, :], y1[:, :], probs[:, 0:1024], wst)
            pb.wait("vector", ld_x[t])
            if t >= 2:
                pb.wait("vector", st_o[t - 2])
            dv_fin[t] = pb.do(
                "vector", lambda e, bt=bt: e.tensor_add(x3o[bt][:, :], xin[bt][:, :], y1[:, :]), inc="dv"
            )
            pb.wait("sync", dv_fin[t])
            st_o[t] = pb.dma("sync", scr_out[128 * t : 128 * t + 128, :], x3o[bt][:, :], f"so{bt}")
            if t + 2 < 2 * NQS:
                ld_q[t + 2] = pb.dma("sync", q_bf[bt][:, :], scr_q[128 * (t + 2) : 128 * (t + 2) + 128, :], f"lq{bt}")
                ld_x[t + 2] = pb.dma("sync", xin[bt][:, :], x_in[128 * (t + 2) : 128 * (t + 2) + 128, :], f"lx{bt}")

    pb.barrier()



def emit_l1_moe(pb, c, ctx, T, x_in, out_t):
    nc = pb.nc
    NT = T // 128
    NH = 2
    TH = T // NH
    NTH = TH // 128
    NE_ALL = 33

    xin = [ctx.enter_context(nc.sbuf_tensor(f"moe_xin{i}", [128, D], F32)) for i in range(2)]
    ln_bf = ctx.enter_context(nc.sbuf_tensor("moe_ln", [128, D], BF16))
    wst = ctx.enter_context(nc.sbuf_tensor("moe_wst", [128, 8], F32))
    tmpf = ctx.enter_context(nc.sbuf_tensor("moe_tmpf", [128, D], F32))
    gwT = ctx.enter_context(nc.sbuf_tensor("moe_gwTs", [128, 8 * 32], BF16))
    flatT = ctx.enter_context(nc.sbuf_tensor("moe_flatT", [128, 8 * TH], BF16))
    comb = ctx.enter_context(nc.sbuf_tensor("moe_comb", [128, 32 * NTH], F32))
    scor = ctx.enter_context(nc.sbuf_tensor("moe_scor", [128, 32], F32))
    gs8 = ctx.enter_context(nc.sbuf_tensor("moe_gs8", [128, 8], F32))
    top8 = ctx.enter_context(nc.sbuf_tensor("moe_top8", [128, 8], F32))
    msk = ctx.enter_context(nc.sbuf_tensor("moe_msk", [128, 32], F32))
    rsum = ctx.enter_context(nc.sbuf_tensor("moe_rsum", [128, 2], F32))
    acc = ctx.enter_context(nc.sbuf_tensor("moe_acc", [128, 1024 * NTH], F32))
    wg = [ctx.enter_context(nc.sbuf_tensor(f"moe_wgs{i}", [128, 8 * 512], BF16)) for i in range(2)]
    wu = [ctx.enter_context(nc.sbuf_tensor(f"moe_wus{i}", [128, 8 * 512], BF16)) for i in range(2)]
    wd = [ctx.enter_context(nc.sbuf_tensor(f"moe_wds{i}", [128, 4 * 1024], BF16)) for i in range(2)]
    hs = [ctx.enter_context(nc.sbuf_tensor(f"moe_hs{i}", [128, 512], F32)) for i in range(2)]
    h_bf = [ctx.enter_context(nc.sbuf_tensor(f"moe_h{i}", [128, 512], BF16)) for i in range(2)]
    hT = [ctx.enter_context(nc.sbuf_tensor(f"moe_hT{i}", [128, 512], BF16)) for i in range(2)]
    y1 = ctx.enter_context(nc.sbuf_tensor("moe_y1", [128, D], F32))
    oo = [ctx.enter_context(nc.sbuf_tensor(f"moe_oo{i}", [128, D], F32)) for i in range(2)]

    pA = [ctx.enter_context(nc.psum_tensor(f"moe_pA{i}", [128, 512], F32)) for i in range(2)]
    pB = [ctx.enter_context(nc.psum_tensor(f"moe_pB{i}", [128, 512], F32)) for i in range(2)]
    pt = ctx.enter_context(nc.psum_tensor("moe_pt", [128, 1024], BF16))
    pC = ctx.enter_context(nc.psum_tensor("moe_pC", [128, D], F32))
    pR = ctx.enter_context(nc.psum_tensor("moe_pR", [128, 32], F32))

    flatT3 = flatT[:, :].rearrange("p (c n) -> p c n", c=8)

    pb.dma("gpsimd", gwT[:, :].rearrange("p (c n) -> p c n", c=8),
           c["moe_gwt"].rearrange("(c p) n -> p c n", p=128), "wl")
    wl_gw = pb.now("wl")

    def load_expert(eidx, slot):
        if eidx < 32:
            wg_src = c["moe_wg"][eidx].rearrange("(c p) n -> p c n", p=128)
            wu_src = c["moe_wu"][eidx].rearrange("(c p) n -> p c n", p=128)
            wd_src = c["moe_wd"][eidx].rearrange("(c p) n -> p c n", p=128)
        else:
            wg_src = c["moe_sg"].rearrange("(c p) n -> p c n", p=128)
            wu_src = c["moe_su"].rearrange("(c p) n -> p c n", p=128)
            wd_src = c["moe_sd"].rearrange("(c p) n -> p c n", p=128)
        pb.dma("gpsimd", wg[slot][:, :].rearrange("p (c n) -> p c n", c=8), wg_src, f"we{slot}")
        pb.dma("gpsimd", wu[slot][:, :].rearrange("p (c n) -> p c n", c=8), wu_src, f"we{slot}")
        return pb.dma("gpsimd", wd[slot][:, :].rearrange("p (c n) -> p c n", c=4), wd_src, f"we{slot}")

    ld_x = {}
    st_o = {}
    for h in range(NH):
        t0 = h * NTH
        for t in range(min(2, NTH)):
            ld_x[t] = pb.dma("sync", xin[t % 2][:, :], x_in[128 * (t0 + t) : 128 * (t0 + t) + 128, :], f"lx{t % 2}")
        dv_fT = None
        for t in range(NTH):
            bt = t % 2
            pb.wait("vector", ld_x[t])
            pb.wait("scalar", ld_x[t])
            apply_v = emit_ln(pb, c, xin[bt][:, :], ln_bf[:, :], tmpf[:, :], wst)
            if t + 2 < NTH:
                pb.wait("sync", apply_v)
                ld_x[t + 2] = pb.dma("sync", xin[bt][:, :], x_in[128 * (t0 + t + 2) : 128 * (t0 + t + 2) + 128, :], f"lx{bt}")
            pb.wait("tensor", apply_v)
            if dv_fT is not None:
                pb.wait("tensor", dv_fT)
            pe_tr = emit_transpose8(pb, c, ln_bf, pt)
            pb.wait("vector", pe_tr)
            dv_fT = pb.do(
                "vector",
                lambda e, t=t: e.tensor_copy(
                    flatT3[:, :, 128 * t : 128 * t + 128],
                    pt[:, :].rearrange("p (c n) -> p c n", c=8),
                ),
                inc="dv",
            )
            pb.wait("tensor", dv_fT)
            pb.wait("tensor", pb.now("ac"))
            if t == 0 and h == 0:
                pb.wait("tensor", wl_gw)

            def lfn(e, t=t):
                ins = None
                for cc in range(8):
                    ins = e.matmul(
                        pR[:, 0:32],
                        flatT3[:, cc, 128 * t : 128 * t + 128],
                        gwT[:, 32 * cc : 32 * cc + 32],
                        start=(cc == 0), stop=(cc == 7),
                    )
                return ins

            pe_lg = pb.do("tensor", lfn, inc="pe")
            pb.wait("scalar", pe_lg)
            ac_sc = pb.do(
                "scalar",
                lambda e: e.activation(out=scor[:, :], in_=pR[:, 0:32], func=AF.Sigmoid),
                inc="ac",
            )
            pb.wait("vector", ac_sc)
            r1 = pb.do(
                "vector",
                lambda e: e.reduce_sum(
                    gs8[:, :], scor[:, :].rearrange("p (g i) -> p g i", g=8), axis=AX.X
                ),
                inc="dv",
            )
            pb.wait("vector", r1)
            r2 = pb.do("vector", lambda e: e.max(out=top8[:, :], in_=gs8[:, :]), inc="dv")
            pb.wait("vector", r2)
            r3 = pb.do(
                "vector",
                lambda e: e.tensor_scalar(msk[:, 0:8], gs8[:, :], top8[:, 3:4], None, op0=ALU.is_ge),
                inc="dv",
            )
            pb.wait("vector", r3)
            r4 = pb.do(
                "vector",
                lambda e: e.tensor_tensor(
                    out=msk[:, :].rearrange("p (g i) -> p g i", g=8),
                    in0=scor[:, :].rearrange("p (g i) -> p g i", g=8),
                    in1=msk[:, 0:8].unsqueeze(2).to_broadcast([128, 8, 4]),
                    op=ALU.mult,
                ),
                inc="dv",
            )
            pb.wait("vector", r4)
            r5 = pb.do("vector", lambda e: e.max(out=top8[:, :], in_=msk[:, :]), inc="dv")
            pb.wait("vector", r5)
            r6 = pb.do(
                "vector",
                lambda e, t=t: e.tensor_scalar(
                    comb[:, 32 * t : 32 * t + 32], msk[:, :], top8[:, 7:8], None, op0=ALU.is_ge
                ),
                inc="dv",
            )
            pb.wait("vector", r6)
            r7 = pb.do(
                "vector",
                lambda e, t=t: e.tensor_mul(
                    comb[:, 32 * t : 32 * t + 32], comb[:, 32 * t : 32 * t + 32], msk[:, :]
                ),
                inc="dv",
            )
            pb.wait("vector", r7)
            r8 = pb.do(
                "vector",
                lambda e, t=t: e.reduce_sum(rsum[:, 0:1], comb[:, 32 * t : 32 * t + 32], axis=AX.X),
                inc="dv",
            )
            pb.wait("vector", r8)
            r9 = pb.do("vector", lambda e: e.tensor_scalar_max(rsum[:, 0:1], rsum[:, 0:1], 1e-12), inc="dv")
            pb.wait("vector", r9)
            r10 = pb.do("vector", lambda e: e.reciprocal(rsum[:, 1:2], rsum[:, 0:1]), inc="dv")
            pb.wait("vector", r10)
            pb.do(
                "vector",
                lambda e, t=t: e.tensor_scalar(
                    comb[:, 32 * t : 32 * t + 32], comb[:, 32 * t : 32 * t + 32],
                    rsum[:, 1:2], None, op0=ALU.mult,
                ),
                inc="dv",
            )
        pb.barrier()

        ld_w = {}
        ld_w[0] = load_expert(32, 0)
        ld_w[1] = load_expert(0, 1)
        seq = [(ei, t) for ei in range(NE_ALL) for t in range(NTH)]
        ac_si_ms, dv_h_ms, dv_hT_ms, dv_acc_ms, pe_dn_ms = {}, {}, {}, {}, {}
        pend = None

        def emit_tail(idx, ei, t, eidx, s2, slot):
            pb.wait("tensor", dv_h_ms[idx])
            if idx >= 1:
                pb.wait("tensor", dv_hT_ms[idx - 1])

            def tfn(e, s2=s2):
                ins = None
                for j in range(4):
                    ins = e.transpose(
                        pt[:, 128 * j : 128 * j + 128],
                        h_bf[s2][:, 128 * j : 128 * j + 128],
                        c["ident_bf"][:, :],
                    )
                return ins

            pe_th = pb.do("tensor", tfn, inc="pe")
            pb.wait("vector", pe_th)
            dv_hT_ms[idx] = pb.do(
                "vector", lambda e, s2=s2: e.tensor_copy(hT[s2][:, :], pt[:, 0:512]), inc="dv"
            )
            pb.wait("tensor", dv_hT_ms[idx])
            if idx >= 1:
                pb.wait("tensor", dv_acc_ms[idx - 1])

            def dfn(e, s2=s2, slot=slot):
                ins = None
                for j in range(4):
                    for half in range(2):
                        ins = e.matmul(
                            pC[:, 512 * half : 512 * half + 512],
                            hT[s2][:, 128 * j : 128 * j + 128],
                            wd[slot][:, 1024 * j + 512 * half : 1024 * j + 512 * half + 512],
                            start=(j == 0), stop=(j == 3),
                        )
                return ins

            pe_dn_ms[idx] = pb.do("tensor", dfn, inc="pe")
            pb.wait("vector", pe_dn_ms[idx])
            if ei == 0:
                dv_acc_ms[idx] = pb.do(
                    "vector",
                    lambda e, t=t: e.tensor_copy(acc[:, 1024 * t : 1024 * t + 1024], pC[:, :]),
                    inc="dv",
                )
            else:
                dv_acc_ms[idx] = pb.do(
                    "vector",
                    lambda e, t=t, eidx=eidx: e.scalar_tensor_tensor(
                        out=acc[:, 1024 * t : 1024 * t + 1024], in0=pC[:, :],
                        scalar=comb[:, 32 * t + eidx : 32 * t + eidx + 1],
                        in1=acc[:, 1024 * t : 1024 * t + 1024],
                        op0=ALU.mult, op1=ALU.add,
                    ),
                    inc="dv",
                )
            if t == NTH - 1 and ei + 2 < NE_ALL:
                pb.wait("gpsimd", pe_dn_ms[idx])
                ld_w[ei + 2] = load_expert(ei + 1, slot)

        for idx, (ei, t) in enumerate(seq):
            slot = ei % 2
            s2 = idx % 2
            eidx = 32 if ei == 0 else ei - 1
            if t == 0:
                pb.wait("tensor", ld_w[ei])
            if idx >= 2:
                pb.wait("tensor", ac_si_ms[idx - 2])
                pb.wait("tensor", dv_h_ms[idx - 2])

            def gfn(e, t=t, slot=slot, s2=s2):
                ins = None
                for cc in range(8):
                    ins = e.matmul(
                        pA[s2][:, 0:512],
                        flatT3[:, cc, 128 * t : 128 * t + 128],
                        wg[slot][:, 512 * cc : 512 * cc + 512],
                        start=(cc == 0), stop=(cc == 7),
                    )
                return ins

            def ufn(e, t=t, slot=slot, s2=s2):
                ins = None
                for cc in range(8):
                    ins = e.matmul(
                        pB[s2][:, 0:512],
                        flatT3[:, cc, 128 * t : 128 * t + 128],
                        wu[slot][:, 512 * cc : 512 * cc + 512],
                        start=(cc == 0), stop=(cc == 7),
                    )
                return ins

            pe_g = pb.do("tensor", gfn, inc="pe")
            pe_u = pb.do("tensor", ufn, inc="pe")
            pb.wait("scalar", pe_g)
            if idx >= 2:
                pb.wait("scalar", dv_h_ms[idx - 2])
            ac_si_ms[idx] = pb.do(
                "scalar",
                lambda e, s2=s2: e.activation(out=hs[s2][:, :], in_=pA[s2][:, :], func=AF.Silu),
                inc="ac",
            )
            pb.wait("vector", ac_si_ms[idx])
            pb.wait("vector", pe_u)
            dv_h_ms[idx] = pb.do(
                "vector",
                lambda e, s2=s2: e.tensor_mul(h_bf[s2][:, :], hs[s2][:, :], pB[s2][:, :]),
                inc="dv",
            )
            if pend is not None:
                emit_tail(*pend)
            pend = (idx, ei, t, eidx, s2, slot)
        emit_tail(*pend)
        pend = None
        pb.barrier()

        for t in range(min(2, NTH)):
            ld_x[t] = pb.dma("sync", xin[t % 2][:, :], x_in[128 * (t0 + t) : 128 * (t0 + t) + 128, :], f"lx{t % 2}")
        for t in range(NTH):
            bt = t % 2
            pb.wait("vector", ld_x[t])
            pb.wait("scalar", ld_x[t])
            emit_ln(pb, c, acc[:, 1024 * t : 1024 * t + 1024], y1[:, :], tmpf[:, :], wst)
            if t >= 2:
                pb.wait("vector", st_o[t - 2])
            dv_f = pb.do(
                "vector", lambda e, bt=bt: e.tensor_add(oo[bt][:, :], xin[bt][:, :], y1[:, :]), inc="dv"
            )
            pb.wait("sync", dv_f)
            st_o[t] = pb.dma("sync", out_t[128 * (t0 + t) : 128 * (t0 + t) + 128, :], oo[bt][:, :], f"so{bt}")
            if t + 2 < NTH:
                pb.wait("sync", dv_f)
                ld_x[t + 2] = pb.dma("sync", xin[bt][:, :], x_in[128 * (t0 + t + 2) : 128 * (t0 + t + 2) + 128, :], f"lx{bt}")
        pb.barrier()



def build_program(T=4096, dev=False, stages=("l0a", "l0f", "l1a", "moe"), n_sweeps=6, paranoid=True):
    nc = bass.Bass("TRN2", target_bir_lowering=False, detect_race_conditions=False)
    scratch_kind = "ExternalOutput" if dev else "Internal"

    c = {}
    c["x"] = nc.dram_tensor("x", [T, D], F32, kind="ExternalInput")[:, :]
    for nm, shp in [
        ("l0_wqt", [D, D]), ("l0_wkt", [D, D]), ("l0_wvt", [D, D]), ("l0_wot", [D, D]),
        ("l0_bq", [1, D]), ("l0_bk", [1, D]), ("l0_bv", [1, D]), ("l0_bo", [1, D]),
        ("l0f_wrt", [D, D]), ("l0f_wkt", [D, D]), ("l0f_wvt", [D, D]),
        ("l0f_mur", [128, 8]), ("l0f_muk", [128, 8]),
        ("l1_wq", [D, D]), ("l1_wk", [D, D]), ("l1_wv", [D, D]),
        ("l1_bq", [1, D]), ("l1_bk", [1, D]), ("l1_bv", [1, D]),
        ("moe_gwt", [D, 32]),
        ("moe_sg", [D, 512]), ("moe_su", [D, 512]), ("moe_sd", [512, D]),
    ]:
        c[nm] = nc.dram_tensor(nm, shp, F32, kind="ExternalInput")[:, :]
    c["moe_wg"] = nc.dram_tensor("moe_wg", [32, D, 512], F32, kind="ExternalInput")[:, :, :]
    c["moe_wu"] = nc.dram_tensor("moe_wu", [32, D, 512], F32, kind="ExternalInput")[:, :, :]
    c["moe_wd"] = nc.dram_tensor("moe_wd", [32, 512, D], F32, kind="ExternalInput")[:, :, :]
    identf = nc.dram_tensor("identf", [128, 128], F32, kind="ExternalInput")[:, :]
    identb = nc.dram_tensor("identb", [128, 128], BF16, kind="ExternalInput")[:, :]
    onesr = nc.dram_tensor("onesr", [1, 128], BF16, kind="ExternalInput")[:, :]
    onesc = nc.dram_tensor("onesc", [128, 1], BF16, kind="ExternalInput")[:, :]
    c["zeros8"] = nc.dram_tensor("zeros8", [128, 8], BF16, kind="ExternalInput")[:, :]

    scr_q = nc.dram_tensor("scr_q", [T, D], BF16, kind=scratch_kind)[:, :]
    scr_k = nc.dram_tensor("scr_k", [T, D], BF16, kind=scratch_kind)[:, :]
    scr_v = nc.dram_tensor("scr_v", [T, D], BF16, kind=scratch_kind)[:, :]
    scr_a = nc.dram_tensor("scr_a", [T, D], F32, kind="ExternalOutput" if dev else "Internal")[:, :]
    scr_b = nc.dram_tensor("scr_b", [T, D], F32, kind="ExternalOutput" if dev else "Internal")[:, :]
    out_t = nc.dram_tensor("out", [T, D], F32, kind="ExternalOutput")[:, :]
    scr_dbg = nc.dram_tensor("scr_dbg", [64, 8192], F32, kind="ExternalOutput" if dev else "Internal")[:, :]

    pb = PB(nc, paranoid=paranoid)
    with ExitStack() as ctx:
        block = ctx.enter_context(nc.Block())
        pb.block = block
        ident_f32 = nc.alloc_sbuf_tensor("g_identf", [128, 128], F32)
        ident_bf = nc.alloc_sbuf_tensor("g_identb", [128, 128], BF16)
        ones_row = nc.alloc_sbuf_tensor("g_onesr", [1, 128], BF16)
        ones_col = nc.alloc_sbuf_tensor("g_onesc", [128, 1], BF16)
        lneps = nc.alloc_sbuf_tensor("g_lneps", [128, 1], F32)
        c30 = nc.alloc_sbuf_tensor("g_c30", [128, 1], F32)
        pb.dma("sync", ident_f32[:, :], identf, "lc")
        pb.dma("sync", ident_bf[:, :], identb, "lc")
        pb.dma("sync", ones_row[:, :], onesr, "lc")
        ldc = pb.dma("sync", ones_col[:, :], onesc, "lc")
        pb.do("vector", lambda e: e.memset(lneps[:, 0:1], LNEPS))
        mz = pb.do("vector", lambda e: e.memset(c30[:, 0:1], -30.0), inc="dv")
        for eng in ("tensor", "vector", "scalar"):
            pb.wait(eng, ldc)
        pb.wait("scalar", mz)
        cc = dict(c)
        cc["lneps"] = lneps
        cc["c30"] = c30
        cc["ident_f32"] = ident_f32
        cc["ident_bf"] = ident_bf
        cc["ones_row"] = ones_row
        cc["ones_col"] = ones_col

        cc["scr_dbg"] = scr_dbg
        if "l0a" in stages:
            with ExitStack() as sctx:
                emit_l0_attn(pb, cc, sctx, T, c["x"], scr_q, scr_k, scr_v, scr_a, n_sweeps=n_sweeps)
                pb.flush(block)
        if "l0f" in stages:
            with ExitStack() as sctx:
                emit_l0_ffn(pb, cc, sctx, T, scr_a, scr_b)
                pb.flush(block)
        if "l1a" in stages:
            with ExitStack() as sctx:
                emit_l1_attn(pb, cc, sctx, T, scr_b, scr_q, scr_a)
                pb.flush(block)
        if "moe" in stages:
            with ExitStack() as sctx:
                emit_l1_moe(pb, cc, sctx, T, scr_a, out_t)
                pb.flush(block)

        pb.flush(block)
    return nc


def prep_inputs(**inp):
    tr = lambda a: np.ascontiguousarray(np.asarray(a, dtype=np.float32).T)
    f = lambda k: np.asarray(inp[k], dtype=np.float32)
    m = {
        "l0_wqt": tr(inp["L0_attn_Wq"]),
        "l0_wkt": tr(inp["L0_attn_Wk"]),
        "l0_wvt": tr(inp["L0_attn_Wv"]),
        "l0_wot": tr(inp["L0_attn_Wo"]),
        "l0_bq": f("L0_attn_bq").reshape(1, D),
        "l0_bk": f("L0_attn_bk").reshape(1, D),
        "l0_bv": f("L0_attn_bv").reshape(1, D),
        "l0_bo": f("L0_attn_bo").reshape(1, D),
        "l0f_wrt": tr(inp["L0_ffn_W_r"]),
        "l0f_wkt": tr(inp["L0_ffn_W_k"]),
        "l0f_wvt": tr(inp["L0_ffn_W_v"]),
        "l0f_mur": np.ascontiguousarray(f("L0_ffn_mu_r").reshape(8, 128).T),
        "l0f_muk": np.ascontiguousarray(f("L0_ffn_mu_k").reshape(8, 128).T),
        "l1_wq": np.ascontiguousarray(f("L1_attn_Wq")),
        "l1_wk": np.ascontiguousarray(f("L1_attn_Wk")),
        "l1_wv": np.ascontiguousarray(f("L1_attn_Wv")),
        "l1_bq": f("L1_attn_bq").reshape(1, D),
        "l1_bk": f("L1_attn_bk").reshape(1, D),
        "l1_bv": f("L1_attn_bv").reshape(1, D),
        "moe_gwt": tr(inp["L1_ffn_gate_w"]),
        "moe_wg": np.ascontiguousarray(np.transpose(f("L1_ffn_expert_gate"), (0, 2, 1))),
        "moe_wu": np.ascontiguousarray(np.transpose(f("L1_ffn_expert_up"), (0, 2, 1))),
        "moe_wd": np.ascontiguousarray(np.transpose(f("L1_ffn_expert_down"), (0, 2, 1))),
        "moe_sg": tr(inp["L1_ffn_shared_gate_w"]),
        "moe_su": tr(inp["L1_ffn_shared_up_w"]),
        "moe_sd": tr(inp["L1_ffn_shared_down_w"]),
        "identf": np.eye(128, dtype=np.float32),
        "identb": np.eye(128, dtype=ml_dtypes.bfloat16),
        "onesr": np.ones((1, 128), dtype=ml_dtypes.bfloat16),
        "onesc": np.ones((128, 1), dtype=ml_dtypes.bfloat16),
        "zeros8": np.zeros((128, 8), dtype=ml_dtypes.bfloat16),
    }
    return m


def kernel(**inp):
    x = np.asarray(inp["L0_attn_x"], dtype=np.float32)
    base = prep_inputs(**inp)
    nc = build_program(T=4096, dev=False)
    in_maps = []
    for b in range(8):
        mm = dict(base)
        mm["x"] = np.ascontiguousarray(x[b])
        in_maps.append(mm)
    res = run_bass_kernel_spmd(nc, in_maps, core_ids=list(range(8)))
    return np.stack([np.asarray(r["out"]) for r in res.results], axis=0)


# version 04
# speedup vs baseline: 1.2158x; 1.0481x over previous
import numpy as np
import ml_dtypes
from contextlib import ExitStack

import concourse.bass as bass
import concourse.mybir as mybir
from concourse.bass_utils import run_bass_kernel_spmd

F32 = mybir.dt.float32
BF16 = mybir.dt.bfloat16
AF = mybir.ActivationFunctionType
ALU = mybir.AluOpType
AX = mybir.AxisListType

D = 1024
F_CUT = 99
DEV_DUMP = False
EPS0 = 1e-6
LNEPS = 1e-5

ENGS = ("sync", "gpsimd", "vector", "scalar", "tensor")


class PB:

    ENG_ORD = {"vector": "zzv", "scalar": "zzs", "tensor": "zzt"}
    MILESTONE_ENG = {"dv": "vector", "ac": "scalar", "pe": "tensor"}

    def __init__(self, nc, paranoid=False):
        self.nc = nc
        self.paranoid = paranoid
        self.streams = {e: [] for e in ENGS}
        self.cnt = {}
        self._sems = {}
        self.dma_sem_eng = {}

    def sem(self, name):
        if name not in self._sems:
            self._sems[name] = self.nc.alloc_semaphore(name)
            self.cnt[name] = 0
        return self._sems[name]

    def do(self, eng, fn, inc=None, n=1):
        if self.paranoid and eng in self.ENG_ORD:
            z = self.ENG_ORD[eng]
            sem = self.sem(z)
            prev = self.cnt[z]
            self.cnt[z] = prev + 1
            tgt = prev + 1
            if prev > 0:
                self.streams[eng].append(
                    lambda e, sem=sem, prev=prev: e.wait_ge(sem, prev)
                )

            def wrapped(e, fn=fn, sem=sem):
                fn(e).then_inc(sem, 1)

            self.streams[eng].append(wrapped)
            return (z, tgt)
        if inc is None:
            self.streams[eng].append(fn)
            return None
        sem = self.sem(inc)
        self.cnt[inc] += n
        tgt = self.cnt[inc]

        def wrapped(e, fn=fn, sem=sem, n=n):
            fn(e).then_inc(sem, n)

        self.streams[eng].append(wrapped)
        return (inc, tgt)

    def wait(self, eng, ms):
        if ms is None:
            return
        name, val = ms
        if val <= 0:
            return
        sem = self.sem(name)
        self.streams[eng].append(lambda e, sem=sem, val=val: e.wait_ge(sem, val))

    def dma(self, eng, out, in_, sem):
        prev = self.dma_sem_eng.get(sem)
        assert prev is None or prev == eng, f"dma sem {sem} used from {prev} and {eng}"
        self.dma_sem_eng[sem] = eng
        return self.do(
            eng, lambda e, out=out, in_=in_: e.dma_start(out=out, in_=in_), inc=sem, n=16
        )

    def now(self, name):
        if self.paranoid and name in self.MILESTONE_ENG:
            z = self.ENG_ORD[self.MILESTONE_ENG[name]]
            return (z, self.cnt.get(z, 0))
        return (name, self.cnt.get(name, 0))

    def barrier(self):
        bar = self.sem("bar")
        for s, eng in self.dma_sem_eng.items():
            self.wait(eng, (s, self.cnt[s]))
        self.cnt["bar"] += len(ENGS)
        tgt = self.cnt["bar"]
        for eng in ENGS:
            self.streams[eng].append(lambda e, bar=bar: e.sem_inc(bar, 1))
            self.streams[eng].append(lambda e, bar=bar, tgt=tgt: e.wait_ge(bar, tgt))

    def flush(self, block):
        for eng in ENGS:
            lst = self.streams[eng]
            if not lst:
                continue
            getattr(block, eng)(lambda e, lst=lst: [f(e) for f in lst])
            self.streams[eng] = []



def emit_ln(pb, c, xin_ap, out_ap, junk, wst, psum_in=False):
    pb.wait("scalar", pb.now("dv"))
    acs = pb.do(
        "scalar",
        lambda e: e.activation(
            out=junk, in_=xin_ap, func=AF.Square, accum_out=wst[:, 3:4]
        ),
        inc="ac",
    )
    if psum_in:
        pb.wait("vector", acs)
    r1 = pb.do("vector", lambda e: e.reduce_sum(wst[:, 2:3], xin_ap, axis=AX.X), inc="dv")
    pb.wait("vector", r1)
    pb.do("vector", lambda e: e.tensor_scalar_mul(wst[:, 1:2], wst[:, 2:3], 1.0 / D))
    pb.wait("vector", acs)
    r3 = pb.do("vector", lambda e: e.tensor_scalar_mul(wst[:, 4:5], wst[:, 3:4], 1.0 / D), inc="dv")
    pb.wait("vector", r3)
    r4 = pb.do("vector", lambda e: e.tensor_mul(wst[:, 5:6], wst[:, 1:2], wst[:, 1:2]), inc="dv")
    pb.wait("vector", r4)
    dvv = pb.do("vector", lambda e: e.tensor_sub(wst[:, 6:7], wst[:, 4:5], wst[:, 5:6]), inc="dv")
    pb.wait("scalar", dvv)
    acv = pb.do(
        "scalar",
        lambda e: e.activation(out=wst[:, 7:8], in_=wst[:, 6:7], func=AF.Sqrt, bias=c["lneps"][:, 0:1]),
        inc="ac",
    )
    pb.wait("vector", acv)
    dvr2 = pb.do("vector", lambda e: e.reciprocal(wst[:, 0:1], wst[:, 7:8]), inc="dv")
    pb.wait("vector", dvr2)
    vap = pb.do(
        "vector",
        lambda e: e.scalar_tensor_tensor(
            out=out_ap, in0=xin_ap, scalar=wst[:, 1:2],
            in1=wst[:, 0:1].to_broadcast([128, D]),
            op0=ALU.subtract, op1=ALU.mult,
        ),
        inc="dv",
    )
    return vap


def emit_proj(pb, c, lnT, w_sbuf, bias_row, psum_ap, n_chunks=8, nout=D):
    nh = (nout + 511) // 512

    def fn(e, lnT=lnT, w=w_sbuf, b=bias_row, p=psum_ap, n=n_chunks, nout=nout, nh=nh):
        ins = None
        for cc in range(n):
            lhs = lnT[:, 128 * cc : 128 * cc + 128]
            for half in range(nh):
                w0 = nout * cc + 512 * half
                nw = min(512, nout - 512 * half)
                ins = e.matmul(
                    p[:, 512 * half : 512 * half + nw],
                    lhs,
                    w[:, w0 : w0 + nw],
                    start=(cc == 0), stop=(cc == n - 1 and b is None),
                )
        if b is not None:
            for half in range(nh):
                nw = min(512, nout - 512 * half)
                ins = e.matmul(
                    p[:, 512 * half : 512 * half + nw],
                    c["ones_row"][0:1, :],
                    b[0:1, 512 * half : 512 * half + nw],
                    start=False, stop=True,
                )
        return ins

    return pb.do("tensor", fn, inc="pe")


def emit_transpose8(pb, c, src_bf, pt, n=8):
    def fn(e, s=src_bf, pt=pt, n=n):
        ins = None
        for cc in range(n):
            ins = e.transpose(
                pt[:, 128 * cc : 128 * cc + 128],
                s[:, 128 * cc : 128 * cc + 128],
                c["ident_bf"][:, :],
            )
        return ins

    return pb.do("tensor", fn, inc="pe")



def emit_l0_attn(pb, c, ctx, T, x_in, scr_q, scr_k, scr_v, scr_out, n_sweeps=6):
    nc = pb.nc
    NT = T // 128

    wq = ctx.enter_context(nc.sbuf_tensor("l0a_wq", [128, 8 * 1024], BF16))
    wk = ctx.enter_context(nc.sbuf_tensor("l0a_wk", [128, 8 * 1024], BF16))
    wv = ctx.enter_context(nc.sbuf_tensor("l0a_wv", [128, 8 * 1024], BF16))
    wo = ctx.enter_context(nc.sbuf_tensor("l0a_wo", [128, 8 * 1024], BF16))
    bq = ctx.enter_context(nc.sbuf_tensor("l0a_bq", [1, D], BF16))
    bk = ctx.enter_context(nc.sbuf_tensor("l0a_bk", [1, D], BF16))
    bv = ctx.enter_context(nc.sbuf_tensor("l0a_bv", [1, D], BF16))
    bo = ctx.enter_context(nc.sbuf_tensor("l0a_bo", [1, D], BF16))
    xin = [ctx.enter_context(nc.sbuf_tensor(f"l0a_xin{i}", [128, D], F32)) for i in range(2)]
    ln_bf = ctx.enter_context(nc.sbuf_tensor("l0a_ln", [128, D], BF16))
    lnT = ctx.enter_context(nc.sbuf_tensor("l0a_lnT", [128, D], BF16))
    q_bf = [ctx.enter_context(nc.sbuf_tensor(f"l0a_q{i}", [128, D], BF16)) for i in range(2)]
    k_bf = [ctx.enter_context(nc.sbuf_tensor(f"l0a_k{i}", [128, D], BF16)) for i in range(2)]
    v_bf = [ctx.enter_context(nc.sbuf_tensor(f"l0a_v{i}", [128, D], BF16)) for i in range(2)]
    wst = ctx.enter_context(nc.sbuf_tensor("l0a_wst", [128, 8], F32))
    sqsum = ctx.enter_context(nc.sbuf_tensor("l0a_qsum", [1, D], F32))
    sksum = ctx.enter_context(nc.sbuf_tensor("l0a_ksum", [1, D], F32))
    rowbf = ctx.enter_context(nc.sbuf_tensor("l0a_rowbf", [1, D], BF16))
    bc1 = ctx.enter_context(nc.sbuf_tensor("l0a_bc1", [128, D], F32))
    bc2 = ctx.enter_context(nc.sbuf_tensor("l0a_bc2", [128, D], F32))
    tmpf = ctx.enter_context(nc.sbuf_tensor("l0a_tmpf", [128, D], F32))
    s16a = ctx.enter_context(nc.sbuf_tensor("l0a_s16a", [128, 16], F32))
    s16b = ctx.enter_context(nc.sbuf_tensor("l0a_s16b", [128, 16], F32))
    si_all = ctx.enter_context(nc.sbuf_tensor("l0a_si", [128, 16 * NT], F32))
    sa_all = ctx.enter_context(nc.sbuf_tensor("l0a_sa", [128, 16 * NT], F32))
    csT = ctx.enter_context(nc.sbuf_tensor("l0a_csT", [16, T], F32))
    scT = ctx.enter_context(nc.sbuf_tensor("l0a_scT", [16, T], F32))
    s16 = ctx.enter_context(nc.sbuf_tensor("l0a_s16", [16, 4], F32))
    kv_s = ctx.enter_context(nc.sbuf_tensor("l0a_kv", [64, 1024], BF16))
    qsiT = ctx.enter_context(nc.sbuf_tensor("l0a_qsiT", [64, 16 * 128], BF16))
    aT = ctx.enter_context(nc.sbuf_tensor("l0a_aT", [128, D], BF16))
    y1 = ctx.enter_context(nc.sbuf_tensor("l0a_y1", [128, D], F32))
    x1o = [ctx.enter_context(nc.sbuf_tensor(f"l0a_x1o{i}", [128, D], F32)) for i in range(2)]

    pA = ctx.enter_context(nc.psum_tensor("l0a_pA", [128, D], F32))
    pB = ctx.enter_context(nc.psum_tensor("l0a_pB", [128, D], F32))
    pt = ctx.enter_context(nc.psum_tensor("l0a_pt", [128, D], BF16))
    ps1 = ctx.enter_context(nc.psum_tensor("l0a_ps1", [1, D], F32))

    for w_s, name in ((wq, "l0_wqt"), (wk, "l0_wkt"), (wv, "l0_wvt"), (wo, "l0_wot")):
        src = c[name].rearrange("(c p) n -> p c n", p=128)
        dst = w_s[:, :].rearrange("p (c n) -> p c n", c=8)
        pb.dma("gpsimd", dst, src, "wl")
    for b_s, name in ((bq, "l0_bq"), (bk, "l0_bk"), (bv, "l0_bv"), (bo, "l0_bo")):
        pb.dma("gpsimd", b_s[:, :], c[name], "wl")
    wl_all = pb.now("wl")

    ld_x = {}
    for t in range(min(2, NT)):
        ld_x[t] = pb.dma("sync", xin[t % 2][:, :], x_in[128 * t : 128 * t + 128, :], f"lx{t % 2}")

    ac_sq, ac_sk, dv_sv, st_q, st_k, st_v = {}, {}, {}, {}, {}, {}
    dv_qsum = dv_ksum = None
    for t in range(NT):
        bt = t % 2
        pb.wait("vector", ld_x[t])
        pb.wait("scalar", ld_x[t])
        apply_v = emit_ln(pb, c, xin[bt][:, :], ln_bf[:, :], tmpf[:, :], wst)
        if t + 2 < NT:
            pb.wait("sync", apply_v)
            ld_x[t + 2] = pb.dma("sync", xin[bt][:, :], x_in[128 * (t + 2) : 128 * (t + 2) + 128, :], f"lx{bt}")
        pb.wait("tensor", apply_v)
        pe_tr = emit_transpose8(pb, c, ln_bf, pt)
        pb.wait("vector", pe_tr)
        dv_lnT = pb.do("vector", lambda e: e.tensor_copy(lnT[:, :], pt[:, :]), inc="dv")
        pb.wait("tensor", dv_lnT)
        if t == 0:
            pb.wait("tensor", wl_all)
        if t >= 1:
            pb.wait("tensor", dv_sv[t - 1])
        pe_q = emit_proj(pb, c, lnT, wq, bq, pA)
        if t >= 1:
            pb.wait("tensor", ac_sk[t - 1])
        pe_k = emit_proj(pb, c, lnT, wk, bk, pB)
        pb.wait("scalar", pe_q)
        if t >= 2:
            pb.wait("scalar", st_q[t - 2])
        ac_sq[t] = pb.do(
            "scalar",
            lambda e, bt=bt: e.activation(out=q_bf[bt][:, :], in_=pA[:, :], func=AF.Sigmoid),
            inc="ac",
        )
        pb.wait("tensor", ac_sq[t])
        pe_v = emit_proj(pb, c, lnT, wv, bv, pA)
        pb.wait("scalar", pe_k)
        if t >= 2:
            pb.wait("scalar", st_k[t - 2])
        ac_sk[t] = pb.do(
            "scalar",
            lambda e, bt=bt: e.activation(out=k_bf[bt][:, :], in_=pB[:, :], func=AF.Sigmoid),
            inc="ac",
        )
        pb.wait("vector", pe_v)
        if t >= 2:
            pb.wait("vector", st_v[t - 2])
        dv_sv[t] = pb.do("vector", lambda e, bt=bt: e.tensor_copy(v_bf[bt][:, :], pA[:, :]), inc="dv")
        pb.wait("tensor", ac_sq[t])
        if dv_ksum is not None:
            pb.wait("tensor", dv_ksum)

        def qsum_fn(e, q=q_bf[bt]):
            e.matmul(ps1[0:1, 0:512], c["ones_col"][:, 0:1], q[:, 0:512], start=True, stop=True)
            return e.matmul(ps1[0:1, 512:1024], c["ones_col"][:, 0:1], q[:, 512:1024], start=True, stop=True)

        pe_qs = pb.do("tensor", qsum_fn, inc="pe")
        pb.wait("vector", pe_qs)
        if t == 0:
            dv_qsum = pb.do("vector", lambda e: e.tensor_copy(sqsum[0:1, :], ps1[0:1, :]), inc="dv")
        else:
            dv_qsum = pb.do("vector", lambda e: e.tensor_add(sqsum[0:1, :], sqsum[0:1, :], ps1[0:1, :]), inc="dv")
        pb.wait("tensor", ac_sk[t])
        pb.wait("tensor", dv_qsum)

        def ksum_fn(e, k=k_bf[bt]):
            e.matmul(ps1[0:1, 0:512], c["ones_col"][:, 0:1], k[:, 0:512], start=True, stop=True)
            return e.matmul(ps1[0:1, 512:1024], c["ones_col"][:, 0:1], k[:, 512:1024], start=True, stop=True)

        pe_ks = pb.do("tensor", ksum_fn, inc="pe")
        pb.wait("vector", pe_ks)
        if t == 0:
            dv_ksum = pb.do("vector", lambda e: e.tensor_copy(sksum[0:1, :], ps1[0:1, :]), inc="dv")
        else:
            dv_ksum = pb.do("vector", lambda e: e.tensor_add(sksum[0:1, :], sksum[0:1, :], ps1[0:1, :]), inc="dv")
        pb.wait("sync", ac_sq[t])
        st_q[t] = pb.dma("sync", scr_q[128 * t : 128 * t + 128, :], q_bf[bt][:, :], f"sq{bt}")
        pb.wait("sync", ac_sk[t])
        st_k[t] = pb.dma("sync", scr_k[128 * t : 128 * t + 128, :], k_bf[bt][:, :], f"sk{bt}")
        pb.wait("sync", dv_sv[t])
        st_v[t] = pb.dma("sync", scr_v[128 * t : 128 * t + 128, :], v_bf[bt][:, :], f"sv{bt}")

    pb.barrier()
    if n_sweeps <= 1:
        return

    def bcast_row(src_row_ap, dst_bc):
        dvr = pb.do(
            "vector",
            lambda e: e.tensor_scalar_add(rowbf[0:1, :], src_row_ap, EPS0),
            inc="dv",
        )
        pb.wait("tensor", dvr)

        def fn(e):
            e.matmul(pA[:, 0:512], c["ones_row"][0:1, :], rowbf[0:1, 0:512], start=True, stop=True)
            return e.matmul(pA[:, 512:1024], c["ones_row"][0:1, :], rowbf[0:1, 512:1024], start=True, stop=True)

        pev = pb.do("tensor", fn, inc="pe")
        pb.wait("vector", pev)
        return pb.do("vector", lambda e: e.tensor_copy(dst_bc[:, :], pA[:, :]), inc="dv")

    bcast_row(sksum[0:1, :], bc1)
    bcast_row(sqsum[0:1, :], bc2)

    ld_q, ld_k, ld_v = {}, {}, {}
    for t in range(min(2, NT)):
        ld_q[t] = pb.dma("sync", q_bf[t % 2][:, :], scr_q[128 * t : 128 * t + 128, :], f"lq{t % 2}")
        ld_k[t] = pb.dma("sync", k_bf[t % 2][:, :], scr_k[128 * t : 128 * t + 128, :], f"lk{t % 2}")

    pe_s2 = pe_s1 = None
    for t in range(NT):
        bt = t % 2
        pb.wait("vector", ld_q[t])
        pb.wait("vector", ld_k[t])
        if pe_s1 is not None:
            pb.wait("vector", pe_s1)
        pb.do(
            "vector",
            lambda e, q=q_bf[bt]: e.scalar_tensor_tensor(
                out=tmpf[:, :], in0=q[:, :], scalar=EPS0, in1=bc1[:, :],
                op0=ALU.add, op1=ALU.mult,
            ),
        )
        rqa = pb.do(
            "vector",
            lambda e: e.reduce_sum(
                s16a[:, :], tmpf[:, :].rearrange("p (h d) -> p h d", h=16), axis=AX.X
            ),
            inc="dv",
        )
        pb.wait("vector", rqa)
        dvrq = pb.do("vector", lambda e, t=t: e.reciprocal(si_all[:, 16 * t : 16 * t + 16], s16a[:, :]), inc="dv")
        pb.wait("vector", dvrq)
        dv_qsi = pb.do(
            "vector",
            lambda e, q=q_bf[bt], t=t: e.tensor_tensor(
                out=ln_bf[:, :].rearrange("p (h d) -> p h d", h=16),
                in0=q[:, :].rearrange("p (h d) -> p h d", h=16),
                in1=si_all[:, 16 * t : 16 * t + 16].unsqueeze(2).to_broadcast([128, 16, 64]),
                op=ALU.mult,
            ),
            inc="dv",
        )
        pb.wait("tensor", dv_qsi)

        def s2fn(e, t=t):
            e.matmul(pB[0:1, 0:512], c["ones_col"][:, 0:1], ln_bf[:, 0:512], start=(t == 0), stop=(t == NT - 1))
            return e.matmul(pB[0:1, 512:1024], c["ones_col"][:, 0:1], ln_bf[:, 512:1024], start=(t == 0), stop=(t == NT - 1))

        pe_s2 = pb.do("tensor", s2fn, inc="pe")
        pb.wait("vector", pe_s2)
        pb.do(
            "vector",
            lambda e, k=k_bf[bt]: e.scalar_tensor_tensor(
                out=tmpf[:, :], in0=k[:, :], scalar=EPS0, in1=bc2[:, :],
                op0=ALU.add, op1=ALU.mult,
            ),
        )
        rkb = pb.do(
            "vector",
            lambda e: e.reduce_sum(
                s16b[:, :], tmpf[:, :].rearrange("p (h d) -> p h d", h=16), axis=AX.X
            ),
            inc="dv",
        )
        pb.wait("vector", rkb)
        dvrk = pb.do("vector", lambda e: e.reciprocal(s16b[:, :], s16b[:, :]), inc="dv")
        pb.wait("vector", dvrk)
        dv_kso = pb.do(
            "vector",
            lambda e, k=k_bf[bt]: e.tensor_tensor(
                out=ln_bf[:, :].rearrange("p (h d) -> p h d", h=16),
                in0=k[:, :].rearrange("p (h d) -> p h d", h=16),
                in1=s16b[:, :].unsqueeze(2).to_broadcast([128, 16, 64]),
                op=ALU.mult,
            ),
            inc="dv",
        )
        pb.wait("tensor", dv_kso)

        def s1fn(e, t=t):
            e.matmul(ps1[0:1, 0:512], c["ones_col"][:, 0:1], ln_bf[:, 0:512], start=(t == 0), stop=(t == NT - 1))
            return e.matmul(ps1[0:1, 512:1024], c["ones_col"][:, 0:1], ln_bf[:, 512:1024], start=(t == 0), stop=(t == NT - 1))

        pe_s1 = pb.do("tensor", s1fn, inc="pe")
        if t + 2 < NT:
            pb.wait("sync", pe_s1)
            ld_q[t + 2] = pb.dma("sync", q_bf[bt][:, :], scr_q[128 * (t + 2) : 128 * (t + 2) + 128, :], f"lq{bt}")
            ld_k[t + 2] = pb.dma("sync", k_bf[bt][:, :], scr_k[128 * (t + 2) : 128 * (t + 2) + 128, :], f"lk{bt}")

    pb.wait("vector", pe_s1)
    pb.wait("vector", pe_s2)
    bcast_row(ps1[0:1, :], bc1)
    bcast_row(pB[0:1, :], bc2)

    pb.barrier()
    if n_sweeps <= 2:
        return

    for t in range(min(2, NT)):
        ld_q[t] = pb.dma("sync", q_bf[t % 2][:, :], scr_q[128 * t : 128 * t + 128, :], f"lq{t % 2}")
        ld_k[t] = pb.dma("sync", k_bf[t % 2][:, :], scr_k[128 * t : 128 * t + 128, :], f"lk{t % 2}")
    ac_sa = pe_trc = None
    dv_cs = {}
    for t in range(NT):
        bt = t % 2
        pb.wait("vector", ld_q[t])
        pb.wait("vector", ld_k[t])
        if ac_sa is not None:
            pb.wait("vector", ac_sa)
        if pe_trc is not None:
            pb.wait("vector", pe_trc)
        pb.do(
            "vector",
            lambda e, q=q_bf[bt]: e.scalar_tensor_tensor(
                out=tmpf[:, :], in0=q[:, :], scalar=EPS0, in1=bc1[:, :],
                op0=ALU.add, op1=ALU.mult,
            ),
        )
        dvr = pb.do(
            "vector",
            lambda e: e.reduce_sum(
                s16a[:, :], tmpf[:, :].rearrange("p (h d) -> p h d", h=16), axis=AX.X
            ),
            inc="dv",
        )
        pb.wait("scalar", dvr)
        ac_sa = pb.do(
            "scalar",
            lambda e, t=t: e.activation(
                out=sa_all[:, 16 * t : 16 * t + 16], in_=s16a[:, :], func=AF.Sigmoid
            ),
            inc="ac",
        )
        pb.do(
            "vector",
            lambda e, k=k_bf[bt]: e.scalar_tensor_tensor(
                out=tmpf[:, :], in0=k[:, :], scalar=EPS0, in1=bc2[:, :],
                op0=ALU.add, op1=ALU.mult,
            ),
        )
        rcb = pb.do(
            "vector",
            lambda e: e.reduce_sum(
                s16b[:, :], tmpf[:, :].rearrange("p (h d) -> p h d", h=16), axis=AX.X
            ),
            inc="dv",
        )
        pb.wait("vector", rcb)
        rcm = pb.do("vector", lambda e: e.tensor_scalar_min(s16b[:, :], s16b[:, :], 1.0), inc="dv")
        pb.wait("vector", rcm)
        dv_cl = pb.do("vector", lambda e: e.tensor_scalar_max(s16b[:, :], s16b[:, :], -1.0), inc="dv")
        pb.wait("tensor", dv_cl)
        if t >= 1:
            pb.wait("tensor", dv_cs[t - 1])
        pe_trc = pb.do(
            "tensor",
            lambda e: e.transpose(pA[0:16, 0:128], s16b[:, :], c["ident_f32"][:, :]),
            inc="pe",
        )
        pb.wait("vector", pe_trc)
        dv_cs[t] = pb.do(
            "vector",
            lambda e, t=t: e.tensor_copy(csT[0:16, 128 * t : 128 * t + 128], pA[0:16, 0:128]),
            inc="dv",
        )
        if t + 2 < NT:
            pb.wait("sync", dv_cs[t])
            ld_q[t + 2] = pb.dma("sync", q_bf[bt][:, :], scr_q[128 * (t + 2) : 128 * (t + 2) + 128, :], f"lq{bt}")
            ld_k[t + 2] = pb.dma("sync", k_bf[bt][:, :], scr_k[128 * (t + 2) : 128 * (t + 2) + 128, :], f"lk{bt}")

    pb.barrier()
    if n_sweeps <= 3:
        return

    rdm = pb.do("vector", lambda e: e.reduce_max(s16[0:16, 0:1], csT[0:16, :], axis=AX.X), inc="dv")
    pb.wait("vector", rdm)
    dvm = pb.do("vector", lambda e: e.tensor_scalar_mul(s16[0:16, 1:2], s16[0:16, 0:1], -1.0), inc="dv")
    pb.wait("scalar", dvm)
    acce = pb.do(
        "scalar",
        lambda e: e.activation(
            out=scT[0:16, :], in_=csT[0:16, :], func=AF.Exp,
            bias=s16[0:16, 1:2], accum_out=s16[0:16, 2:3],
        ),
        inc="ac",
    )
    pb.wait("vector", acce)
    rrc = pb.do("vector", lambda e: e.reciprocal(s16[0:16, 3:4], s16[0:16, 2:3]), inc="dv")
    pb.wait("vector", rrc)
    dvsc = pb.do("vector", lambda e: e.tensor_scalar_mul(s16[0:16, 3:4], s16[0:16, 3:4], float(T)), inc="dv")
    pb.wait("vector", dvsc)
    dvscT = pb.do(
        "vector",
        lambda e: e.tensor_scalar(
            scT[0:16, :], scT[0:16, :], s16[0:16, 3:4], None, op0=ALU.mult
        ),
        inc="dv",
    )
    if DEV_DUMP:
        pb.wait("sync", dvscT)
        pb.dma("sync", c["scr_dbg"][0:16, 0:T], csT[0:16, :], "dbg1")
        pb.dma("sync", c["scr_dbg"][16:32, 0:T], scT[0:16, :], "sdbg2")
        pb.dma("sync", c["scr_dbg"][32:33, 0:D], bc1[0:1, :], "dbg3")
        pb.dma("sync", c["scr_dbg"][33:34, 0:D], bc2[0:1, :], "dbg4")
        pb.dma("sync", c["scr_dbg"][34:35, 0:D], sksum[0:1, :], "dbg5")
        pb.dma("sync", c["scr_dbg"][35:36, 0:D], sqsum[0:1, :], "dbg6")
        pb.dma("sync", c["scr_dbg"][36:37, 0:16].rearrange("a (b c) -> a b c", b=4), s16[0:16, 0:4].rearrange("(a b) c -> a b c", a=1), "dbg7") if False else None
        pb.dma("sync", c["scr_dbg"][40:56, 0:4], s16[0:16, 0:4], "dbg8")

    pb.barrier()
    if n_sweeps <= 4:
        return

    for t in range(min(2, NT)):
        ld_k[t] = pb.dma("sync", k_bf[t % 2][:, :], scr_k[128 * t : 128 * t + 128, :], f"lk{t % 2}")
        ld_v[t] = pb.dma("sync", v_bf[t % 2][:, :], scr_v[128 * t : 128 * t + 128, :], f"lv{t % 2}")
    dv_vsc = {}
    pe_kv = None
    for t in range(NT):
        bt = t % 2
        if t >= 1:
            pb.wait("tensor", dv_vsc[t - 1])
        pe_tr = pb.do(
            "tensor",
            lambda e, t=t: e.transpose(
                pA[:, 0:16], scT[0:16, 128 * t : 128 * t + 128], c["ident_f32"][0:16, 0:16]
            ),
            inc="pe",
        )
        pb.wait("vector", pe_tr)
        pb.wait("vector", ld_v[t])
        if pe_kv is not None:
            pb.wait("vector", pe_kv)
        dvcp = pb.do("vector", lambda e: e.tensor_copy(s16a[:, :], pA[:, 0:16]), inc="dv")
        pb.wait("vector", dvcp)
        dv_vsc[t] = pb.do(
            "vector",
            lambda e, v=v_bf[bt]: e.tensor_tensor(
                out=ln_bf[:, :].rearrange("p (h d) -> p h d", h=16),
                in0=v[:, :].rearrange("p (h d) -> p h d", h=16),
                in1=s16a[:, :].unsqueeze(2).to_broadcast([128, 16, 64]),
                op=ALU.mult,
            ),
            inc="dv",
        )
        pb.wait("tensor", dv_vsc[t])
        pb.wait("tensor", ld_k[t])

        def kvfn(e, k=k_bf[bt], t=t):
            ins = None
            for h in range(16):
                ins = e.matmul(
                    pB[0:64, 64 * h : 64 * h + 64],
                    k[:, 64 * h : 64 * h + 64],
                    ln_bf[:, 64 * h : 64 * h + 64],
                    start=(t == 0 and h % 8 == 0),
                    stop=(t == NT - 1 and h % 8 == 7),
                )
            return ins

        pe_kv = pb.do("tensor", kvfn, inc="pe")
        if t + 2 < NT:
            pb.wait("sync", pe_kv)
            pb.wait("sync", dv_vsc[t])
            ld_k[t + 2] = pb.dma("sync", k_bf[bt][:, :], scr_k[128 * (t + 2) : 128 * (t + 2) + 128, :], f"lk{bt}")
            ld_v[t + 2] = pb.dma("sync", v_bf[bt][:, :], scr_v[128 * (t + 2) : 128 * (t + 2) + 128, :], f"lv{bt}")
    pb.wait("vector", pe_kv)
    dv_kvs = pb.do("vector", lambda e: e.tensor_copy(kv_s[0:64, :], pB[0:64, :]), inc="dv")

    pb.barrier()
    if n_sweeps <= 5:
        return

    for t in range(min(2, NT)):
        ld_q[t] = pb.dma("sync", q_bf[t % 2][:, :], scr_q[128 * t : 128 * t + 128, :], f"lq{t % 2}")
        ld_x[t] = pb.dma("sync", xin[t % 2][:, :], x_in[128 * t : 128 * t + 128, :], f"lx{t % 2}")
    st_o = {}
    dv_fin = {}
    pe_y = None
    for t in range(NT):
        bt = t % 2
        pb.wait("vector", ld_q[t])
        if pe_y is not None:
            pb.wait("vector", pe_y)
        if F_CUT < 9 and t >= 0:
            emit_f_cut(pb, c, t, bt, NT, locals())
            continue
        dv_q = pb.do(
            "vector",
            lambda e, q=q_bf[bt], t=t: e.tensor_tensor(
                out=ln_bf[:, :].rearrange("p (h d) -> p h d", h=16),
                in0=q[:, :].rearrange("p (h d) -> p h d", h=16),
                in1=si_all[:, 16 * t : 16 * t + 16].unsqueeze(2).to_broadcast([128, 16, 64]),
                op=ALU.mult,
            ),
            inc="dv",
        )
        pb.wait("tensor", dv_q)
        if t >= 1:
            pb.wait("tensor", dv_fin[t - 1])
        dv_qT = None
        for grp in range(2):
            def trf(e, grp=grp):
                ins = None
                for hh in range(8):
                    h = 8 * grp + hh
                    ins = e.transpose(
                        pt[0:64, 128 * hh : 128 * hh + 128],
                        ln_bf[:, 64 * h : 64 * h + 64],
                        c["ident_bf"][:, :],
                    )
                return ins

            if grp == 1:
                pb.wait("tensor", dv_qT)
            pe_tr = pb.do("tensor", trf, inc="pe")
            pb.wait("vector", pe_tr)
            dv_qT = pb.do(
                "vector",
                lambda e, grp=grp: e.tensor_copy(
                    qsiT[0:64, 1024 * grp : 1024 * grp + 1024], pt[0:64, 0:1024]
                ),
                inc="dv",
            )
        pb.wait("tensor", dv_qT)

        def outfn(e):
            ins = None
            for h in range(16):
                ins = e.matmul(
                    pA[:, 64 * h : 64 * h + 64],
                    qsiT[0:64, 128 * h : 128 * h + 128],
                    kv_s[0:64, 64 * h : 64 * h + 64],
                    start=True, stop=True,
                )
            return ins

        pe_out = pb.do("tensor", outfn, inc="pe")
        pb.wait("vector", pe_out)
        dv_at = pb.do(
            "vector",
            lambda e, t=t: e.tensor_tensor(
                out=ln_bf[:, :].rearrange("p (h d) -> p h d", h=16),
                in0=pA[:, :].rearrange("p (h d) -> p h d", h=16),
                in1=sa_all[:, 16 * t : 16 * t + 16].unsqueeze(2).to_broadcast([128, 16, 64]),
                op=ALU.mult,
            ),
            inc="dv",
        )
        pb.wait("tensor", dv_at)
        pe_tr2 = emit_transpose8(pb, c, ln_bf, pt)
        pb.wait("vector", pe_tr2)
        dv_aT = pb.do("vector", lambda e: e.tensor_copy(aT[:, :], pt[:, :]), inc="dv")
        pb.wait("tensor", dv_aT)
        pe_y = emit_proj(pb, c, aT, wo, bo, pB)
        pb.wait("vector", pe_y)
        pb.wait("scalar", pe_y)
        emit_ln(pb, c, pB[:, :], y1[:, :], tmpf[:, :], wst, psum_in=True)
        pb.wait("vector", ld_x[t])
        if t >= 2:
            pb.wait("vector", st_o[t - 2])
        dv_fin[t] = pb.do(
            "vector", lambda e, bt=bt: e.tensor_add(x1o[bt][:, :], xin[bt][:, :], y1[:, :]), inc="dv"
        )
        pb.wait("sync", dv_fin[t])
        st_o[t] = pb.dma("sync", scr_out[128 * t : 128 * t + 128, :], x1o[bt][:, :], f"so{bt}")
        if t + 2 < NT:
            ld_q[t + 2] = pb.dma("sync", q_bf[bt][:, :], scr_q[128 * (t + 2) : 128 * (t + 2) + 128, :], f"lq{bt}")
            ld_x[t + 2] = pb.dma("sync", xin[bt][:, :], x_in[128 * (t + 2) : 128 * (t + 2) + 128, :], f"lx{bt}")

    pb.barrier()




def emit_f_cut(pb, c, t, bt, NT, L):
    xin, ln_bf, q_bf, qsiT, kv_s, sa_all, si_all, aT, y1, x1o, tmpf, wst = (
        L["xin"], L["ln_bf"], L["q_bf"], L["qsiT"], L["kv_s"], L["sa_all"],
        L["si_all"], L["aT"], L["y1"], L["x1o"], L["tmpf"], L["wst"])
    pA, pB, pt = L["pA"], L["pB"], L["pt"]
    wo, bo = L["wo"], L["bo"]
    scr_out, scr_q, x_in = L["scr_out"], L["scr_q"], L["x_in"]
    ld_q, ld_x, st_o, dv_fin = L["ld_q"], L["ld_x"], L["st_o"], L["dv_fin"]
    ALU_ = ALU
    last_dv = None
    dv_q = pb.do(
        "vector",
        lambda e, q=q_bf[bt], t=t: e.tensor_tensor(
            out=ln_bf[:, :].rearrange("p (h d) -> p h d", h=16),
            in0=q[:, :].rearrange("p (h d) -> p h d", h=16),
            in1=si_all[:, 16 * t : 16 * t + 16].unsqueeze(2).to_broadcast([128, 16, 64]),
            op=ALU_.mult,
        ),
        inc="dv",
    )
    last_pe = None
    if F_CUT >= 2:
        pb.wait("tensor", dv_q)
        if t >= 1:
            pb.wait("tensor", dv_fin[t - 1])
        dv_qT = None
        for grp in range(2):
            def trf(e, grp=grp):
                ins = None
                for hh in range(8):
                    h = 8 * grp + hh
                    ins = e.transpose(
                        pt[0:64, 128 * hh : 128 * hh + 128],
                        ln_bf[:, 64 * h : 64 * h + 64],
                        c["ident_bf"][:, :],
                    )
                return ins
            if grp == 1:
                pb.wait("tensor", dv_qT)
            pe_tr = pb.do("tensor", trf, inc="pe")
            pb.wait("vector", pe_tr)
            dv_qT = pb.do(
                "vector",
                lambda e, grp=grp: e.tensor_copy(
                    qsiT[0:64, 1024 * grp : 1024 * grp + 1024], pt[0:64, 0:1024]
                ),
                inc="dv",
            )
        last_pe = pe_tr
    if F_CUT >= 3:
        pb.wait("tensor", dv_qT)
        def outfn(e):
            ins = None
            for h in range(16):
                ins = e.matmul(
                    pA[:, 64 * h : 64 * h + 64],
                    qsiT[0:64, 128 * h : 128 * h + 128],
                    kv_s[0:64, 64 * h : 64 * h + 64],
                    start=True, stop=True,
                )
            return ins
        pe_out = pb.do("tensor", outfn, inc="pe")
        pb.wait("vector", pe_out)
        dv_at = pb.do(
            "vector",
            lambda e, t=t: e.tensor_tensor(
                out=ln_bf[:, :].rearrange("p (h d) -> p h d", h=16),
                in0=pA[:, :].rearrange("p (h d) -> p h d", h=16),
                in1=sa_all[:, 16 * t : 16 * t + 16].unsqueeze(2).to_broadcast([128, 16, 64]),
                op=ALU_.mult,
            ),
            inc="dv",
        )
        last_pe = pe_out
    if F_CUT >= 4:
        pb.wait("tensor", dv_at)
        pe_tr2 = emit_transpose8(pb, c, ln_bf, pt)
        pb.wait("vector", pe_tr2)
        dv_aT = pb.do("vector", lambda e: e.tensor_copy(aT[:, :], pt[:, :]), inc="dv")
        last_pe = pe_tr2
    if F_CUT >= 5:
        pb.wait("tensor", dv_aT)
        pe_y = emit_proj(pb, c, aT, wo, bo, pB)
        last_pe = pe_y
    if F_CUT == 6:
        pb.wait("vector", pe_y)
        pb.wait("scalar", pe_y)
        acs = pb.do(
            "scalar",
            lambda e: e.activation(out=tmpf[:, :], in_=pB[:, :], func=AF.Square, accum_out=wst[:, 3:4]),
            inc="ac",
        )
        pb.wait("vector", acs)
        pb.do("vector", lambda e: e.tensor_copy(y1[:, :], pB[:, :]))
    elif F_CUT == 7:
        pb.wait("vector", pe_y)
        pb.wait("scalar", pe_y)
        pb.wait("scalar", pb.now("dv"))
        acs = pb.do(
            "scalar",
            lambda e: e.activation(out=tmpf[:, :], in_=pB[:, :], func=AF.Square, accum_out=wst[:, 3:4]),
            inc="ac",
        )
        pb.do("vector", lambda e: e.reduce_sum(wst[:, 2:3], pB[:, :], axis=AX.X))
        pb.do("vector", lambda e: e.tensor_scalar_mul(wst[:, 1:2], wst[:, 2:3], 1.0 / D))
        pb.wait("vector", acs)
        pb.do("vector", lambda e: e.tensor_scalar_mul(wst[:, 4:5], wst[:, 3:4], 1.0 / D))
        pb.do("vector", lambda e: e.tensor_mul(wst[:, 5:6], wst[:, 1:2], wst[:, 1:2]))
        dvv = pb.do("vector", lambda e: e.tensor_sub(wst[:, 6:7], wst[:, 4:5], wst[:, 5:6]), inc="dv")
        pb.wait("scalar", dvv)
        acv = pb.do(
            "scalar",
            lambda e: e.activation(out=wst[:, 7:8], in_=wst[:, 6:7], func=AF.Sqrt, bias=c["lneps"][:, 0:1]),
            inc="ac",
        )
        pb.wait("vector", acv)
        dvr2 = pb.do("vector", lambda e: e.reciprocal(wst[:, 0:1], wst[:, 7:8]), inc="dv")
        pb.wait("vector", dvr2)
        pb.do("vector", lambda e: e.tensor_copy(y1[:, :], pB[:, :]))
    elif F_CUT >= 8:
        pb.wait("vector", pe_y)
        pb.wait("scalar", pe_y)
        emit_ln(pb, c, pB[:, :], y1[:, :], tmpf[:, :], wst, psum_in=True)
    pb.wait("vector", ld_x[t])
    if t >= 2:
        pb.wait("vector", st_o[t - 2])
    if F_CUT >= 6:
        pass
    if F_CUT >= 6:
        dv_fin[t] = pb.do(
            "vector", lambda e, bt=bt: e.tensor_add(x1o[bt][:, :], xin[bt][:, :], y1[:, :]), inc="dv"
        )
    else:
        if last_pe is not None:
            pb.wait("vector", last_pe)
        dv_fin[t] = pb.do(
            "vector", lambda e, bt=bt: e.tensor_copy(x1o[bt][:, :], xin[bt][:, :]), inc="dv"
        )
    pb.wait("sync", dv_fin[t])
    st_o[t] = pb.dma("sync", scr_out[128 * t : 128 * t + 128, :], x1o[bt][:, :], f"so{bt}")
    if t + 2 < NT:
        ld_q[t + 2] = pb.dma("sync", q_bf[bt][:, :], scr_q[128 * (t + 2) : 128 * (t + 2) + 128, :], f"lq{bt}")
        ld_x[t + 2] = pb.dma("sync", xin[bt][:, :], x_in[128 * (t + 2) : 128 * (t + 2) + 128, :], f"lx{bt}")




def emit_l0_ffn(pb, c, ctx, T, x_in, scr_out):
    nc = pb.nc
    NT = T // 128

    wr = ctx.enter_context(nc.sbuf_tensor("l0f_wr", [128, 8 * 1024], BF16))
    wk2 = ctx.enter_context(nc.sbuf_tensor("l0f_wk", [128, 8 * 1024], BF16))
    wv2 = ctx.enter_context(nc.sbuf_tensor("l0f_wv", [128, 8 * 1024], BF16))
    mur = ctx.enter_context(nc.sbuf_tensor("l0f_murs", [128, 8], F32))
    muk = ctx.enter_context(nc.sbuf_tensor("l0f_muks", [128, 8], F32))
    omr = ctx.enter_context(nc.sbuf_tensor("l0f_omr", [128, 8], F32))
    omk = ctx.enter_context(nc.sbuf_tensor("l0f_omk", [128, 8], F32))
    zr = ctx.enter_context(nc.sbuf_tensor("l0f_zr", [128, 8], BF16))
    xin = [ctx.enter_context(nc.sbuf_tensor(f"l0f_xin{i}", [128, D], F32)) for i in range(2)]
    ln_bf = ctx.enter_context(nc.sbuf_tensor("l0f_ln", [128, D], BF16))
    LT = ctx.enter_context(nc.sbuf_tensor("l0f_LT", [128, 8 * 129], BF16))
    xrT = ctx.enter_context(nc.sbuf_tensor("l0f_xrT", [128, D], BF16))
    xkT = ctx.enter_context(nc.sbuf_tensor("l0f_xkT", [128, D], BF16))
    r_f = ctx.enter_context(nc.sbuf_tensor("l0f_r", [128, D], F32))
    relu_f = ctx.enter_context(nc.sbuf_tensor("l0f_relu", [128, D], F32))
    ka_bf = ctx.enter_context(nc.sbuf_tensor("l0f_ka", [128, D], BF16))
    kaT = ctx.enter_context(nc.sbuf_tensor("l0f_kaT", [128, D], BF16))
    yb = ctx.enter_context(nc.sbuf_tensor("l0f_yb", [128, D], F32))
    y1 = ctx.enter_context(nc.sbuf_tensor("l0f_y1", [128, D], F32))
    wst = ctx.enter_context(nc.sbuf_tensor("l0f_wst", [128, 8], F32))
    tmpf = ctx.enter_context(nc.sbuf_tensor("l0f_tmpf", [128, D], F32))
    x2o = [ctx.enter_context(nc.sbuf_tensor(f"l0f_x2o{i}", [128, D], F32)) for i in range(2)]

    pA = ctx.enter_context(nc.psum_tensor("l0f_pA", [128, D], F32))
    pB = ctx.enter_context(nc.psum_tensor("l0f_pB", [128, D], F32))
    pt = ctx.enter_context(nc.psum_tensor("l0f_pt", [128, D], BF16))

    for w_s, name in ((wr, "l0f_wrt"), (wk2, "l0f_wkt"), (wv2, "l0f_wvt")):
        src = c[name].rearrange("(c p) n -> p c n", p=128)
        dst = w_s[:, :].rearrange("p (c n) -> p c n", c=8)
        pb.dma("gpsimd", dst, src, "wl")
    pb.dma("gpsimd", mur[:, :], c["l0f_mur"], "wl")
    pb.dma("gpsimd", muk[:, :], c["l0f_muk"], "wl")
    pb.dma("gpsimd", zr[:, :], c["zeros8"], "wl")
    wl_all = pb.now("wl")
    pb.wait("vector", wl_all)
    pb.do("vector", lambda e: e.tensor_scalar_sub(omr[:, :], mur[:, :], 1.0))
    pb.do("vector", lambda e: e.tensor_scalar_mul(omr[:, :], omr[:, :], -1.0))
    pb.do("vector", lambda e: e.tensor_scalar_sub(omk[:, :], muk[:, :], 1.0))
    dv_om = pb.do("vector", lambda e: e.tensor_scalar_mul(omk[:, :], omk[:, :], -1.0), inc="dv")

    LT3 = LT[:, :].rearrange("p (c n) -> p c n", c=8)
    ld_x = {}
    for t in range(min(2, NT)):
        ld_x[t] = pb.dma("sync", xin[t % 2][:, :], x_in[128 * t : 128 * t + 128, :], f"lx{t % 2}")
    st_o = {}
    pe_v = pe_r = pe_k = None
    ac_rl = None
    dv_fin = {}
    for t in range(NT):
        bt = t % 2
        pb.wait("vector", ld_x[t])
        pb.wait("scalar", ld_x[t])
        apply_v = emit_ln(pb, c, xin[bt][:, :], ln_bf[:, :], tmpf[:, :], wst)
        if t == 0:
            dv_pc = pb.do("vector", lambda e: e.tensor_copy(LT3[:, :, 0], zr[:, :]), inc="dv")
        else:
            dv_pc = pb.do("vector", lambda e: e.tensor_copy(LT3[:, :, 0], LT3[:, :, 128]), inc="dv")
        pb.wait("tensor", apply_v)
        pb.wait("tensor", dv_pc)
        if t >= 1:
            pb.wait("tensor", pe_k)
        pe_tr = emit_transpose8(pb, c, ln_bf, pt)
        pb.wait("vector", pe_tr)
        dv_LT = pb.do(
            "vector",
            lambda e: e.tensor_copy(
                LT3[:, :, 1:129], pt[:, :].rearrange("p (c n) -> p c n", c=8)
            ),
            inc="dv",
        )
        if t == 0:
            pb.wait("vector", dv_om)
        for cc in range(8):
            pb.do(
                "vector",
                lambda e, cc=cc: e.tensor_scalar(
                    xrT[:, 128 * cc : 128 * cc + 128], LT3[:, cc, 1:129],
                    mur[:, cc : cc + 1], None, op0=ALU.mult,
                ),
            )
            pb.do(
                "vector",
                lambda e, cc=cc: e.scalar_tensor_tensor(
                    out=xrT[:, 128 * cc : 128 * cc + 128], in0=LT3[:, cc, 0:128],
                    scalar=omr[:, cc : cc + 1], in1=xrT[:, 128 * cc : 128 * cc + 128],
                    op0=ALU.mult, op1=ALU.add,
                ),
            )
            pb.do(
                "vector",
                lambda e, cc=cc: e.tensor_scalar(
                    xkT[:, 128 * cc : 128 * cc + 128], LT3[:, cc, 1:129],
                    muk[:, cc : cc + 1], None, op0=ALU.mult,
                ),
            )
            pb.do(
                "vector",
                lambda e, cc=cc: e.scalar_tensor_tensor(
                    out=xkT[:, 128 * cc : 128 * cc + 128], in0=LT3[:, cc, 0:128],
                    scalar=omk[:, cc : cc + 1], in1=xkT[:, 128 * cc : 128 * cc + 128],
                    op0=ALU.mult, op1=ALU.add,
                ),
            )
        dv_mix = pb.do("vector", lambda e: e.tensor_copy(wst[:, 0:1], omr[:, 0:1]), inc="dv")
        if DEV_DUMP and t == 0:
            pb.wait("gpsimd", dv_mix)
            pb.dma("gpsimd", c["scr_dbg"][0:64, 0:1024], xrT[0:64, :], "gdbg1")
            pb.dma("gpsimd", c["scr_dbg"][0:64, 1024:2048], xkT[0:64, :], "gdbg2")
            pb.dma("gpsimd", c["scr_dbg"][0:64, 2048:3072], LT[0:64, 0:1024], "gdbg3")
        pb.wait("tensor", dv_mix)
        if t == 0:
            pb.wait("tensor", wl_all)
        if pe_v is not None:
            pb.wait("tensor", pb.now("dv"))
        pe_r = emit_proj(pb, c, xrT, wr, None, pA)
        if ac_rl is not None:
            pb.wait("tensor", ac_rl)
        pe_k = emit_proj(pb, c, xkT, wk2, None, pB)
        pb.wait("scalar", pe_r)
        ac_sg = pb.do(
            "scalar",
            lambda e: e.activation(out=r_f[:, :], in_=pA[:, :], func=AF.Sigmoid),
            inc="ac",
        )
        pb.wait("scalar", pe_k)
        ac_rl = pb.do(
            "scalar",
            lambda e: e.activation(out=relu_f[:, :], in_=pB[:, :], func=AF.Relu),
            inc="ac",
        )
        pb.wait("vector", ac_rl)
        dv_ka = pb.do("vector", lambda e: e.tensor_mul(ka_bf[:, :], relu_f[:, :], relu_f[:, :]), inc="dv")
        if DEV_DUMP and t == 0:
            pb.wait("sync", ac_sg)
            pb.dma("sync", c["scr_dbg"][0:64, 3072:4096], r_f[0:64, :], "dbg4")
            pb.wait("sync", dv_ka)
            pb.dma("sync", c["scr_dbg"][0:64, 4096:5120], relu_f[0:64, :], "dbg5")
        pb.wait("tensor", dv_ka)
        pe_tr2 = emit_transpose8(pb, c, ka_bf, pt)
        pb.wait("vector", pe_tr2)
        dv_kaT = pb.do("vector", lambda e: e.tensor_copy(kaT[:, :], pt[:, :]), inc="dv")
        pb.wait("tensor", dv_kaT)
        pb.wait("tensor", ac_sg)
        pe_v = emit_proj(pb, c, kaT, wv2, None, pA)
        pb.wait("vector", pe_v)
        dv_y = pb.do("vector", lambda e: e.tensor_mul(yb[:, :], r_f[:, :], pA[:, :]), inc="dv")
        pb.wait("scalar", dv_y)
        emit_ln(pb, c, yb[:, :], y1[:, :], tmpf[:, :], wst)
        if t >= 2:
            pb.wait("vector", st_o[t - 2])
        dv_fin[t] = pb.do(
            "vector", lambda e, bt=bt: e.tensor_add(x2o[bt][:, :], xin[bt][:, :], y1[:, :]), inc="dv"
        )
        pb.wait("sync", dv_fin[t])
        st_o[t] = pb.dma("sync", scr_out[128 * t : 128 * t + 128, :], x2o[bt][:, :], f"so{bt}")
        if t + 2 < NT:
            ld_x[t + 2] = pb.dma("sync", xin[bt][:, :], x_in[128 * (t + 2) : 128 * (t + 2) + 128, :], f"lx{bt}")

    pb.barrier()



def emit_l1_attn(pb, c, ctx, T, x_in, scr_q, scr_out):
    nc = pb.nc
    NT = T // 128
    NKB = T // 128

    xin = [ctx.enter_context(nc.sbuf_tensor(f"l1_xin{i}", [128, D], F32)) for i in range(2)]
    ln_bf = ctx.enter_context(nc.sbuf_tensor("l1_ln", [128, D], BF16))
    lnT = ctx.enter_context(nc.sbuf_tensor("l1_lnT", [128, D], BF16))
    wst = ctx.enter_context(nc.sbuf_tensor("l1_wst", [128, 8], F32))
    kT = ctx.enter_context(nc.sbuf_tensor("l1_kT", [128, 8 * T], BF16))
    V = ctx.enter_context(nc.sbuf_tensor("l1_V", [128, T * 8], BF16))
    q_bf = [ctx.enter_context(nc.sbuf_tensor(f"l1_q{i}", [128, D], BF16)) for i in range(2)]
    probs = ctx.enter_context(nc.sbuf_tensor("l1_probs", [128, T], BF16))
    pTs = ctx.enter_context(nc.sbuf_tensor("l1_pTs", [128, 1024], BF16))
    rs = ctx.enter_context(nc.sbuf_tensor("l1_rs", [128, 4], F32))
    ctxb = ctx.enter_context(nc.sbuf_tensor("l1_ctx", [128, D], BF16))
    y1 = ctx.enter_context(nc.sbuf_tensor("l1_y1", [128, D], F32))
    x3o = [ctx.enter_context(nc.sbuf_tensor(f"l1_x3o{i}", [128, D], F32)) for i in range(2)]

    kT3 = kT[:, :].rearrange("p (g n) -> p g n", g=8)
    V3 = V[:, :].rearrange("p (t n) -> p t n", t=NT)

    ph1 = ExitStack()
    pt = ph1.enter_context(nc.psum_tensor("l1_pt", [128, D], BF16))
    pA = ph1.enter_context(nc.psum_tensor("l1_pA", [128, D], F32))
    pB = ph1.enter_context(nc.psum_tensor("l1_pB", [128, D], F32))
    wq = ph1.enter_context(nc.sbuf_tensor("l1_wqs", [128, 8 * 1024], BF16))
    wk = ph1.enter_context(nc.sbuf_tensor("l1_wks", [128, 8 * 1024], BF16))
    bq = ph1.enter_context(nc.sbuf_tensor("l1_bqs", [1, D], BF16))
    bk = ph1.enter_context(nc.sbuf_tensor("l1_bks", [1, D], BF16))

    for w_s, name in ((wq, "l1_wq"), (wk, "l1_wk")):
        src = c[name].rearrange("(c p) n -> p c n", p=128)
        dst = w_s[:, :].rearrange("p (c n) -> p c n", c=8)
        pb.dma("gpsimd", dst, src, "wl")
    pb.dma("gpsimd", bq[:, :], c["l1_bq"], "wl")
    pb.dma("gpsimd", bk[:, :], c["l1_bk"], "wl")
    wl_all = pb.now("wl")

    ld_x = {}
    for t in range(min(2, NT)):
        ld_x[t] = pb.dma("sync", xin[t % 2][:, :], x_in[128 * t : 128 * t + 128, :], f"lx{t % 2}")
    st_q = {}
    ac_q = None
    dv_kT = None
    for t in range(NT):
        bt = t % 2
        pb.wait("vector", ld_x[t])
        pb.wait("scalar", ld_x[t])
        apply_v = emit_ln(pb, c, xin[bt][:, :], ln_bf[:, :], probs[:, 0:1024], wst)
        if t + 2 < NT:
            pb.wait("sync", apply_v)
            ld_x[t + 2] = pb.dma("sync", xin[bt][:, :], x_in[128 * (t + 2) : 128 * (t + 2) + 128, :], f"lx{bt}")
        pb.wait("tensor", apply_v)
        if dv_kT is not None:
            pb.wait("tensor", dv_kT)
        pe_tr = emit_transpose8(pb, c, ln_bf, pt)
        pb.wait("vector", pe_tr)
        dv_lnT = pb.do("vector", lambda e: e.tensor_copy(lnT[:, :], pt[:, :]), inc="dv")
        pb.wait("tensor", dv_lnT)
        if t == 0:
            pb.wait("tensor", wl_all)
        if ac_q is not None:
            pb.wait("tensor", ac_q)
        pe_q = emit_proj(pb, c, lnT, wq, bq, pA)
        pe_k = emit_proj(pb, c, lnT, wk, bk, pB)
        pb.wait("scalar", pe_q)
        if t >= 2:
            pb.wait("scalar", st_q[t - 2])
        ac_q = pb.do(
            "scalar",
            lambda e, bt=bt: e.activation(out=q_bf[bt][:, :], in_=pA[:, :], func=AF.Identity),
            inc="ac",
        )
        pb.wait("sync", ac_q)
        st_q[t] = pb.dma("sync", scr_q[128 * t : 128 * t + 128, :], q_bf[bt][:, :], f"sq{bt}")
        pb.wait("vector", pe_k)
        dv_kb = pb.do("vector", lambda e: e.tensor_copy(ln_bf[:, :], pB[:, :]), inc="dv")
        pb.wait("tensor", dv_kb)
        pe_tr2 = emit_transpose8(pb, c, ln_bf, pt)
        pb.wait("vector", pe_tr2)
        dv_kT = pb.do(
            "vector",
            lambda e, t=t: e.tensor_copy(
                kT3[:, :, 128 * t : 128 * t + 128],
                pt[:, :].rearrange("p (c n) -> p c n", c=8),
            ),
            inc="dv",
        )
    pb.barrier()

    wv = wq
    bv = bq
    src = c["l1_wv"].rearrange("(c p) n -> p c n", p=128)
    pb.dma("gpsimd", wv[:, :].rearrange("p (c n) -> p c n", c=8), src, "wl")
    pb.dma("gpsimd", bv[:, :], c["l1_bv"], "wl")
    wl_all = pb.now("wl")
    for t in range(min(2, NT)):
        ld_x[t] = pb.dma("sync", xin[t % 2][:, :], x_in[128 * t : 128 * t + 128, :], f"lx{t % 2}")
    dv_V = None
    for t in range(NT):
        bt = t % 2
        pb.wait("vector", ld_x[t])
        pb.wait("scalar", ld_x[t])
        apply_v = emit_ln(pb, c, xin[bt][:, :], ln_bf[:, :], probs[:, 0:1024], wst)
        if t + 2 < NT:
            pb.wait("sync", apply_v)
            ld_x[t + 2] = pb.dma("sync", xin[bt][:, :], x_in[128 * (t + 2) : 128 * (t + 2) + 128, :], f"lx{bt}")
        pb.wait("tensor", apply_v)
        if dv_V is not None:
            pb.wait("tensor", dv_V)
        pe_tr = emit_transpose8(pb, c, ln_bf, pt)
        pb.wait("vector", pe_tr)
        dv_lnT = pb.do("vector", lambda e: e.tensor_copy(lnT[:, :], pt[:, :]), inc="dv")
        pb.wait("tensor", dv_lnT)
        if t == 0:
            pb.wait("tensor", wl_all)
        pe_v = emit_proj(pb, c, lnT, wv, bv, pA)
        pb.wait("vector", pe_v)
        dv_V = pb.do(
            "vector",
            lambda e, t=t: e.tensor_copy(V3[:, t, :], pA[:, :]),
            inc="dv",
        )
    pb.barrier()

    pb.flush(pb.block)
    ph1.close()
    QS = 256
    NQS = T // QS
    KHALF = T // 2
    pST = ctx.enter_context(nc.psum_tensor("l1_pST", [128, 2048], F32))
    pCtx = ctx.enter_context(nc.psum_tensor("l1_pCtx", [128, 2048], F32))
    qTs = ctx.enter_context(nc.sbuf_tensor("l1_qTs", [128, 8 * QS], BF16))
    Ebuf = [ctx.enter_context(nc.sbuf_tensor(f"l1_Ebuf{i}", [128, 8 * QS], BF16)) for i in range(2)]
    Pbuf = [ctx.enter_context(nc.sbuf_tensor(f"l1_Pbuf{i}", [128, 8 * QS], BF16)) for i in range(2)]
    cTs = ctx.enter_context(nc.sbuf_tensor("l1_cTs", [128, 8 * QS], BF16))
    Dred = [ctx.enter_context(nc.sbuf_tensor(f"l1_D{i}", [128, QS], F32)) for i in range(2)]
    Dinv = [ctx.enter_context(nc.sbuf_tensor(f"l1_Dinv{i}", [128, QS], BF16)) for i in range(2)]

    ld_q = {}
    ld_x = {}
    for t in range(min(2, 2 * NQS)):
        ld_q[t] = pb.dma("sync", q_bf[t % 2][:, :], scr_q[128 * t : 128 * t + 128, :], f"lq{t % 2}")
        ld_x[t] = pb.dma("sync", xin[t % 2][:, :], x_in[128 * t : 128 * t + 128, :], f"lx{t % 2}")
    st_o = {}
    dv_fin = {}
    ac_exp = None
    dv_ct2 = None
    pvk_prev = None
    dv_qT = None
    pvk_kb = {}
    dv_P_ms = {}

    def emit_pv(kb):
        kp = kb % 2
        pb.wait("tensor", dv_P_ms[kb])
        if kb == 0 and dv_ct2 is not None:
            pb.wait("tensor", dv_ct2)

        def pvfn(e, kb=kb, kp=kp):
            ins = None
            for g in range(8):
                ins = e.matmul(
                    pCtx[:, 256 * g : 256 * g + 256],
                    V3[:, kb, 128 * g : 128 * g + 128],
                    Pbuf[kp][:, 256 * g : 256 * g + 256],
                    start=(kb == 0 and g % 2 == 0),
                    stop=(kb == NKB - 1 and g % 2 == 1),
                )
            return ins

        pvk_kb[kb] = pb.do("tensor", pvfn, inc="pe")

    for qs in range(NQS):
        ms_tq = []
        for sub in range(2):
            t = 2 * qs + sub
            bt = t % 2
            pb.wait("sync", ld_q[t])
            if pvk_prev is not None:
                pb.wait("sync", pvk_prev)
            pb.dma_sem_eng.setdefault(f"tq{sub}", "sync")
            for g in range(8):
                dv_qT = pb.do(
                    "sync",
                    lambda e, g=g, sub=sub, bt=bt: e.dma_start_transpose(
                        qTs[:, 256 * g + 128 * sub : 256 * g + 128 * sub + 128],
                        q_bf[bt][:, 128 * g : 128 * g + 128],
                    ),
                    inc=f"tq{sub}", n=16,
                )
            ms_tq.append(dv_qT)
        dv_P_ms.clear()
        ac_exp_ms = {}
        pend_kb = None
        for kb in range(NKB):
            kp = kb % 2
            if kb == 0:
                for ms in ms_tq:
                    pb.wait("tensor", ms)
                pb.wait("tensor", pb.now("ac"))
            if kb >= 1:
                pb.wait("tensor", ac_exp_ms[kb - 1])

            def stfn(e, kb=kb):
                ins = None
                for g in range(8):
                    ins = e.matmul(
                        pST[:, 256 * g : 256 * g + 256],
                        kT3[:, g, 128 * kb : 128 * kb + 128],
                        qTs[:, 256 * g : 256 * g + 256],
                        start=True, stop=True,
                    )
                return ins

            pe_st = pb.do("tensor", stfn, inc="pe")
            pb.wait("scalar", pe_st)
            if kb >= 2:
                pb.wait("scalar", dv_P_ms[kb - 2])
            else:
                pb.wait("scalar", pb.now("dv"))
            ac_exp_ms[kb] = pb.do(
                "scalar",
                lambda e, kp=kp: e.activation(
                    out=Ebuf[kp][:, :], in_=pST[:, :], func=AF.Exp,
                    bias=c["c30"][:, 0:1], scale=1.0 / 32.0,
                ),
                inc="ac",
            )
            pb.wait("vector", ac_exp_ms[kb])
            pb.do(
                "vector",
                lambda e, kp=kp: e.reduce_sum(
                    Dred[kp][:, :],
                    Ebuf[kp][:, :].rearrange("p (g q) -> p q g", g=8),
                    axis=AX.X,
                ),
            )

            def _recf(e, kp=kp):
                with nc.allow_low_precision(reason="Dinv feeds a bf16 matmul operand"):
                    return e.reciprocal(Dinv[kp][:, :], Dred[kp][:, :])

            dvr = pb.do("vector", _recf, inc="dv")
            pb.wait("vector", dvr)
            if kb >= 2:
                pb.wait("vector", pvk_kb[kb - 2])
            elif pvk_prev is not None:
                pb.wait("vector", pvk_prev)
            dv_P_ms[kb] = pb.do(
                "vector",
                lambda e, kp=kp: e.tensor_tensor(
                    out=Pbuf[kp][:, :].rearrange("p (g q) -> p g q", g=8),
                    in0=Ebuf[kp][:, :].rearrange("p (g q) -> p g q", g=8),
                    in1=Dinv[kp][:, :].unsqueeze(1).to_broadcast([128, 8, QS]),
                    op=ALU.mult,
                ),
                inc="dv",
            )
            if pend_kb is not None:
                emit_pv(pend_kb)
            pend_kb = kb
        emit_pv(pend_kb)
        pvk_prev = pvk_kb[NKB - 1]
        pb.wait("vector", pvk_prev)
        dv_ce = pb.do("vector", lambda e: e.tensor_copy(cTs[:, :], pCtx[:, :]), inc="dv")
        dv_ct2 = None
        for sub in range(2):
            t = 2 * qs + sub
            bt = t % 2
            pb.wait("sync", dv_ce)
            if t >= 1 and dv_fin.get(t - 1) is not None:
                pb.wait("sync", dv_fin[t - 1])
            pb.dma_sem_eng.setdefault(f"tc{sub}", "sync")
            for g in range(8):
                bt_ms = pb.do(
                    "sync",
                    lambda e, g=g, sub=sub: e.dma_start_transpose(
                        ctxb[:, 128 * g : 128 * g + 128],
                        cTs[:, 256 * g + 128 * sub : 256 * g + 128 * sub + 128],
                    ),
                    inc=f"tc{sub}", n=16,
                )
            dv_ct2 = bt_ms
            pb.wait("scalar", bt_ms)
            pb.wait("vector", bt_ms)
            emit_ln(pb, c, ctxb[:, :], y1[:, :], probs[:, 0:1024], wst)
            pb.wait("vector", ld_x[t])
            if t >= 2:
                pb.wait("vector", st_o[t - 2])
            dv_fin[t] = pb.do(
                "vector", lambda e, bt=bt: e.tensor_add(x3o[bt][:, :], xin[bt][:, :], y1[:, :]), inc="dv"
            )
            pb.wait("sync", dv_fin[t])
            st_o[t] = pb.dma("sync", scr_out[128 * t : 128 * t + 128, :], x3o[bt][:, :], f"so{bt}")
            if t + 2 < 2 * NQS:
                ld_q[t + 2] = pb.dma("sync", q_bf[bt][:, :], scr_q[128 * (t + 2) : 128 * (t + 2) + 128, :], f"lq{bt}")
                ld_x[t + 2] = pb.dma("sync", xin[bt][:, :], x_in[128 * (t + 2) : 128 * (t + 2) + 128, :], f"lx{bt}")

    pb.barrier()



def emit_l1_moe(pb, c, ctx, T, x_in, out_t):
    nc = pb.nc
    NT = T // 128
    NH = 2
    TH = T // NH
    NTH = TH // 128
    NE_ALL = 33

    xin = [ctx.enter_context(nc.sbuf_tensor(f"moe_xin{i}", [128, D], F32)) for i in range(2)]
    ln_bf = ctx.enter_context(nc.sbuf_tensor("moe_ln", [128, D], BF16))
    wst = ctx.enter_context(nc.sbuf_tensor("moe_wst", [128, 8], F32))
    tmpf = ctx.enter_context(nc.sbuf_tensor("moe_tmpf", [128, D], F32))
    gwT = ctx.enter_context(nc.sbuf_tensor("moe_gwTs", [128, 8 * 32], BF16))
    flatT = ctx.enter_context(nc.sbuf_tensor("moe_flatT", [128, 8 * TH], BF16))
    comb = ctx.enter_context(nc.sbuf_tensor("moe_comb", [128, 32 * NTH], F32))
    scor = ctx.enter_context(nc.sbuf_tensor("moe_scor", [128, 32], F32))
    gs8 = ctx.enter_context(nc.sbuf_tensor("moe_gs8", [128, 8], F32))
    top8 = ctx.enter_context(nc.sbuf_tensor("moe_top8", [128, 8], F32))
    msk = ctx.enter_context(nc.sbuf_tensor("moe_msk", [128, 32], F32))
    rsum = ctx.enter_context(nc.sbuf_tensor("moe_rsum", [128, 2], F32))
    acc = ctx.enter_context(nc.sbuf_tensor("moe_acc", [128, 1024 * NTH], F32))
    wg = [ctx.enter_context(nc.sbuf_tensor(f"moe_wgs{i}", [128, 8 * 512], BF16)) for i in range(2)]
    wu = [ctx.enter_context(nc.sbuf_tensor(f"moe_wus{i}", [128, 8 * 512], BF16)) for i in range(2)]
    wd = [ctx.enter_context(nc.sbuf_tensor(f"moe_wds{i}", [128, 4 * 1024], BF16)) for i in range(2)]
    hs = [ctx.enter_context(nc.sbuf_tensor(f"moe_hs{i}", [128, 512], F32)) for i in range(2)]
    h_bf = [ctx.enter_context(nc.sbuf_tensor(f"moe_h{i}", [128, 512], BF16)) for i in range(2)]
    hT = [ctx.enter_context(nc.sbuf_tensor(f"moe_hT{i}", [128, 512], BF16)) for i in range(2)]
    y1 = ctx.enter_context(nc.sbuf_tensor("moe_y1", [128, D], F32))
    oo = [ctx.enter_context(nc.sbuf_tensor(f"moe_oo{i}", [128, D], F32)) for i in range(2)]

    pA = [ctx.enter_context(nc.psum_tensor(f"moe_pA{i}", [128, 512], F32)) for i in range(2)]
    pB = [ctx.enter_context(nc.psum_tensor(f"moe_pB{i}", [128, 512], F32)) for i in range(2)]
    pt = ctx.enter_context(nc.psum_tensor("moe_pt", [128, 1024], BF16))
    pC = ctx.enter_context(nc.psum_tensor("moe_pC", [128, D], F32))
    pR = ctx.enter_context(nc.psum_tensor("moe_pR", [128, 32], F32))

    flatT3 = flatT[:, :].rearrange("p (c n) -> p c n", c=8)

    pb.dma("gpsimd", gwT[:, :].rearrange("p (c n) -> p c n", c=8),
           c["moe_gwt"].rearrange("(c p) n -> p c n", p=128), "wl")
    wl_gw = pb.now("wl")

    def load_expert(eidx, slot):
        if eidx < 32:
            wg_src = c["moe_wg"][eidx].rearrange("(c p) n -> p c n", p=128)
            wu_src = c["moe_wu"][eidx].rearrange("(c p) n -> p c n", p=128)
            wd_src = c["moe_wd"][eidx].rearrange("(c p) n -> p c n", p=128)
        else:
            wg_src = c["moe_sg"].rearrange("(c p) n -> p c n", p=128)
            wu_src = c["moe_su"].rearrange("(c p) n -> p c n", p=128)
            wd_src = c["moe_sd"].rearrange("(c p) n -> p c n", p=128)
        pb.dma("gpsimd", wg[slot][:, :].rearrange("p (c n) -> p c n", c=8), wg_src, f"we{slot}")
        pb.dma("gpsimd", wu[slot][:, :].rearrange("p (c n) -> p c n", c=8), wu_src, f"we{slot}")
        return pb.dma("gpsimd", wd[slot][:, :].rearrange("p (c n) -> p c n", c=4), wd_src, f"we{slot}")

    ld_x = {}
    st_o = {}
    for h in range(NH):
        t0 = h * NTH
        for t in range(min(2, NTH)):
            ld_x[t] = pb.dma("sync", xin[t % 2][:, :], x_in[128 * (t0 + t) : 128 * (t0 + t) + 128, :], f"lx{t % 2}")
        dv_fT = None
        for t in range(NTH):
            bt = t % 2
            pb.wait("vector", ld_x[t])
            pb.wait("scalar", ld_x[t])
            apply_v = emit_ln(pb, c, xin[bt][:, :], ln_bf[:, :], tmpf[:, :], wst)
            if t + 2 < NTH:
                pb.wait("sync", apply_v)
                ld_x[t + 2] = pb.dma("sync", xin[bt][:, :], x_in[128 * (t0 + t + 2) : 128 * (t0 + t + 2) + 128, :], f"lx{bt}")
            pb.wait("tensor", apply_v)
            if dv_fT is not None:
                pb.wait("tensor", dv_fT)
            pe_tr = emit_transpose8(pb, c, ln_bf, pt)
            pb.wait("vector", pe_tr)
            dv_fT = pb.do(
                "vector",
                lambda e, t=t: e.tensor_copy(
                    flatT3[:, :, 128 * t : 128 * t + 128],
                    pt[:, :].rearrange("p (c n) -> p c n", c=8),
                ),
                inc="dv",
            )
            pb.wait("tensor", dv_fT)
            pb.wait("tensor", pb.now("ac"))
            if t == 0 and h == 0:
                pb.wait("tensor", wl_gw)

            def lfn(e, t=t):
                ins = None
                for cc in range(8):
                    ins = e.matmul(
                        pR[:, 0:32],
                        flatT3[:, cc, 128 * t : 128 * t + 128],
                        gwT[:, 32 * cc : 32 * cc + 32],
                        start=(cc == 0), stop=(cc == 7),
                    )
                return ins

            pe_lg = pb.do("tensor", lfn, inc="pe")
            pb.wait("scalar", pe_lg)
            ac_sc = pb.do(
                "scalar",
                lambda e: e.activation(out=scor[:, :], in_=pR[:, 0:32], func=AF.Sigmoid),
                inc="ac",
            )
            pb.wait("vector", ac_sc)
            r1 = pb.do(
                "vector",
                lambda e: e.reduce_sum(
                    gs8[:, :], scor[:, :].rearrange("p (g i) -> p g i", g=8), axis=AX.X
                ),
                inc="dv",
            )
            pb.wait("vector", r1)
            r2 = pb.do("vector", lambda e: e.max(out=top8[:, :], in_=gs8[:, :]), inc="dv")
            pb.wait("vector", r2)
            r3 = pb.do(
                "vector",
                lambda e: e.tensor_scalar(msk[:, 0:8], gs8[:, :], top8[:, 3:4], None, op0=ALU.is_ge),
                inc="dv",
            )
            pb.wait("vector", r3)
            r4 = pb.do(
                "vector",
                lambda e: e.tensor_tensor(
                    out=msk[:, :].rearrange("p (g i) -> p g i", g=8),
                    in0=scor[:, :].rearrange("p (g i) -> p g i", g=8),
                    in1=msk[:, 0:8].unsqueeze(2).to_broadcast([128, 8, 4]),
                    op=ALU.mult,
                ),
                inc="dv",
            )
            pb.wait("vector", r4)
            r5 = pb.do("vector", lambda e: e.max(out=top8[:, :], in_=msk[:, :]), inc="dv")
            pb.wait("vector", r5)
            r6 = pb.do(
                "vector",
                lambda e, t=t: e.tensor_scalar(
                    comb[:, 32 * t : 32 * t + 32], msk[:, :], top8[:, 7:8], None, op0=ALU.is_ge
                ),
                inc="dv",
            )
            pb.wait("vector", r6)
            r7 = pb.do(
                "vector",
                lambda e, t=t: e.tensor_mul(
                    comb[:, 32 * t : 32 * t + 32], comb[:, 32 * t : 32 * t + 32], msk[:, :]
                ),
                inc="dv",
            )
            pb.wait("vector", r7)
            r8 = pb.do(
                "vector",
                lambda e, t=t: e.reduce_sum(rsum[:, 0:1], comb[:, 32 * t : 32 * t + 32], axis=AX.X),
                inc="dv",
            )
            pb.wait("vector", r8)
            r9 = pb.do("vector", lambda e: e.tensor_scalar_max(rsum[:, 0:1], rsum[:, 0:1], 1e-12), inc="dv")
            pb.wait("vector", r9)
            r10 = pb.do("vector", lambda e: e.reciprocal(rsum[:, 1:2], rsum[:, 0:1]), inc="dv")
            pb.wait("vector", r10)
            pb.do(
                "vector",
                lambda e, t=t: e.tensor_scalar(
                    comb[:, 32 * t : 32 * t + 32], comb[:, 32 * t : 32 * t + 32],
                    rsum[:, 1:2], None, op0=ALU.mult,
                ),
                inc="dv",
            )
        pb.barrier()

        ld_w = {}
        ld_w[0] = load_expert(32, 0)
        ld_w[1] = load_expert(0, 1)
        seq = [(ei, t) for ei in range(NE_ALL) for t in range(NTH)]
        ac_si_ms, dv_h_ms, dv_hT_ms, dv_acc_ms, pe_dn_ms = {}, {}, {}, {}, {}
        pend = None

        def emit_tail(idx, ei, t, eidx, s2, slot):
            pb.wait("tensor", dv_h_ms[idx])
            if idx >= 1:
                pb.wait("tensor", dv_hT_ms[idx - 1])

            def tfn(e, s2=s2):
                ins = None
                for j in range(4):
                    ins = e.transpose(
                        pt[:, 128 * j : 128 * j + 128],
                        h_bf[s2][:, 128 * j : 128 * j + 128],
                        c["ident_bf"][:, :],
                    )
                return ins

            pe_th = pb.do("tensor", tfn, inc="pe")
            pb.wait("vector", pe_th)
            dv_hT_ms[idx] = pb.do(
                "vector", lambda e, s2=s2: e.tensor_copy(hT[s2][:, :], pt[:, 0:512]), inc="dv"
            )
            pb.wait("tensor", dv_hT_ms[idx])
            if idx >= 1:
                pb.wait("tensor", dv_acc_ms[idx - 1])

            def dfn(e, s2=s2, slot=slot):
                ins = None
                for j in range(4):
                    for half in range(2):
                        ins = e.matmul(
                            pC[:, 512 * half : 512 * half + 512],
                            hT[s2][:, 128 * j : 128 * j + 128],
                            wd[slot][:, 1024 * j + 512 * half : 1024 * j + 512 * half + 512],
                            start=(j == 0), stop=(j == 3),
                        )
                return ins

            pe_dn_ms[idx] = pb.do("tensor", dfn, inc="pe")
            pb.wait("vector", pe_dn_ms[idx])
            if ei == 0:
                dv_acc_ms[idx] = pb.do(
                    "vector",
                    lambda e, t=t: e.tensor_copy(acc[:, 1024 * t : 1024 * t + 1024], pC[:, :]),
                    inc="dv",
                )
            else:
                dv_acc_ms[idx] = pb.do(
                    "vector",
                    lambda e, t=t, eidx=eidx: e.scalar_tensor_tensor(
                        out=acc[:, 1024 * t : 1024 * t + 1024], in0=pC[:, :],
                        scalar=comb[:, 32 * t + eidx : 32 * t + eidx + 1],
                        in1=acc[:, 1024 * t : 1024 * t + 1024],
                        op0=ALU.mult, op1=ALU.add,
                    ),
                    inc="dv",
                )
            if t == NTH - 1 and ei + 2 < NE_ALL:
                pb.wait("gpsimd", pe_dn_ms[idx])
                ld_w[ei + 2] = load_expert(ei + 1, slot)

        for idx, (ei, t) in enumerate(seq):
            slot = ei % 2
            s2 = idx % 2
            eidx = 32 if ei == 0 else ei - 1
            if t == 0:
                pb.wait("tensor", ld_w[ei])
            if idx >= 2:
                pb.wait("tensor", ac_si_ms[idx - 2])
                pb.wait("tensor", dv_h_ms[idx - 2])

            def gufn(e, t=t, slot=slot, s2=s2):
                ins = None
                for cc in range(8):
                    e.matmul(
                        pA[s2][:, 0:512],
                        flatT3[:, cc, 128 * t : 128 * t + 128],
                        wg[slot][:, 512 * cc : 512 * cc + 512],
                        start=(cc == 0), stop=(cc == 7),
                    )
                    ins = e.matmul(
                        pB[s2][:, 0:512],
                        flatT3[:, cc, 128 * t : 128 * t + 128],
                        wu[slot][:, 512 * cc : 512 * cc + 512],
                        start=(cc == 0), stop=(cc == 7),
                    )
                return ins

            pe_g = pe_u = pb.do("tensor", gufn, inc="pe")
            pb.wait("scalar", pe_g)
            if idx >= 2:
                pb.wait("scalar", dv_h_ms[idx - 2])
            ac_si_ms[idx] = pb.do(
                "scalar",
                lambda e, s2=s2: e.activation(out=hs[s2][:, :], in_=pA[s2][:, :], func=AF.Silu),
                inc="ac",
            )
            pb.wait("vector", ac_si_ms[idx])
            pb.wait("vector", pe_u)
            dv_h_ms[idx] = pb.do(
                "vector",
                lambda e, s2=s2: e.tensor_mul(h_bf[s2][:, :], hs[s2][:, :], pB[s2][:, :]),
                inc="dv",
            )
            if pend is not None:
                emit_tail(*pend)
            pend = (idx, ei, t, eidx, s2, slot)
        emit_tail(*pend)
        pend = None
        pb.barrier()

        for t in range(min(2, NTH)):
            ld_x[t] = pb.dma("sync", xin[t % 2][:, :], x_in[128 * (t0 + t) : 128 * (t0 + t) + 128, :], f"lx{t % 2}")
        for t in range(NTH):
            bt = t % 2
            pb.wait("vector", ld_x[t])
            pb.wait("scalar", ld_x[t])
            emit_ln(pb, c, acc[:, 1024 * t : 1024 * t + 1024], y1[:, :], tmpf[:, :], wst)
            if t >= 2:
                pb.wait("vector", st_o[t - 2])
            dv_f = pb.do(
                "vector", lambda e, bt=bt: e.tensor_add(oo[bt][:, :], xin[bt][:, :], y1[:, :]), inc="dv"
            )
            pb.wait("sync", dv_f)
            st_o[t] = pb.dma("sync", out_t[128 * (t0 + t) : 128 * (t0 + t) + 128, :], oo[bt][:, :], f"so{bt}")
            if t + 2 < NTH:
                pb.wait("sync", dv_f)
                ld_x[t + 2] = pb.dma("sync", xin[bt][:, :], x_in[128 * (t0 + t + 2) : 128 * (t0 + t + 2) + 128, :], f"lx{bt}")
        pb.barrier()



def build_program(T=4096, dev=False, stages=("l0a", "l0f", "l1a", "moe"), n_sweeps=6, paranoid=True):
    nc = bass.Bass("TRN2", target_bir_lowering=False, detect_race_conditions=False)
    scratch_kind = "ExternalOutput" if dev else "Internal"

    c = {}
    c["x"] = nc.dram_tensor("x", [T, D], F32, kind="ExternalInput")[:, :]
    for nm, shp in [
        ("l0_wqt", [D, D]), ("l0_wkt", [D, D]), ("l0_wvt", [D, D]), ("l0_wot", [D, D]),
        ("l0_bq", [1, D]), ("l0_bk", [1, D]), ("l0_bv", [1, D]), ("l0_bo", [1, D]),
        ("l0f_wrt", [D, D]), ("l0f_wkt", [D, D]), ("l0f_wvt", [D, D]),
        ("l0f_mur", [128, 8]), ("l0f_muk", [128, 8]),
        ("l1_wq", [D, D]), ("l1_wk", [D, D]), ("l1_wv", [D, D]),
        ("l1_bq", [1, D]), ("l1_bk", [1, D]), ("l1_bv", [1, D]),
        ("moe_gwt", [D, 32]),
        ("moe_sg", [D, 512]), ("moe_su", [D, 512]), ("moe_sd", [512, D]),
    ]:
        c[nm] = nc.dram_tensor(nm, shp, F32, kind="ExternalInput")[:, :]
    c["moe_wg"] = nc.dram_tensor("moe_wg", [32, D, 512], F32, kind="ExternalInput")[:, :, :]
    c["moe_wu"] = nc.dram_tensor("moe_wu", [32, D, 512], F32, kind="ExternalInput")[:, :, :]
    c["moe_wd"] = nc.dram_tensor("moe_wd", [32, 512, D], F32, kind="ExternalInput")[:, :, :]
    identf = nc.dram_tensor("identf", [128, 128], F32, kind="ExternalInput")[:, :]
    identb = nc.dram_tensor("identb", [128, 128], BF16, kind="ExternalInput")[:, :]
    onesr = nc.dram_tensor("onesr", [1, 128], BF16, kind="ExternalInput")[:, :]
    onesc = nc.dram_tensor("onesc", [128, 1], BF16, kind="ExternalInput")[:, :]
    c["zeros8"] = nc.dram_tensor("zeros8", [128, 8], BF16, kind="ExternalInput")[:, :]

    scr_q = nc.dram_tensor("scr_q", [T, D], BF16, kind=scratch_kind)[:, :]
    scr_k = nc.dram_tensor("scr_k", [T, D], BF16, kind=scratch_kind)[:, :]
    scr_v = nc.dram_tensor("scr_v", [T, D], BF16, kind=scratch_kind)[:, :]
    scr_a = nc.dram_tensor("scr_a", [T, D], F32, kind="ExternalOutput" if dev else "Internal")[:, :]
    scr_b = nc.dram_tensor("scr_b", [T, D], F32, kind="ExternalOutput" if dev else "Internal")[:, :]
    out_t = nc.dram_tensor("out", [T, D], F32, kind="ExternalOutput")[:, :]
    scr_dbg = nc.dram_tensor("scr_dbg", [64, 8192], F32, kind="ExternalOutput" if dev else "Internal")[:, :]

    pb = PB(nc, paranoid=paranoid)
    with ExitStack() as ctx:
        block = ctx.enter_context(nc.Block())
        pb.block = block
        ident_f32 = nc.alloc_sbuf_tensor("g_identf", [128, 128], F32)
        ident_bf = nc.alloc_sbuf_tensor("g_identb", [128, 128], BF16)
        ones_row = nc.alloc_sbuf_tensor("g_onesr", [1, 128], BF16)
        ones_col = nc.alloc_sbuf_tensor("g_onesc", [128, 1], BF16)
        lneps = nc.alloc_sbuf_tensor("g_lneps", [128, 1], F32)
        c30 = nc.alloc_sbuf_tensor("g_c30", [128, 1], F32)
        pb.dma("sync", ident_f32[:, :], identf, "lc")
        pb.dma("sync", ident_bf[:, :], identb, "lc")
        pb.dma("sync", ones_row[:, :], onesr, "lc")
        ldc = pb.dma("sync", ones_col[:, :], onesc, "lc")
        pb.do("vector", lambda e: e.memset(lneps[:, 0:1], LNEPS))
        mz = pb.do("vector", lambda e: e.memset(c30[:, 0:1], -30.0), inc="dv")
        for eng in ("tensor", "vector", "scalar"):
            pb.wait(eng, ldc)
        pb.wait("scalar", mz)
        cc = dict(c)
        cc["lneps"] = lneps
        cc["c30"] = c30
        cc["ident_f32"] = ident_f32
        cc["ident_bf"] = ident_bf
        cc["ones_row"] = ones_row
        cc["ones_col"] = ones_col

        cc["scr_dbg"] = scr_dbg
        if "l0a" in stages:
            with ExitStack() as sctx:
                emit_l0_attn(pb, cc, sctx, T, c["x"], scr_q, scr_k, scr_v, scr_a, n_sweeps=n_sweeps)
                pb.flush(block)
        if "l0f" in stages:
            with ExitStack() as sctx:
                emit_l0_ffn(pb, cc, sctx, T, scr_a, scr_b)
                pb.flush(block)
        if "l1a" in stages:
            with ExitStack() as sctx:
                emit_l1_attn(pb, cc, sctx, T, scr_b, scr_q, scr_a)
                pb.flush(block)
        if "moe" in stages:
            with ExitStack() as sctx:
                emit_l1_moe(pb, cc, sctx, T, scr_a, out_t)
                pb.flush(block)

        pb.flush(block)
    return nc


def prep_inputs(**inp):
    tr = lambda a: np.ascontiguousarray(np.asarray(a, dtype=np.float32).T)
    f = lambda k: np.asarray(inp[k], dtype=np.float32)
    m = {
        "l0_wqt": tr(inp["L0_attn_Wq"]),
        "l0_wkt": tr(inp["L0_attn_Wk"]),
        "l0_wvt": tr(inp["L0_attn_Wv"]),
        "l0_wot": tr(inp["L0_attn_Wo"]),
        "l0_bq": f("L0_attn_bq").reshape(1, D),
        "l0_bk": f("L0_attn_bk").reshape(1, D),
        "l0_bv": f("L0_attn_bv").reshape(1, D),
        "l0_bo": f("L0_attn_bo").reshape(1, D),
        "l0f_wrt": tr(inp["L0_ffn_W_r"]),
        "l0f_wkt": tr(inp["L0_ffn_W_k"]),
        "l0f_wvt": tr(inp["L0_ffn_W_v"]),
        "l0f_mur": np.ascontiguousarray(f("L0_ffn_mu_r").reshape(8, 128).T),
        "l0f_muk": np.ascontiguousarray(f("L0_ffn_mu_k").reshape(8, 128).T),
        "l1_wq": np.ascontiguousarray(f("L1_attn_Wq")),
        "l1_wk": np.ascontiguousarray(f("L1_attn_Wk")),
        "l1_wv": np.ascontiguousarray(f("L1_attn_Wv")),
        "l1_bq": f("L1_attn_bq").reshape(1, D),
        "l1_bk": f("L1_attn_bk").reshape(1, D),
        "l1_bv": f("L1_attn_bv").reshape(1, D),
        "moe_gwt": tr(inp["L1_ffn_gate_w"]),
        "moe_wg": np.ascontiguousarray(np.transpose(f("L1_ffn_expert_gate"), (0, 2, 1))),
        "moe_wu": np.ascontiguousarray(np.transpose(f("L1_ffn_expert_up"), (0, 2, 1))),
        "moe_wd": np.ascontiguousarray(np.transpose(f("L1_ffn_expert_down"), (0, 2, 1))),
        "moe_sg": tr(inp["L1_ffn_shared_gate_w"]),
        "moe_su": tr(inp["L1_ffn_shared_up_w"]),
        "moe_sd": tr(inp["L1_ffn_shared_down_w"]),
        "identf": np.eye(128, dtype=np.float32),
        "identb": np.eye(128, dtype=ml_dtypes.bfloat16),
        "onesr": np.ones((1, 128), dtype=ml_dtypes.bfloat16),
        "onesc": np.ones((128, 1), dtype=ml_dtypes.bfloat16),
        "zeros8": np.zeros((128, 8), dtype=ml_dtypes.bfloat16),
    }
    return m


def kernel(**inp):
    x = np.asarray(inp["L0_attn_x"], dtype=np.float32)
    base = prep_inputs(**inp)
    nc = build_program(T=4096, dev=False)
    in_maps = []
    for b in range(8):
        mm = dict(base)
        mm["x"] = np.ascontiguousarray(x[b])
        in_maps.append(mm)
    res = run_bass_kernel_spmd(nc, in_maps, core_ids=list(range(8)))
    return np.stack([np.asarray(r["out"]) for r in res.results], axis=0)


# version 05
# speedup vs baseline: 1.3615x; 1.1199x over previous
import numpy as np
import ml_dtypes
from contextlib import ExitStack

import concourse.bass as bass
import concourse.mybir as mybir
from concourse.bass_utils import run_bass_kernel_spmd

F32 = mybir.dt.float32
BF16 = mybir.dt.bfloat16
AF = mybir.ActivationFunctionType
ALU = mybir.AluOpType
AX = mybir.AxisListType

D = 1024
F_CUT = 99
DEV_DUMP = False
EPS0 = 1e-6
LNEPS = 1e-5

ENGS = ("sync", "gpsimd", "vector", "scalar", "tensor")


class PB:

    ENG_ORD = {"vector": "zzv", "scalar": "zzs", "tensor": "zzt"}
    MILESTONE_ENG = {"dv": "vector", "ac": "scalar", "pe": "tensor"}

    def __init__(self, nc, paranoid=False):
        self.nc = nc
        self.paranoid = paranoid
        self.streams = {e: [] for e in ENGS}
        self.cnt = {}
        self._sems = {}
        self.dma_sem_eng = {}

    def sem(self, name):
        if name not in self._sems:
            self._sems[name] = self.nc.alloc_semaphore(name)
            self.cnt[name] = 0
        return self._sems[name]

    def do(self, eng, fn, inc=None, n=1):
        if self.paranoid and eng in self.ENG_ORD:
            z = self.ENG_ORD[eng]
            sem = self.sem(z)
            prev = self.cnt[z]
            self.cnt[z] = prev + 1
            tgt = prev + 1
            if prev > 0:
                self.streams[eng].append(
                    lambda e, sem=sem, prev=prev: e.wait_ge(sem, prev)
                )

            def wrapped(e, fn=fn, sem=sem):
                fn(e).then_inc(sem, 1)

            self.streams[eng].append(wrapped)
            return (z, tgt)
        if inc is None:
            self.streams[eng].append(fn)
            return None
        sem = self.sem(inc)
        self.cnt[inc] += n
        tgt = self.cnt[inc]

        def wrapped(e, fn=fn, sem=sem, n=n):
            fn(e).then_inc(sem, n)

        self.streams[eng].append(wrapped)
        return (inc, tgt)

    def wait(self, eng, ms):
        if ms is None:
            return
        name, val = ms
        if val <= 0:
            return
        sem = self.sem(name)
        self.streams[eng].append(lambda e, sem=sem, val=val: e.wait_ge(sem, val))

    def dma(self, eng, out, in_, sem):
        prev = self.dma_sem_eng.get(sem)
        assert prev is None or prev == eng, f"dma sem {sem} used from {prev} and {eng}"
        self.dma_sem_eng[sem] = eng
        return self.do(
            eng, lambda e, out=out, in_=in_: e.dma_start(out=out, in_=in_), inc=sem, n=16
        )

    def now(self, name):
        if self.paranoid and name in self.MILESTONE_ENG:
            z = self.ENG_ORD[self.MILESTONE_ENG[name]]
            return (z, self.cnt.get(z, 0))
        return (name, self.cnt.get(name, 0))

    def barrier(self):
        bar = self.sem("bar")
        for s, eng in self.dma_sem_eng.items():
            self.wait(eng, (s, self.cnt[s]))
        self.cnt["bar"] += len(ENGS)
        tgt = self.cnt["bar"]
        for eng in ENGS:
            self.streams[eng].append(lambda e, bar=bar: e.sem_inc(bar, 1))
            self.streams[eng].append(lambda e, bar=bar, tgt=tgt: e.wait_ge(bar, tgt))

    def flush(self, block):
        for eng in ENGS:
            lst = self.streams[eng]
            if not lst:
                continue
            getattr(block, eng)(lambda e, lst=lst: [f(e) for f in lst])
            self.streams[eng] = []



def emit_ln(pb, c, xin_ap, out_ap, junk, wst, psum_in=False):
    pb.wait("scalar", pb.now("dv"))
    acs = pb.do(
        "scalar",
        lambda e: e.activation(
            out=junk, in_=xin_ap, func=AF.Square, accum_out=wst[:, 3:4]
        ),
        inc="ac",
    )
    if psum_in:
        pb.wait("vector", acs)
    r1 = pb.do("vector", lambda e: e.reduce_sum(wst[:, 2:3], xin_ap, axis=AX.X), inc="dv")
    pb.wait("vector", r1)
    pb.do("vector", lambda e: e.tensor_scalar_mul(wst[:, 1:2], wst[:, 2:3], 1.0 / D))
    pb.wait("vector", acs)
    r3 = pb.do("vector", lambda e: e.tensor_scalar_mul(wst[:, 4:5], wst[:, 3:4], 1.0 / D), inc="dv")
    pb.wait("vector", r3)
    r4 = pb.do("vector", lambda e: e.tensor_mul(wst[:, 5:6], wst[:, 1:2], wst[:, 1:2]), inc="dv")
    pb.wait("vector", r4)
    dvv = pb.do("vector", lambda e: e.tensor_sub(wst[:, 6:7], wst[:, 4:5], wst[:, 5:6]), inc="dv")
    pb.wait("scalar", dvv)
    acv = pb.do(
        "scalar",
        lambda e: e.activation(out=wst[:, 7:8], in_=wst[:, 6:7], func=AF.Sqrt, bias=c["lneps"][:, 0:1]),
        inc="ac",
    )
    pb.wait("vector", acv)
    dvr2 = pb.do("vector", lambda e: e.reciprocal(wst[:, 0:1], wst[:, 7:8]), inc="dv")
    pb.wait("vector", dvr2)
    vap = pb.do(
        "vector",
        lambda e: e.scalar_tensor_tensor(
            out=out_ap, in0=xin_ap, scalar=wst[:, 1:2],
            in1=wst[:, 0:1].to_broadcast([128, D]),
            op0=ALU.subtract, op1=ALU.mult,
        ),
        inc="dv",
    )
    return vap


def emit_proj(pb, c, lnT, w_sbuf, bias_row, psum_ap, n_chunks=8, nout=D):
    nh = (nout + 511) // 512

    def fn(e, lnT=lnT, w=w_sbuf, b=bias_row, p=psum_ap, n=n_chunks, nout=nout, nh=nh):
        ins = None
        for cc in range(n):
            lhs = lnT[:, 128 * cc : 128 * cc + 128]
            for half in range(nh):
                w0 = nout * cc + 512 * half
                nw = min(512, nout - 512 * half)
                ins = e.matmul(
                    p[:, 512 * half : 512 * half + nw],
                    lhs,
                    w[:, w0 : w0 + nw],
                    start=(cc == 0), stop=(cc == n - 1 and b is None),
                )
        if b is not None:
            for half in range(nh):
                nw = min(512, nout - 512 * half)
                ins = e.matmul(
                    p[:, 512 * half : 512 * half + nw],
                    c["ones_row"][0:1, :],
                    b[0:1, 512 * half : 512 * half + nw],
                    start=False, stop=True,
                )
        return ins

    return pb.do("tensor", fn, inc="pe")


def emit_transpose8(pb, c, src_bf, pt, n=8):
    def fn(e, s=src_bf, pt=pt, n=n):
        ins = None
        for cc in range(n):
            ins = e.transpose(
                pt[:, 128 * cc : 128 * cc + 128],
                s[:, 128 * cc : 128 * cc + 128],
                c["ident_bf"][:, :],
            )
        return ins

    return pb.do("tensor", fn, inc="pe")



def emit_l0_attn(pb, c, ctx, T, x_in, scr_q, scr_k, scr_v, scr_out, n_sweeps=6):
    nc = pb.nc
    NT = T // 128

    wq = ctx.enter_context(nc.sbuf_tensor("l0a_wq", [128, 8 * 1024], BF16))
    wk = ctx.enter_context(nc.sbuf_tensor("l0a_wk", [128, 8 * 1024], BF16))
    wv = ctx.enter_context(nc.sbuf_tensor("l0a_wv", [128, 8 * 1024], BF16))
    wo = ctx.enter_context(nc.sbuf_tensor("l0a_wo", [128, 8 * 1024], BF16))
    bq = ctx.enter_context(nc.sbuf_tensor("l0a_bq", [1, D], BF16))
    bk = ctx.enter_context(nc.sbuf_tensor("l0a_bk", [1, D], BF16))
    bv = ctx.enter_context(nc.sbuf_tensor("l0a_bv", [1, D], BF16))
    bo = ctx.enter_context(nc.sbuf_tensor("l0a_bo", [1, D], BF16))
    xin = [ctx.enter_context(nc.sbuf_tensor(f"l0a_xin{i}", [128, D], F32)) for i in range(2)]
    ln_bf = ctx.enter_context(nc.sbuf_tensor("l0a_ln", [128, D], BF16))
    lnT = ctx.enter_context(nc.sbuf_tensor("l0a_lnT", [128, D], BF16))
    q_bf = [ctx.enter_context(nc.sbuf_tensor(f"l0a_q{i}", [128, D], BF16)) for i in range(2)]
    k_bf = [ctx.enter_context(nc.sbuf_tensor(f"l0a_k{i}", [128, D], BF16)) for i in range(2)]
    v_bf = [ctx.enter_context(nc.sbuf_tensor(f"l0a_v{i}", [128, D], BF16)) for i in range(2)]
    wst = ctx.enter_context(nc.sbuf_tensor("l0a_wst", [128, 8], F32))
    sqsum = ctx.enter_context(nc.sbuf_tensor("l0a_qsum", [1, D], F32))
    sksum = ctx.enter_context(nc.sbuf_tensor("l0a_ksum", [1, D], F32))
    rowbf = ctx.enter_context(nc.sbuf_tensor("l0a_rowbf", [1, D], BF16))
    bc1 = ctx.enter_context(nc.sbuf_tensor("l0a_bc1", [128, D], F32))
    bc2 = ctx.enter_context(nc.sbuf_tensor("l0a_bc2", [128, D], F32))
    tmpf = ctx.enter_context(nc.sbuf_tensor("l0a_tmpf", [128, D], F32))
    s16a = ctx.enter_context(nc.sbuf_tensor("l0a_s16a", [128, 16], F32))
    s16b = ctx.enter_context(nc.sbuf_tensor("l0a_s16b", [128, 16], F32))
    si_all = ctx.enter_context(nc.sbuf_tensor("l0a_si", [128, 16 * NT], F32))
    sa_all = ctx.enter_context(nc.sbuf_tensor("l0a_sa", [128, 16 * NT], F32))
    csT = ctx.enter_context(nc.sbuf_tensor("l0a_csT", [16, T], F32))
    scT = ctx.enter_context(nc.sbuf_tensor("l0a_scT", [16, T], F32))
    s16 = ctx.enter_context(nc.sbuf_tensor("l0a_s16", [16, 4], F32))
    kv_s = ctx.enter_context(nc.sbuf_tensor("l0a_kv", [64, 1024], BF16))
    qsiT = ctx.enter_context(nc.sbuf_tensor("l0a_qsiT", [64, 16 * 128], BF16))
    aT = ctx.enter_context(nc.sbuf_tensor("l0a_aT", [128, D], BF16))
    y1 = ctx.enter_context(nc.sbuf_tensor("l0a_y1", [128, D], F32))
    x1o = [ctx.enter_context(nc.sbuf_tensor(f"l0a_x1o{i}", [128, D], F32)) for i in range(2)]

    pA = ctx.enter_context(nc.psum_tensor("l0a_pA", [128, D], F32))
    pB = ctx.enter_context(nc.psum_tensor("l0a_pB", [128, D], F32))
    pt = ctx.enter_context(nc.psum_tensor("l0a_pt", [128, D], BF16))
    ps1 = ctx.enter_context(nc.psum_tensor("l0a_ps1", [1, D], F32))

    for w_s, name in ((wq, "l0_wqt"), (wk, "l0_wkt"), (wv, "l0_wvt"), (wo, "l0_wot")):
        src = c[name].rearrange("(c p) n -> p c n", p=128)
        dst = w_s[:, :].rearrange("p (c n) -> p c n", c=8)
        pb.dma("gpsimd", dst, src, "wl")
    for b_s, name in ((bq, "l0_bq"), (bk, "l0_bk"), (bv, "l0_bv"), (bo, "l0_bo")):
        pb.dma("gpsimd", b_s[:, :], c[name], "wl")
    wl_all = pb.now("wl")

    ld_x = {}
    for t in range(min(2, NT)):
        ld_x[t] = pb.dma("sync", xin[t % 2][:, :], x_in[128 * t : 128 * t + 128, :], f"lx{t % 2}")

    ac_sq, ac_sk, dv_sv, st_q, st_k, st_v = {}, {}, {}, {}, {}, {}
    dv_qsum = dv_ksum = None
    for t in range(NT):
        bt = t % 2
        pb.wait("vector", ld_x[t])
        pb.wait("scalar", ld_x[t])
        apply_v = emit_ln(pb, c, xin[bt][:, :], ln_bf[:, :], tmpf[:, :], wst)
        if t + 2 < NT:
            pb.wait("sync", apply_v)
            ld_x[t + 2] = pb.dma("sync", xin[bt][:, :], x_in[128 * (t + 2) : 128 * (t + 2) + 128, :], f"lx{bt}")
        pb.wait("tensor", apply_v)
        pe_tr = emit_transpose8(pb, c, ln_bf, pt)
        pb.wait("vector", pe_tr)
        dv_lnT = pb.do("vector", lambda e: e.tensor_copy(lnT[:, :], pt[:, :]), inc="dv")
        pb.wait("tensor", dv_lnT)
        if t == 0:
            pb.wait("tensor", wl_all)
        if t >= 1:
            pb.wait("tensor", dv_sv[t - 1])
        pe_q = emit_proj(pb, c, lnT, wq, bq, pA)
        if t >= 1:
            pb.wait("tensor", ac_sk[t - 1])
        pe_k = emit_proj(pb, c, lnT, wk, bk, pB)
        pb.wait("scalar", pe_q)
        if t >= 2:
            pb.wait("scalar", st_q[t - 2])
        ac_sq[t] = pb.do(
            "scalar",
            lambda e, bt=bt: e.activation(out=q_bf[bt][:, :], in_=pA[:, :], func=AF.Sigmoid),
            inc="ac",
        )
        pb.wait("tensor", ac_sq[t])
        pe_v = emit_proj(pb, c, lnT, wv, bv, pA)
        pb.wait("scalar", pe_k)
        if t >= 2:
            pb.wait("scalar", st_k[t - 2])
        ac_sk[t] = pb.do(
            "scalar",
            lambda e, bt=bt: e.activation(out=k_bf[bt][:, :], in_=pB[:, :], func=AF.Sigmoid),
            inc="ac",
        )
        pb.wait("vector", pe_v)
        if t >= 2:
            pb.wait("vector", st_v[t - 2])
        dv_sv[t] = pb.do("vector", lambda e, bt=bt: e.tensor_copy(v_bf[bt][:, :], pA[:, :]), inc="dv")
        pb.wait("tensor", ac_sq[t])
        if dv_ksum is not None:
            pb.wait("tensor", dv_ksum)

        def qsum_fn(e, q=q_bf[bt]):
            e.matmul(ps1[0:1, 0:512], c["ones_col"][:, 0:1], q[:, 0:512], start=True, stop=True)
            return e.matmul(ps1[0:1, 512:1024], c["ones_col"][:, 0:1], q[:, 512:1024], start=True, stop=True)

        pe_qs = pb.do("tensor", qsum_fn, inc="pe")
        pb.wait("vector", pe_qs)
        if t == 0:
            dv_qsum = pb.do("vector", lambda e: e.tensor_copy(sqsum[0:1, :], ps1[0:1, :]), inc="dv")
        else:
            dv_qsum = pb.do("vector", lambda e: e.tensor_add(sqsum[0:1, :], sqsum[0:1, :], ps1[0:1, :]), inc="dv")
        pb.wait("tensor", ac_sk[t])
        pb.wait("tensor", dv_qsum)

        def ksum_fn(e, k=k_bf[bt]):
            e.matmul(ps1[0:1, 0:512], c["ones_col"][:, 0:1], k[:, 0:512], start=True, stop=True)
            return e.matmul(ps1[0:1, 512:1024], c["ones_col"][:, 0:1], k[:, 512:1024], start=True, stop=True)

        pe_ks = pb.do("tensor", ksum_fn, inc="pe")
        pb.wait("vector", pe_ks)
        if t == 0:
            dv_ksum = pb.do("vector", lambda e: e.tensor_copy(sksum[0:1, :], ps1[0:1, :]), inc="dv")
        else:
            dv_ksum = pb.do("vector", lambda e: e.tensor_add(sksum[0:1, :], sksum[0:1, :], ps1[0:1, :]), inc="dv")
        pb.wait("sync", ac_sq[t])
        st_q[t] = pb.dma("sync", scr_q[128 * t : 128 * t + 128, :], q_bf[bt][:, :], f"sq{bt}")
        pb.wait("sync", ac_sk[t])
        st_k[t] = pb.dma("sync", scr_k[128 * t : 128 * t + 128, :], k_bf[bt][:, :], f"sk{bt}")
        pb.wait("sync", dv_sv[t])
        st_v[t] = pb.dma("sync", scr_v[128 * t : 128 * t + 128, :], v_bf[bt][:, :], f"sv{bt}")

    pb.barrier()
    if n_sweeps <= 1:
        return

    def bcast_row(src_row_ap, dst_bc):
        dvr = pb.do(
            "vector",
            lambda e: e.tensor_scalar_add(rowbf[0:1, :], src_row_ap, EPS0),
            inc="dv",
        )
        pb.wait("tensor", dvr)

        def fn(e):
            e.matmul(pA[:, 0:512], c["ones_row"][0:1, :], rowbf[0:1, 0:512], start=True, stop=True)
            return e.matmul(pA[:, 512:1024], c["ones_row"][0:1, :], rowbf[0:1, 512:1024], start=True, stop=True)

        pev = pb.do("tensor", fn, inc="pe")
        pb.wait("vector", pev)
        return pb.do("vector", lambda e: e.tensor_copy(dst_bc[:, :], pA[:, :]), inc="dv")

    bcast_row(sksum[0:1, :], bc1)
    bcast_row(sqsum[0:1, :], bc2)

    ld_q, ld_k, ld_v = {}, {}, {}
    for t in range(min(2, NT)):
        ld_q[t] = pb.dma("sync", q_bf[t % 2][:, :], scr_q[128 * t : 128 * t + 128, :], f"lq{t % 2}")
        ld_k[t] = pb.dma("sync", k_bf[t % 2][:, :], scr_k[128 * t : 128 * t + 128, :], f"lk{t % 2}")

    pe_s2 = pe_s1 = None
    for t in range(NT):
        bt = t % 2
        pb.wait("vector", ld_q[t])
        pb.wait("vector", ld_k[t])
        if pe_s1 is not None:
            pb.wait("vector", pe_s1)
        pb.do(
            "vector",
            lambda e, q=q_bf[bt]: e.scalar_tensor_tensor(
                out=tmpf[:, :], in0=q[:, :], scalar=EPS0, in1=bc1[:, :],
                op0=ALU.add, op1=ALU.mult,
            ),
        )
        rqa = pb.do(
            "vector",
            lambda e: e.reduce_sum(
                s16a[:, :], tmpf[:, :].rearrange("p (h d) -> p h d", h=16), axis=AX.X
            ),
            inc="dv",
        )
        pb.wait("vector", rqa)
        dvrq = pb.do("vector", lambda e, t=t: e.reciprocal(si_all[:, 16 * t : 16 * t + 16], s16a[:, :]), inc="dv")
        pb.wait("vector", dvrq)
        dv_qsi = pb.do(
            "vector",
            lambda e, q=q_bf[bt], t=t: e.tensor_tensor(
                out=ln_bf[:, :].rearrange("p (h d) -> p h d", h=16),
                in0=q[:, :].rearrange("p (h d) -> p h d", h=16),
                in1=si_all[:, 16 * t : 16 * t + 16].unsqueeze(2).to_broadcast([128, 16, 64]),
                op=ALU.mult,
            ),
            inc="dv",
        )
        pb.wait("tensor", dv_qsi)

        def s2fn(e, t=t):
            e.matmul(pB[0:1, 0:512], c["ones_col"][:, 0:1], ln_bf[:, 0:512], start=(t == 0), stop=(t == NT - 1))
            return e.matmul(pB[0:1, 512:1024], c["ones_col"][:, 0:1], ln_bf[:, 512:1024], start=(t == 0), stop=(t == NT - 1))

        pe_s2 = pb.do("tensor", s2fn, inc="pe")
        pb.wait("vector", pe_s2)
        pb.do(
            "vector",
            lambda e, k=k_bf[bt]: e.scalar_tensor_tensor(
                out=tmpf[:, :], in0=k[:, :], scalar=EPS0, in1=bc2[:, :],
                op0=ALU.add, op1=ALU.mult,
            ),
        )
        rkb = pb.do(
            "vector",
            lambda e: e.reduce_sum(
                s16b[:, :], tmpf[:, :].rearrange("p (h d) -> p h d", h=16), axis=AX.X
            ),
            inc="dv",
        )
        pb.wait("vector", rkb)
        dvrk = pb.do("vector", lambda e: e.reciprocal(s16b[:, :], s16b[:, :]), inc="dv")
        pb.wait("vector", dvrk)
        dv_kso = pb.do(
            "vector",
            lambda e, k=k_bf[bt]: e.tensor_tensor(
                out=ln_bf[:, :].rearrange("p (h d) -> p h d", h=16),
                in0=k[:, :].rearrange("p (h d) -> p h d", h=16),
                in1=s16b[:, :].unsqueeze(2).to_broadcast([128, 16, 64]),
                op=ALU.mult,
            ),
            inc="dv",
        )
        pb.wait("tensor", dv_kso)

        def s1fn(e, t=t):
            e.matmul(ps1[0:1, 0:512], c["ones_col"][:, 0:1], ln_bf[:, 0:512], start=(t == 0), stop=(t == NT - 1))
            return e.matmul(ps1[0:1, 512:1024], c["ones_col"][:, 0:1], ln_bf[:, 512:1024], start=(t == 0), stop=(t == NT - 1))

        pe_s1 = pb.do("tensor", s1fn, inc="pe")
        if t + 2 < NT:
            pb.wait("sync", pe_s1)
            ld_q[t + 2] = pb.dma("sync", q_bf[bt][:, :], scr_q[128 * (t + 2) : 128 * (t + 2) + 128, :], f"lq{bt}")
            ld_k[t + 2] = pb.dma("sync", k_bf[bt][:, :], scr_k[128 * (t + 2) : 128 * (t + 2) + 128, :], f"lk{bt}")

    pb.wait("vector", pe_s1)
    pb.wait("vector", pe_s2)
    bcast_row(ps1[0:1, :], bc1)
    bcast_row(pB[0:1, :], bc2)

    pb.barrier()
    if n_sweeps <= 2:
        return

    for t in range(min(2, NT)):
        ld_q[t] = pb.dma("sync", q_bf[t % 2][:, :], scr_q[128 * t : 128 * t + 128, :], f"lq{t % 2}")
        ld_k[t] = pb.dma("sync", k_bf[t % 2][:, :], scr_k[128 * t : 128 * t + 128, :], f"lk{t % 2}")
    ac_sa = pe_trc = None
    dv_cs = {}
    for t in range(NT):
        bt = t % 2
        pb.wait("vector", ld_q[t])
        pb.wait("vector", ld_k[t])
        if ac_sa is not None:
            pb.wait("vector", ac_sa)
        if pe_trc is not None:
            pb.wait("vector", pe_trc)
        pb.do(
            "vector",
            lambda e, q=q_bf[bt]: e.scalar_tensor_tensor(
                out=tmpf[:, :], in0=q[:, :], scalar=EPS0, in1=bc1[:, :],
                op0=ALU.add, op1=ALU.mult,
            ),
        )
        dvr = pb.do(
            "vector",
            lambda e: e.reduce_sum(
                s16a[:, :], tmpf[:, :].rearrange("p (h d) -> p h d", h=16), axis=AX.X
            ),
            inc="dv",
        )
        pb.wait("scalar", dvr)
        ac_sa = pb.do(
            "scalar",
            lambda e, t=t: e.activation(
                out=sa_all[:, 16 * t : 16 * t + 16], in_=s16a[:, :], func=AF.Sigmoid
            ),
            inc="ac",
        )
        pb.do(
            "vector",
            lambda e, k=k_bf[bt]: e.scalar_tensor_tensor(
                out=tmpf[:, :], in0=k[:, :], scalar=EPS0, in1=bc2[:, :],
                op0=ALU.add, op1=ALU.mult,
            ),
        )
        rcb = pb.do(
            "vector",
            lambda e: e.reduce_sum(
                s16b[:, :], tmpf[:, :].rearrange("p (h d) -> p h d", h=16), axis=AX.X
            ),
            inc="dv",
        )
        pb.wait("vector", rcb)
        rcm = pb.do("vector", lambda e: e.tensor_scalar_min(s16b[:, :], s16b[:, :], 1.0), inc="dv")
        pb.wait("vector", rcm)
        dv_cl = pb.do("vector", lambda e: e.tensor_scalar_max(s16b[:, :], s16b[:, :], -1.0), inc="dv")
        pb.wait("tensor", dv_cl)
        if t >= 1:
            pb.wait("tensor", dv_cs[t - 1])
        pe_trc = pb.do(
            "tensor",
            lambda e: e.transpose(pA[0:16, 0:128], s16b[:, :], c["ident_f32"][:, :]),
            inc="pe",
        )
        pb.wait("vector", pe_trc)
        dv_cs[t] = pb.do(
            "vector",
            lambda e, t=t: e.tensor_copy(csT[0:16, 128 * t : 128 * t + 128], pA[0:16, 0:128]),
            inc="dv",
        )
        if t + 2 < NT:
            pb.wait("sync", dv_cs[t])
            ld_q[t + 2] = pb.dma("sync", q_bf[bt][:, :], scr_q[128 * (t + 2) : 128 * (t + 2) + 128, :], f"lq{bt}")
            ld_k[t + 2] = pb.dma("sync", k_bf[bt][:, :], scr_k[128 * (t + 2) : 128 * (t + 2) + 128, :], f"lk{bt}")

    pb.barrier()
    if n_sweeps <= 3:
        return

    rdm = pb.do("vector", lambda e: e.reduce_max(s16[0:16, 0:1], csT[0:16, :], axis=AX.X), inc="dv")
    pb.wait("vector", rdm)
    dvm = pb.do("vector", lambda e: e.tensor_scalar_mul(s16[0:16, 1:2], s16[0:16, 0:1], -1.0), inc="dv")
    pb.wait("scalar", dvm)
    acce = pb.do(
        "scalar",
        lambda e: e.activation(
            out=scT[0:16, :], in_=csT[0:16, :], func=AF.Exp,
            bias=s16[0:16, 1:2], accum_out=s16[0:16, 2:3],
        ),
        inc="ac",
    )
    pb.wait("vector", acce)
    rrc = pb.do("vector", lambda e: e.reciprocal(s16[0:16, 3:4], s16[0:16, 2:3]), inc="dv")
    pb.wait("vector", rrc)
    dvsc = pb.do("vector", lambda e: e.tensor_scalar_mul(s16[0:16, 3:4], s16[0:16, 3:4], float(T)), inc="dv")
    pb.wait("vector", dvsc)
    dvscT = pb.do(
        "vector",
        lambda e: e.tensor_scalar(
            scT[0:16, :], scT[0:16, :], s16[0:16, 3:4], None, op0=ALU.mult
        ),
        inc="dv",
    )
    if DEV_DUMP:
        pb.wait("sync", dvscT)
        pb.dma("sync", c["scr_dbg"][0:16, 0:T], csT[0:16, :], "dbg1")
        pb.dma("sync", c["scr_dbg"][16:32, 0:T], scT[0:16, :], "sdbg2")
        pb.dma("sync", c["scr_dbg"][32:33, 0:D], bc1[0:1, :], "dbg3")
        pb.dma("sync", c["scr_dbg"][33:34, 0:D], bc2[0:1, :], "dbg4")
        pb.dma("sync", c["scr_dbg"][34:35, 0:D], sksum[0:1, :], "dbg5")
        pb.dma("sync", c["scr_dbg"][35:36, 0:D], sqsum[0:1, :], "dbg6")
        pb.dma("sync", c["scr_dbg"][36:37, 0:16].rearrange("a (b c) -> a b c", b=4), s16[0:16, 0:4].rearrange("(a b) c -> a b c", a=1), "dbg7") if False else None
        pb.dma("sync", c["scr_dbg"][40:56, 0:4], s16[0:16, 0:4], "dbg8")

    pb.barrier()
    if n_sweeps <= 4:
        return

    for t in range(min(2, NT)):
        ld_k[t] = pb.dma("sync", k_bf[t % 2][:, :], scr_k[128 * t : 128 * t + 128, :], f"lk{t % 2}")
        ld_v[t] = pb.dma("sync", v_bf[t % 2][:, :], scr_v[128 * t : 128 * t + 128, :], f"lv{t % 2}")
    dv_vsc = {}
    pe_kv = None
    for t in range(NT):
        bt = t % 2
        if t >= 1:
            pb.wait("tensor", dv_vsc[t - 1])
        pe_tr = pb.do(
            "tensor",
            lambda e, t=t: e.transpose(
                pA[:, 0:16], scT[0:16, 128 * t : 128 * t + 128], c["ident_f32"][0:16, 0:16]
            ),
            inc="pe",
        )
        pb.wait("vector", pe_tr)
        pb.wait("vector", ld_v[t])
        if pe_kv is not None:
            pb.wait("vector", pe_kv)
        dvcp = pb.do("vector", lambda e: e.tensor_copy(s16a[:, :], pA[:, 0:16]), inc="dv")
        pb.wait("vector", dvcp)
        dv_vsc[t] = pb.do(
            "vector",
            lambda e, v=v_bf[bt]: e.tensor_tensor(
                out=ln_bf[:, :].rearrange("p (h d) -> p h d", h=16),
                in0=v[:, :].rearrange("p (h d) -> p h d", h=16),
                in1=s16a[:, :].unsqueeze(2).to_broadcast([128, 16, 64]),
                op=ALU.mult,
            ),
            inc="dv",
        )
        pb.wait("tensor", dv_vsc[t])
        pb.wait("tensor", ld_k[t])

        def kvfn(e, k=k_bf[bt], t=t):
            ins = None
            for h in range(16):
                ins = e.matmul(
                    pB[0:64, 64 * h : 64 * h + 64],
                    k[:, 64 * h : 64 * h + 64],
                    ln_bf[:, 64 * h : 64 * h + 64],
                    start=(t == 0 and h % 8 == 0),
                    stop=(t == NT - 1 and h % 8 == 7),
                )
            return ins

        pe_kv = pb.do("tensor", kvfn, inc="pe")
        if t + 2 < NT:
            pb.wait("sync", pe_kv)
            pb.wait("sync", dv_vsc[t])
            ld_k[t + 2] = pb.dma("sync", k_bf[bt][:, :], scr_k[128 * (t + 2) : 128 * (t + 2) + 128, :], f"lk{bt}")
            ld_v[t + 2] = pb.dma("sync", v_bf[bt][:, :], scr_v[128 * (t + 2) : 128 * (t + 2) + 128, :], f"lv{bt}")
    pb.wait("vector", pe_kv)
    dv_kvs = pb.do("vector", lambda e: e.tensor_copy(kv_s[0:64, :], pB[0:64, :]), inc="dv")

    pb.barrier()
    if n_sweeps <= 5:
        return

    for t in range(min(2, NT)):
        ld_q[t] = pb.dma("sync", q_bf[t % 2][:, :], scr_q[128 * t : 128 * t + 128, :], f"lq{t % 2}")
        ld_x[t] = pb.dma("sync", xin[t % 2][:, :], x_in[128 * t : 128 * t + 128, :], f"lx{t % 2}")
    st_o = {}
    dv_fin = {}
    pe_y = None
    for t in range(NT):
        bt = t % 2
        pb.wait("vector", ld_q[t])
        if pe_y is not None:
            pb.wait("vector", pe_y)
        if F_CUT < 9 and t >= 0:
            emit_f_cut(pb, c, t, bt, NT, locals())
            continue
        dv_q = pb.do(
            "vector",
            lambda e, q=q_bf[bt], t=t: e.tensor_tensor(
                out=ln_bf[:, :].rearrange("p (h d) -> p h d", h=16),
                in0=q[:, :].rearrange("p (h d) -> p h d", h=16),
                in1=si_all[:, 16 * t : 16 * t + 16].unsqueeze(2).to_broadcast([128, 16, 64]),
                op=ALU.mult,
            ),
            inc="dv",
        )
        pb.wait("tensor", dv_q)
        if t >= 1:
            pb.wait("tensor", dv_fin[t - 1])
        dv_qT = None
        for grp in range(2):
            def trf(e, grp=grp):
                ins = None
                for hh in range(8):
                    h = 8 * grp + hh
                    ins = e.transpose(
                        pt[0:64, 128 * hh : 128 * hh + 128],
                        ln_bf[:, 64 * h : 64 * h + 64],
                        c["ident_bf"][:, :],
                    )
                return ins

            if grp == 1:
                pb.wait("tensor", dv_qT)
            pe_tr = pb.do("tensor", trf, inc="pe")
            pb.wait("vector", pe_tr)
            dv_qT = pb.do(
                "vector",
                lambda e, grp=grp: e.tensor_copy(
                    qsiT[0:64, 1024 * grp : 1024 * grp + 1024], pt[0:64, 0:1024]
                ),
                inc="dv",
            )
        pb.wait("tensor", dv_qT)

        def outfn(e):
            ins = None
            for h in range(16):
                ins = e.matmul(
                    pA[:, 64 * h : 64 * h + 64],
                    qsiT[0:64, 128 * h : 128 * h + 128],
                    kv_s[0:64, 64 * h : 64 * h + 64],
                    start=True, stop=True,
                )
            return ins

        pe_out = pb.do("tensor", outfn, inc="pe")
        pb.wait("vector", pe_out)
        dv_at = pb.do(
            "vector",
            lambda e, t=t: e.tensor_tensor(
                out=ln_bf[:, :].rearrange("p (h d) -> p h d", h=16),
                in0=pA[:, :].rearrange("p (h d) -> p h d", h=16),
                in1=sa_all[:, 16 * t : 16 * t + 16].unsqueeze(2).to_broadcast([128, 16, 64]),
                op=ALU.mult,
            ),
            inc="dv",
        )
        pb.wait("tensor", dv_at)
        pe_tr2 = emit_transpose8(pb, c, ln_bf, pt)
        pb.wait("vector", pe_tr2)
        dv_aT = pb.do("vector", lambda e: e.tensor_copy(aT[:, :], pt[:, :]), inc="dv")
        pb.wait("tensor", dv_aT)
        pe_y = emit_proj(pb, c, aT, wo, bo, pB)
        pb.wait("vector", pe_y)
        pb.wait("scalar", pe_y)
        emit_ln(pb, c, pB[:, :], y1[:, :], tmpf[:, :], wst, psum_in=True)
        pb.wait("vector", ld_x[t])
        if t >= 2:
            pb.wait("vector", st_o[t - 2])
        dv_fin[t] = pb.do(
            "vector", lambda e, bt=bt: e.tensor_add(x1o[bt][:, :], xin[bt][:, :], y1[:, :]), inc="dv"
        )
        pb.wait("sync", dv_fin[t])
        st_o[t] = pb.dma("sync", scr_out[128 * t : 128 * t + 128, :], x1o[bt][:, :], f"so{bt}")
        if t + 2 < NT:
            ld_q[t + 2] = pb.dma("sync", q_bf[bt][:, :], scr_q[128 * (t + 2) : 128 * (t + 2) + 128, :], f"lq{bt}")
            ld_x[t + 2] = pb.dma("sync", xin[bt][:, :], x_in[128 * (t + 2) : 128 * (t + 2) + 128, :], f"lx{bt}")

    pb.barrier()




def emit_f_cut(pb, c, t, bt, NT, L):
    xin, ln_bf, q_bf, qsiT, kv_s, sa_all, si_all, aT, y1, x1o, tmpf, wst = (
        L["xin"], L["ln_bf"], L["q_bf"], L["qsiT"], L["kv_s"], L["sa_all"],
        L["si_all"], L["aT"], L["y1"], L["x1o"], L["tmpf"], L["wst"])
    pA, pB, pt = L["pA"], L["pB"], L["pt"]
    wo, bo = L["wo"], L["bo"]
    scr_out, scr_q, x_in = L["scr_out"], L["scr_q"], L["x_in"]
    ld_q, ld_x, st_o, dv_fin = L["ld_q"], L["ld_x"], L["st_o"], L["dv_fin"]
    ALU_ = ALU
    last_dv = None
    dv_q = pb.do(
        "vector",
        lambda e, q=q_bf[bt], t=t: e.tensor_tensor(
            out=ln_bf[:, :].rearrange("p (h d) -> p h d", h=16),
            in0=q[:, :].rearrange("p (h d) -> p h d", h=16),
            in1=si_all[:, 16 * t : 16 * t + 16].unsqueeze(2).to_broadcast([128, 16, 64]),
            op=ALU_.mult,
        ),
        inc="dv",
    )
    last_pe = None
    if F_CUT >= 2:
        pb.wait("tensor", dv_q)
        if t >= 1:
            pb.wait("tensor", dv_fin[t - 1])
        dv_qT = None
        for grp in range(2):
            def trf(e, grp=grp):
                ins = None
                for hh in range(8):
                    h = 8 * grp + hh
                    ins = e.transpose(
                        pt[0:64, 128 * hh : 128 * hh + 128],
                        ln_bf[:, 64 * h : 64 * h + 64],
                        c["ident_bf"][:, :],
                    )
                return ins
            if grp == 1:
                pb.wait("tensor", dv_qT)
            pe_tr = pb.do("tensor", trf, inc="pe")
            pb.wait("vector", pe_tr)
            dv_qT = pb.do(
                "vector",
                lambda e, grp=grp: e.tensor_copy(
                    qsiT[0:64, 1024 * grp : 1024 * grp + 1024], pt[0:64, 0:1024]
                ),
                inc="dv",
            )
        last_pe = pe_tr
    if F_CUT >= 3:
        pb.wait("tensor", dv_qT)
        def outfn(e):
            ins = None
            for h in range(16):
                ins = e.matmul(
                    pA[:, 64 * h : 64 * h + 64],
                    qsiT[0:64, 128 * h : 128 * h + 128],
                    kv_s[0:64, 64 * h : 64 * h + 64],
                    start=True, stop=True,
                )
            return ins
        pe_out = pb.do("tensor", outfn, inc="pe")
        pb.wait("vector", pe_out)
        dv_at = pb.do(
            "vector",
            lambda e, t=t: e.tensor_tensor(
                out=ln_bf[:, :].rearrange("p (h d) -> p h d", h=16),
                in0=pA[:, :].rearrange("p (h d) -> p h d", h=16),
                in1=sa_all[:, 16 * t : 16 * t + 16].unsqueeze(2).to_broadcast([128, 16, 64]),
                op=ALU_.mult,
            ),
            inc="dv",
        )
        last_pe = pe_out
    if F_CUT >= 4:
        pb.wait("tensor", dv_at)
        pe_tr2 = emit_transpose8(pb, c, ln_bf, pt)
        pb.wait("vector", pe_tr2)
        dv_aT = pb.do("vector", lambda e: e.tensor_copy(aT[:, :], pt[:, :]), inc="dv")
        last_pe = pe_tr2
    if F_CUT >= 5:
        pb.wait("tensor", dv_aT)
        pe_y = emit_proj(pb, c, aT, wo, bo, pB)
        last_pe = pe_y
    if F_CUT == 6:
        pb.wait("vector", pe_y)
        pb.wait("scalar", pe_y)
        acs = pb.do(
            "scalar",
            lambda e: e.activation(out=tmpf[:, :], in_=pB[:, :], func=AF.Square, accum_out=wst[:, 3:4]),
            inc="ac",
        )
        pb.wait("vector", acs)
        pb.do("vector", lambda e: e.tensor_copy(y1[:, :], pB[:, :]))
    elif F_CUT == 7:
        pb.wait("vector", pe_y)
        pb.wait("scalar", pe_y)
        pb.wait("scalar", pb.now("dv"))
        acs = pb.do(
            "scalar",
            lambda e: e.activation(out=tmpf[:, :], in_=pB[:, :], func=AF.Square, accum_out=wst[:, 3:4]),
            inc="ac",
        )
        pb.do("vector", lambda e: e.reduce_sum(wst[:, 2:3], pB[:, :], axis=AX.X))
        pb.do("vector", lambda e: e.tensor_scalar_mul(wst[:, 1:2], wst[:, 2:3], 1.0 / D))
        pb.wait("vector", acs)
        pb.do("vector", lambda e: e.tensor_scalar_mul(wst[:, 4:5], wst[:, 3:4], 1.0 / D))
        pb.do("vector", lambda e: e.tensor_mul(wst[:, 5:6], wst[:, 1:2], wst[:, 1:2]))
        dvv = pb.do("vector", lambda e: e.tensor_sub(wst[:, 6:7], wst[:, 4:5], wst[:, 5:6]), inc="dv")
        pb.wait("scalar", dvv)
        acv = pb.do(
            "scalar",
            lambda e: e.activation(out=wst[:, 7:8], in_=wst[:, 6:7], func=AF.Sqrt, bias=c["lneps"][:, 0:1]),
            inc="ac",
        )
        pb.wait("vector", acv)
        dvr2 = pb.do("vector", lambda e: e.reciprocal(wst[:, 0:1], wst[:, 7:8]), inc="dv")
        pb.wait("vector", dvr2)
        pb.do("vector", lambda e: e.tensor_copy(y1[:, :], pB[:, :]))
    elif F_CUT >= 8:
        pb.wait("vector", pe_y)
        pb.wait("scalar", pe_y)
        emit_ln(pb, c, pB[:, :], y1[:, :], tmpf[:, :], wst, psum_in=True)
    pb.wait("vector", ld_x[t])
    if t >= 2:
        pb.wait("vector", st_o[t - 2])
    if F_CUT >= 6:
        pass
    if F_CUT >= 6:
        dv_fin[t] = pb.do(
            "vector", lambda e, bt=bt: e.tensor_add(x1o[bt][:, :], xin[bt][:, :], y1[:, :]), inc="dv"
        )
    else:
        if last_pe is not None:
            pb.wait("vector", last_pe)
        dv_fin[t] = pb.do(
            "vector", lambda e, bt=bt: e.tensor_copy(x1o[bt][:, :], xin[bt][:, :]), inc="dv"
        )
    pb.wait("sync", dv_fin[t])
    st_o[t] = pb.dma("sync", scr_out[128 * t : 128 * t + 128, :], x1o[bt][:, :], f"so{bt}")
    if t + 2 < NT:
        ld_q[t + 2] = pb.dma("sync", q_bf[bt][:, :], scr_q[128 * (t + 2) : 128 * (t + 2) + 128, :], f"lq{bt}")
        ld_x[t + 2] = pb.dma("sync", xin[bt][:, :], x_in[128 * (t + 2) : 128 * (t + 2) + 128, :], f"lx{bt}")




def emit_l0_ffn(pb, c, ctx, T, x_in, scr_out):
    nc = pb.nc
    NT = T // 128

    wr = ctx.enter_context(nc.sbuf_tensor("l0f_wr", [128, 8 * 1024], BF16))
    wk2 = ctx.enter_context(nc.sbuf_tensor("l0f_wk", [128, 8 * 1024], BF16))
    wv2 = ctx.enter_context(nc.sbuf_tensor("l0f_wv", [128, 8 * 1024], BF16))
    mur = ctx.enter_context(nc.sbuf_tensor("l0f_murs", [128, 8], F32))
    muk = ctx.enter_context(nc.sbuf_tensor("l0f_muks", [128, 8], F32))
    omr = ctx.enter_context(nc.sbuf_tensor("l0f_omr", [128, 8], F32))
    omk = ctx.enter_context(nc.sbuf_tensor("l0f_omk", [128, 8], F32))
    zr = ctx.enter_context(nc.sbuf_tensor("l0f_zr", [128, 8], BF16))
    xin = [ctx.enter_context(nc.sbuf_tensor(f"l0f_xin{i}", [128, D], F32)) for i in range(2)]
    ln_bf = ctx.enter_context(nc.sbuf_tensor("l0f_ln", [128, D], BF16))
    LT = ctx.enter_context(nc.sbuf_tensor("l0f_LT", [128, 8 * 129], BF16))
    xrT = ctx.enter_context(nc.sbuf_tensor("l0f_xrT", [128, D], BF16))
    xkT = ctx.enter_context(nc.sbuf_tensor("l0f_xkT", [128, D], BF16))
    r_f = ctx.enter_context(nc.sbuf_tensor("l0f_r", [128, D], F32))
    relu_f = ctx.enter_context(nc.sbuf_tensor("l0f_relu", [128, D], F32))
    ka_bf = ctx.enter_context(nc.sbuf_tensor("l0f_ka", [128, D], BF16))
    kaT = ctx.enter_context(nc.sbuf_tensor("l0f_kaT", [128, D], BF16))
    yb = ctx.enter_context(nc.sbuf_tensor("l0f_yb", [128, D], F32))
    y1 = ctx.enter_context(nc.sbuf_tensor("l0f_y1", [128, D], F32))
    wst = ctx.enter_context(nc.sbuf_tensor("l0f_wst", [128, 8], F32))
    tmpf = ctx.enter_context(nc.sbuf_tensor("l0f_tmpf", [128, D], F32))
    x2o = [ctx.enter_context(nc.sbuf_tensor(f"l0f_x2o{i}", [128, D], F32)) for i in range(2)]

    pA = ctx.enter_context(nc.psum_tensor("l0f_pA", [128, D], F32))
    pB = ctx.enter_context(nc.psum_tensor("l0f_pB", [128, D], F32))
    pt = ctx.enter_context(nc.psum_tensor("l0f_pt", [128, D], BF16))

    for w_s, name in ((wr, "l0f_wrt"), (wk2, "l0f_wkt"), (wv2, "l0f_wvt")):
        src = c[name].rearrange("(c p) n -> p c n", p=128)
        dst = w_s[:, :].rearrange("p (c n) -> p c n", c=8)
        pb.dma("gpsimd", dst, src, "wl")
    pb.dma("gpsimd", mur[:, :], c["l0f_mur"], "wl")
    pb.dma("gpsimd", muk[:, :], c["l0f_muk"], "wl")
    pb.dma("gpsimd", zr[:, :], c["zeros8"], "wl")
    wl_all = pb.now("wl")
    pb.wait("vector", wl_all)
    pb.do("vector", lambda e: e.tensor_scalar_sub(omr[:, :], mur[:, :], 1.0))
    pb.do("vector", lambda e: e.tensor_scalar_mul(omr[:, :], omr[:, :], -1.0))
    pb.do("vector", lambda e: e.tensor_scalar_sub(omk[:, :], muk[:, :], 1.0))
    dv_om = pb.do("vector", lambda e: e.tensor_scalar_mul(omk[:, :], omk[:, :], -1.0), inc="dv")

    LT3 = LT[:, :].rearrange("p (c n) -> p c n", c=8)
    ld_x = {}
    for t in range(min(2, NT)):
        ld_x[t] = pb.dma("sync", xin[t % 2][:, :], x_in[128 * t : 128 * t + 128, :], f"lx{t % 2}")
    st_o = {}
    pe_v = pe_r = pe_k = None
    ac_rl = None
    dv_fin = {}
    for t in range(NT):
        bt = t % 2
        pb.wait("vector", ld_x[t])
        pb.wait("scalar", ld_x[t])
        apply_v = emit_ln(pb, c, xin[bt][:, :], ln_bf[:, :], tmpf[:, :], wst)
        if t == 0:
            dv_pc = pb.do("vector", lambda e: e.tensor_copy(LT3[:, :, 0], zr[:, :]), inc="dv")
        else:
            dv_pc = pb.do("vector", lambda e: e.tensor_copy(LT3[:, :, 0], LT3[:, :, 128]), inc="dv")
        pb.wait("tensor", apply_v)
        pb.wait("tensor", dv_pc)
        if t >= 1:
            pb.wait("tensor", pe_k)
        pe_tr = emit_transpose8(pb, c, ln_bf, pt)
        pb.wait("vector", pe_tr)
        dv_LT = pb.do(
            "vector",
            lambda e: e.tensor_copy(
                LT3[:, :, 1:129], pt[:, :].rearrange("p (c n) -> p c n", c=8)
            ),
            inc="dv",
        )
        if t == 0:
            pb.wait("vector", dv_om)
        for cc in range(8):
            pb.do(
                "vector",
                lambda e, cc=cc: e.tensor_scalar(
                    xrT[:, 128 * cc : 128 * cc + 128], LT3[:, cc, 1:129],
                    mur[:, cc : cc + 1], None, op0=ALU.mult,
                ),
            )
            pb.do(
                "vector",
                lambda e, cc=cc: e.scalar_tensor_tensor(
                    out=xrT[:, 128 * cc : 128 * cc + 128], in0=LT3[:, cc, 0:128],
                    scalar=omr[:, cc : cc + 1], in1=xrT[:, 128 * cc : 128 * cc + 128],
                    op0=ALU.mult, op1=ALU.add,
                ),
            )
            pb.do(
                "vector",
                lambda e, cc=cc: e.tensor_scalar(
                    xkT[:, 128 * cc : 128 * cc + 128], LT3[:, cc, 1:129],
                    muk[:, cc : cc + 1], None, op0=ALU.mult,
                ),
            )
            pb.do(
                "vector",
                lambda e, cc=cc: e.scalar_tensor_tensor(
                    out=xkT[:, 128 * cc : 128 * cc + 128], in0=LT3[:, cc, 0:128],
                    scalar=omk[:, cc : cc + 1], in1=xkT[:, 128 * cc : 128 * cc + 128],
                    op0=ALU.mult, op1=ALU.add,
                ),
            )
        dv_mix = pb.do("vector", lambda e: e.tensor_copy(wst[:, 0:1], omr[:, 0:1]), inc="dv")
        if DEV_DUMP and t == 0:
            pb.wait("gpsimd", dv_mix)
            pb.dma("gpsimd", c["scr_dbg"][0:64, 0:1024], xrT[0:64, :], "gdbg1")
            pb.dma("gpsimd", c["scr_dbg"][0:64, 1024:2048], xkT[0:64, :], "gdbg2")
            pb.dma("gpsimd", c["scr_dbg"][0:64, 2048:3072], LT[0:64, 0:1024], "gdbg3")
        pb.wait("tensor", dv_mix)
        if t == 0:
            pb.wait("tensor", wl_all)
        if pe_v is not None:
            pb.wait("tensor", pb.now("dv"))
        pe_r = emit_proj(pb, c, xrT, wr, None, pA)
        if ac_rl is not None:
            pb.wait("tensor", ac_rl)
        pe_k = emit_proj(pb, c, xkT, wk2, None, pB)
        pb.wait("scalar", pe_r)
        ac_sg = pb.do(
            "scalar",
            lambda e: e.activation(out=r_f[:, :], in_=pA[:, :], func=AF.Sigmoid),
            inc="ac",
        )
        pb.wait("scalar", pe_k)
        ac_rl = pb.do(
            "scalar",
            lambda e: e.activation(out=relu_f[:, :], in_=pB[:, :], func=AF.Relu),
            inc="ac",
        )
        pb.wait("vector", ac_rl)
        dv_ka = pb.do("vector", lambda e: e.tensor_mul(ka_bf[:, :], relu_f[:, :], relu_f[:, :]), inc="dv")
        if DEV_DUMP and t == 0:
            pb.wait("sync", ac_sg)
            pb.dma("sync", c["scr_dbg"][0:64, 3072:4096], r_f[0:64, :], "dbg4")
            pb.wait("sync", dv_ka)
            pb.dma("sync", c["scr_dbg"][0:64, 4096:5120], relu_f[0:64, :], "dbg5")
        pb.wait("tensor", dv_ka)
        pe_tr2 = emit_transpose8(pb, c, ka_bf, pt)
        pb.wait("vector", pe_tr2)
        dv_kaT = pb.do("vector", lambda e: e.tensor_copy(kaT[:, :], pt[:, :]), inc="dv")
        pb.wait("tensor", dv_kaT)
        pb.wait("tensor", ac_sg)
        pe_v = emit_proj(pb, c, kaT, wv2, None, pA)
        pb.wait("vector", pe_v)
        dv_y = pb.do("vector", lambda e: e.tensor_mul(yb[:, :], r_f[:, :], pA[:, :]), inc="dv")
        pb.wait("scalar", dv_y)
        emit_ln(pb, c, yb[:, :], y1[:, :], tmpf[:, :], wst)
        if t >= 2:
            pb.wait("vector", st_o[t - 2])
        dv_fin[t] = pb.do(
            "vector", lambda e, bt=bt: e.tensor_add(x2o[bt][:, :], xin[bt][:, :], y1[:, :]), inc="dv"
        )
        pb.wait("sync", dv_fin[t])
        st_o[t] = pb.dma("sync", scr_out[128 * t : 128 * t + 128, :], x2o[bt][:, :], f"so{bt}")
        if t + 2 < NT:
            ld_x[t + 2] = pb.dma("sync", xin[bt][:, :], x_in[128 * (t + 2) : 128 * (t + 2) + 128, :], f"lx{bt}")

    pb.barrier()



def emit_l1_attn(pb, c, ctx, T, x_in, scr_q, scr_out):
    nc = pb.nc
    NT = T // 128
    NKB = T // 128

    xin = [ctx.enter_context(nc.sbuf_tensor(f"l1_xin{i}", [128, D], F32)) for i in range(2)]
    ln_bf = ctx.enter_context(nc.sbuf_tensor("l1_ln", [128, D], BF16))
    lnT = ctx.enter_context(nc.sbuf_tensor("l1_lnT", [128, D], BF16))
    wst = ctx.enter_context(nc.sbuf_tensor("l1_wst", [128, 8], F32))
    kT = ctx.enter_context(nc.sbuf_tensor("l1_kT", [128, 8 * T], BF16))
    V = ctx.enter_context(nc.sbuf_tensor("l1_V", [128, T * 8], BF16))
    q_bf = [ctx.enter_context(nc.sbuf_tensor(f"l1_q{i}", [128, D], BF16)) for i in range(2)]
    probs = ctx.enter_context(nc.sbuf_tensor("l1_probs", [128, T], BF16))
    pTs = ctx.enter_context(nc.sbuf_tensor("l1_pTs", [128, 1024], BF16))
    rs = ctx.enter_context(nc.sbuf_tensor("l1_rs", [128, 4], F32))
    ctxb = ctx.enter_context(nc.sbuf_tensor("l1_ctx", [128, D], BF16))
    y1 = ctx.enter_context(nc.sbuf_tensor("l1_y1", [128, D], F32))
    x3o = [ctx.enter_context(nc.sbuf_tensor(f"l1_x3o{i}", [128, D], F32)) for i in range(2)]

    kT3 = kT[:, :].rearrange("p (g n) -> p g n", g=8)
    V3 = V[:, :].rearrange("p (t n) -> p t n", t=NT)

    ph1 = ExitStack()
    pt = ph1.enter_context(nc.psum_tensor("l1_pt", [128, D], BF16))
    pA = ph1.enter_context(nc.psum_tensor("l1_pA", [128, D], F32))
    pB = ph1.enter_context(nc.psum_tensor("l1_pB", [128, D], F32))
    wq = ph1.enter_context(nc.sbuf_tensor("l1_wqs", [128, 8 * 1024], BF16))
    wk = ph1.enter_context(nc.sbuf_tensor("l1_wks", [128, 8 * 1024], BF16))
    bq = ph1.enter_context(nc.sbuf_tensor("l1_bqs", [1, D], BF16))
    bk = ph1.enter_context(nc.sbuf_tensor("l1_bks", [1, D], BF16))

    for w_s, name in ((wq, "l1_wq"), (wk, "l1_wk")):
        src = c[name].rearrange("(c p) n -> p c n", p=128)
        dst = w_s[:, :].rearrange("p (c n) -> p c n", c=8)
        pb.dma("gpsimd", dst, src, "wl")
    pb.dma("gpsimd", bq[:, :], c["l1_bq"], "wl")
    pb.dma("gpsimd", bk[:, :], c["l1_bk"], "wl")
    wl_all = pb.now("wl")

    ld_x = {}
    for t in range(min(2, NT)):
        ld_x[t] = pb.dma("sync", xin[t % 2][:, :], x_in[128 * t : 128 * t + 128, :], f"lx{t % 2}")
    st_q = {}
    ac_q = None
    dv_kT = None
    for t in range(NT):
        bt = t % 2
        pb.wait("vector", ld_x[t])
        pb.wait("scalar", ld_x[t])
        apply_v = emit_ln(pb, c, xin[bt][:, :], ln_bf[:, :], probs[:, 0:1024], wst)
        if t + 2 < NT:
            pb.wait("sync", apply_v)
            ld_x[t + 2] = pb.dma("sync", xin[bt][:, :], x_in[128 * (t + 2) : 128 * (t + 2) + 128, :], f"lx{bt}")
        pb.wait("tensor", apply_v)
        if dv_kT is not None:
            pb.wait("tensor", dv_kT)
        pe_tr = emit_transpose8(pb, c, ln_bf, pt)
        pb.wait("vector", pe_tr)
        dv_lnT = pb.do("vector", lambda e: e.tensor_copy(lnT[:, :], pt[:, :]), inc="dv")
        pb.wait("tensor", dv_lnT)
        if t == 0:
            pb.wait("tensor", wl_all)
        if ac_q is not None:
            pb.wait("tensor", ac_q)
        pe_q = emit_proj(pb, c, lnT, wq, bq, pA)
        pe_k = emit_proj(pb, c, lnT, wk, bk, pB)
        pb.wait("scalar", pe_q)
        if t >= 2:
            pb.wait("scalar", st_q[t - 2])
        ac_q = pb.do(
            "scalar",
            lambda e, bt=bt: e.activation(out=q_bf[bt][:, :], in_=pA[:, :], func=AF.Identity),
            inc="ac",
        )
        pb.wait("sync", ac_q)
        st_q[t] = pb.dma("sync", scr_q[128 * t : 128 * t + 128, :], q_bf[bt][:, :], f"sq{bt}")
        pb.wait("vector", pe_k)
        dv_kb = pb.do("vector", lambda e: e.tensor_copy(ln_bf[:, :], pB[:, :]), inc="dv")
        pb.wait("tensor", dv_kb)
        pe_tr2 = emit_transpose8(pb, c, ln_bf, pt)
        pb.wait("vector", pe_tr2)
        dv_kT = pb.do(
            "vector",
            lambda e, t=t: e.tensor_copy(
                kT3[:, :, 128 * t : 128 * t + 128],
                pt[:, :].rearrange("p (c n) -> p c n", c=8),
            ),
            inc="dv",
        )
    pb.barrier()

    wv = wq
    bv = bq
    src = c["l1_wv"].rearrange("(c p) n -> p c n", p=128)
    pb.dma("gpsimd", wv[:, :].rearrange("p (c n) -> p c n", c=8), src, "wl")
    pb.dma("gpsimd", bv[:, :], c["l1_bv"], "wl")
    wl_all = pb.now("wl")
    for t in range(min(2, NT)):
        ld_x[t] = pb.dma("sync", xin[t % 2][:, :], x_in[128 * t : 128 * t + 128, :], f"lx{t % 2}")
    dv_V = None
    for t in range(NT):
        bt = t % 2
        pb.wait("vector", ld_x[t])
        pb.wait("scalar", ld_x[t])
        apply_v = emit_ln(pb, c, xin[bt][:, :], ln_bf[:, :], probs[:, 0:1024], wst)
        if t + 2 < NT:
            pb.wait("sync", apply_v)
            ld_x[t + 2] = pb.dma("sync", xin[bt][:, :], x_in[128 * (t + 2) : 128 * (t + 2) + 128, :], f"lx{bt}")
        pb.wait("tensor", apply_v)
        if dv_V is not None:
            pb.wait("tensor", dv_V)
        pe_tr = emit_transpose8(pb, c, ln_bf, pt)
        pb.wait("vector", pe_tr)
        dv_lnT = pb.do("vector", lambda e: e.tensor_copy(lnT[:, :], pt[:, :]), inc="dv")
        pb.wait("tensor", dv_lnT)
        if t == 0:
            pb.wait("tensor", wl_all)
        pe_v = emit_proj(pb, c, lnT, wv, bv, pA)
        pb.wait("vector", pe_v)
        dv_V = pb.do(
            "vector",
            lambda e, t=t: e.tensor_copy(V3[:, t, :], pA[:, :]),
            inc="dv",
        )
    pb.barrier()

    pb.flush(pb.block)
    ph1.close()
    QS = 256
    NQS = T // QS
    KHALF = T // 2
    pST = ctx.enter_context(nc.psum_tensor("l1_pST", [128, 2048], F32))
    pCtx = ctx.enter_context(nc.psum_tensor("l1_pCtx", [128, 2048], F32))
    qTs = ctx.enter_context(nc.sbuf_tensor("l1_qTs", [128, 8 * QS], BF16))
    Ebuf = [ctx.enter_context(nc.sbuf_tensor(f"l1_Ebuf{i}", [128, 8 * QS], BF16)) for i in range(2)]
    Pbuf = [ctx.enter_context(nc.sbuf_tensor(f"l1_Pbuf{i}", [128, 8 * QS], BF16)) for i in range(2)]
    cTs = ctx.enter_context(nc.sbuf_tensor("l1_cTs", [128, 8 * QS], BF16))
    Dred = [ctx.enter_context(nc.sbuf_tensor(f"l1_D{i}", [128, QS], F32)) for i in range(2)]
    Dinv = [ctx.enter_context(nc.sbuf_tensor(f"l1_Dinv{i}", [128, QS], BF16)) for i in range(2)]
    Dln = [ctx.enter_context(nc.sbuf_tensor(f"l1_Dln{i}", [128, QS], F32)) for i in range(2)]

    ld_q = {}
    ld_x = {}
    for t in range(min(2, 2 * NQS)):
        ld_q[t] = pb.dma("sync", q_bf[t % 2][:, :], scr_q[128 * t : 128 * t + 128, :], f"lq{t % 2}")
        ld_x[t] = pb.dma("sync", xin[t % 2][:, :], x_in[128 * t : 128 * t + 128, :], f"lx{t % 2}")
    st_o = {}
    dv_fin = {}
    ac_exp = None
    dv_ct2 = None
    pvk_prev = None
    dv_qT = None
    pvk_kb = {}
    dv_P_ms = {}

    def emit_pv(kb):
        kp = kb % 2
        pb.wait("tensor", dv_P_ms[kb])
        if kb == 0 and dv_ct2 is not None:
            pb.wait("tensor", dv_ct2)

        def pvfn(e, kb=kb, kp=kp):
            ins = None
            for g in range(8):
                ins = e.matmul(
                    pCtx[:, 256 * g : 256 * g + 256],
                    V3[:, kb, 128 * g : 128 * g + 128],
                    Pbuf[kp][:, 256 * g : 256 * g + 256],
                    start=(kb == 0 and g % 2 == 0),
                    stop=(kb == NKB - 1 and g % 2 == 1),
                )
            return ins

        pvk_kb[kb] = pb.do("tensor", pvfn, inc="pe")

    for qs in range(NQS):
        ms_tq = []
        for sub in range(2):
            t = 2 * qs + sub
            bt = t % 2
            pb.wait("sync", ld_q[t])
            if pvk_prev is not None:
                pb.wait("sync", pvk_prev)
            pb.dma_sem_eng.setdefault(f"tq{sub}", "sync")
            for g in range(8):
                dv_qT = pb.do(
                    "sync",
                    lambda e, g=g, sub=sub, bt=bt: e.dma_start_transpose(
                        qTs[:, 256 * g + 128 * sub : 256 * g + 128 * sub + 128],
                        q_bf[bt][:, 128 * g : 128 * g + 128],
                    ),
                    inc=f"tq{sub}", n=16,
                )
            ms_tq.append(dv_qT)
        dv_P_ms.clear()
        ac_exp_ms = {}
        pend_kb = None
        for kb in range(NKB):
            kp = kb % 2
            if kb == 0:
                for ms in ms_tq:
                    pb.wait("tensor", ms)
                pb.wait("tensor", pb.now("ac"))
            if kb >= 1:
                pb.wait("tensor", ac_exp_ms[kb - 1])

            def stfn(e, kb=kb):
                ins = None
                for g in range(8):
                    ins = e.matmul(
                        pST[:, 256 * g : 256 * g + 256],
                        kT3[:, g, 128 * kb : 128 * kb + 128],
                        qTs[:, 256 * g : 256 * g + 256],
                        start=True, stop=True,
                    )
                return ins

            pe_st = pb.do("tensor", stfn, inc="pe")
            pb.wait("scalar", pe_st)
            if kb >= 2:
                pb.wait("scalar", dv_P_ms[kb - 2])
            else:
                pb.wait("scalar", pb.now("dv"))
            ac_exp_ms[kb] = pb.do(
                "scalar",
                lambda e, kp=kp: e.activation(
                    out=Ebuf[kp][:, :], in_=pST[:, :], func=AF.Exp,
                    bias=c["c30"][:, 0:1], scale=1.0 / 32.0,
                ),
                inc="ac",
            )
            pb.wait("vector", ac_exp_ms[kb])
            dvred = pb.do(
                "vector",
                lambda e, kp=kp: e.reduce_sum(
                    Dred[kp][:, :],
                    Ebuf[kp][:, :].rearrange("p (g q) -> p q g", g=8),
                    axis=AX.X,
                ),
                inc="dv",
            )
            pb.wait("scalar", dvred)

            def _lnf(e, kp=kp):
                return e.activation(out=Dln[kp][:, :], in_=Dred[kp][:, :], func=AF.Ln)

            def _exf(e, kp=kp):
                with nc.allow_low_precision(reason="Dinv feeds a bf16 matmul operand"):
                    return e.activation(out=Dinv[kp][:, :], in_=Dln[kp][:, :], func=AF.Exp, scale=-1.0)

            pb.do("scalar", _lnf)
            dvr = pb.do("scalar", _exf, inc="ac")
            pb.wait("vector", dvr)
            if kb >= 2:
                pb.wait("vector", pvk_kb[kb - 2])
            elif pvk_prev is not None:
                pb.wait("vector", pvk_prev)
            dv_P_ms[kb] = pb.do(
                "vector",
                lambda e, kp=kp: e.tensor_tensor(
                    out=Pbuf[kp][:, :].rearrange("p (g q) -> p g q", g=8),
                    in0=Ebuf[kp][:, :].rearrange("p (g q) -> p g q", g=8),
                    in1=Dinv[kp][:, :].unsqueeze(1).to_broadcast([128, 8, QS]),
                    op=ALU.mult,
                ),
                inc="dv",
            )
            if pend_kb is not None:
                emit_pv(pend_kb)
            pend_kb = kb
        emit_pv(pend_kb)
        pvk_prev = pvk_kb[NKB - 1]
        pb.wait("vector", pvk_prev)
        dv_ce = pb.do("vector", lambda e: e.tensor_copy(cTs[:, :], pCtx[:, :]), inc="dv")
        dv_ct2 = None
        for sub in range(2):
            t = 2 * qs + sub
            bt = t % 2
            pb.wait("sync", dv_ce)
            if t >= 1 and dv_fin.get(t - 1) is not None:
                pb.wait("sync", dv_fin[t - 1])
            pb.dma_sem_eng.setdefault(f"tc{sub}", "sync")
            for g in range(8):
                bt_ms = pb.do(
                    "sync",
                    lambda e, g=g, sub=sub: e.dma_start_transpose(
                        ctxb[:, 128 * g : 128 * g + 128],
                        cTs[:, 256 * g + 128 * sub : 256 * g + 128 * sub + 128],
                    ),
                    inc=f"tc{sub}", n=16,
                )
            dv_ct2 = bt_ms
            pb.wait("scalar", bt_ms)
            pb.wait("vector", bt_ms)
            emit_ln(pb, c, ctxb[:, :], y1[:, :], probs[:, 0:1024], wst)
            pb.wait("vector", ld_x[t])
            if t >= 2:
                pb.wait("vector", st_o[t - 2])
            dv_fin[t] = pb.do(
                "vector", lambda e, bt=bt: e.tensor_add(x3o[bt][:, :], xin[bt][:, :], y1[:, :]), inc="dv"
            )
            pb.wait("sync", dv_fin[t])
            st_o[t] = pb.dma("sync", scr_out[128 * t : 128 * t + 128, :], x3o[bt][:, :], f"so{bt}")
            if t + 2 < 2 * NQS:
                ld_q[t + 2] = pb.dma("sync", q_bf[bt][:, :], scr_q[128 * (t + 2) : 128 * (t + 2) + 128, :], f"lq{bt}")
                ld_x[t + 2] = pb.dma("sync", xin[bt][:, :], x_in[128 * (t + 2) : 128 * (t + 2) + 128, :], f"lx{bt}")

    pb.barrier()



def emit_l1_moe(pb, c, ctx, T, x_in, out_t):
    nc = pb.nc
    NT = T // 128
    NH = 2
    TH = T // NH
    NTH = TH // 128
    NE_ALL = 33

    xin = [ctx.enter_context(nc.sbuf_tensor(f"moe_xin{i}", [128, D], F32)) for i in range(2)]
    ln_bf = ctx.enter_context(nc.sbuf_tensor("moe_ln", [128, D], BF16))
    wst = ctx.enter_context(nc.sbuf_tensor("moe_wst", [128, 8], F32))
    tmpf = ctx.enter_context(nc.sbuf_tensor("moe_tmpf", [128, D], F32))
    gwT = ctx.enter_context(nc.sbuf_tensor("moe_gwTs", [128, 8 * 32], BF16))
    flatT = ctx.enter_context(nc.sbuf_tensor("moe_flatT", [128, 8 * TH], BF16))
    comb = ctx.enter_context(nc.sbuf_tensor("moe_comb", [128, 32 * NTH], F32))
    scor = ctx.enter_context(nc.sbuf_tensor("moe_scor", [128, 32], F32))
    gs8 = ctx.enter_context(nc.sbuf_tensor("moe_gs8", [128, 8], F32))
    top8 = ctx.enter_context(nc.sbuf_tensor("moe_top8", [128, 8], F32))
    msk = ctx.enter_context(nc.sbuf_tensor("moe_msk", [128, 32], F32))
    rsum = ctx.enter_context(nc.sbuf_tensor("moe_rsum", [128, 2], F32))
    acc = ctx.enter_context(nc.sbuf_tensor("moe_acc", [128, 1024 * NTH], F32))
    wg = [ctx.enter_context(nc.sbuf_tensor(f"moe_wgs{i}", [128, 8 * 512], BF16)) for i in range(2)]
    wu = [ctx.enter_context(nc.sbuf_tensor(f"moe_wus{i}", [128, 8 * 512], BF16)) for i in range(2)]
    wd = [ctx.enter_context(nc.sbuf_tensor(f"moe_wds{i}", [128, 4 * 1024], BF16)) for i in range(2)]
    hs = [ctx.enter_context(nc.sbuf_tensor(f"moe_hs{i}", [128, 512], F32)) for i in range(2)]
    h_bf = [ctx.enter_context(nc.sbuf_tensor(f"moe_h{i}", [128, 512], BF16)) for i in range(2)]
    hT = [ctx.enter_context(nc.sbuf_tensor(f"moe_hT{i}", [128, 512], BF16)) for i in range(2)]
    y1 = ctx.enter_context(nc.sbuf_tensor("moe_y1", [128, D], F32))
    oo = [ctx.enter_context(nc.sbuf_tensor(f"moe_oo{i}", [128, D], F32)) for i in range(2)]

    pA = [ctx.enter_context(nc.psum_tensor(f"moe_pA{i}", [128, 512], F32)) for i in range(2)]
    pB = [ctx.enter_context(nc.psum_tensor(f"moe_pB{i}", [128, 512], F32)) for i in range(2)]
    pt = ctx.enter_context(nc.psum_tensor("moe_pt", [128, 1024], BF16))
    pC = ctx.enter_context(nc.psum_tensor("moe_pC", [128, D], F32))
    pR = ctx.enter_context(nc.psum_tensor("moe_pR", [128, 32], F32))

    flatT3 = flatT[:, :].rearrange("p (c n) -> p c n", c=8)

    pb.dma("gpsimd", gwT[:, :].rearrange("p (c n) -> p c n", c=8),
           c["moe_gwt"].rearrange("(c p) n -> p c n", p=128), "wl")
    wl_gw = pb.now("wl")

    def load_expert(eidx, slot):
        if eidx < 32:
            wg_src = c["moe_wg"][eidx].rearrange("(c p) n -> p c n", p=128)
            wu_src = c["moe_wu"][eidx].rearrange("(c p) n -> p c n", p=128)
            wd_src = c["moe_wd"][eidx].rearrange("(c p) n -> p c n", p=128)
        else:
            wg_src = c["moe_sg"].rearrange("(c p) n -> p c n", p=128)
            wu_src = c["moe_su"].rearrange("(c p) n -> p c n", p=128)
            wd_src = c["moe_sd"].rearrange("(c p) n -> p c n", p=128)
        pb.dma("gpsimd", wg[slot][:, :].rearrange("p (c n) -> p c n", c=8), wg_src, f"we{slot}")
        pb.dma("gpsimd", wu[slot][:, :].rearrange("p (c n) -> p c n", c=8), wu_src, f"we{slot}")
        return pb.dma("gpsimd", wd[slot][:, :].rearrange("p (c n) -> p c n", c=4), wd_src, f"we{slot}")

    ld_x = {}
    st_o = {}
    for h in range(NH):
        t0 = h * NTH
        for t in range(min(2, NTH)):
            ld_x[t] = pb.dma("sync", xin[t % 2][:, :], x_in[128 * (t0 + t) : 128 * (t0 + t) + 128, :], f"lx{t % 2}")
        dv_fT = None
        for t in range(NTH):
            bt = t % 2
            pb.wait("vector", ld_x[t])
            pb.wait("scalar", ld_x[t])
            apply_v = emit_ln(pb, c, xin[bt][:, :], ln_bf[:, :], tmpf[:, :], wst)
            if t + 2 < NTH:
                pb.wait("sync", apply_v)
                ld_x[t + 2] = pb.dma("sync", xin[bt][:, :], x_in[128 * (t0 + t + 2) : 128 * (t0 + t + 2) + 128, :], f"lx{bt}")
            pb.wait("tensor", apply_v)
            if dv_fT is not None:
                pb.wait("tensor", dv_fT)
            pe_tr = emit_transpose8(pb, c, ln_bf, pt)
            pb.wait("vector", pe_tr)
            dv_fT = pb.do(
                "vector",
                lambda e, t=t: e.tensor_copy(
                    flatT3[:, :, 128 * t : 128 * t + 128],
                    pt[:, :].rearrange("p (c n) -> p c n", c=8),
                ),
                inc="dv",
            )
            pb.wait("tensor", dv_fT)
            pb.wait("tensor", pb.now("ac"))
            if t == 0 and h == 0:
                pb.wait("tensor", wl_gw)

            def lfn(e, t=t):
                ins = None
                for cc in range(8):
                    ins = e.matmul(
                        pR[:, 0:32],
                        flatT3[:, cc, 128 * t : 128 * t + 128],
                        gwT[:, 32 * cc : 32 * cc + 32],
                        start=(cc == 0), stop=(cc == 7),
                    )
                return ins

            pe_lg = pb.do("tensor", lfn, inc="pe")
            pb.wait("scalar", pe_lg)
            ac_sc = pb.do(
                "scalar",
                lambda e: e.activation(out=scor[:, :], in_=pR[:, 0:32], func=AF.Sigmoid),
                inc="ac",
            )
            pb.wait("vector", ac_sc)
            r1 = pb.do(
                "vector",
                lambda e: e.reduce_sum(
                    gs8[:, :], scor[:, :].rearrange("p (g i) -> p g i", g=8), axis=AX.X
                ),
                inc="dv",
            )
            pb.wait("vector", r1)
            r2 = pb.do("vector", lambda e: e.max(out=top8[:, :], in_=gs8[:, :]), inc="dv")
            pb.wait("vector", r2)
            r3 = pb.do(
                "vector",
                lambda e: e.tensor_scalar(msk[:, 0:8], gs8[:, :], top8[:, 3:4], None, op0=ALU.is_ge),
                inc="dv",
            )
            pb.wait("vector", r3)
            r4 = pb.do(
                "vector",
                lambda e: e.tensor_tensor(
                    out=msk[:, :].rearrange("p (g i) -> p g i", g=8),
                    in0=scor[:, :].rearrange("p (g i) -> p g i", g=8),
                    in1=msk[:, 0:8].unsqueeze(2).to_broadcast([128, 8, 4]),
                    op=ALU.mult,
                ),
                inc="dv",
            )
            pb.wait("vector", r4)
            r5 = pb.do("vector", lambda e: e.max(out=top8[:, :], in_=msk[:, :]), inc="dv")
            pb.wait("vector", r5)
            r6 = pb.do(
                "vector",
                lambda e, t=t: e.tensor_scalar(
                    comb[:, 32 * t : 32 * t + 32], msk[:, :], top8[:, 7:8], None, op0=ALU.is_ge
                ),
                inc="dv",
            )
            pb.wait("vector", r6)
            r7 = pb.do(
                "vector",
                lambda e, t=t: e.tensor_mul(
                    comb[:, 32 * t : 32 * t + 32], comb[:, 32 * t : 32 * t + 32], msk[:, :]
                ),
                inc="dv",
            )
            pb.wait("vector", r7)
            r8 = pb.do(
                "vector",
                lambda e, t=t: e.reduce_sum(rsum[:, 0:1], comb[:, 32 * t : 32 * t + 32], axis=AX.X),
                inc="dv",
            )
            pb.wait("vector", r8)
            r9 = pb.do("vector", lambda e: e.tensor_scalar_max(rsum[:, 0:1], rsum[:, 0:1], 1e-12), inc="dv")
            pb.wait("vector", r9)
            r10 = pb.do("vector", lambda e: e.reciprocal(rsum[:, 1:2], rsum[:, 0:1]), inc="dv")
            pb.wait("vector", r10)
            pb.do(
                "vector",
                lambda e, t=t: e.tensor_scalar(
                    comb[:, 32 * t : 32 * t + 32], comb[:, 32 * t : 32 * t + 32],
                    rsum[:, 1:2], None, op0=ALU.mult,
                ),
                inc="dv",
            )
        pb.barrier()

        ld_w = {}
        ld_w[0] = load_expert(32, 0)
        ld_w[1] = load_expert(0, 1)
        seq = [(ei, t) for ei in range(NE_ALL) for t in range(NTH)]
        ac_si_ms, dv_h_ms, dv_hT_ms, dv_acc_ms, pe_dn_ms = {}, {}, {}, {}, {}
        pend = None

        def emit_tail(idx, ei, t, eidx, s2, slot):
            pb.wait("tensor", dv_h_ms[idx])
            if idx >= 1:
                pb.wait("tensor", dv_hT_ms[idx - 1])

            def tfn(e, s2=s2):
                ins = None
                for j in range(4):
                    ins = e.transpose(
                        pt[:, 128 * j : 128 * j + 128],
                        h_bf[s2][:, 128 * j : 128 * j + 128],
                        c["ident_bf"][:, :],
                    )
                return ins

            pe_th = pb.do("tensor", tfn, inc="pe")
            pb.wait("vector", pe_th)
            dv_hT_ms[idx] = pb.do(
                "vector", lambda e, s2=s2: e.tensor_copy(hT[s2][:, :], pt[:, 0:512]), inc="dv"
            )
            pb.wait("tensor", dv_hT_ms[idx])
            if idx >= 1:
                pb.wait("tensor", dv_acc_ms[idx - 1])

            def dfn(e, s2=s2, slot=slot):
                ins = None
                for j in range(4):
                    for half in range(2):
                        ins = e.matmul(
                            pC[:, 512 * half : 512 * half + 512],
                            hT[s2][:, 128 * j : 128 * j + 128],
                            wd[slot][:, 1024 * j + 512 * half : 1024 * j + 512 * half + 512],
                            start=(j == 0), stop=(j == 3),
                        )
                return ins

            pe_dn_ms[idx] = pb.do("tensor", dfn, inc="pe")
            pb.wait("vector", pe_dn_ms[idx])
            if ei == 0:
                dv_acc_ms[idx] = pb.do(
                    "vector",
                    lambda e, t=t: e.tensor_copy(acc[:, 1024 * t : 1024 * t + 1024], pC[:, :]),
                    inc="dv",
                )
            else:
                dv_acc_ms[idx] = pb.do(
                    "vector",
                    lambda e, t=t, eidx=eidx: e.scalar_tensor_tensor(
                        out=acc[:, 1024 * t : 1024 * t + 1024], in0=pC[:, :],
                        scalar=comb[:, 32 * t + eidx : 32 * t + eidx + 1],
                        in1=acc[:, 1024 * t : 1024 * t + 1024],
                        op0=ALU.mult, op1=ALU.add,
                    ),
                    inc="dv",
                )
            if t == NTH - 1 and ei + 2 < NE_ALL:
                pb.wait("gpsimd", pe_dn_ms[idx])
                ld_w[ei + 2] = load_expert(ei + 1, slot)

        for idx, (ei, t) in enumerate(seq):
            slot = ei % 2
            s2 = idx % 2
            eidx = 32 if ei == 0 else ei - 1
            if t == 0:
                pb.wait("tensor", ld_w[ei])
            if idx >= 2:
                pb.wait("tensor", ac_si_ms[idx - 2])
                pb.wait("tensor", dv_h_ms[idx - 2])

            def gfn(e, t=t, slot=slot, s2=s2):
                ins = None
                for cc in range(8):
                    ins = e.matmul(
                        pA[s2][:, 0:512],
                        flatT3[:, cc, 128 * t : 128 * t + 128],
                        wg[slot][:, 512 * cc : 512 * cc + 512],
                        start=(cc == 0), stop=(cc == 7),
                    )
                return ins

            def ufn(e, t=t, slot=slot, s2=s2):
                ins = None
                for cc in range(8):
                    ins = e.matmul(
                        pB[s2][:, 0:512],
                        flatT3[:, cc, 128 * t : 128 * t + 128],
                        wu[slot][:, 512 * cc : 512 * cc + 512],
                        start=(cc == 0), stop=(cc == 7),
                    )
                return ins

            pe_g = pb.do("tensor", gfn, inc="pe")
            pe_u = pb.do("tensor", ufn, inc="pe")
            pb.wait("scalar", pe_g)
            if idx >= 2:
                pb.wait("scalar", dv_h_ms[idx - 2])
            ac_si_ms[idx] = pb.do(
                "scalar",
                lambda e, s2=s2: e.activation(out=hs[s2][:, :], in_=pA[s2][:, :], func=AF.Silu),
                inc="ac",
            )
            pb.wait("vector", ac_si_ms[idx])
            pb.wait("vector", pe_u)
            dv_h_ms[idx] = pb.do(
                "vector",
                lambda e, s2=s2: e.tensor_mul(h_bf[s2][:, :], hs[s2][:, :], pB[s2][:, :]),
                inc="dv",
            )
            if pend is not None:
                emit_tail(*pend)
            pend = (idx, ei, t, eidx, s2, slot)
        emit_tail(*pend)
        pend = None
        pb.barrier()

        for t in range(min(2, NTH)):
            ld_x[t] = pb.dma("sync", xin[t % 2][:, :], x_in[128 * (t0 + t) : 128 * (t0 + t) + 128, :], f"lx{t % 2}")
        for t in range(NTH):
            bt = t % 2
            pb.wait("vector", ld_x[t])
            pb.wait("scalar", ld_x[t])
            emit_ln(pb, c, acc[:, 1024 * t : 1024 * t + 1024], y1[:, :], tmpf[:, :], wst)
            if t >= 2:
                pb.wait("vector", st_o[t - 2])
            dv_f = pb.do(
                "vector", lambda e, bt=bt: e.tensor_add(oo[bt][:, :], xin[bt][:, :], y1[:, :]), inc="dv"
            )
            pb.wait("sync", dv_f)
            st_o[t] = pb.dma("sync", out_t[128 * (t0 + t) : 128 * (t0 + t) + 128, :], oo[bt][:, :], f"so{bt}")
            if t + 2 < NTH:
                pb.wait("sync", dv_f)
                ld_x[t + 2] = pb.dma("sync", xin[bt][:, :], x_in[128 * (t0 + t + 2) : 128 * (t0 + t + 2) + 128, :], f"lx{bt}")
        pb.barrier()



def build_program(T=4096, dev=False, stages=("l0a", "l0f", "l1a", "moe"), n_sweeps=6, paranoid=True):
    nc = bass.Bass("TRN2", target_bir_lowering=False, detect_race_conditions=False)
    scratch_kind = "ExternalOutput" if dev else "Internal"

    c = {}
    c["x"] = nc.dram_tensor("x", [T, D], F32, kind="ExternalInput")[:, :]
    for nm, shp in [
        ("l0_wqt", [D, D]), ("l0_wkt", [D, D]), ("l0_wvt", [D, D]), ("l0_wot", [D, D]),
        ("l0_bq", [1, D]), ("l0_bk", [1, D]), ("l0_bv", [1, D]), ("l0_bo", [1, D]),
        ("l0f_wrt", [D, D]), ("l0f_wkt", [D, D]), ("l0f_wvt", [D, D]),
        ("l0f_mur", [128, 8]), ("l0f_muk", [128, 8]),
        ("l1_wq", [D, D]), ("l1_wk", [D, D]), ("l1_wv", [D, D]),
        ("l1_bq", [1, D]), ("l1_bk", [1, D]), ("l1_bv", [1, D]),
        ("moe_gwt", [D, 32]),
        ("moe_sg", [D, 512]), ("moe_su", [D, 512]), ("moe_sd", [512, D]),
    ]:
        c[nm] = nc.dram_tensor(nm, shp, F32, kind="ExternalInput")[:, :]
    c["moe_wg"] = nc.dram_tensor("moe_wg", [32, D, 512], F32, kind="ExternalInput")[:, :, :]
    c["moe_wu"] = nc.dram_tensor("moe_wu", [32, D, 512], F32, kind="ExternalInput")[:, :, :]
    c["moe_wd"] = nc.dram_tensor("moe_wd", [32, 512, D], F32, kind="ExternalInput")[:, :, :]
    identf = nc.dram_tensor("identf", [128, 128], F32, kind="ExternalInput")[:, :]
    identb = nc.dram_tensor("identb", [128, 128], BF16, kind="ExternalInput")[:, :]
    onesr = nc.dram_tensor("onesr", [1, 128], BF16, kind="ExternalInput")[:, :]
    onesc = nc.dram_tensor("onesc", [128, 1], BF16, kind="ExternalInput")[:, :]
    c["zeros8"] = nc.dram_tensor("zeros8", [128, 8], BF16, kind="ExternalInput")[:, :]

    scr_q = nc.dram_tensor("scr_q", [T, D], BF16, kind=scratch_kind)[:, :]
    scr_k = nc.dram_tensor("scr_k", [T, D], BF16, kind=scratch_kind)[:, :]
    scr_v = nc.dram_tensor("scr_v", [T, D], BF16, kind=scratch_kind)[:, :]
    scr_a = nc.dram_tensor("scr_a", [T, D], F32, kind="ExternalOutput" if dev else "Internal")[:, :]
    scr_b = nc.dram_tensor("scr_b", [T, D], F32, kind="ExternalOutput" if dev else "Internal")[:, :]
    out_t = nc.dram_tensor("out", [T, D], F32, kind="ExternalOutput")[:, :]
    scr_dbg = nc.dram_tensor("scr_dbg", [64, 8192], F32, kind="ExternalOutput" if dev else "Internal")[:, :]

    pb = PB(nc, paranoid=paranoid)
    with ExitStack() as ctx:
        block = ctx.enter_context(nc.Block())
        pb.block = block
        ident_f32 = nc.alloc_sbuf_tensor("g_identf", [128, 128], F32)
        ident_bf = nc.alloc_sbuf_tensor("g_identb", [128, 128], BF16)
        ones_row = nc.alloc_sbuf_tensor("g_onesr", [1, 128], BF16)
        ones_col = nc.alloc_sbuf_tensor("g_onesc", [128, 1], BF16)
        lneps = nc.alloc_sbuf_tensor("g_lneps", [128, 1], F32)
        c30 = nc.alloc_sbuf_tensor("g_c30", [128, 1], F32)
        pb.dma("sync", ident_f32[:, :], identf, "lc")
        pb.dma("sync", ident_bf[:, :], identb, "lc")
        pb.dma("sync", ones_row[:, :], onesr, "lc")
        ldc = pb.dma("sync", ones_col[:, :], onesc, "lc")
        pb.do("vector", lambda e: e.memset(lneps[:, 0:1], LNEPS))
        mz = pb.do("vector", lambda e: e.memset(c30[:, 0:1], -30.0), inc="dv")
        for eng in ("tensor", "vector", "scalar"):
            pb.wait(eng, ldc)
        pb.wait("scalar", mz)
        cc = dict(c)
        cc["lneps"] = lneps
        cc["c30"] = c30
        cc["ident_f32"] = ident_f32
        cc["ident_bf"] = ident_bf
        cc["ones_row"] = ones_row
        cc["ones_col"] = ones_col

        cc["scr_dbg"] = scr_dbg
        if "l0a" in stages:
            with ExitStack() as sctx:
                emit_l0_attn(pb, cc, sctx, T, c["x"], scr_q, scr_k, scr_v, scr_a, n_sweeps=n_sweeps)
                pb.flush(block)
        if "l0f" in stages:
            with ExitStack() as sctx:
                emit_l0_ffn(pb, cc, sctx, T, scr_a, scr_b)
                pb.flush(block)
        if "l1a" in stages:
            with ExitStack() as sctx:
                emit_l1_attn(pb, cc, sctx, T, scr_b, scr_q, scr_a)
                pb.flush(block)
        if "moe" in stages:
            with ExitStack() as sctx:
                emit_l1_moe(pb, cc, sctx, T, scr_a, out_t)
                pb.flush(block)

        pb.flush(block)
    return nc


def prep_inputs(**inp):
    tr = lambda a: np.ascontiguousarray(np.asarray(a, dtype=np.float32).T)
    f = lambda k: np.asarray(inp[k], dtype=np.float32)
    m = {
        "l0_wqt": tr(inp["L0_attn_Wq"]),
        "l0_wkt": tr(inp["L0_attn_Wk"]),
        "l0_wvt": tr(inp["L0_attn_Wv"]),
        "l0_wot": tr(inp["L0_attn_Wo"]),
        "l0_bq": f("L0_attn_bq").reshape(1, D),
        "l0_bk": f("L0_attn_bk").reshape(1, D),
        "l0_bv": f("L0_attn_bv").reshape(1, D),
        "l0_bo": f("L0_attn_bo").reshape(1, D),
        "l0f_wrt": tr(inp["L0_ffn_W_r"]),
        "l0f_wkt": tr(inp["L0_ffn_W_k"]),
        "l0f_wvt": tr(inp["L0_ffn_W_v"]),
        "l0f_mur": np.ascontiguousarray(f("L0_ffn_mu_r").reshape(8, 128).T),
        "l0f_muk": np.ascontiguousarray(f("L0_ffn_mu_k").reshape(8, 128).T),
        "l1_wq": np.ascontiguousarray(f("L1_attn_Wq")),
        "l1_wk": np.ascontiguousarray(f("L1_attn_Wk")),
        "l1_wv": np.ascontiguousarray(f("L1_attn_Wv")),
        "l1_bq": f("L1_attn_bq").reshape(1, D),
        "l1_bk": f("L1_attn_bk").reshape(1, D),
        "l1_bv": f("L1_attn_bv").reshape(1, D),
        "moe_gwt": tr(inp["L1_ffn_gate_w"]),
        "moe_wg": np.ascontiguousarray(np.transpose(f("L1_ffn_expert_gate"), (0, 2, 1))),
        "moe_wu": np.ascontiguousarray(np.transpose(f("L1_ffn_expert_up"), (0, 2, 1))),
        "moe_wd": np.ascontiguousarray(np.transpose(f("L1_ffn_expert_down"), (0, 2, 1))),
        "moe_sg": tr(inp["L1_ffn_shared_gate_w"]),
        "moe_su": tr(inp["L1_ffn_shared_up_w"]),
        "moe_sd": tr(inp["L1_ffn_shared_down_w"]),
        "identf": np.eye(128, dtype=np.float32),
        "identb": np.eye(128, dtype=ml_dtypes.bfloat16),
        "onesr": np.ones((1, 128), dtype=ml_dtypes.bfloat16),
        "onesc": np.ones((128, 1), dtype=ml_dtypes.bfloat16),
        "zeros8": np.zeros((128, 8), dtype=ml_dtypes.bfloat16),
    }
    return m


def kernel(**inp):
    x = np.asarray(inp["L0_attn_x"], dtype=np.float32)
    base = prep_inputs(**inp)
    nc = build_program(T=4096, dev=False)
    in_maps = []
    for b in range(8):
        mm = dict(base)
        mm["x"] = np.ascontiguousarray(x[b])
        in_maps.append(mm)
    res = run_bass_kernel_spmd(nc, in_maps, core_ids=list(range(8)))
    return np.stack([np.asarray(r["out"]) for r in res.results], axis=0)


# version 06
# speedup vs baseline: 1.4291x; 1.0496x over previous
import numpy as np
import ml_dtypes
from contextlib import ExitStack

import concourse.bass as bass
import concourse.mybir as mybir
from concourse.bass_utils import run_bass_kernel_spmd

F32 = mybir.dt.float32
BF16 = mybir.dt.bfloat16
AF = mybir.ActivationFunctionType
ALU = mybir.AluOpType
AX = mybir.AxisListType

D = 1024
F_CUT = 99
DEV_DUMP = False
EPS0 = 1e-6
LNEPS = 1e-5

ENGS = ("sync", "gpsimd", "vector", "scalar", "tensor")


class PB:

    ENG_ORD = {"vector": "zzv", "scalar": "zzs"}
    MILESTONE_ENG = {"dv": "vector", "ac": "scalar"}

    def __init__(self, nc, paranoid=False):
        self.nc = nc
        self.paranoid = paranoid
        self.streams = {e: [] for e in ENGS}
        self.cnt = {}
        self._sems = {}
        self.dma_sem_eng = {}

    def sem(self, name):
        if name not in self._sems:
            self._sems[name] = self.nc.alloc_semaphore(name)
            self.cnt[name] = 0
        return self._sems[name]

    def do(self, eng, fn, inc=None, n=1):
        if self.paranoid and eng in self.ENG_ORD:
            z = self.ENG_ORD[eng]
            sem = self.sem(z)
            prev = self.cnt[z]
            self.cnt[z] = prev + 1
            tgt = prev + 1
            if prev > 0:
                self.streams[eng].append(
                    lambda e, sem=sem, prev=prev: e.wait_ge(sem, prev)
                )

            def wrapped(e, fn=fn, sem=sem):
                fn(e).then_inc(sem, 1)

            self.streams[eng].append(wrapped)
            return (z, tgt)
        if inc is None:
            self.streams[eng].append(fn)
            return None
        sem = self.sem(inc)
        self.cnt[inc] += n
        tgt = self.cnt[inc]

        def wrapped(e, fn=fn, sem=sem, n=n):
            fn(e).then_inc(sem, n)

        self.streams[eng].append(wrapped)
        return (inc, tgt)

    def wait(self, eng, ms):
        if ms is None:
            return
        name, val = ms
        if val <= 0:
            return
        sem = self.sem(name)
        self.streams[eng].append(lambda e, sem=sem, val=val: e.wait_ge(sem, val))

    def dma(self, eng, out, in_, sem):
        prev = self.dma_sem_eng.get(sem)
        assert prev is None or prev == eng, f"dma sem {sem} used from {prev} and {eng}"
        self.dma_sem_eng[sem] = eng
        return self.do(
            eng, lambda e, out=out, in_=in_: e.dma_start(out=out, in_=in_), inc=sem, n=16
        )

    def now(self, name):
        if self.paranoid and name in self.MILESTONE_ENG:
            z = self.ENG_ORD[self.MILESTONE_ENG[name]]
            return (z, self.cnt.get(z, 0))
        return (name, self.cnt.get(name, 0))

    def barrier(self):
        bar = self.sem("bar")
        for s, eng in self.dma_sem_eng.items():
            self.wait(eng, (s, self.cnt[s]))
        self.cnt["bar"] += len(ENGS)
        tgt = self.cnt["bar"]
        for eng in ENGS:
            self.streams[eng].append(lambda e, bar=bar: e.sem_inc(bar, 1))
            self.streams[eng].append(lambda e, bar=bar, tgt=tgt: e.wait_ge(bar, tgt))

    def flush(self, block):
        for eng in ENGS:
            lst = self.streams[eng]
            if not lst:
                continue
            getattr(block, eng)(lambda e, lst=lst: [f(e) for f in lst])
            self.streams[eng] = []



def emit_ln(pb, c, xin_ap, out_ap, junk, wst, psum_in=False):
    pb.wait("scalar", pb.now("dv"))
    acs = pb.do(
        "scalar",
        lambda e: e.activation(
            out=junk, in_=xin_ap, func=AF.Square, accum_out=wst[:, 3:4]
        ),
        inc="ac",
    )
    if psum_in:
        pb.wait("vector", acs)
    r1 = pb.do("vector", lambda e: e.reduce_sum(wst[:, 2:3], xin_ap, axis=AX.X), inc="dv")
    pb.wait("vector", r1)
    pb.do("vector", lambda e: e.tensor_scalar_mul(wst[:, 1:2], wst[:, 2:3], 1.0 / D))
    pb.wait("vector", acs)
    r3 = pb.do("vector", lambda e: e.tensor_scalar_mul(wst[:, 4:5], wst[:, 3:4], 1.0 / D), inc="dv")
    pb.wait("vector", r3)
    r4 = pb.do("vector", lambda e: e.tensor_mul(wst[:, 5:6], wst[:, 1:2], wst[:, 1:2]), inc="dv")
    pb.wait("vector", r4)
    dvv = pb.do("vector", lambda e: e.tensor_sub(wst[:, 6:7], wst[:, 4:5], wst[:, 5:6]), inc="dv")
    pb.wait("scalar", dvv)
    acv = pb.do(
        "scalar",
        lambda e: e.activation(out=wst[:, 7:8], in_=wst[:, 6:7], func=AF.Sqrt, bias=c["lneps"][:, 0:1]),
        inc="ac",
    )
    pb.wait("vector", acv)
    dvr2 = pb.do("vector", lambda e: e.reciprocal(wst[:, 0:1], wst[:, 7:8]), inc="dv")
    pb.wait("vector", dvr2)
    vap = pb.do(
        "vector",
        lambda e: e.scalar_tensor_tensor(
            out=out_ap, in0=xin_ap, scalar=wst[:, 1:2],
            in1=wst[:, 0:1].to_broadcast([128, D]),
            op0=ALU.subtract, op1=ALU.mult,
        ),
        inc="dv",
    )
    return vap


def emit_proj(pb, c, lnT, w_sbuf, bias_row, psum_ap, n_chunks=8, nout=D):
    nh = (nout + 511) // 512

    def fn(e, lnT=lnT, w=w_sbuf, b=bias_row, p=psum_ap, n=n_chunks, nout=nout, nh=nh):
        ins = None
        for cc in range(n):
            lhs = lnT[:, 128 * cc : 128 * cc + 128]
            for half in range(nh):
                w0 = nout * cc + 512 * half
                nw = min(512, nout - 512 * half)
                ins = e.matmul(
                    p[:, 512 * half : 512 * half + nw],
                    lhs,
                    w[:, w0 : w0 + nw],
                    start=(cc == 0), stop=(cc == n - 1 and b is None),
                )
        if b is not None:
            for half in range(nh):
                nw = min(512, nout - 512 * half)
                ins = e.matmul(
                    p[:, 512 * half : 512 * half + nw],
                    c["ones_row"][0:1, :],
                    b[0:1, 512 * half : 512 * half + nw],
                    start=False, stop=True,
                )
        return ins

    return pb.do("tensor", fn, inc="pe")


def emit_transpose8(pb, c, src_bf, pt, n=8):
    def fn(e, s=src_bf, pt=pt, n=n):
        ins = None
        for cc in range(n):
            ins = e.transpose(
                pt[:, 128 * cc : 128 * cc + 128],
                s[:, 128 * cc : 128 * cc + 128],
                c["ident_bf"][:, :],
            )
        return ins

    return pb.do("tensor", fn, inc="pe")



def emit_l0_attn(pb, c, ctx, T, x_in, scr_q, scr_k, scr_v, scr_out, n_sweeps=6):
    nc = pb.nc
    NT = T // 128

    wq = ctx.enter_context(nc.sbuf_tensor("l0a_wq", [128, 8 * 1024], BF16))
    wk = ctx.enter_context(nc.sbuf_tensor("l0a_wk", [128, 8 * 1024], BF16))
    wv = ctx.enter_context(nc.sbuf_tensor("l0a_wv", [128, 8 * 1024], BF16))
    wo = ctx.enter_context(nc.sbuf_tensor("l0a_wo", [128, 8 * 1024], BF16))
    bq = ctx.enter_context(nc.sbuf_tensor("l0a_bq", [1, D], BF16))
    bk = ctx.enter_context(nc.sbuf_tensor("l0a_bk", [1, D], BF16))
    bv = ctx.enter_context(nc.sbuf_tensor("l0a_bv", [1, D], BF16))
    bo = ctx.enter_context(nc.sbuf_tensor("l0a_bo", [1, D], BF16))
    xin = [ctx.enter_context(nc.sbuf_tensor(f"l0a_xin{i}", [128, D], F32)) for i in range(2)]
    ln_bf = ctx.enter_context(nc.sbuf_tensor("l0a_ln", [128, D], BF16))
    lnT = ctx.enter_context(nc.sbuf_tensor("l0a_lnT", [128, D], BF16))
    q_bf = [ctx.enter_context(nc.sbuf_tensor(f"l0a_q{i}", [128, D], BF16)) for i in range(2)]
    k_bf = [ctx.enter_context(nc.sbuf_tensor(f"l0a_k{i}", [128, D], BF16)) for i in range(2)]
    v_bf = [ctx.enter_context(nc.sbuf_tensor(f"l0a_v{i}", [128, D], BF16)) for i in range(2)]
    wst = ctx.enter_context(nc.sbuf_tensor("l0a_wst", [128, 8], F32))
    sqsum = ctx.enter_context(nc.sbuf_tensor("l0a_qsum", [1, D], F32))
    sksum = ctx.enter_context(nc.sbuf_tensor("l0a_ksum", [1, D], F32))
    rowbf = ctx.enter_context(nc.sbuf_tensor("l0a_rowbf", [1, D], BF16))
    bc1 = ctx.enter_context(nc.sbuf_tensor("l0a_bc1", [128, D], F32))
    bc2 = ctx.enter_context(nc.sbuf_tensor("l0a_bc2", [128, D], F32))
    tmpf = ctx.enter_context(nc.sbuf_tensor("l0a_tmpf", [128, D], F32))
    s16a = ctx.enter_context(nc.sbuf_tensor("l0a_s16a", [128, 16], F32))
    s16b = ctx.enter_context(nc.sbuf_tensor("l0a_s16b", [128, 16], F32))
    si_all = ctx.enter_context(nc.sbuf_tensor("l0a_si", [128, 16 * NT], F32))
    sa_all = ctx.enter_context(nc.sbuf_tensor("l0a_sa", [128, 16 * NT], F32))
    csT = ctx.enter_context(nc.sbuf_tensor("l0a_csT", [16, T], F32))
    scT = ctx.enter_context(nc.sbuf_tensor("l0a_scT", [16, T], F32))
    s16 = ctx.enter_context(nc.sbuf_tensor("l0a_s16", [16, 4], F32))
    kv_s = ctx.enter_context(nc.sbuf_tensor("l0a_kv", [64, 1024], BF16))
    qsiT = ctx.enter_context(nc.sbuf_tensor("l0a_qsiT", [64, 16 * 128], BF16))
    aT = ctx.enter_context(nc.sbuf_tensor("l0a_aT", [128, D], BF16))
    y1 = ctx.enter_context(nc.sbuf_tensor("l0a_y1", [128, D], F32))
    x1o = [ctx.enter_context(nc.sbuf_tensor(f"l0a_x1o{i}", [128, D], F32)) for i in range(2)]

    pA = ctx.enter_context(nc.psum_tensor("l0a_pA", [128, D], F32))
    pB = ctx.enter_context(nc.psum_tensor("l0a_pB", [128, D], F32))
    pt = ctx.enter_context(nc.psum_tensor("l0a_pt", [128, D], BF16))
    ps1 = ctx.enter_context(nc.psum_tensor("l0a_ps1", [1, D], F32))

    for w_s, name in ((wq, "l0_wqt"), (wk, "l0_wkt"), (wv, "l0_wvt"), (wo, "l0_wot")):
        src = c[name].rearrange("(c p) n -> p c n", p=128)
        dst = w_s[:, :].rearrange("p (c n) -> p c n", c=8)
        pb.dma("gpsimd", dst, src, "wl")
    for b_s, name in ((bq, "l0_bq"), (bk, "l0_bk"), (bv, "l0_bv"), (bo, "l0_bo")):
        pb.dma("gpsimd", b_s[:, :], c[name], "wl")
    wl_all = pb.now("wl")

    ld_x = {}
    for t in range(min(2, NT)):
        ld_x[t] = pb.dma("sync", xin[t % 2][:, :], x_in[128 * t : 128 * t + 128, :], f"lx{t % 2}")

    ac_sq, ac_sk, dv_sv, st_q, st_k, st_v = {}, {}, {}, {}, {}, {}
    dv_qsum = dv_ksum = None
    for t in range(NT):
        bt = t % 2
        pb.wait("vector", ld_x[t])
        pb.wait("scalar", ld_x[t])
        apply_v = emit_ln(pb, c, xin[bt][:, :], ln_bf[:, :], tmpf[:, :], wst)
        if t + 2 < NT:
            pb.wait("sync", apply_v)
            ld_x[t + 2] = pb.dma("sync", xin[bt][:, :], x_in[128 * (t + 2) : 128 * (t + 2) + 128, :], f"lx{bt}")
        pb.wait("tensor", apply_v)
        pe_tr = emit_transpose8(pb, c, ln_bf, pt)
        pb.wait("vector", pe_tr)
        dv_lnT = pb.do("vector", lambda e: e.tensor_copy(lnT[:, :], pt[:, :]), inc="dv")
        pb.wait("tensor", dv_lnT)
        if t == 0:
            pb.wait("tensor", wl_all)
        if t >= 1:
            pb.wait("tensor", dv_sv[t - 1])
        pe_q = emit_proj(pb, c, lnT, wq, bq, pA)
        if t >= 1:
            pb.wait("tensor", ac_sk[t - 1])
        pe_k = emit_proj(pb, c, lnT, wk, bk, pB)
        pb.wait("scalar", pe_q)
        if t >= 2:
            pb.wait("scalar", st_q[t - 2])
        ac_sq[t] = pb.do(
            "scalar",
            lambda e, bt=bt: e.activation(out=q_bf[bt][:, :], in_=pA[:, :], func=AF.Sigmoid),
            inc="ac",
        )
        pb.wait("tensor", ac_sq[t])
        pe_v = emit_proj(pb, c, lnT, wv, bv, pA)
        pb.wait("scalar", pe_k)
        if t >= 2:
            pb.wait("scalar", st_k[t - 2])
        ac_sk[t] = pb.do(
            "scalar",
            lambda e, bt=bt: e.activation(out=k_bf[bt][:, :], in_=pB[:, :], func=AF.Sigmoid),
            inc="ac",
        )
        pb.wait("vector", pe_v)
        if t >= 2:
            pb.wait("vector", st_v[t - 2])
        dv_sv[t] = pb.do("vector", lambda e, bt=bt: e.tensor_copy(v_bf[bt][:, :], pA[:, :]), inc="dv")
        pb.wait("tensor", ac_sq[t])
        if dv_ksum is not None:
            pb.wait("tensor", dv_ksum)

        def qsum_fn(e, q=q_bf[bt]):
            e.matmul(ps1[0:1, 0:512], c["ones_col"][:, 0:1], q[:, 0:512], start=True, stop=True)
            return e.matmul(ps1[0:1, 512:1024], c["ones_col"][:, 0:1], q[:, 512:1024], start=True, stop=True)

        pe_qs = pb.do("tensor", qsum_fn, inc="pe")
        pb.wait("vector", pe_qs)
        if t == 0:
            dv_qsum = pb.do("vector", lambda e: e.tensor_copy(sqsum[0:1, :], ps1[0:1, :]), inc="dv")
        else:
            dv_qsum = pb.do("vector", lambda e: e.tensor_add(sqsum[0:1, :], sqsum[0:1, :], ps1[0:1, :]), inc="dv")
        pb.wait("tensor", ac_sk[t])
        pb.wait("tensor", dv_qsum)

        def ksum_fn(e, k=k_bf[bt]):
            e.matmul(ps1[0:1, 0:512], c["ones_col"][:, 0:1], k[:, 0:512], start=True, stop=True)
            return e.matmul(ps1[0:1, 512:1024], c["ones_col"][:, 0:1], k[:, 512:1024], start=True, stop=True)

        pe_ks = pb.do("tensor", ksum_fn, inc="pe")
        pb.wait("vector", pe_ks)
        if t == 0:
            dv_ksum = pb.do("vector", lambda e: e.tensor_copy(sksum[0:1, :], ps1[0:1, :]), inc="dv")
        else:
            dv_ksum = pb.do("vector", lambda e: e.tensor_add(sksum[0:1, :], sksum[0:1, :], ps1[0:1, :]), inc="dv")
        pb.wait("sync", ac_sq[t])
        st_q[t] = pb.dma("sync", scr_q[128 * t : 128 * t + 128, :], q_bf[bt][:, :], f"sq{bt}")
        pb.wait("sync", ac_sk[t])
        st_k[t] = pb.dma("sync", scr_k[128 * t : 128 * t + 128, :], k_bf[bt][:, :], f"sk{bt}")
        pb.wait("sync", dv_sv[t])
        st_v[t] = pb.dma("sync", scr_v[128 * t : 128 * t + 128, :], v_bf[bt][:, :], f"sv{bt}")

    pb.barrier()
    if n_sweeps <= 1:
        return

    def bcast_row(src_row_ap, dst_bc):
        dvr = pb.do(
            "vector",
            lambda e: e.tensor_scalar_add(rowbf[0:1, :], src_row_ap, EPS0),
            inc="dv",
        )
        pb.wait("tensor", dvr)

        def fn(e):
            e.matmul(pA[:, 0:512], c["ones_row"][0:1, :], rowbf[0:1, 0:512], start=True, stop=True)
            return e.matmul(pA[:, 512:1024], c["ones_row"][0:1, :], rowbf[0:1, 512:1024], start=True, stop=True)

        pev = pb.do("tensor", fn, inc="pe")
        pb.wait("vector", pev)
        return pb.do("vector", lambda e: e.tensor_copy(dst_bc[:, :], pA[:, :]), inc="dv")

    bcast_row(sksum[0:1, :], bc1)
    bcast_row(sqsum[0:1, :], bc2)

    ld_q, ld_k, ld_v = {}, {}, {}
    for t in range(min(2, NT)):
        ld_q[t] = pb.dma("sync", q_bf[t % 2][:, :], scr_q[128 * t : 128 * t + 128, :], f"lq{t % 2}")
        ld_k[t] = pb.dma("sync", k_bf[t % 2][:, :], scr_k[128 * t : 128 * t + 128, :], f"lk{t % 2}")

    pe_s2 = pe_s1 = None
    for t in range(NT):
        bt = t % 2
        pb.wait("vector", ld_q[t])
        pb.wait("vector", ld_k[t])
        if pe_s1 is not None:
            pb.wait("vector", pe_s1)
        pb.do(
            "vector",
            lambda e, q=q_bf[bt]: e.scalar_tensor_tensor(
                out=tmpf[:, :], in0=q[:, :], scalar=EPS0, in1=bc1[:, :],
                op0=ALU.add, op1=ALU.mult,
            ),
        )
        rqa = pb.do(
            "vector",
            lambda e: e.reduce_sum(
                s16a[:, :], tmpf[:, :].rearrange("p (h d) -> p h d", h=16), axis=AX.X
            ),
            inc="dv",
        )
        pb.wait("vector", rqa)
        dvrq = pb.do("vector", lambda e, t=t: e.reciprocal(si_all[:, 16 * t : 16 * t + 16], s16a[:, :]), inc="dv")
        pb.wait("vector", dvrq)
        dv_qsi = pb.do(
            "vector",
            lambda e, q=q_bf[bt], t=t: e.tensor_tensor(
                out=ln_bf[:, :].rearrange("p (h d) -> p h d", h=16),
                in0=q[:, :].rearrange("p (h d) -> p h d", h=16),
                in1=si_all[:, 16 * t : 16 * t + 16].unsqueeze(2).to_broadcast([128, 16, 64]),
                op=ALU.mult,
            ),
            inc="dv",
        )
        pb.wait("tensor", dv_qsi)

        def s2fn(e, t=t):
            e.matmul(pB[0:1, 0:512], c["ones_col"][:, 0:1], ln_bf[:, 0:512], start=(t == 0), stop=(t == NT - 1))
            return e.matmul(pB[0:1, 512:1024], c["ones_col"][:, 0:1], ln_bf[:, 512:1024], start=(t == 0), stop=(t == NT - 1))

        pe_s2 = pb.do("tensor", s2fn, inc="pe")
        pb.wait("vector", pe_s2)
        pb.do(
            "vector",
            lambda e, k=k_bf[bt]: e.scalar_tensor_tensor(
                out=tmpf[:, :], in0=k[:, :], scalar=EPS0, in1=bc2[:, :],
                op0=ALU.add, op1=ALU.mult,
            ),
        )
        rkb = pb.do(
            "vector",
            lambda e: e.reduce_sum(
                s16b[:, :], tmpf[:, :].rearrange("p (h d) -> p h d", h=16), axis=AX.X
            ),
            inc="dv",
        )
        pb.wait("vector", rkb)
        dvrk = pb.do("vector", lambda e: e.reciprocal(s16b[:, :], s16b[:, :]), inc="dv")
        pb.wait("vector", dvrk)
        dv_kso = pb.do(
            "vector",
            lambda e, k=k_bf[bt]: e.tensor_tensor(
                out=ln_bf[:, :].rearrange("p (h d) -> p h d", h=16),
                in0=k[:, :].rearrange("p (h d) -> p h d", h=16),
                in1=s16b[:, :].unsqueeze(2).to_broadcast([128, 16, 64]),
                op=ALU.mult,
            ),
            inc="dv",
        )
        pb.wait("tensor", dv_kso)

        def s1fn(e, t=t):
            e.matmul(ps1[0:1, 0:512], c["ones_col"][:, 0:1], ln_bf[:, 0:512], start=(t == 0), stop=(t == NT - 1))
            return e.matmul(ps1[0:1, 512:1024], c["ones_col"][:, 0:1], ln_bf[:, 512:1024], start=(t == 0), stop=(t == NT - 1))

        pe_s1 = pb.do("tensor", s1fn, inc="pe")
        if t + 2 < NT:
            pb.wait("sync", pe_s1)
            ld_q[t + 2] = pb.dma("sync", q_bf[bt][:, :], scr_q[128 * (t + 2) : 128 * (t + 2) + 128, :], f"lq{bt}")
            ld_k[t + 2] = pb.dma("sync", k_bf[bt][:, :], scr_k[128 * (t + 2) : 128 * (t + 2) + 128, :], f"lk{bt}")

    pb.wait("vector", pe_s1)
    pb.wait("vector", pe_s2)
    bcast_row(ps1[0:1, :], bc1)
    bcast_row(pB[0:1, :], bc2)

    pb.barrier()
    if n_sweeps <= 2:
        return

    for t in range(min(2, NT)):
        ld_q[t] = pb.dma("sync", q_bf[t % 2][:, :], scr_q[128 * t : 128 * t + 128, :], f"lq{t % 2}")
        ld_k[t] = pb.dma("sync", k_bf[t % 2][:, :], scr_k[128 * t : 128 * t + 128, :], f"lk{t % 2}")
    ac_sa = pe_trc = None
    dv_cs = {}
    for t in range(NT):
        bt = t % 2
        pb.wait("vector", ld_q[t])
        pb.wait("vector", ld_k[t])
        if ac_sa is not None:
            pb.wait("vector", ac_sa)
        if pe_trc is not None:
            pb.wait("vector", pe_trc)
        pb.do(
            "vector",
            lambda e, q=q_bf[bt]: e.scalar_tensor_tensor(
                out=tmpf[:, :], in0=q[:, :], scalar=EPS0, in1=bc1[:, :],
                op0=ALU.add, op1=ALU.mult,
            ),
        )
        dvr = pb.do(
            "vector",
            lambda e: e.reduce_sum(
                s16a[:, :], tmpf[:, :].rearrange("p (h d) -> p h d", h=16), axis=AX.X
            ),
            inc="dv",
        )
        pb.wait("scalar", dvr)
        ac_sa = pb.do(
            "scalar",
            lambda e, t=t: e.activation(
                out=sa_all[:, 16 * t : 16 * t + 16], in_=s16a[:, :], func=AF.Sigmoid
            ),
            inc="ac",
        )
        pb.do(
            "vector",
            lambda e, k=k_bf[bt]: e.scalar_tensor_tensor(
                out=tmpf[:, :], in0=k[:, :], scalar=EPS0, in1=bc2[:, :],
                op0=ALU.add, op1=ALU.mult,
            ),
        )
        rcb = pb.do(
            "vector",
            lambda e: e.reduce_sum(
                s16b[:, :], tmpf[:, :].rearrange("p (h d) -> p h d", h=16), axis=AX.X
            ),
            inc="dv",
        )
        pb.wait("vector", rcb)
        rcm = pb.do("vector", lambda e: e.tensor_scalar_min(s16b[:, :], s16b[:, :], 1.0), inc="dv")
        pb.wait("vector", rcm)
        dv_cl = pb.do("vector", lambda e: e.tensor_scalar_max(s16b[:, :], s16b[:, :], -1.0), inc="dv")
        pb.wait("tensor", dv_cl)
        if t >= 1:
            pb.wait("tensor", dv_cs[t - 1])
        pe_trc = pb.do(
            "tensor",
            lambda e: e.transpose(pA[0:16, 0:128], s16b[:, :], c["ident_f32"][:, :]),
            inc="pe",
        )
        pb.wait("vector", pe_trc)
        dv_cs[t] = pb.do(
            "vector",
            lambda e, t=t: e.tensor_copy(csT[0:16, 128 * t : 128 * t + 128], pA[0:16, 0:128]),
            inc="dv",
        )
        if t + 2 < NT:
            pb.wait("sync", dv_cs[t])
            ld_q[t + 2] = pb.dma("sync", q_bf[bt][:, :], scr_q[128 * (t + 2) : 128 * (t + 2) + 128, :], f"lq{bt}")
            ld_k[t + 2] = pb.dma("sync", k_bf[bt][:, :], scr_k[128 * (t + 2) : 128 * (t + 2) + 128, :], f"lk{bt}")

    pb.barrier()
    if n_sweeps <= 3:
        return

    rdm = pb.do("vector", lambda e: e.reduce_max(s16[0:16, 0:1], csT[0:16, :], axis=AX.X), inc="dv")
    pb.wait("vector", rdm)
    dvm = pb.do("vector", lambda e: e.tensor_scalar_mul(s16[0:16, 1:2], s16[0:16, 0:1], -1.0), inc="dv")
    pb.wait("scalar", dvm)
    acce = pb.do(
        "scalar",
        lambda e: e.activation(
            out=scT[0:16, :], in_=csT[0:16, :], func=AF.Exp,
            bias=s16[0:16, 1:2], accum_out=s16[0:16, 2:3],
        ),
        inc="ac",
    )
    pb.wait("vector", acce)
    rrc = pb.do("vector", lambda e: e.reciprocal(s16[0:16, 3:4], s16[0:16, 2:3]), inc="dv")
    pb.wait("vector", rrc)
    dvsc = pb.do("vector", lambda e: e.tensor_scalar_mul(s16[0:16, 3:4], s16[0:16, 3:4], float(T)), inc="dv")
    pb.wait("vector", dvsc)
    dvscT = pb.do(
        "vector",
        lambda e: e.tensor_scalar(
            scT[0:16, :], scT[0:16, :], s16[0:16, 3:4], None, op0=ALU.mult
        ),
        inc="dv",
    )
    if DEV_DUMP:
        pb.wait("sync", dvscT)
        pb.dma("sync", c["scr_dbg"][0:16, 0:T], csT[0:16, :], "dbg1")
        pb.dma("sync", c["scr_dbg"][16:32, 0:T], scT[0:16, :], "sdbg2")
        pb.dma("sync", c["scr_dbg"][32:33, 0:D], bc1[0:1, :], "dbg3")
        pb.dma("sync", c["scr_dbg"][33:34, 0:D], bc2[0:1, :], "dbg4")
        pb.dma("sync", c["scr_dbg"][34:35, 0:D], sksum[0:1, :], "dbg5")
        pb.dma("sync", c["scr_dbg"][35:36, 0:D], sqsum[0:1, :], "dbg6")
        pb.dma("sync", c["scr_dbg"][36:37, 0:16].rearrange("a (b c) -> a b c", b=4), s16[0:16, 0:4].rearrange("(a b) c -> a b c", a=1), "dbg7") if False else None
        pb.dma("sync", c["scr_dbg"][40:56, 0:4], s16[0:16, 0:4], "dbg8")

    pb.barrier()
    if n_sweeps <= 4:
        return

    for t in range(min(2, NT)):
        ld_k[t] = pb.dma("sync", k_bf[t % 2][:, :], scr_k[128 * t : 128 * t + 128, :], f"lk{t % 2}")
        ld_v[t] = pb.dma("sync", v_bf[t % 2][:, :], scr_v[128 * t : 128 * t + 128, :], f"lv{t % 2}")
    dv_vsc = {}
    pe_kv = None
    for t in range(NT):
        bt = t % 2
        if t >= 1:
            pb.wait("tensor", dv_vsc[t - 1])
        pe_tr = pb.do(
            "tensor",
            lambda e, t=t: e.transpose(
                pA[:, 0:16], scT[0:16, 128 * t : 128 * t + 128], c["ident_f32"][0:16, 0:16]
            ),
            inc="pe",
        )
        pb.wait("vector", pe_tr)
        pb.wait("vector", ld_v[t])
        if pe_kv is not None:
            pb.wait("vector", pe_kv)
        dvcp = pb.do("vector", lambda e: e.tensor_copy(s16a[:, :], pA[:, 0:16]), inc="dv")
        pb.wait("vector", dvcp)
        dv_vsc[t] = pb.do(
            "vector",
            lambda e, v=v_bf[bt]: e.tensor_tensor(
                out=ln_bf[:, :].rearrange("p (h d) -> p h d", h=16),
                in0=v[:, :].rearrange("p (h d) -> p h d", h=16),
                in1=s16a[:, :].unsqueeze(2).to_broadcast([128, 16, 64]),
                op=ALU.mult,
            ),
            inc="dv",
        )
        pb.wait("tensor", dv_vsc[t])
        pb.wait("tensor", ld_k[t])

        def kvfn(e, k=k_bf[bt], t=t):
            ins = None
            for h in range(16):
                ins = e.matmul(
                    pB[0:64, 64 * h : 64 * h + 64],
                    k[:, 64 * h : 64 * h + 64],
                    ln_bf[:, 64 * h : 64 * h + 64],
                    start=(t == 0 and h % 8 == 0),
                    stop=(t == NT - 1 and h % 8 == 7),
                )
            return ins

        pe_kv = pb.do("tensor", kvfn, inc="pe")
        if t + 2 < NT:
            pb.wait("sync", pe_kv)
            pb.wait("sync", dv_vsc[t])
            ld_k[t + 2] = pb.dma("sync", k_bf[bt][:, :], scr_k[128 * (t + 2) : 128 * (t + 2) + 128, :], f"lk{bt}")
            ld_v[t + 2] = pb.dma("sync", v_bf[bt][:, :], scr_v[128 * (t + 2) : 128 * (t + 2) + 128, :], f"lv{bt}")
    pb.wait("vector", pe_kv)
    dv_kvs = pb.do("vector", lambda e: e.tensor_copy(kv_s[0:64, :], pB[0:64, :]), inc="dv")

    pb.barrier()
    if n_sweeps <= 5:
        return

    for t in range(min(2, NT)):
        ld_q[t] = pb.dma("sync", q_bf[t % 2][:, :], scr_q[128 * t : 128 * t + 128, :], f"lq{t % 2}")
        ld_x[t] = pb.dma("sync", xin[t % 2][:, :], x_in[128 * t : 128 * t + 128, :], f"lx{t % 2}")
    st_o = {}
    dv_fin = {}
    pe_y = None
    for t in range(NT):
        bt = t % 2
        pb.wait("vector", ld_q[t])
        if pe_y is not None:
            pb.wait("vector", pe_y)
        if F_CUT < 9 and t >= 0:
            emit_f_cut(pb, c, t, bt, NT, locals())
            continue
        dv_q = pb.do(
            "vector",
            lambda e, q=q_bf[bt], t=t: e.tensor_tensor(
                out=ln_bf[:, :].rearrange("p (h d) -> p h d", h=16),
                in0=q[:, :].rearrange("p (h d) -> p h d", h=16),
                in1=si_all[:, 16 * t : 16 * t + 16].unsqueeze(2).to_broadcast([128, 16, 64]),
                op=ALU.mult,
            ),
            inc="dv",
        )
        pb.wait("tensor", dv_q)
        if t >= 1:
            pb.wait("tensor", dv_fin[t - 1])
        dv_qT = None
        for grp in range(2):
            def trf(e, grp=grp):
                ins = None
                for hh in range(8):
                    h = 8 * grp + hh
                    ins = e.transpose(
                        pt[0:64, 128 * hh : 128 * hh + 128],
                        ln_bf[:, 64 * h : 64 * h + 64],
                        c["ident_bf"][:, :],
                    )
                return ins

            if grp == 1:
                pb.wait("tensor", dv_qT)
            pe_tr = pb.do("tensor", trf, inc="pe")
            pb.wait("vector", pe_tr)
            dv_qT = pb.do(
                "vector",
                lambda e, grp=grp: e.tensor_copy(
                    qsiT[0:64, 1024 * grp : 1024 * grp + 1024], pt[0:64, 0:1024]
                ),
                inc="dv",
            )
        pb.wait("tensor", dv_qT)

        def outfn(e):
            ins = None
            for h in range(16):
                ins = e.matmul(
                    pA[:, 64 * h : 64 * h + 64],
                    qsiT[0:64, 128 * h : 128 * h + 128],
                    kv_s[0:64, 64 * h : 64 * h + 64],
                    start=True, stop=True,
                )
            return ins

        pe_out = pb.do("tensor", outfn, inc="pe")
        pb.wait("vector", pe_out)
        dv_at = pb.do(
            "vector",
            lambda e, t=t: e.tensor_tensor(
                out=ln_bf[:, :].rearrange("p (h d) -> p h d", h=16),
                in0=pA[:, :].rearrange("p (h d) -> p h d", h=16),
                in1=sa_all[:, 16 * t : 16 * t + 16].unsqueeze(2).to_broadcast([128, 16, 64]),
                op=ALU.mult,
            ),
            inc="dv",
        )
        pb.wait("tensor", dv_at)
        pe_tr2 = emit_transpose8(pb, c, ln_bf, pt)
        pb.wait("vector", pe_tr2)
        dv_aT = pb.do("vector", lambda e: e.tensor_copy(aT[:, :], pt[:, :]), inc="dv")
        pb.wait("tensor", dv_aT)
        pe_y = emit_proj(pb, c, aT, wo, bo, pB)
        pb.wait("vector", pe_y)
        pb.wait("scalar", pe_y)
        emit_ln(pb, c, pB[:, :], y1[:, :], tmpf[:, :], wst, psum_in=True)
        pb.wait("vector", ld_x[t])
        if t >= 2:
            pb.wait("vector", st_o[t - 2])
        dv_fin[t] = pb.do(
            "vector", lambda e, bt=bt: e.tensor_add(x1o[bt][:, :], xin[bt][:, :], y1[:, :]), inc="dv"
        )
        pb.wait("sync", dv_fin[t])
        st_o[t] = pb.dma("sync", scr_out[128 * t : 128 * t + 128, :], x1o[bt][:, :], f"so{bt}")
        if t + 2 < NT:
            ld_q[t + 2] = pb.dma("sync", q_bf[bt][:, :], scr_q[128 * (t + 2) : 128 * (t + 2) + 128, :], f"lq{bt}")
            ld_x[t + 2] = pb.dma("sync", xin[bt][:, :], x_in[128 * (t + 2) : 128 * (t + 2) + 128, :], f"lx{bt}")

    pb.barrier()




def emit_f_cut(pb, c, t, bt, NT, L):
    xin, ln_bf, q_bf, qsiT, kv_s, sa_all, si_all, aT, y1, x1o, tmpf, wst = (
        L["xin"], L["ln_bf"], L["q_bf"], L["qsiT"], L["kv_s"], L["sa_all"],
        L["si_all"], L["aT"], L["y1"], L["x1o"], L["tmpf"], L["wst"])
    pA, pB, pt = L["pA"], L["pB"], L["pt"]
    wo, bo = L["wo"], L["bo"]
    scr_out, scr_q, x_in = L["scr_out"], L["scr_q"], L["x_in"]
    ld_q, ld_x, st_o, dv_fin = L["ld_q"], L["ld_x"], L["st_o"], L["dv_fin"]
    ALU_ = ALU
    last_dv = None
    dv_q = pb.do(
        "vector",
        lambda e, q=q_bf[bt], t=t: e.tensor_tensor(
            out=ln_bf[:, :].rearrange("p (h d) -> p h d", h=16),
            in0=q[:, :].rearrange("p (h d) -> p h d", h=16),
            in1=si_all[:, 16 * t : 16 * t + 16].unsqueeze(2).to_broadcast([128, 16, 64]),
            op=ALU_.mult,
        ),
        inc="dv",
    )
    last_pe = None
    if F_CUT >= 2:
        pb.wait("tensor", dv_q)
        if t >= 1:
            pb.wait("tensor", dv_fin[t - 1])
        dv_qT = None
        for grp in range(2):
            def trf(e, grp=grp):
                ins = None
                for hh in range(8):
                    h = 8 * grp + hh
                    ins = e.transpose(
                        pt[0:64, 128 * hh : 128 * hh + 128],
                        ln_bf[:, 64 * h : 64 * h + 64],
                        c["ident_bf"][:, :],
                    )
                return ins
            if grp == 1:
                pb.wait("tensor", dv_qT)
            pe_tr = pb.do("tensor", trf, inc="pe")
            pb.wait("vector", pe_tr)
            dv_qT = pb.do(
                "vector",
                lambda e, grp=grp: e.tensor_copy(
                    qsiT[0:64, 1024 * grp : 1024 * grp + 1024], pt[0:64, 0:1024]
                ),
                inc="dv",
            )
        last_pe = pe_tr
    if F_CUT >= 3:
        pb.wait("tensor", dv_qT)
        def outfn(e):
            ins = None
            for h in range(16):
                ins = e.matmul(
                    pA[:, 64 * h : 64 * h + 64],
                    qsiT[0:64, 128 * h : 128 * h + 128],
                    kv_s[0:64, 64 * h : 64 * h + 64],
                    start=True, stop=True,
                )
            return ins
        pe_out = pb.do("tensor", outfn, inc="pe")
        pb.wait("vector", pe_out)
        dv_at = pb.do(
            "vector",
            lambda e, t=t: e.tensor_tensor(
                out=ln_bf[:, :].rearrange("p (h d) -> p h d", h=16),
                in0=pA[:, :].rearrange("p (h d) -> p h d", h=16),
                in1=sa_all[:, 16 * t : 16 * t + 16].unsqueeze(2).to_broadcast([128, 16, 64]),
                op=ALU_.mult,
            ),
            inc="dv",
        )
        last_pe = pe_out
    if F_CUT >= 4:
        pb.wait("tensor", dv_at)
        pe_tr2 = emit_transpose8(pb, c, ln_bf, pt)
        pb.wait("vector", pe_tr2)
        dv_aT = pb.do("vector", lambda e: e.tensor_copy(aT[:, :], pt[:, :]), inc="dv")
        last_pe = pe_tr2
    if F_CUT >= 5:
        pb.wait("tensor", dv_aT)
        pe_y = emit_proj(pb, c, aT, wo, bo, pB)
        last_pe = pe_y
    if F_CUT == 6:
        pb.wait("vector", pe_y)
        pb.wait("scalar", pe_y)
        acs = pb.do(
            "scalar",
            lambda e: e.activation(out=tmpf[:, :], in_=pB[:, :], func=AF.Square, accum_out=wst[:, 3:4]),
            inc="ac",
        )
        pb.wait("vector", acs)
        pb.do("vector", lambda e: e.tensor_copy(y1[:, :], pB[:, :]))
    elif F_CUT == 7:
        pb.wait("vector", pe_y)
        pb.wait("scalar", pe_y)
        pb.wait("scalar", pb.now("dv"))
        acs = pb.do(
            "scalar",
            lambda e: e.activation(out=tmpf[:, :], in_=pB[:, :], func=AF.Square, accum_out=wst[:, 3:4]),
            inc="ac",
        )
        pb.do("vector", lambda e: e.reduce_sum(wst[:, 2:3], pB[:, :], axis=AX.X))
        pb.do("vector", lambda e: e.tensor_scalar_mul(wst[:, 1:2], wst[:, 2:3], 1.0 / D))
        pb.wait("vector", acs)
        pb.do("vector", lambda e: e.tensor_scalar_mul(wst[:, 4:5], wst[:, 3:4], 1.0 / D))
        pb.do("vector", lambda e: e.tensor_mul(wst[:, 5:6], wst[:, 1:2], wst[:, 1:2]))
        dvv = pb.do("vector", lambda e: e.tensor_sub(wst[:, 6:7], wst[:, 4:5], wst[:, 5:6]), inc="dv")
        pb.wait("scalar", dvv)
        acv = pb.do(
            "scalar",
            lambda e: e.activation(out=wst[:, 7:8], in_=wst[:, 6:7], func=AF.Sqrt, bias=c["lneps"][:, 0:1]),
            inc="ac",
        )
        pb.wait("vector", acv)
        dvr2 = pb.do("vector", lambda e: e.reciprocal(wst[:, 0:1], wst[:, 7:8]), inc="dv")
        pb.wait("vector", dvr2)
        pb.do("vector", lambda e: e.tensor_copy(y1[:, :], pB[:, :]))
    elif F_CUT >= 8:
        pb.wait("vector", pe_y)
        pb.wait("scalar", pe_y)
        emit_ln(pb, c, pB[:, :], y1[:, :], tmpf[:, :], wst, psum_in=True)
    pb.wait("vector", ld_x[t])
    if t >= 2:
        pb.wait("vector", st_o[t - 2])
    if F_CUT >= 6:
        pass
    if F_CUT >= 6:
        dv_fin[t] = pb.do(
            "vector", lambda e, bt=bt: e.tensor_add(x1o[bt][:, :], xin[bt][:, :], y1[:, :]), inc="dv"
        )
    else:
        if last_pe is not None:
            pb.wait("vector", last_pe)
        dv_fin[t] = pb.do(
            "vector", lambda e, bt=bt: e.tensor_copy(x1o[bt][:, :], xin[bt][:, :]), inc="dv"
        )
    pb.wait("sync", dv_fin[t])
    st_o[t] = pb.dma("sync", scr_out[128 * t : 128 * t + 128, :], x1o[bt][:, :], f"so{bt}")
    if t + 2 < NT:
        ld_q[t + 2] = pb.dma("sync", q_bf[bt][:, :], scr_q[128 * (t + 2) : 128 * (t + 2) + 128, :], f"lq{bt}")
        ld_x[t + 2] = pb.dma("sync", xin[bt][:, :], x_in[128 * (t + 2) : 128 * (t + 2) + 128, :], f"lx{bt}")




def emit_l0_ffn(pb, c, ctx, T, x_in, scr_out):
    nc = pb.nc
    NT = T // 128

    wr = ctx.enter_context(nc.sbuf_tensor("l0f_wr", [128, 8 * 1024], BF16))
    wk2 = ctx.enter_context(nc.sbuf_tensor("l0f_wk", [128, 8 * 1024], BF16))
    wv2 = ctx.enter_context(nc.sbuf_tensor("l0f_wv", [128, 8 * 1024], BF16))
    mur = ctx.enter_context(nc.sbuf_tensor("l0f_murs", [128, 8], F32))
    muk = ctx.enter_context(nc.sbuf_tensor("l0f_muks", [128, 8], F32))
    omr = ctx.enter_context(nc.sbuf_tensor("l0f_omr", [128, 8], F32))
    omk = ctx.enter_context(nc.sbuf_tensor("l0f_omk", [128, 8], F32))
    zr = ctx.enter_context(nc.sbuf_tensor("l0f_zr", [128, 8], BF16))
    xin = [ctx.enter_context(nc.sbuf_tensor(f"l0f_xin{i}", [128, D], F32)) for i in range(2)]
    ln_bf = ctx.enter_context(nc.sbuf_tensor("l0f_ln", [128, D], BF16))
    LT = ctx.enter_context(nc.sbuf_tensor("l0f_LT", [128, 8 * 129], BF16))
    xrT = ctx.enter_context(nc.sbuf_tensor("l0f_xrT", [128, D], BF16))
    xkT = ctx.enter_context(nc.sbuf_tensor("l0f_xkT", [128, D], BF16))
    r_f = ctx.enter_context(nc.sbuf_tensor("l0f_r", [128, D], F32))
    relu_f = ctx.enter_context(nc.sbuf_tensor("l0f_relu", [128, D], F32))
    ka_bf = ctx.enter_context(nc.sbuf_tensor("l0f_ka", [128, D], BF16))
    kaT = ctx.enter_context(nc.sbuf_tensor("l0f_kaT", [128, D], BF16))
    yb = ctx.enter_context(nc.sbuf_tensor("l0f_yb", [128, D], F32))
    y1 = ctx.enter_context(nc.sbuf_tensor("l0f_y1", [128, D], F32))
    wst = ctx.enter_context(nc.sbuf_tensor("l0f_wst", [128, 8], F32))
    tmpf = ctx.enter_context(nc.sbuf_tensor("l0f_tmpf", [128, D], F32))
    x2o = [ctx.enter_context(nc.sbuf_tensor(f"l0f_x2o{i}", [128, D], F32)) for i in range(2)]

    pA = ctx.enter_context(nc.psum_tensor("l0f_pA", [128, D], F32))
    pB = ctx.enter_context(nc.psum_tensor("l0f_pB", [128, D], F32))
    pt = ctx.enter_context(nc.psum_tensor("l0f_pt", [128, D], BF16))

    for w_s, name in ((wr, "l0f_wrt"), (wk2, "l0f_wkt"), (wv2, "l0f_wvt")):
        src = c[name].rearrange("(c p) n -> p c n", p=128)
        dst = w_s[:, :].rearrange("p (c n) -> p c n", c=8)
        pb.dma("gpsimd", dst, src, "wl")
    pb.dma("gpsimd", mur[:, :], c["l0f_mur"], "wl")
    pb.dma("gpsimd", muk[:, :], c["l0f_muk"], "wl")
    pb.dma("gpsimd", zr[:, :], c["zeros8"], "wl")
    wl_all = pb.now("wl")
    pb.wait("vector", wl_all)
    pb.do("vector", lambda e: e.tensor_scalar_sub(omr[:, :], mur[:, :], 1.0))
    pb.do("vector", lambda e: e.tensor_scalar_mul(omr[:, :], omr[:, :], -1.0))
    pb.do("vector", lambda e: e.tensor_scalar_sub(omk[:, :], muk[:, :], 1.0))
    dv_om = pb.do("vector", lambda e: e.tensor_scalar_mul(omk[:, :], omk[:, :], -1.0), inc="dv")

    LT3 = LT[:, :].rearrange("p (c n) -> p c n", c=8)
    ld_x = {}
    for t in range(min(2, NT)):
        ld_x[t] = pb.dma("sync", xin[t % 2][:, :], x_in[128 * t : 128 * t + 128, :], f"lx{t % 2}")
    st_o = {}
    pe_v = pe_r = pe_k = None
    ac_rl = None
    dv_fin = {}
    for t in range(NT):
        bt = t % 2
        pb.wait("vector", ld_x[t])
        pb.wait("scalar", ld_x[t])
        apply_v = emit_ln(pb, c, xin[bt][:, :], ln_bf[:, :], tmpf[:, :], wst)
        if t == 0:
            dv_pc = pb.do("vector", lambda e: e.tensor_copy(LT3[:, :, 0], zr[:, :]), inc="dv")
        else:
            dv_pc = pb.do("vector", lambda e: e.tensor_copy(LT3[:, :, 0], LT3[:, :, 128]), inc="dv")
        pb.wait("tensor", apply_v)
        pb.wait("tensor", dv_pc)
        if t >= 1:
            pb.wait("tensor", pe_k)
        pe_tr = emit_transpose8(pb, c, ln_bf, pt)
        pb.wait("vector", pe_tr)
        dv_LT = pb.do(
            "vector",
            lambda e: e.tensor_copy(
                LT3[:, :, 1:129], pt[:, :].rearrange("p (c n) -> p c n", c=8)
            ),
            inc="dv",
        )
        if t == 0:
            pb.wait("vector", dv_om)
        for cc in range(8):
            pb.do(
                "vector",
                lambda e, cc=cc: e.tensor_scalar(
                    xrT[:, 128 * cc : 128 * cc + 128], LT3[:, cc, 1:129],
                    mur[:, cc : cc + 1], None, op0=ALU.mult,
                ),
            )
            pb.do(
                "vector",
                lambda e, cc=cc: e.scalar_tensor_tensor(
                    out=xrT[:, 128 * cc : 128 * cc + 128], in0=LT3[:, cc, 0:128],
                    scalar=omr[:, cc : cc + 1], in1=xrT[:, 128 * cc : 128 * cc + 128],
                    op0=ALU.mult, op1=ALU.add,
                ),
            )
            pb.do(
                "vector",
                lambda e, cc=cc: e.tensor_scalar(
                    xkT[:, 128 * cc : 128 * cc + 128], LT3[:, cc, 1:129],
                    muk[:, cc : cc + 1], None, op0=ALU.mult,
                ),
            )
            pb.do(
                "vector",
                lambda e, cc=cc: e.scalar_tensor_tensor(
                    out=xkT[:, 128 * cc : 128 * cc + 128], in0=LT3[:, cc, 0:128],
                    scalar=omk[:, cc : cc + 1], in1=xkT[:, 128 * cc : 128 * cc + 128],
                    op0=ALU.mult, op1=ALU.add,
                ),
            )
        dv_mix = pb.do("vector", lambda e: e.tensor_copy(wst[:, 0:1], omr[:, 0:1]), inc="dv")
        if DEV_DUMP and t == 0:
            pb.wait("gpsimd", dv_mix)
            pb.dma("gpsimd", c["scr_dbg"][0:64, 0:1024], xrT[0:64, :], "gdbg1")
            pb.dma("gpsimd", c["scr_dbg"][0:64, 1024:2048], xkT[0:64, :], "gdbg2")
            pb.dma("gpsimd", c["scr_dbg"][0:64, 2048:3072], LT[0:64, 0:1024], "gdbg3")
        pb.wait("tensor", dv_mix)
        if t == 0:
            pb.wait("tensor", wl_all)
        if pe_v is not None:
            pb.wait("tensor", pb.now("dv"))
        pe_r = emit_proj(pb, c, xrT, wr, None, pA)
        if ac_rl is not None:
            pb.wait("tensor", ac_rl)
        pe_k = emit_proj(pb, c, xkT, wk2, None, pB)
        pb.wait("scalar", pe_r)
        ac_sg = pb.do(
            "scalar",
            lambda e: e.activation(out=r_f[:, :], in_=pA[:, :], func=AF.Sigmoid),
            inc="ac",
        )
        pb.wait("scalar", pe_k)
        ac_rl = pb.do(
            "scalar",
            lambda e: e.activation(out=relu_f[:, :], in_=pB[:, :], func=AF.Relu),
            inc="ac",
        )
        pb.wait("vector", ac_rl)
        dv_ka = pb.do("vector", lambda e: e.tensor_mul(ka_bf[:, :], relu_f[:, :], relu_f[:, :]), inc="dv")
        if DEV_DUMP and t == 0:
            pb.wait("sync", ac_sg)
            pb.dma("sync", c["scr_dbg"][0:64, 3072:4096], r_f[0:64, :], "dbg4")
            pb.wait("sync", dv_ka)
            pb.dma("sync", c["scr_dbg"][0:64, 4096:5120], relu_f[0:64, :], "dbg5")
        pb.wait("tensor", dv_ka)
        pe_tr2 = emit_transpose8(pb, c, ka_bf, pt)
        pb.wait("vector", pe_tr2)
        dv_kaT = pb.do("vector", lambda e: e.tensor_copy(kaT[:, :], pt[:, :]), inc="dv")
        pb.wait("tensor", dv_kaT)
        pb.wait("tensor", ac_sg)
        pe_v = emit_proj(pb, c, kaT, wv2, None, pA)
        pb.wait("vector", pe_v)
        dv_y = pb.do("vector", lambda e: e.tensor_mul(yb[:, :], r_f[:, :], pA[:, :]), inc="dv")
        pb.wait("scalar", dv_y)
        emit_ln(pb, c, yb[:, :], y1[:, :], tmpf[:, :], wst)
        if t >= 2:
            pb.wait("vector", st_o[t - 2])
        dv_fin[t] = pb.do(
            "vector", lambda e, bt=bt: e.tensor_add(x2o[bt][:, :], xin[bt][:, :], y1[:, :]), inc="dv"
        )
        pb.wait("sync", dv_fin[t])
        st_o[t] = pb.dma("sync", scr_out[128 * t : 128 * t + 128, :], x2o[bt][:, :], f"so{bt}")
        if t + 2 < NT:
            ld_x[t + 2] = pb.dma("sync", xin[bt][:, :], x_in[128 * (t + 2) : 128 * (t + 2) + 128, :], f"lx{bt}")

    pb.barrier()



def emit_l1_attn(pb, c, ctx, T, x_in, scr_q, scr_out):
    nc = pb.nc
    NT = T // 128
    NKB = T // 128

    xin = [ctx.enter_context(nc.sbuf_tensor(f"l1_xin{i}", [128, D], F32)) for i in range(2)]
    ln_bf = ctx.enter_context(nc.sbuf_tensor("l1_ln", [128, D], BF16))
    lnT = ctx.enter_context(nc.sbuf_tensor("l1_lnT", [128, D], BF16))
    wst = ctx.enter_context(nc.sbuf_tensor("l1_wst", [128, 8], F32))
    kT = ctx.enter_context(nc.sbuf_tensor("l1_kT", [128, 8 * T], BF16))
    V = ctx.enter_context(nc.sbuf_tensor("l1_V", [128, T * 8], BF16))
    q_bf = [ctx.enter_context(nc.sbuf_tensor(f"l1_q{i}", [128, D], BF16)) for i in range(2)]
    probs = ctx.enter_context(nc.sbuf_tensor("l1_probs", [128, T], BF16))
    pTs = ctx.enter_context(nc.sbuf_tensor("l1_pTs", [128, 1024], BF16))
    rs = ctx.enter_context(nc.sbuf_tensor("l1_rs", [128, 4], F32))
    ctxb = ctx.enter_context(nc.sbuf_tensor("l1_ctx", [128, D], BF16))
    y1 = ctx.enter_context(nc.sbuf_tensor("l1_y1", [128, D], F32))
    x3o = [ctx.enter_context(nc.sbuf_tensor(f"l1_x3o{i}", [128, D], F32)) for i in range(2)]

    kT3 = kT[:, :].rearrange("p (g n) -> p g n", g=8)
    V3 = V[:, :].rearrange("p (t n) -> p t n", t=NT)

    ph1 = ExitStack()
    pt = ph1.enter_context(nc.psum_tensor("l1_pt", [128, D], BF16))
    pA = ph1.enter_context(nc.psum_tensor("l1_pA", [128, D], F32))
    pB = ph1.enter_context(nc.psum_tensor("l1_pB", [128, D], F32))
    wq = ph1.enter_context(nc.sbuf_tensor("l1_wqs", [128, 8 * 1024], BF16))
    wk = ph1.enter_context(nc.sbuf_tensor("l1_wks", [128, 8 * 1024], BF16))
    bq = ph1.enter_context(nc.sbuf_tensor("l1_bqs", [1, D], BF16))
    bk = ph1.enter_context(nc.sbuf_tensor("l1_bks", [1, D], BF16))

    for w_s, name in ((wq, "l1_wq"), (wk, "l1_wk")):
        src = c[name].rearrange("(c p) n -> p c n", p=128)
        dst = w_s[:, :].rearrange("p (c n) -> p c n", c=8)
        pb.dma("gpsimd", dst, src, "wl")
    pb.dma("gpsimd", bq[:, :], c["l1_bq"], "wl")
    pb.dma("gpsimd", bk[:, :], c["l1_bk"], "wl")
    wl_all = pb.now("wl")

    ld_x = {}
    for t in range(min(2, NT)):
        ld_x[t] = pb.dma("sync", xin[t % 2][:, :], x_in[128 * t : 128 * t + 128, :], f"lx{t % 2}")
    st_q = {}
    ac_q = None
    dv_kT = None
    for t in range(NT):
        bt = t % 2
        pb.wait("vector", ld_x[t])
        pb.wait("scalar", ld_x[t])
        apply_v = emit_ln(pb, c, xin[bt][:, :], ln_bf[:, :], probs[:, 0:1024], wst)
        if t + 2 < NT:
            pb.wait("sync", apply_v)
            ld_x[t + 2] = pb.dma("sync", xin[bt][:, :], x_in[128 * (t + 2) : 128 * (t + 2) + 128, :], f"lx{bt}")
        pb.wait("tensor", apply_v)
        if dv_kT is not None:
            pb.wait("tensor", dv_kT)
        pe_tr = emit_transpose8(pb, c, ln_bf, pt)
        pb.wait("vector", pe_tr)
        dv_lnT = pb.do("vector", lambda e: e.tensor_copy(lnT[:, :], pt[:, :]), inc="dv")
        pb.wait("tensor", dv_lnT)
        if t == 0:
            pb.wait("tensor", wl_all)
        if ac_q is not None:
            pb.wait("tensor", ac_q)
        pe_q = emit_proj(pb, c, lnT, wq, bq, pA)
        pe_k = emit_proj(pb, c, lnT, wk, bk, pB)
        pb.wait("scalar", pe_q)
        if t >= 2:
            pb.wait("scalar", st_q[t - 2])
        ac_q = pb.do(
            "scalar",
            lambda e, bt=bt: e.activation(out=q_bf[bt][:, :], in_=pA[:, :], func=AF.Identity),
            inc="ac",
        )
        pb.wait("sync", ac_q)
        st_q[t] = pb.dma("sync", scr_q[128 * t : 128 * t + 128, :], q_bf[bt][:, :], f"sq{bt}")
        pb.wait("vector", pe_k)
        dv_kb = pb.do("vector", lambda e: e.tensor_copy(ln_bf[:, :], pB[:, :]), inc="dv")
        pb.wait("tensor", dv_kb)
        pe_tr2 = emit_transpose8(pb, c, ln_bf, pt)
        pb.wait("vector", pe_tr2)
        dv_kT = pb.do(
            "vector",
            lambda e, t=t: e.tensor_copy(
                kT3[:, :, 128 * t : 128 * t + 128],
                pt[:, :].rearrange("p (c n) -> p c n", c=8),
            ),
            inc="dv",
        )
    pb.barrier()

    wv = wq
    bv = bq
    src = c["l1_wv"].rearrange("(c p) n -> p c n", p=128)
    pb.dma("gpsimd", wv[:, :].rearrange("p (c n) -> p c n", c=8), src, "wl")
    pb.dma("gpsimd", bv[:, :], c["l1_bv"], "wl")
    wl_all = pb.now("wl")
    for t in range(min(2, NT)):
        ld_x[t] = pb.dma("sync", xin[t % 2][:, :], x_in[128 * t : 128 * t + 128, :], f"lx{t % 2}")
    dv_V = None
    for t in range(NT):
        bt = t % 2
        pb.wait("vector", ld_x[t])
        pb.wait("scalar", ld_x[t])
        apply_v = emit_ln(pb, c, xin[bt][:, :], ln_bf[:, :], probs[:, 0:1024], wst)
        if t + 2 < NT:
            pb.wait("sync", apply_v)
            ld_x[t + 2] = pb.dma("sync", xin[bt][:, :], x_in[128 * (t + 2) : 128 * (t + 2) + 128, :], f"lx{bt}")
        pb.wait("tensor", apply_v)
        if dv_V is not None:
            pb.wait("tensor", dv_V)
        pe_tr = emit_transpose8(pb, c, ln_bf, pt)
        pb.wait("vector", pe_tr)
        dv_lnT = pb.do("vector", lambda e: e.tensor_copy(lnT[:, :], pt[:, :]), inc="dv")
        pb.wait("tensor", dv_lnT)
        if t == 0:
            pb.wait("tensor", wl_all)
        pe_v = emit_proj(pb, c, lnT, wv, bv, pA)
        pb.wait("vector", pe_v)
        dv_V = pb.do(
            "vector",
            lambda e, t=t: e.tensor_copy(V3[:, t, :], pA[:, :]),
            inc="dv",
        )
    pb.barrier()

    pb.flush(pb.block)
    ph1.close()
    QS = 256
    NQS = T // QS
    KHALF = T // 2
    pST = ctx.enter_context(nc.psum_tensor("l1_pST", [128, 2048], F32))
    pCtx = ctx.enter_context(nc.psum_tensor("l1_pCtx", [128, 2048], F32))
    qTs = ctx.enter_context(nc.sbuf_tensor("l1_qTs", [128, 8 * QS], BF16))
    Ebuf = [ctx.enter_context(nc.sbuf_tensor(f"l1_Ebuf{i}", [128, 8 * QS], BF16)) for i in range(2)]
    Pbuf = [ctx.enter_context(nc.sbuf_tensor(f"l1_Pbuf{i}", [128, 8 * QS], BF16)) for i in range(2)]
    cTs = ctx.enter_context(nc.sbuf_tensor("l1_cTs", [128, 8 * QS], BF16))
    Dred = [ctx.enter_context(nc.sbuf_tensor(f"l1_D{i}", [128, QS], F32)) for i in range(2)]
    Dinv = [ctx.enter_context(nc.sbuf_tensor(f"l1_Dinv{i}", [128, QS], BF16)) for i in range(2)]
    Dln = [ctx.enter_context(nc.sbuf_tensor(f"l1_Dln{i}", [128, QS], F32)) for i in range(2)]

    ld_q = {}
    ld_x = {}
    for t in range(min(2, 2 * NQS)):
        ld_q[t] = pb.dma("sync", q_bf[t % 2][:, :], scr_q[128 * t : 128 * t + 128, :], f"lq{t % 2}")
        ld_x[t] = pb.dma("sync", xin[t % 2][:, :], x_in[128 * t : 128 * t + 128, :], f"lx{t % 2}")
    st_o = {}
    dv_fin = {}
    ac_exp = None
    dv_ct2 = None
    pvk_prev = None
    dv_qT = None
    pvk_kb = {}
    dv_P_ms = {}

    def emit_pv(kb):
        kp = kb % 2
        pb.wait("tensor", dv_P_ms[kb])
        if kb == 0 and dv_ct2 is not None:
            pb.wait("tensor", dv_ct2)

        def pvfn(e, kb=kb, kp=kp):
            ins = None
            for g in range(8):
                ins = e.matmul(
                    pCtx[:, 256 * g : 256 * g + 256],
                    V3[:, kb, 128 * g : 128 * g + 128],
                    Pbuf[kp][:, 256 * g : 256 * g + 256],
                    start=(kb == 0 and g % 2 == 0),
                    stop=(kb == NKB - 1 and g % 2 == 1),
                )
            return ins

        pvk_kb[kb] = pb.do("tensor", pvfn, inc="pe")

    for qs in range(NQS):
        ms_tq = []
        for sub in range(2):
            t = 2 * qs + sub
            bt = t % 2
            pb.wait("sync", ld_q[t])
            if pvk_prev is not None:
                pb.wait("sync", pvk_prev)
            pb.dma_sem_eng.setdefault(f"tq{sub}", "sync")
            for g in range(8):
                dv_qT = pb.do(
                    "sync",
                    lambda e, g=g, sub=sub, bt=bt: e.dma_start_transpose(
                        qTs[:, 256 * g + 128 * sub : 256 * g + 128 * sub + 128],
                        q_bf[bt][:, 128 * g : 128 * g + 128],
                    ),
                    inc=f"tq{sub}", n=16,
                )
            ms_tq.append(dv_qT)
        dv_P_ms.clear()
        ac_exp_ms = {}
        pend_kb = None
        for kb in range(NKB):
            kp = kb % 2
            if kb == 0:
                for ms in ms_tq:
                    pb.wait("tensor", ms)
                pb.wait("tensor", pb.now("ac"))
            if kb >= 1:
                pb.wait("tensor", ac_exp_ms[kb - 1])

            def stfn(e, kb=kb):
                ins = None
                for g in range(8):
                    ins = e.matmul(
                        pST[:, 256 * g : 256 * g + 256],
                        kT3[:, g, 128 * kb : 128 * kb + 128],
                        qTs[:, 256 * g : 256 * g + 256],
                        start=True, stop=True,
                    )
                return ins

            pe_st = pb.do("tensor", stfn, inc="pe")
            pb.wait("scalar", pe_st)
            if kb >= 2:
                pb.wait("scalar", dv_P_ms[kb - 2])
            else:
                pb.wait("scalar", pb.now("dv"))
            ac_exp_ms[kb] = pb.do(
                "scalar",
                lambda e, kp=kp: e.activation(
                    out=Ebuf[kp][:, :], in_=pST[:, :], func=AF.Exp,
                    bias=c["c30"][:, 0:1], scale=1.0 / 32.0,
                ),
                inc="ac",
            )
            pb.wait("vector", ac_exp_ms[kb])
            dvred = pb.do(
                "vector",
                lambda e, kp=kp: e.reduce_sum(
                    Dred[kp][:, :],
                    Ebuf[kp][:, :].rearrange("p (g q) -> p q g", g=8),
                    axis=AX.X,
                ),
                inc="dv",
            )
            pb.wait("scalar", dvred)

            def _lnf(e, kp=kp):
                return e.activation(out=Dln[kp][:, :], in_=Dred[kp][:, :], func=AF.Ln)

            def _exf(e, kp=kp):
                with nc.allow_low_precision(reason="Dinv feeds a bf16 matmul operand"):
                    return e.activation(out=Dinv[kp][:, :], in_=Dln[kp][:, :], func=AF.Exp, scale=-1.0)

            pb.do("scalar", _lnf)
            dvr = pb.do("scalar", _exf, inc="ac")
            pb.wait("vector", dvr)
            if kb >= 2:
                pb.wait("vector", pvk_kb[kb - 2])
            elif pvk_prev is not None:
                pb.wait("vector", pvk_prev)
            dv_P_ms[kb] = pb.do(
                "vector",
                lambda e, kp=kp: e.tensor_tensor(
                    out=Pbuf[kp][:, :].rearrange("p (g q) -> p g q", g=8),
                    in0=Ebuf[kp][:, :].rearrange("p (g q) -> p g q", g=8),
                    in1=Dinv[kp][:, :].unsqueeze(1).to_broadcast([128, 8, QS]),
                    op=ALU.mult,
                ),
                inc="dv",
            )
            if pend_kb is not None:
                emit_pv(pend_kb)
            pend_kb = kb
        emit_pv(pend_kb)
        pvk_prev = pvk_kb[NKB - 1]
        pb.wait("vector", pvk_prev)
        dv_ce = pb.do("vector", lambda e: e.tensor_copy(cTs[:, :], pCtx[:, :]), inc="dv")
        dv_ct2 = None
        for sub in range(2):
            t = 2 * qs + sub
            bt = t % 2
            pb.wait("sync", dv_ce)
            if t >= 1 and dv_fin.get(t - 1) is not None:
                pb.wait("sync", dv_fin[t - 1])
            pb.dma_sem_eng.setdefault(f"tc{sub}", "sync")
            for g in range(8):
                bt_ms = pb.do(
                    "sync",
                    lambda e, g=g, sub=sub: e.dma_start_transpose(
                        ctxb[:, 128 * g : 128 * g + 128],
                        cTs[:, 256 * g + 128 * sub : 256 * g + 128 * sub + 128],
                    ),
                    inc=f"tc{sub}", n=16,
                )
            dv_ct2 = bt_ms
            pb.wait("scalar", bt_ms)
            pb.wait("vector", bt_ms)
            emit_ln(pb, c, ctxb[:, :], y1[:, :], probs[:, 0:1024], wst)
            pb.wait("vector", ld_x[t])
            if t >= 2:
                pb.wait("vector", st_o[t - 2])
            dv_fin[t] = pb.do(
                "vector", lambda e, bt=bt: e.tensor_add(x3o[bt][:, :], xin[bt][:, :], y1[:, :]), inc="dv"
            )
            pb.wait("sync", dv_fin[t])
            st_o[t] = pb.dma("sync", scr_out[128 * t : 128 * t + 128, :], x3o[bt][:, :], f"so{bt}")
            if t + 2 < 2 * NQS:
                ld_q[t + 2] = pb.dma("sync", q_bf[bt][:, :], scr_q[128 * (t + 2) : 128 * (t + 2) + 128, :], f"lq{bt}")
                ld_x[t + 2] = pb.dma("sync", xin[bt][:, :], x_in[128 * (t + 2) : 128 * (t + 2) + 128, :], f"lx{bt}")

    pb.barrier()



def emit_l1_moe(pb, c, ctx, T, x_in, out_t):
    nc = pb.nc
    NT = T // 128
    NH = 2
    TH = T // NH
    NTH = TH // 128
    NE_ALL = 33

    xin = [ctx.enter_context(nc.sbuf_tensor(f"moe_xin{i}", [128, D], F32)) for i in range(2)]
    ln_bf = ctx.enter_context(nc.sbuf_tensor("moe_ln", [128, D], BF16))
    wst = ctx.enter_context(nc.sbuf_tensor("moe_wst", [128, 8], F32))
    tmpf = ctx.enter_context(nc.sbuf_tensor("moe_tmpf", [128, D], F32))
    gwT = ctx.enter_context(nc.sbuf_tensor("moe_gwTs", [128, 8 * 32], BF16))
    flatT = ctx.enter_context(nc.sbuf_tensor("moe_flatT", [128, 8 * TH], BF16))
    comb = ctx.enter_context(nc.sbuf_tensor("moe_comb", [128, 32 * NTH], F32))
    scor = ctx.enter_context(nc.sbuf_tensor("moe_scor", [128, 32], F32))
    gs8 = ctx.enter_context(nc.sbuf_tensor("moe_gs8", [128, 8], F32))
    top8 = ctx.enter_context(nc.sbuf_tensor("moe_top8", [128, 8], F32))
    msk = ctx.enter_context(nc.sbuf_tensor("moe_msk", [128, 32], F32))
    rsum = ctx.enter_context(nc.sbuf_tensor("moe_rsum", [128, 2], F32))
    acc = ctx.enter_context(nc.sbuf_tensor("moe_acc", [128, 1024 * NTH], F32))
    wg = [ctx.enter_context(nc.sbuf_tensor(f"moe_wgs{i}", [128, 8 * 512], BF16)) for i in range(2)]
    wu = [ctx.enter_context(nc.sbuf_tensor(f"moe_wus{i}", [128, 8 * 512], BF16)) for i in range(2)]
    wd = [ctx.enter_context(nc.sbuf_tensor(f"moe_wds{i}", [128, 4 * 1024], BF16)) for i in range(2)]
    hs = [ctx.enter_context(nc.sbuf_tensor(f"moe_hs{i}", [128, 512], F32)) for i in range(2)]
    h_bf = [ctx.enter_context(nc.sbuf_tensor(f"moe_h{i}", [128, 512], BF16)) for i in range(2)]
    hT = [ctx.enter_context(nc.sbuf_tensor(f"moe_hT{i}", [128, 512], BF16)) for i in range(2)]
    y1 = ctx.enter_context(nc.sbuf_tensor("moe_y1", [128, D], F32))
    oo = [ctx.enter_context(nc.sbuf_tensor(f"moe_oo{i}", [128, D], F32)) for i in range(2)]

    pA = [ctx.enter_context(nc.psum_tensor(f"moe_pA{i}", [128, 512], F32)) for i in range(2)]
    pB = [ctx.enter_context(nc.psum_tensor(f"moe_pB{i}", [128, 512], F32)) for i in range(2)]
    pt = ctx.enter_context(nc.psum_tensor("moe_pt", [128, 1024], BF16))
    pC = ctx.enter_context(nc.psum_tensor("moe_pC", [128, D], F32))
    pR = ctx.enter_context(nc.psum_tensor("moe_pR", [128, 32], F32))

    flatT3 = flatT[:, :].rearrange("p (c n) -> p c n", c=8)

    pb.dma("gpsimd", gwT[:, :].rearrange("p (c n) -> p c n", c=8),
           c["moe_gwt"].rearrange("(c p) n -> p c n", p=128), "wl")
    wl_gw = pb.now("wl")

    def load_expert(eidx, slot):
        if eidx < 32:
            wg_src = c["moe_wg"][eidx].rearrange("(c p) n -> p c n", p=128)
            wu_src = c["moe_wu"][eidx].rearrange("(c p) n -> p c n", p=128)
            wd_src = c["moe_wd"][eidx].rearrange("(c p) n -> p c n", p=128)
        else:
            wg_src = c["moe_sg"].rearrange("(c p) n -> p c n", p=128)
            wu_src = c["moe_su"].rearrange("(c p) n -> p c n", p=128)
            wd_src = c["moe_sd"].rearrange("(c p) n -> p c n", p=128)
        pb.dma("gpsimd", wg[slot][:, :].rearrange("p (c n) -> p c n", c=8), wg_src, f"we{slot}")
        pb.dma("gpsimd", wu[slot][:, :].rearrange("p (c n) -> p c n", c=8), wu_src, f"we{slot}")
        return pb.dma("gpsimd", wd[slot][:, :].rearrange("p (c n) -> p c n", c=4), wd_src, f"we{slot}")

    ld_x = {}
    st_o = {}
    for h in range(NH):
        t0 = h * NTH
        for t in range(min(2, NTH)):
            ld_x[t] = pb.dma("sync", xin[t % 2][:, :], x_in[128 * (t0 + t) : 128 * (t0 + t) + 128, :], f"lx{t % 2}")
        dv_fT = None
        for t in range(NTH):
            bt = t % 2
            pb.wait("vector", ld_x[t])
            pb.wait("scalar", ld_x[t])
            apply_v = emit_ln(pb, c, xin[bt][:, :], ln_bf[:, :], tmpf[:, :], wst)
            if t + 2 < NTH:
                pb.wait("sync", apply_v)
                ld_x[t + 2] = pb.dma("sync", xin[bt][:, :], x_in[128 * (t0 + t + 2) : 128 * (t0 + t + 2) + 128, :], f"lx{bt}")
            pb.wait("tensor", apply_v)
            if dv_fT is not None:
                pb.wait("tensor", dv_fT)
            pe_tr = emit_transpose8(pb, c, ln_bf, pt)
            pb.wait("vector", pe_tr)
            dv_fT = pb.do(
                "vector",
                lambda e, t=t: e.tensor_copy(
                    flatT3[:, :, 128 * t : 128 * t + 128],
                    pt[:, :].rearrange("p (c n) -> p c n", c=8),
                ),
                inc="dv",
            )
            pb.wait("tensor", dv_fT)
            pb.wait("tensor", pb.now("ac"))
            if t == 0 and h == 0:
                pb.wait("tensor", wl_gw)

            def lfn(e, t=t):
                ins = None
                for cc in range(8):
                    ins = e.matmul(
                        pR[:, 0:32],
                        flatT3[:, cc, 128 * t : 128 * t + 128],
                        gwT[:, 32 * cc : 32 * cc + 32],
                        start=(cc == 0), stop=(cc == 7),
                    )
                return ins

            pe_lg = pb.do("tensor", lfn, inc="pe")
            pb.wait("scalar", pe_lg)
            ac_sc = pb.do(
                "scalar",
                lambda e: e.activation(out=scor[:, :], in_=pR[:, 0:32], func=AF.Sigmoid),
                inc="ac",
            )
            pb.wait("vector", ac_sc)
            r1 = pb.do(
                "vector",
                lambda e: e.reduce_sum(
                    gs8[:, :], scor[:, :].rearrange("p (g i) -> p g i", g=8), axis=AX.X
                ),
                inc="dv",
            )
            pb.wait("vector", r1)
            r2 = pb.do("vector", lambda e: e.max(out=top8[:, :], in_=gs8[:, :]), inc="dv")
            pb.wait("vector", r2)
            r3 = pb.do(
                "vector",
                lambda e: e.tensor_scalar(msk[:, 0:8], gs8[:, :], top8[:, 3:4], None, op0=ALU.is_ge),
                inc="dv",
            )
            pb.wait("vector", r3)
            r4 = pb.do(
                "vector",
                lambda e: e.tensor_tensor(
                    out=msk[:, :].rearrange("p (g i) -> p g i", g=8),
                    in0=scor[:, :].rearrange("p (g i) -> p g i", g=8),
                    in1=msk[:, 0:8].unsqueeze(2).to_broadcast([128, 8, 4]),
                    op=ALU.mult,
                ),
                inc="dv",
            )
            pb.wait("vector", r4)
            r5 = pb.do("vector", lambda e: e.max(out=top8[:, :], in_=msk[:, :]), inc="dv")
            pb.wait("vector", r5)
            r6 = pb.do(
                "vector",
                lambda e, t=t: e.tensor_scalar(
                    comb[:, 32 * t : 32 * t + 32], msk[:, :], top8[:, 7:8], None, op0=ALU.is_ge
                ),
                inc="dv",
            )
            pb.wait("vector", r6)
            r7 = pb.do(
                "vector",
                lambda e, t=t: e.tensor_mul(
                    comb[:, 32 * t : 32 * t + 32], comb[:, 32 * t : 32 * t + 32], msk[:, :]
                ),
                inc="dv",
            )
            pb.wait("vector", r7)
            r8 = pb.do(
                "vector",
                lambda e, t=t: e.reduce_sum(rsum[:, 0:1], comb[:, 32 * t : 32 * t + 32], axis=AX.X),
                inc="dv",
            )
            pb.wait("vector", r8)
            r9 = pb.do("vector", lambda e: e.tensor_scalar_max(rsum[:, 0:1], rsum[:, 0:1], 1e-12), inc="dv")
            pb.wait("vector", r9)
            r10 = pb.do("vector", lambda e: e.reciprocal(rsum[:, 1:2], rsum[:, 0:1]), inc="dv")
            pb.wait("vector", r10)
            pb.do(
                "vector",
                lambda e, t=t: e.tensor_scalar(
                    comb[:, 32 * t : 32 * t + 32], comb[:, 32 * t : 32 * t + 32],
                    rsum[:, 1:2], None, op0=ALU.mult,
                ),
                inc="dv",
            )
        pb.barrier()

        ld_w = {}
        ld_w[0] = load_expert(32, 0)
        ld_w[1] = load_expert(0, 1)
        seq = [(ei, t) for ei in range(NE_ALL) for t in range(NTH)]
        ac_si_ms, dv_h_ms, dv_hT_ms, dv_acc_ms, pe_dn_ms = {}, {}, {}, {}, {}
        pend = None

        def emit_tail(idx, ei, t, eidx, s2, slot):
            pb.wait("tensor", dv_h_ms[idx])
            if idx >= 1:
                pb.wait("tensor", dv_hT_ms[idx - 1])

            def tfn(e, s2=s2):
                ins = None
                for j in range(4):
                    ins = e.transpose(
                        pt[:, 128 * j : 128 * j + 128],
                        h_bf[s2][:, 128 * j : 128 * j + 128],
                        c["ident_bf"][:, :],
                    )
                return ins

            pe_th = pb.do("tensor", tfn, inc="pe")
            pb.wait("vector", pe_th)
            dv_hT_ms[idx] = pb.do(
                "vector", lambda e, s2=s2: e.tensor_copy(hT[s2][:, :], pt[:, 0:512]), inc="dv"
            )
            pb.wait("tensor", dv_hT_ms[idx])
            if idx >= 1:
                pb.wait("tensor", dv_acc_ms[idx - 1])

            def dfn(e, s2=s2, slot=slot):
                ins = None
                for j in range(4):
                    for half in range(2):
                        ins = e.matmul(
                            pC[:, 512 * half : 512 * half + 512],
                            hT[s2][:, 128 * j : 128 * j + 128],
                            wd[slot][:, 1024 * j + 512 * half : 1024 * j + 512 * half + 512],
                            start=(j == 0), stop=(j == 3),
                        )
                return ins

            pe_dn_ms[idx] = pb.do("tensor", dfn, inc="pe")
            pb.wait("vector", pe_dn_ms[idx])
            if ei == 0:
                dv_acc_ms[idx] = pb.do(
                    "vector",
                    lambda e, t=t: e.tensor_copy(acc[:, 1024 * t : 1024 * t + 1024], pC[:, :]),
                    inc="dv",
                )
            else:
                dv_acc_ms[idx] = pb.do(
                    "vector",
                    lambda e, t=t, eidx=eidx: e.scalar_tensor_tensor(
                        out=acc[:, 1024 * t : 1024 * t + 1024], in0=pC[:, :],
                        scalar=comb[:, 32 * t + eidx : 32 * t + eidx + 1],
                        in1=acc[:, 1024 * t : 1024 * t + 1024],
                        op0=ALU.mult, op1=ALU.add,
                    ),
                    inc="dv",
                )
            if t == NTH - 1 and ei + 2 < NE_ALL:
                pb.wait("gpsimd", pe_dn_ms[idx])
                ld_w[ei + 2] = load_expert(ei + 1, slot)

        for idx, (ei, t) in enumerate(seq):
            slot = ei % 2
            s2 = idx % 2
            eidx = 32 if ei == 0 else ei - 1
            if t == 0:
                pb.wait("tensor", ld_w[ei])
            if idx >= 2:
                pb.wait("tensor", ac_si_ms[idx - 2])
                pb.wait("tensor", dv_h_ms[idx - 2])

            def gfn(e, t=t, slot=slot, s2=s2):
                ins = None
                for cc in range(8):
                    ins = e.matmul(
                        pA[s2][:, 0:512],
                        flatT3[:, cc, 128 * t : 128 * t + 128],
                        wg[slot][:, 512 * cc : 512 * cc + 512],
                        start=(cc == 0), stop=(cc == 7),
                    )
                return ins

            def ufn(e, t=t, slot=slot, s2=s2):
                ins = None
                for cc in range(8):
                    ins = e.matmul(
                        pB[s2][:, 0:512],
                        flatT3[:, cc, 128 * t : 128 * t + 128],
                        wu[slot][:, 512 * cc : 512 * cc + 512],
                        start=(cc == 0), stop=(cc == 7),
                    )
                return ins

            pe_g = pb.do("tensor", gfn, inc="pe")
            pe_u = pb.do("tensor", ufn, inc="pe")
            pb.wait("scalar", pe_g)
            if idx >= 2:
                pb.wait("scalar", dv_h_ms[idx - 2])
            ac_si_ms[idx] = pb.do(
                "scalar",
                lambda e, s2=s2: e.activation(out=hs[s2][:, :], in_=pA[s2][:, :], func=AF.Silu),
                inc="ac",
            )
            pb.wait("vector", ac_si_ms[idx])
            pb.wait("vector", pe_u)
            dv_h_ms[idx] = pb.do(
                "vector",
                lambda e, s2=s2: e.tensor_mul(h_bf[s2][:, :], hs[s2][:, :], pB[s2][:, :]),
                inc="dv",
            )
            if pend is not None:
                emit_tail(*pend)
            pend = (idx, ei, t, eidx, s2, slot)
        emit_tail(*pend)
        pend = None
        pb.barrier()

        for t in range(min(2, NTH)):
            ld_x[t] = pb.dma("sync", xin[t % 2][:, :], x_in[128 * (t0 + t) : 128 * (t0 + t) + 128, :], f"lx{t % 2}")
        for t in range(NTH):
            bt = t % 2
            pb.wait("vector", ld_x[t])
            pb.wait("scalar", ld_x[t])
            emit_ln(pb, c, acc[:, 1024 * t : 1024 * t + 1024], y1[:, :], tmpf[:, :], wst)
            if t >= 2:
                pb.wait("vector", st_o[t - 2])
            dv_f = pb.do(
                "vector", lambda e, bt=bt: e.tensor_add(oo[bt][:, :], xin[bt][:, :], y1[:, :]), inc="dv"
            )
            pb.wait("sync", dv_f)
            st_o[t] = pb.dma("sync", out_t[128 * (t0 + t) : 128 * (t0 + t) + 128, :], oo[bt][:, :], f"so{bt}")
            if t + 2 < NTH:
                pb.wait("sync", dv_f)
                ld_x[t + 2] = pb.dma("sync", xin[bt][:, :], x_in[128 * (t0 + t + 2) : 128 * (t0 + t + 2) + 128, :], f"lx{bt}")
        pb.barrier()



def build_program(T=4096, dev=False, stages=("l0a", "l0f", "l1a", "moe"), n_sweeps=6, paranoid=True):
    nc = bass.Bass("TRN2", target_bir_lowering=False, detect_race_conditions=False)
    scratch_kind = "ExternalOutput" if dev else "Internal"

    c = {}
    c["x"] = nc.dram_tensor("x", [T, D], F32, kind="ExternalInput")[:, :]
    for nm, shp in [
        ("l0_wqt", [D, D]), ("l0_wkt", [D, D]), ("l0_wvt", [D, D]), ("l0_wot", [D, D]),
        ("l0_bq", [1, D]), ("l0_bk", [1, D]), ("l0_bv", [1, D]), ("l0_bo", [1, D]),
        ("l0f_wrt", [D, D]), ("l0f_wkt", [D, D]), ("l0f_wvt", [D, D]),
        ("l0f_mur", [128, 8]), ("l0f_muk", [128, 8]),
        ("l1_wq", [D, D]), ("l1_wk", [D, D]), ("l1_wv", [D, D]),
        ("l1_bq", [1, D]), ("l1_bk", [1, D]), ("l1_bv", [1, D]),
        ("moe_gwt", [D, 32]),
        ("moe_sg", [D, 512]), ("moe_su", [D, 512]), ("moe_sd", [512, D]),
    ]:
        c[nm] = nc.dram_tensor(nm, shp, F32, kind="ExternalInput")[:, :]
    c["moe_wg"] = nc.dram_tensor("moe_wg", [32, D, 512], F32, kind="ExternalInput")[:, :, :]
    c["moe_wu"] = nc.dram_tensor("moe_wu", [32, D, 512], F32, kind="ExternalInput")[:, :, :]
    c["moe_wd"] = nc.dram_tensor("moe_wd", [32, 512, D], F32, kind="ExternalInput")[:, :, :]
    identf = nc.dram_tensor("identf", [128, 128], F32, kind="ExternalInput")[:, :]
    identb = nc.dram_tensor("identb", [128, 128], BF16, kind="ExternalInput")[:, :]
    onesr = nc.dram_tensor("onesr", [1, 128], BF16, kind="ExternalInput")[:, :]
    onesc = nc.dram_tensor("onesc", [128, 1], BF16, kind="ExternalInput")[:, :]
    c["zeros8"] = nc.dram_tensor("zeros8", [128, 8], BF16, kind="ExternalInput")[:, :]

    scr_q = nc.dram_tensor("scr_q", [T, D], BF16, kind=scratch_kind)[:, :]
    scr_k = nc.dram_tensor("scr_k", [T, D], BF16, kind=scratch_kind)[:, :]
    scr_v = nc.dram_tensor("scr_v", [T, D], BF16, kind=scratch_kind)[:, :]
    scr_a = nc.dram_tensor("scr_a", [T, D], F32, kind="ExternalOutput" if dev else "Internal")[:, :]
    scr_b = nc.dram_tensor("scr_b", [T, D], F32, kind="ExternalOutput" if dev else "Internal")[:, :]
    out_t = nc.dram_tensor("out", [T, D], F32, kind="ExternalOutput")[:, :]
    scr_dbg = nc.dram_tensor("scr_dbg", [64, 8192], F32, kind="ExternalOutput" if dev else "Internal")[:, :]

    pb = PB(nc, paranoid=paranoid)
    with ExitStack() as ctx:
        block = ctx.enter_context(nc.Block())
        pb.block = block
        ident_f32 = nc.alloc_sbuf_tensor("g_identf", [128, 128], F32)
        ident_bf = nc.alloc_sbuf_tensor("g_identb", [128, 128], BF16)
        ones_row = nc.alloc_sbuf_tensor("g_onesr", [1, 128], BF16)
        ones_col = nc.alloc_sbuf_tensor("g_onesc", [128, 1], BF16)
        lneps = nc.alloc_sbuf_tensor("g_lneps", [128, 1], F32)
        c30 = nc.alloc_sbuf_tensor("g_c30", [128, 1], F32)
        pb.dma("sync", ident_f32[:, :], identf, "lc")
        pb.dma("sync", ident_bf[:, :], identb, "lc")
        pb.dma("sync", ones_row[:, :], onesr, "lc")
        ldc = pb.dma("sync", ones_col[:, :], onesc, "lc")
        pb.do("vector", lambda e: e.memset(lneps[:, 0:1], LNEPS))
        mz = pb.do("vector", lambda e: e.memset(c30[:, 0:1], -30.0), inc="dv")
        for eng in ("tensor", "vector", "scalar"):
            pb.wait(eng, ldc)
        pb.wait("scalar", mz)
        cc = dict(c)
        cc["lneps"] = lneps
        cc["c30"] = c30
        cc["ident_f32"] = ident_f32
        cc["ident_bf"] = ident_bf
        cc["ones_row"] = ones_row
        cc["ones_col"] = ones_col

        cc["scr_dbg"] = scr_dbg
        if "l0a" in stages:
            with ExitStack() as sctx:
                emit_l0_attn(pb, cc, sctx, T, c["x"], scr_q, scr_k, scr_v, scr_a, n_sweeps=n_sweeps)
                pb.flush(block)
        if "l0f" in stages:
            with ExitStack() as sctx:
                emit_l0_ffn(pb, cc, sctx, T, scr_a, scr_b)
                pb.flush(block)
        if "l1a" in stages:
            with ExitStack() as sctx:
                emit_l1_attn(pb, cc, sctx, T, scr_b, scr_q, scr_a)
                pb.flush(block)
        if "moe" in stages:
            with ExitStack() as sctx:
                emit_l1_moe(pb, cc, sctx, T, scr_a, out_t)
                pb.flush(block)

        pb.flush(block)
    return nc


def prep_inputs(**inp):
    tr = lambda a: np.ascontiguousarray(np.asarray(a, dtype=np.float32).T)
    f = lambda k: np.asarray(inp[k], dtype=np.float32)
    m = {
        "l0_wqt": tr(inp["L0_attn_Wq"]),
        "l0_wkt": tr(inp["L0_attn_Wk"]),
        "l0_wvt": tr(inp["L0_attn_Wv"]),
        "l0_wot": tr(inp["L0_attn_Wo"]),
        "l0_bq": f("L0_attn_bq").reshape(1, D),
        "l0_bk": f("L0_attn_bk").reshape(1, D),
        "l0_bv": f("L0_attn_bv").reshape(1, D),
        "l0_bo": f("L0_attn_bo").reshape(1, D),
        "l0f_wrt": tr(inp["L0_ffn_W_r"]),
        "l0f_wkt": tr(inp["L0_ffn_W_k"]),
        "l0f_wvt": tr(inp["L0_ffn_W_v"]),
        "l0f_mur": np.ascontiguousarray(f("L0_ffn_mu_r").reshape(8, 128).T),
        "l0f_muk": np.ascontiguousarray(f("L0_ffn_mu_k").reshape(8, 128).T),
        "l1_wq": np.ascontiguousarray(f("L1_attn_Wq")),
        "l1_wk": np.ascontiguousarray(f("L1_attn_Wk")),
        "l1_wv": np.ascontiguousarray(f("L1_attn_Wv")),
        "l1_bq": f("L1_attn_bq").reshape(1, D),
        "l1_bk": f("L1_attn_bk").reshape(1, D),
        "l1_bv": f("L1_attn_bv").reshape(1, D),
        "moe_gwt": tr(inp["L1_ffn_gate_w"]),
        "moe_wg": np.ascontiguousarray(np.transpose(f("L1_ffn_expert_gate"), (0, 2, 1))),
        "moe_wu": np.ascontiguousarray(np.transpose(f("L1_ffn_expert_up"), (0, 2, 1))),
        "moe_wd": np.ascontiguousarray(np.transpose(f("L1_ffn_expert_down"), (0, 2, 1))),
        "moe_sg": tr(inp["L1_ffn_shared_gate_w"]),
        "moe_su": tr(inp["L1_ffn_shared_up_w"]),
        "moe_sd": tr(inp["L1_ffn_shared_down_w"]),
        "identf": np.eye(128, dtype=np.float32),
        "identb": np.eye(128, dtype=ml_dtypes.bfloat16),
        "onesr": np.ones((1, 128), dtype=ml_dtypes.bfloat16),
        "onesc": np.ones((128, 1), dtype=ml_dtypes.bfloat16),
        "zeros8": np.zeros((128, 8), dtype=ml_dtypes.bfloat16),
    }
    return m


def kernel(**inp):
    x = np.asarray(inp["L0_attn_x"], dtype=np.float32)
    base = prep_inputs(**inp)
    nc = build_program(T=4096, dev=False)
    in_maps = []
    for b in range(8):
        mm = dict(base)
        mm["x"] = np.ascontiguousarray(x[b])
        in_maps.append(mm)
    res = run_bass_kernel_spmd(nc, in_maps, core_ids=list(range(8)))
    return np.stack([np.asarray(r["out"]) for r in res.results], axis=0)
